# version 03
# baseline (speedup 1.0000x reference)
import numpy as np
import concourse.bass as bass
import concourse.mybir as mybir
from concourse.bass_utils import run_bass_kernel_spmd

F32 = mybir.dt.float32
BF16 = mybir.dt.bfloat16
AF = mybir.ActivationFunctionType
ALU = mybir.AluOpType

B, L, D = 4, 2048, 4096
DH, NST, DTR, XPR = 2048, 16, 128, 160
NCH, LC = 4, 512
SBT = 256
NDT = 16
OH, NOT, NKT = 2048, 16, 32
EPS = 1e-5
RG = [[0, 1], [2, 3], [4, 5], [6, 7]]
SORT8 = [(0, 1), (2, 3), (4, 5), (6, 7), (0, 2), (1, 3), (4, 6), (5, 7),
         (1, 2), (5, 6), (0, 4), (3, 7), (1, 5), (2, 6), (1, 4), (3, 6),
         (2, 4), (3, 5), (3, 4)]
DEBUG = False
SERIAL_EMIT = False


class Prog:
    ENGINES = ("sync", "vector", "scalar", "gpsimd", "tensor")

    def __init__(self):
        self.ops = []
        self.last_w = {}
        self.readers = {}
        self.fams = {}
        self.fam_R = {}

    def op(self, eng, fn, reads=(), writes=(), kind="c", name="", fam=None, R=0):
        deps = set()
        for kk in reads:
            if kk in self.last_w:
                deps.add(self.last_w[kk])
        for kk in writes:
            for r in self.readers.get(kk, ()):
                deps.add(r)
            if kk in self.last_w:
                deps.add(self.last_w[kk])
        i = len(self.ops)
        if kind in ("d", "cc"):
            assert fam is not None, name
            self.fams.setdefault(fam, []).append(i)
            self.fam_R[fam] = max(self.fam_R.get(fam, 0), R)
        self.ops.append(dict(eng=eng, fn=fn, deps=deps, kind=kind, name=name, fam=fam))
        for kk in reads:
            self.readers.setdefault(kk, []).append(i)
        for kk in writes:
            self.last_w[kk] = i
            self.readers[kk] = []
        return i

    def finalize(self):
        ops = self.ops
        self.need_inc = [False] * len(ops)
        for o in ops:
            for d in o["deps"]:
                p = ops[d]
                if p["kind"] == "c" and p["eng"] == o["eng"]:
                    continue
                if p["kind"] == "c":
                    self.need_inc[d] = True
        cnt = {}
        self.val = [0] * len(ops)
        for i, o in enumerate(ops):
            if o["kind"] == "c" and self.need_inc[i]:
                ch = (o["eng"], "c")
                cnt[ch] = cnt.get(ch, 0) + 1
                self.val[i] = cnt[ch]
        self.fam_idx = {}
        for fam, lst in self.fams.items():
            for q, gi in enumerate(lst):
                self.fam_idx[gi] = q

    def check_deadlock(self):
        import bisect
        ops = self.ops
        streams = {en: [i for i in range(len(ops)) if ops[i]["eng"] == en] for en in self.ENGINES}
        pos = {en: 0 for en in self.ENGINES}
        cnt = {}

        def waits_of(i):
            o = ops[i]
            ws = []
            for d in o["deps"]:
                p = ops[d]
                if p["kind"] == "c":
                    if p["eng"] == o["eng"]:
                        continue
                    ws.append((("c", p["eng"]), self.val[d]))
                else:
                    fam = p["fam"]
                    lst = self.fams[fam]
                    cx = max(bisect.bisect_left(lst, i), self.fam_idx[d] + 1)
                    tgt = min(cx + self.fam_R[fam], len(lst))
                    if ops[i]["kind"] == p["kind"] and ops[i].get("fam") == fam:
                        tgt = min(tgt, self.fam_idx[i])
                    tgt = max(tgt, self.fam_idx[d] + 1)
                    step = 16 if p["kind"] == "d" else 1
                    ws.append((("f", fam), tgt * step))
            return ws

        progressed = True
        while progressed:
            progressed = False
            for en in self.ENGINES:
                while pos[en] < len(streams[en]):
                    i = streams[en][pos[en]]
                    if any(cnt.get(ch, 0) < v for ch, v in waits_of(i)):
                        break
                    o = ops[i]
                    if o["kind"] == "c":
                        if self.need_inc[i]:
                            ch = ("c", en)
                            cnt[ch] = cnt.get(ch, 0) + 1
                    else:
                        step = 16 if o["kind"] == "d" else 1
                        ch = ("f", o["fam"])
                        cnt[ch] = cnt.get(ch, 0) + step
                    pos[en] += 1
                    progressed = True
        stuck = {en: pos[en] for en in self.ENGINES if pos[en] < len(streams[en])}
        if stuck:
            for en, p in stuck.items():
                i = streams[en][p]
                o = ops[i]
                print(f"DEADLOCK: {en} pos {p}/{len(streams[en])} op {i} kind={o['kind']} fam={o['fam']} name={o['name']}")
                for d in sorted(o["deps"]):
                    pp = ops[d]
                    print(f"   dep {d}: eng={pp['eng']} kind={pp['kind']} fam={pp['fam']} name={pp['name']} val={self.val[d]}")
                print("   waits:", waits_of(i), "cur:", {ch: cnt.get(ch, 0) for ch, v in waits_of(i)})
            raise RuntimeError("deadlock detected")
        print("deadlock check: OK")

    def emit_engine(self, ename, eng, chans):
        ops = self.ops
        import bisect
        waited = {}
        for i, o in enumerate(ops):
            if o["eng"] != ename:
                continue
            for d in sorted(o["deps"]):
                p = ops[d]
                if p["kind"] == "c":
                    if p["eng"] == ename:
                        continue
                    ch = ("c", p["eng"])
                    v = self.val[d]
                    assert v > 0, (i, o["name"], d, p["name"])
                    sem = chans[(p["eng"], "c")]
                else:
                    fam = p["fam"]
                    lst = self.fams[fam]
                    cx = bisect.bisect_left(lst, i)
                    cx = max(cx, self.fam_idx[d] + 1)
                    tgt = min(cx + self.fam_R[fam], len(lst))
                    if o["kind"] == p["kind"] and o.get("fam") == fam:
                        tgt = min(tgt, self.fam_idx[i])
                    tgt = max(tgt, self.fam_idx[d] + 1)
                    step = 16 if p["kind"] == "d" else 1
                    ch = ("f", fam)
                    v = tgt * step
                    sem = chans[("fam", fam)]
                if waited.get(ch, 0) >= v:
                    continue
                eng.wait_ge(sem, v)
                waited[ch] = v
            inst = o["fn"](eng)
            if o["kind"] == "c":
                if self.need_inc[i]:
                    inst.then_inc(chans[(ename, "c")], 1)
            else:
                step = 16 if o["kind"] == "d" else 1
                inst.then_inc(chans[("fam", o["fam"])], step)


def build_nc():
    nc = bass.Bass("TRN2", target_bir_lowering=False)
    P = Prog()

    xt = nc.dram_tensor("xt", [D, L], F32, kind="ExternalInput").ap()
    xpw = nc.dram_tensor("xpw", [128, 32 * XPR], F32, kind="ExternalInput").ap()
    dtwi = nc.dram_tensor("dtwi", [128, DH], F32, kind="ExternalInput").ap()
    alog = nc.dram_tensor("alog", [128, NDT * NST], F32, kind="ExternalInput").ap()
    dvec = nc.dram_tensor("dvec", [128, NDT], F32, kind="ExternalInput").ap()
    dtbi = nc.dram_tensor("dtbi", [128, NDT], F32, kind="ExternalInput").ap()
    wbs = nc.dram_tensor("wbs", [NOT, 128, NKT * 128], F32, kind="ExternalInput").ap()
    bbt = nc.dram_tensor("bbt", [128, NOT], F32, kind="ExternalInput").ap()
    wrti = nc.dram_tensor("wrti", [128, NKT * 8], F32, kind="ExternalInput").ap()
    brt = nc.dram_tensor("brt", [8, 1], F32, kind="ExternalInput").ap()
    las = nc.dram_tensor("las", [128, NKT * 128], F32, kind="ExternalInput").ap()
    lbsi = nc.dram_tensor("lbsi", [128, OH], F32, kind="ExternalInput").ap()
    idni = nc.dram_tensor("idni", [128, 128], F32, kind="ExternalInput").ap()
    sel8i = nc.dram_tensor("sel8i", [8, 128], F32, kind="ExternalInput").ap()
    out_c = nc.dram_tensor("out_c", [OH, L], F32, kind="ExternalOutput").ap()
    if DEBUG:
        dbg_res = nc.dram_tensor("dbg_res", [DH, L], F32, kind="ExternalOutput").ap()
        dbg_xn = nc.dram_tensor("dbg_xn", [D, L], F32, kind="ExternalOutput").ap()
        dbg_cr = nc.dram_tensor("dbg_cr", [128, NDT * NST], F32, kind="ExternalOutput").ap()
        dbg_bp = nc.dram_tensor("dbg_bp", [128, 8 * LC], F32, kind="ExternalOutput").ap()
        dbg_cp = nc.dram_tensor("dbg_cp", [128, 8 * LC], F32, kind="ExternalOutput").ap()
        dbg_dt = nc.dram_tensor("dbg_dt", [128, LC], F32, kind="ExternalOutput").ap()
        dbg_v = nc.dram_tensor("dbg_v", [128, LC], F32, kind="ExternalOutput").ap()

    xph_hi = nc.dram_tensor("xph_hi", [128, 32 * XPR], BF16).ap()
    xph_lo = nc.dram_tensor("xph_lo", [128, 32 * XPR], BF16).ap()
    lbh_hi = nc.dram_tensor("lbh_hi", [128, OH], BF16).ap()
    lbh_lo = nc.dram_tensor("lbh_lo", [128, OH], BF16).ap()
    lah_hi = nc.dram_tensor("lah_hi", [128, NKT * 128], BF16).ap()
    lah_lo = nc.dram_tensor("lah_lo", [128, NKT * 128], BF16).ap()
    wh_hi = nc.dram_tensor("wh_hi", [NOT, 128, NKT * 128], BF16).ap()
    wh_lo = nc.dram_tensor("wh_lo", [NOT, 128, NKT * 128], BF16).ap()
    yscr = nc.dram_tensor("yscr", [DH, LC], F32).ap()
    bchbm = nc.dram_tensor("bchbm", [32, LC], F32).ap()
    r2scr = nc.dram_tensor("r2scr", [OH, LC], F32).ap()
    ag1i = [nc.dram_tensor(f"ag1i{k}", [DH, LC], F32).ap() for k in range(NCH)]
    ag1o = [nc.dram_tensor(f"ag1o{k}", [2 * DH, LC], F32).ap() for k in range(NCH)]
    ag2i = [nc.dram_tensor(f"ag2i{k}", [2, LC], F32).ap() for k in range(NCH)]
    ag2o = [nc.dram_tensor(f"ag2o{k}", [2, LC], F32).ap() for k in range(NCH)]

    from contextlib import ExitStack
    stack = ExitStack()
    e = stack.enter_context

    sb = {}
    def S(name, shape, dt=F32):
        sb[name] = e(nc.sbuf_tensor(name, shape, dt))
        return sb[name]

    S("ones_r", [1, 128]); S("ones_c", [128, 1])
    S("idn", [128, 128]); S("sel8", [8, 128])
    S("A", [128, NDT * NST]); S("onepD", [128, NDT]); S("dtb", [128, NDT])
    S("bb", [128, NOT]); S("br", [8, 1])
    S("dtw_hi", [128, DH], BF16); S("dtw_lo", [128, DH], BF16)
    S("wr_hi", [128, NKT * 8], BF16); S("wr_lo", [128, NKT * 8], BF16)
    S("carry", [128, NDT * NST])
    S("xpw_hi", [128, 4 * XPR], BF16); S("xpw_lo", [128, 4 * XPR], BF16)
    S("Bp", [128, 8 * LC]); S("Cp", [128, 8 * LC])
    S("dAz0", [128, 4 * LC]); S("dAz1", [128, 4 * LC])
    S("dBx0", [128, 4 * LC]); S("dBx1", [128, 4 * LC])
    S("hp", [128, 4 * LC])
    for j in range(2):
        S(f"xhi{j}", [128, LC], BF16); S(f"xlo{j}", [128, LC], BF16)
        S(f"dtt{j}", [128, LC]); S(f"xot{j}", [128, LC]); S(f"vt{j}", [128, LC])
        S(f"yh{j}", [128, LC]); S(f"res{j}", [128, LC])
    S("ysc", [128, LC]); S("zero4", [128, 4]); S("tc4", [128, 4])
    S("zz", [128, 4 * LC])
    S("lbt_hi0", [128, 128], BF16); S("lbt_lo0", [128, 128], BF16)
    S("lbt_hi1", [128, 128], BF16); S("lbt_lo1", [128, 128], BF16)
    S("dtin_hi", [128, LC], BF16); S("dtin_lo", [128, LC], BF16)
    S("bc_c", [32, LC])
    S("stat_sb", [64, LC])
    S("xn_hi", [128, NKT * SBT], BF16); S("xn_lo", [128, NKT * SBT], BF16)
    S("wla", [128, 8192], BF16)
    for j in range(2):
        S(f"gt{j}", [128, SBT])
        S(f"fft{j}", [128, SBT]); S(f"rr{j}", [128, SBT])
        S(f"r2l{j}", [128, LC])
    S("sq2", [128, SBT])
    S("sq2b", [128, LC])
    S("stv", [1, 3 * LC])
    S("mbc", [128, LC]); S("rbc", [128, LC])
    S("rt8", [8, SBT]); S("lgt", [128, 16])
    S("sortp", [128, 48])
    S("wfin", [128, 16]); S("wt8", [8, SBT]); S("wbc", [128, SBT])
    S("lh", [128, SBT])
    S("hw_hi", [128, SBT], BF16); S("hw_lo", [128, SBT], BF16)

    wla = sb["wla"]
    wla_f32 = wla[:].bitcast(F32)
    wf32 = wla_f32[:, 0:2048]
    cst_hi = wla[:, 4096:6144]
    cst_lo = wla[:, 6144:8192]
    WQ = ("wla0", "wla1", "wla2", "wla3")

    ps = {}
    def PSUM(name, shape):
        ps[name] = e(nc.psum_tensor(name, shape, F32))
    PSUM("xp1", [128, LC])
    PSUM("xp2", [64, LC])
    PSUM("bcb", [128, LC])
    PSUM("st", [64, LC])
    PSUM("rtl", [128, SBT])
    PSUM("misc", [128, LC])
    PSUM("ff0", [128, SBT]); PSUM("ff1", [128, SBT])

    chans = {}
    for en in Prog.ENGINES:
        chans[(en, "c")] = e(nc.semaphore(f"c_{en}"))
    FAMS = ["consts", "xphw", "lahw", "wbmw", "wla", "xlr0", "xlr1", "xot0", "xot1", "xpw",
            "bp", "cp", "gt0", "gt1", "rr0", "rr1", "yscl", "bchbm", "stv", "out",
            "yscr0", "yscr1", "r2s0", "r2s1", "agw0", "agw1", "arw", "ag2w", "lbhw", "lbt",
            "cag0", "cag1", "car0", "car1", "car20", "car21", "dbg"]
    for f in FAMS:
        chans[("fam", f)] = e(nc.semaphore(f"f_{f}"))

    def dma(eng, out_ap, in_ap, reads, writes, name="", fam="consts", R=0):
        P.op(eng, lambda E, o=out_ap, i=in_ap: E.dma_start(out=o, in_=i),
             reads=reads, writes=writes, kind="d", name=name, fam=fam, R=R)

    C = P.op

    C("vector", lambda E: E.memset(sb["ones_r"][:], 1.0), (), ("ones_r",))
    C("vector", lambda E: E.memset(sb["ones_c"][:], 1.0), (), ("ones_c",))
    C("vector", lambda E: E.memset(sb["zero4"][:], 0.0), (), ("zero4",))
    dma("sync", sb["idn"][:], idni, (), ("idn",))
    dma("sync", sb["sel8"][:], sel8i, (), ("sel8",))
    dma("sync", sb["A"][:], alog, (), ("A",))
    C("scalar", lambda E: E.activation(sb["A"][:], sb["A"][:], AF.Exp), ("A",), ("A",))
    C("vector", lambda E: E.tensor_single_scalar(sb["A"][:], sb["A"][:], -1.0, ALU.mult), ("A",), ("A",))
    dma("sync", sb["onepD"][:], dvec, (), ("onepD",))
    C("vector", lambda E: E.tensor_single_scalar(sb["onepD"][:], sb["onepD"][:], 1.0, ALU.add), ("onepD",), ("onepD",))
    dma("sync", sb["dtb"][:], dtbi, (), ("dtb",))
    dma("sync", sb["bb"][:], bbt, (), ("bb",))
    dma("sync", sb["br"][:], brt, (), ("br",))

    def stage_cast(src_ap, cols, hi_dst, hi_key, lo_dst, lo_key):
        dma("sync", wf32[:, 0:cols], src_ap, ("wla0", "wla1"), ("wla0", "wla1"), fam="wla")
        C("scalar", lambda E, c=cols, h=hi_dst: E.activation(h, wf32[:, 0:c], AF.Copy),
          ("wla0", "wla1"), (hi_key,))
        C("vector", lambda E, c=cols, h=hi_dst, l=lo_dst: E.tensor_tensor(l, wf32[:, 0:c], h, ALU.subtract),
          ("wla0", "wla1", hi_key), (lo_key,))

    stage_cast(dtwi, DH, sb["dtw_hi"][:], "dtw_hi", sb["dtw_lo"][:], "dtw_lo")
    stage_cast(wrti, NKT * 8, sb["wr_hi"][:], "wr_hi", sb["wr_lo"][:], "wr_lo")

    def cast_to_hbm(nm, src_ap, dst_hi, dst_lo, total):
        off = 0
        while off < total:
            w = min(2048, total - off)
            dma("sync", wf32[:, 0:w], src_ap[:, off:off + w], ("wla0", "wla1"), ("wla0", "wla1"), fam="wla")
            C("scalar", lambda E, w=w: E.activation(cst_hi[:, 0:w], wf32[:, 0:w], AF.Copy),
              ("wla0", "wla1"), ("wla2",))
            C("vector", lambda E, w=w: E.tensor_tensor(cst_lo[:, 0:w], wf32[:, 0:w], cst_hi[:, 0:w], ALU.subtract),
              ("wla0", "wla1", "wla2"), ("wla3",))
            dma("sync", dst_hi[:, off:off + w], cst_hi[:, 0:w], ("wla2",), (f"{nm}H",), fam=nm + "w")
            dma("sync", dst_lo[:, off:off + w], cst_lo[:, 0:w], ("wla3",), (f"{nm}L",), fam=nm + "w")
            off += w

    cast_to_hbm("xph", xpw, xph_hi, xph_lo, 32 * XPR)
    cast_to_hbm("lah", las, lah_hi, lah_lo, NKT * 128)
    cast_to_hbm("lbh", lbsi, lbh_hi, lbh_lo, OH)

    wblocks = [(o, hf) for o in range(NOT) for hf in range(2)]
    widx = [0]

    def emit_wcast(nblk):
        for _ in range(nblk):
            if widx[0] >= len(wblocks):
                return
            o, hf = wblocks[widx[0]]
            widx[0] += 1
            sl = slice(hf * 2048, (hf + 1) * 2048)
            dma("sync", wf32[:], wbs[o, :, sl], ("wla0", "wla1"), ("wla0", "wla1"), fam="wla")
            C("scalar", lambda E: E.activation(cst_hi[:], wf32[:], AF.Copy),
              ("wla0", "wla1"), ("wla2",))
            C("vector", lambda E: E.tensor_tensor(cst_lo[:], wf32[:], cst_hi[:], ALU.subtract),
              ("wla0", "wla1", "wla2"), ("wla3",))
            dma("sync", wh_hi[o, :, sl], cst_hi[:], ("wla2",), (f"wbm{o}_{hf}H",), fam="wbmw")
            dma("sync", wh_lo[o, :, sl], cst_lo[:], ("wla3",), (f"wbm{o}_{hf}L",), fam="wbmw")

    def scan_side(k):
        c0 = k * LC
        n_mm = [0]
        for pc in range(8):
            poff = pc * 4 * XPR
            dma("sync", sb["xpw_hi"][:], xph_hi[:, poff:poff + 4 * XPR], ("xphH", "xpw_hi"), ("xpw_hi",), fam="xpw")
            dma("sync", sb["xpw_lo"][:], xph_lo[:, poff:poff + 4 * XPR], ("xphL", "xpw_lo"), ("xpw_lo",), fam="xpw")
            for tl in range(4):
                t = pc * 4 + tl
                j = t % 2
                dma("sync", sb[f"r2l{j}"][:], xt[t * 128:(t + 1) * 128, c0:c0 + LC], (), (f"xlr{j}",), fam=f"xlr{j}")
                C("scalar", lambda E, j=j: E.activation(sb[f"xhi{j}"][:], sb[f"r2l{j}"][:], AF.Copy),
                  (f"xlr{j}",), (f"xhi{j}",))
                C("vector", lambda E, j=j: E.tensor_tensor(sb[f"xlo{j}"][:], sb[f"r2l{j}"][:], sb[f"xhi{j}"][:], ALU.subtract),
                  (f"xlr{j}", f"xhi{j}"), (f"xlo{j}",))
                for (wk, xk) in (("hi", "hi"), ("hi", "lo"), ("lo", "hi")):
                    s0 = (n_mm[0] == 0)
                    s1 = (n_mm[0] == 95)
                    C("tensor", lambda E, wk=wk, xk=xk, j=j, tl=tl, s0=s0, s1=s1: E.matmul(
                        ps["xp1"][:], sb[f"xpw_{wk}"][:, tl * XPR:tl * XPR + 128], sb[f"x{xk}{j}"][:],
                        start=s0, stop=s1),
                        (f"xpw_{wk}", f"x{xk}{j}"), ("xp1",))
                    C("tensor", lambda E, wk=wk, xk=xk, j=j, tl=tl, s0=s0, s1=s1: E.matmul(
                        ps["bcb"][0:32, :], sb[f"xpw_{wk}"][:, tl * XPR + 128:(tl + 1) * XPR], sb[f"x{xk}{j}"][:],
                        start=s0, stop=s1),
                        (f"xpw_{wk}", f"x{xk}{j}"), ("bcb",))
                    n_mm[0] += 1
        C("scalar", lambda E: E.activation(sb["dtin_hi"][:], ps["xp1"][:], AF.Copy),
          ("xp1",), ("dtin_hi",))
        C("vector", lambda E: E.tensor_tensor(sb["dtin_lo"][:], ps["xp1"][:], sb["dtin_hi"][:], ALU.subtract),
          ("xp1", "dtin_hi"), ("dtin_lo",))
        C("scalar", lambda E: E.activation(sb["bc_c"][:], ps["bcb"][0:32, :], AF.Copy),
          ("bcb",), ("bc_c",))
        dma("sync", bchbm, sb["bc_c"][:], ("bc_c",), ("bchbm",), fam="bchbm")

    def bcast_half(k, h):
        srcB = bchbm[h * 8:(h + 1) * 8, :].flatten().unsqueeze(0).broadcast_to([128, 8 * LC])
        srcC = bchbm[16 + h * 8:16 + (h + 1) * 8, :].flatten().unsqueeze(0).broadcast_to([128, 8 * LC])
        dma("sync", sb["Bp"][:], srcB, ("bchbm",), tuple(f"Bp{z}" for z in range(8)), fam="bp")
        dma("sync", sb["Cp"][:], srcC, ("bchbm",), tuple(f"Cp{z}" for z in range(8)), fam="cp")
        if DEBUG and k == 1 and h == 0:
            dma("sync", dbg_bp, sb["Bp"][:], tuple(f"Bp{z}" for z in range(8)), ("dbgbp",), fam="dbg")
            dma("sync", dbg_cp, sb["Cp"][:], tuple(f"Cp{z}" for z in range(8)), ("dbgcp",), fam="dbg")

    def dtile(k, h, i):
        j = i % 2
        for pi, (wk, xk) in enumerate((("hi", "hi"), ("hi", "lo"), ("lo", "hi"))):
            C("tensor", lambda E, wk=wk, xk=xk, i=i, s=(pi == 0), t=(pi == 2): E.matmul(
                ps["xp1"][:], sb[f"dtw_{wk}"][:, i * 128:(i + 1) * 128], sb[f"dtin_{xk}"][:],
                start=s, stop=t),
                (f"dtw_{wk}", f"dtin_{xk}"), ("xp1",))
        C("scalar", lambda E, i=i, j=j: E.activation(sb[f"dtt{j}"][:], ps["xp1"][:], AF.Exp, bias=sb["dtb"][:, i:i + 1]),
          ("xp1", "dtb"), (f"dtt{j}",))
        C("scalar", lambda E, j=j: E.activation(sb[f"dtt{j}"][:], sb[f"dtt{j}"][:], AF.Ln, bias=1.0),
          (f"dtt{j}",), (f"dtt{j}",))
        dma("sync", sb[f"xot{j}"][:], xt[i * 128:(i + 1) * 128, k * LC:(k + 1) * LC], (), (f"xot{j}",), fam=f"xot{j}")
        C("gpsimd", lambda E, j=j: E.tensor_tensor(sb[f"vt{j}"][:], sb[f"dtt{j}"][:], sb[f"xot{j}"][:], ALU.mult),
          (f"dtt{j}", f"xot{j}"), (f"vt{j}",))
        if DEBUG and k == 1 and h == 0 and i == 0:
            dma("sync", dbg_dt, sb[f"dtt{j}"][:], (f"dtt{j}",), ("dbgdt",), fam="dbg")
            dma("sync", dbg_v, sb[f"vt{j}"][:], (f"vt{j}",), ("dbgv",), fam="dbg")

    ucnt = [0]

    def quarter(k, h, i, q):
        j = i % 2
        ub = ucnt[0] % 2
        ucnt[0] += 1
        dA = f"dAz{ub}"
        dB = f"dBx{ub}"
        for jj in range(4):
            n = h * 8 + q * 4 + jj
            col = i * NST + n
            C("scalar", lambda E, jj=jj, col=col, j=j, dA=dA: E.activation(
                sb[dA][:, jj * LC:(jj + 1) * LC], sb[f"dtt{j}"][:], AF.Exp,
                scale=sb["A"][:, col:col + 1]),
                (f"dtt{j}", "A"), (dA,))
        vb = sb[f"vt{j}"][:].unsqueeze(1).broadcast_to([128, 4, LC])
        C("gpsimd", lambda E, q=q, dB=dB, vb=vb: E.tensor_tensor(
            sb[dB][:].rearrange("p (n l) -> p n l", n=4),
            sb["Bp"][:, q * 4 * LC:(q + 1) * 4 * LC].rearrange("p (n l) -> p n l", n=4),
            vb, ALU.mult),
            tuple([f"vt{j}"] + [f"Bp{q * 4 + z}" for z in range(4)]), (dB,))
        colb0 = i * NST + h * 8 + q * 4
        if k > 0:
            dA_f = sb[dA][:].rearrange("p (n l) -> p n l", n=4)[:, :, 0]
            dB_f = sb[dB][:].rearrange("p (n l) -> p n l", n=4)[:, :, 0]
            C("gpsimd", lambda E, dA_f=dA_f, colb0=colb0: E.tensor_tensor(
                sb["tc4"][:], dA_f, sb["carry"][:, colb0:colb0 + 4], ALU.mult),
                tuple([dA] + [f"carry{colb0 + z}" for z in range(4)]), ("tc4",))
            C("gpsimd", lambda E, dB_f=dB_f: E.tensor_tensor(
                dB_f, dB_f, sb["tc4"][:], ALU.add),
                (dB, "tc4"), (dB,))
        for jj in range(4):
            rd = [dA, dB]
            C("vector", lambda E, jj=jj, dA=dA, dB=dB: E.tensor_tensor_scan(
                sb["hp"][:, jj * LC:(jj + 1) * LC],
                sb[dA][:, jj * LC:(jj + 1) * LC],
                sb[dB][:, jj * LC:(jj + 1) * LC],
                0.0, ALU.mult, ALU.add),
                tuple(rd), ("hp",))
        C("vector", lambda E, q=q: E.tensor_tensor(
            sb["zz"][:], sb["hp"][:], sb["Cp"][:, q * 4 * LC:(q + 1) * 4 * LC], ALU.mult),
            tuple(["hp"] + [f"Cp{q * 4 + z}" for z in range(4)]), ("zz",))
        colb = i * NST + h * 8 + q * 4
        C("vector", lambda E, colb=colb: E.tensor_tensor(
            sb["carry"][:, colb:colb + 4],
            sb["hp"][:].rearrange("p (n l) -> p n l", n=4)[:, :, LC - 1],
            sb["zero4"][:], ALU.add),
            ("hp", "zero4"), tuple(f"carry{colb + z}" for z in range(4)))
        C("gpsimd", lambda E: E.tensor_tensor(
            sb["zz"][:, 0:2 * LC], sb["zz"][:, 0:2 * LC], sb["zz"][:, 2 * LC:4 * LC], ALU.add),
            ("zz",), ("zz",))
        j2 = i % 2
        if q == 0:
            C("gpsimd", lambda E, j2=j2: E.tensor_tensor(
                sb[f"yh{j2}"][:], sb["zz"][:, 0:LC], sb["zz"][:, LC:2 * LC], ALU.add),
                ("zz",), (f"yh{j2}",))
        else:
            C("gpsimd", lambda E: E.tensor_tensor(
                sb["zz"][:, 2 * LC:3 * LC], sb["zz"][:, 0:LC], sb["zz"][:, LC:2 * LC], ALU.add),
                ("zz",), ("zz",))
            C("gpsimd", lambda E, j2=j2: E.tensor_tensor(
                sb[f"yh{j2}"][:], sb[f"yh{j2}"][:], sb["zz"][:, 2 * LC:3 * LC], ALU.add),
                ("zz", f"yh{j2}"), (f"yh{j2}",))

    def dtile_end(k, h, i):
        j = i % 2
        if h == 0:
            dma("gpsimd", yscr[i * 128:(i + 1) * 128, :], sb[f"yh{j}"][:],
                (f"yh{j}",), (f"yscr{i}",), fam=f"yscr{k % 2}")
        else:
            dma("sync", sb["ysc"][:], yscr[i * 128:(i + 1) * 128, :],
                (f"yscr{i}",), ("ysc",), fam="yscl")
            C("vector", lambda E, j=j: E.tensor_tensor(sb[f"yh{j}"][:], sb[f"yh{j}"][:], sb["ysc"][:], ALU.add),
              (f"yh{j}", "ysc"), (f"yh{j}",))
            C("vector", lambda E, j=j, i=i: E.scalar_tensor_tensor(
                sb[f"res{j}"][:], sb[f"xot{j}"][:], sb["onepD"][:, i:i + 1], sb[f"yh{j}"][:],
                ALU.mult, ALU.add),
                (f"xot{j}", "onepD", f"yh{j}"), (f"res{j}",))
            dma("gpsimd", ag1i[k][i * 128:(i + 1) * 128, :], sb[f"res{j}"][:],
                (f"res{j}",), (f"ag1w{k}_{i}",), fam=f"agw{k % 2}")
            if DEBUG:
                dma("gpsimd", dbg_res[i * 128:(i + 1) * 128, k * LC:(k + 1) * LC], sb[f"res{j}"][:],
                    (f"res{j}",), (f"dbgres{k}_{i}",), fam="dbg")

    def ag1(k):
        reads = tuple(f"ag1w{k}_{i}" for i in range(NDT))
        P.op("gpsimd", lambda E: E.collective_compute(
            "AllGather", ALU.bypass, replica_groups=RG, ins=[ag1i[k]], outs=[ag1o[k]]),
            reads=reads, writes=(f"gath{k}",), kind="cc", name=f"ag1_{k}", fam=f"cag{k % 2}")

    def ln_stats(m_name, r_name, width=LC):
        stv = sb["stv"]
        s0 = stv[:, 0:width]; s1 = stv[:, LC:LC + width]; s2 = stv[:, 2 * LC:2 * LC + width]
        C("vector", lambda E: E.tensor_single_scalar(s0, s0, 1.0 / D, ALU.mult),
          ("stv0",), ("stv0",))
        C("tensor", lambda E, w=width: E.matmul(ps["misc"][:, 0:w], sb["ones_r"][0:1, :], s0[0:1, :], start=True, stop=True),
          ("stv0", "ones_r"), ("misc",))
        C("scalar", lambda E, w=width: E.activation(sb["mbc"][:, 0:w], ps["misc"][:, 0:w], AF.Copy),
          ("misc",), (m_name, "mbc"))
        C("vector", lambda E: E.tensor_single_scalar(s1, s1, 1.0 / D, ALU.mult),
          ("stv1",), ("stv1",))
        C("vector", lambda E: E.tensor_tensor(s2, s0, s0, ALU.mult),
          ("stv0",), ("stv2",))
        C("vector", lambda E: E.tensor_tensor(s1, s1, s2, ALU.subtract),
          ("stv1", "stv2"), ("stv1",))
        C("vector", lambda E: E.tensor_single_scalar(s1, s1, EPS, ALU.add),
          ("stv1",), ("stv1",))
        C("scalar", lambda E: E.activation(s2, s1, AF.Ln),
          ("stv1",), ("stv2",))
        C("scalar", lambda E: E.activation(s2, s2, AF.Exp, scale=-0.5),
          ("stv2",), ("stv2",))
        C("vector", lambda E: E.tensor_tensor(s0, s2, s2, ALU.mult),
          ("stv2",), ("stv0",))
        C("vector", lambda E: E.tensor_tensor(s0, s0, s1, ALU.mult),
          ("stv0", "stv1"), ("stv0",))
        C("vector", lambda E: E.tensor_scalar(s0, s0, -0.5, 1.5, ALU.mult, ALU.add),
          ("stv0",), ("stv0",))
        C("vector", lambda E: E.tensor_tensor(s2, s2, s0, ALU.mult),
          ("stv2", "stv0"), ("stv2",))
        C("tensor", lambda E, w=width: E.matmul(ps["misc"][:, 0:w], sb["ones_r"][0:1, :], s2[0:1, :], start=True, stop=True),
          ("stv2", "ones_r"), ("misc",))
        C("scalar", lambda E, w=width: E.activation(sb["rbc"][:, 0:w], ps["misc"][:, 0:w], AF.Copy),
          ("misc",), (r_name, "rbc"))

    def sparsemax_ops(sbi):
        pool = sb["sortp"]
        W2 = 2
        def sl(s):
            return pool[:, s * W2:(s + 1) * W2]
        slots = {}
        free = list(range(24))
        for ee in range(8):
            s0 = free.pop(0)
            slots[ee] = s0
            C("vector", lambda E, ee=ee, s0=s0: E.tensor_copy(sl(s0), sb["lgt"][:, ee * W2:(ee + 1) * W2]),
              ("lgt",), (f"sp{s0}",))
        for (i, jx) in SORT8:
            si, sj = slots[i], slots[jx]
            a = free.pop(0); b2 = free.pop(0)
            C("vector", lambda E, si=si, sj=sj, a=a: E.tensor_tensor(sl(a), sl(si), sl(sj), ALU.max),
              (f"sp{si}", f"sp{sj}"), (f"sp{a}",))
            C("vector", lambda E, si=si, sj=sj, b2=b2: E.tensor_tensor(sl(b2), sl(si), sl(sj), ALU.min),
              (f"sp{si}", f"sp{sj}"), (f"sp{b2}",))
            free.append(si); free.append(sj)
            slots[i], slots[jx] = a, b2
        cum = {}
        c0s = free.pop(0)
        cum[0] = c0s
        C("vector", lambda E, s=slots[0], c=c0s: E.tensor_copy(sl(c), sl(s)),
          (f"sp{slots[0]}",), (f"sp{c0s}",))
        for ee in range(1, 8):
            cs = free.pop(0)
            C("vector", lambda E, p=cum[ee - 1], z=slots[ee], cs=cs: E.tensor_tensor(sl(cs), sl(p), sl(z), ALU.add),
              (f"sp{cum[ee - 1]}", f"sp{slots[ee]}"), (f"sp{cs}",))
            cum[ee] = cs
        tsl = free.pop(0); msl = free.pop(0); ksl = free.pop(0)
        for ee in range(8):
            C("vector", lambda E, z=slots[ee], c=cum[ee], t=tsl, ee=ee: E.scalar_tensor_tensor(
                sl(t), sl(z), float(ee + 1), sl(c), ALU.mult, ALU.subtract),
                (f"sp{slots[ee]}", f"sp{cum[ee]}"), (f"sp{tsl}",))
            C("vector", lambda E, t=tsl, m=msl: E.tensor_single_scalar(sl(m), sl(t), -1.0, ALU.is_gt),
              (f"sp{tsl}",), (f"sp{msl}",))
            if ee == 0:
                C("vector", lambda E, m=msl, ks=ksl: E.tensor_copy(sl(ks), sl(m)),
                  (f"sp{msl}",), (f"sp{ksl}",))
            else:
                C("vector", lambda E, m=msl, ks=ksl: E.tensor_tensor(sl(ks), sl(ks), sl(m), ALU.add),
                  (f"sp{msl}", f"sp{ksl}"), (f"sp{ksl}",))
        acc, eq = tsl, msl
        t2 = free[0]
        for ee in range(8):
            C("vector", lambda E, ks=ksl, eqs=eq, ee=ee: E.tensor_single_scalar(sl(eqs), sl(ks), float(ee + 1), ALU.is_equal),
              (f"sp{ksl}",), (f"sp{eq}",))
            if ee == 0:
                C("vector", lambda E, c=cum[ee], eqs=eq, a=acc: E.tensor_tensor(sl(a), sl(c), sl(eqs), ALU.mult),
                  (f"sp{cum[ee]}", f"sp{eq}"), (f"sp{acc}",))
            else:
                C("vector", lambda E, c=cum[ee], eqs=eq, t2=t2: E.tensor_tensor(sl(t2), sl(c), sl(eqs), ALU.mult),
                  (f"sp{cum[ee]}", f"sp{eq}"), (f"sp{t2}",))
                C("vector", lambda E, a=acc, t2=t2: E.tensor_tensor(sl(a), sl(a), sl(t2), ALU.add),
                  (f"sp{acc}", f"sp{t2}"), (f"sp{acc}",))
        C("vector", lambda E, ks=ksl, eqs=eq: E.reciprocal(sl(eqs), sl(ks)),
          (f"sp{ksl}",), (f"sp{eq}",))
        C("vector", lambda E, a=acc: E.tensor_single_scalar(sl(a), sl(a), -1.0, ALU.add),
          (f"sp{acc}",), (f"sp{acc}",))
        C("vector", lambda E, a=acc, eqs=eq: E.tensor_tensor(sl(a), sl(a), sl(eqs), ALU.mult),
          (f"sp{acc}", f"sp{eq}"), (f"sp{acc}",))
        for ee in range(8):
            C("vector", lambda E, ee=ee, a=acc: E.tensor_tensor(
                sb["wfin"][:, ee * W2:(ee + 1) * W2], sb["lgt"][:, ee * W2:(ee + 1) * W2], sl(a), ALU.subtract),
                (f"sp{acc}", "lgt"), (f"wf{ee}",))
            C("vector", lambda E, ee=ee: E.tensor_single_scalar(
                sb["wfin"][:, ee * W2:(ee + 1) * W2], sb["wfin"][:, ee * W2:(ee + 1) * W2], 0.0, ALU.max),
                (f"wf{ee}",), (f"wf{ee}", "wfin"))

    def ffn_prep(k, sbi):
        cs = sbi * SBT
        for t in range(NKT):
            j = t % 2
            row = t * 128
            dma("sync", sb[f"gt{j}"][:], ag1o[k][row:row + 128, cs:cs + SBT], (f"gath{k}",), (f"gt{j}",), fam=f"gt{j}")
            C("scalar", lambda E, j=j: E.activation(sb["sq2"][:], sb[f"gt{j}"][:], AF.Square),
              (f"gt{j}",), ("sq2",))
            C("tensor", lambda E, j=j, s=(t == 0), st=(t == NKT - 1): E.matmul(
                ps["st"][0:1, 0:SBT], sb["ones_c"][:, 0:1], sb[f"gt{j}"][:], start=s, stop=st),
                ("ones_c", f"gt{j}"), ("st", "st0"))
            C("tensor", lambda E, s=(t == 0), st=(t == NKT - 1): E.matmul(
                ps["st"][32:33, 0:SBT], sb["ones_c"][:, 0:1], sb["sq2"][:], start=s, stop=st),
                ("ones_c", "sq2"), ("st", "st1"))
        stv = sb["stv"]
        C("scalar", lambda E: E.activation(stv[0:1, 0:SBT], ps["st"][0:1, 0:SBT], AF.Copy),
          ("st0", "st1", "st"), ("stv0",))
        C("scalar", lambda E: E.activation(stv[0:1, LC:LC + SBT], ps["st"][32:33, 0:SBT], AF.Copy),
          ("st0", "st1", "st"), ("stv1",))
        ln_stats("mbc_v", "rbc_v", width=SBT)
        for t in range(NKT):
            j = t % 2
            row = t * 128
            dma("sync", sb[f"gt{j}"][:], ag1o[k][row:row + 128, cs:cs + SBT], (f"gath{k}",), (f"gt{j}",), fam=f"gt{j}")
            C("vector", lambda E, j=j: E.tensor_tensor(sb[f"gt{j}"][:], sb[f"gt{j}"][:], sb["mbc"][:, 0:SBT], ALU.subtract),
              (f"gt{j}", "mbc_v", "mbc"), (f"gt{j}",))
            C("vector", lambda E, j=j: E.tensor_tensor(sb[f"gt{j}"][:], sb[f"gt{j}"][:], sb["rbc"][:, 0:SBT], ALU.mult),
              (f"gt{j}", "rbc_v", "rbc"), (f"gt{j}",))
            C("scalar", lambda E, j=j, t=t: E.activation(sb["xn_hi"][:, t * SBT:(t + 1) * SBT], sb[f"gt{j}"][:], AF.Copy),
              (f"gt{j}",), (f"xnh{t}",))
            C("vector", lambda E, j=j, t=t: E.tensor_tensor(
                sb["xn_lo"][:, t * SBT:(t + 1) * SBT], sb[f"gt{j}"][:], sb["xn_hi"][:, t * SBT:(t + 1) * SBT], ALU.subtract),
                (f"gt{j}", f"xnh{t}"), (f"xnl{t}",))
            if DEBUG:
                dma("sync", dbg_xn[t * 128:(t + 1) * 128, k * LC + cs:k * LC + cs + SBT], sb[f"gt{j}"][:],
                    (f"gt{j}",), (f"dbgxn{k}_{sbi}_{t}",), fam="dbg")
        nm = [0]
        for t in range(NKT):
            for (wk, xk) in (("hi", "hi"), ("hi", "lo"), ("lo", "hi")):
                C("tensor", lambda E, t=t, wk=wk, xk=xk, s=(nm[0] == 0), st=(nm[0] == 95): E.matmul(
                    ps["rtl"][0:8, :], sb[f"wr_{wk}"][:, t * 8:(t + 1) * 8],
                    sb["xn_hi" if xk == "hi" else "xn_lo"][:, t * SBT:(t + 1) * SBT],
                    start=s, stop=st),
                    (f"wr_{wk}", f"xn{'h' if xk == 'hi' else 'l'}{t}"), ("rtl",))
                nm[0] += 1
        C("scalar", lambda E: E.activation(sb["rt8"][:], ps["rtl"][0:8, :], AF.Identity, bias=sb["br"][:, 0:1]),
          ("rtl", "br"), ("rt8",))
        dma("sync", wla[:, 0:4096], lah_hi, ("lahH", "wla0", "wla1"), ("wla0", "wla1"), fam="wla")
        dma("sync", wla[:, 4096:8192], lah_lo, ("lahL", "wla2", "wla3"), ("wla2", "wla3"), fam="wla")
        nm = [0]
        for t in range(NKT):
            for (wk, xk) in (("hi", "hi"), ("hi", "lo"), ("lo", "hi")):
                off = t * 128 if wk == "hi" else 4096 + t * 128
                rk = ("wla0", "wla1") if wk == "hi" else ("wla2", "wla3")
                C("tensor", lambda E, off=off, t=t, xk=xk, s=(nm[0] == 0), st=(nm[0] == 95): E.matmul(
                    ps["rtl"][:], wla[:, off:off + 128],
                    sb["xn_hi" if xk == "hi" else "xn_lo"][:, t * SBT:(t + 1) * SBT],
                    start=s, stop=st),
                    rk + (f"xn{'h' if xk == 'hi' else 'l'}{t}", "rt8"), ("rtl",))
                nm[0] += 1
        C("scalar", lambda E: E.activation(sb["lh"][:], ps["rtl"][:], AF.Copy),
          ("rtl",), ("lh",))
        lgt_v = sb["lgt"][:].rearrange("p (e g) -> p e g", g=2)
        for g in range(2):
            C("tensor", lambda E, g=g: E.transpose(ps["misc"][:, g * 8:(g + 1) * 8],
                                                   sb["rt8"][:, g * 128:(g + 1) * 128], sb["idn"][0:8, 0:8]),
              ("rt8", "idn"), ("misc",))
            C("scalar", lambda E, g=g, lv=lgt_v: E.activation(lv[:, :, g], ps["misc"][:, g * 8:(g + 1) * 8], AF.Copy),
              ("misc",), ("lgt",))
        sparsemax_ops(sbi)
        wf_keys = tuple(f"wf{ee}" for ee in range(8))
        wfin_v = sb["wfin"][:].rearrange("p (e g) -> p e g", g=2)
        for g in range(2):
            C("tensor", lambda E, g=g, wv=wfin_v: E.transpose(ps["misc"][0:8, 128 + g * 128:128 + (g + 1) * 128],
                                                              wv[:, :, g], sb["idn"][:]),
              wf_keys + ("idn",), ("misc",))
            C("scalar", lambda E, g=g: E.activation(sb["wt8"][:, g * 128:(g + 1) * 128],
                                                    ps["misc"][0:8, 128 + g * 128:128 + (g + 1) * 128], AF.Copy),
              ("misc",), ("wt8",))
        C("tensor", lambda E: E.matmul(ps["misc"][:, 0:SBT], sb["sel8"][:], sb["wt8"][:], start=True, stop=True),
          ("sel8", "wt8"), ("misc",))
        C("scalar", lambda E: E.activation(sb["wbc"][:], ps["misc"][:, 0:SBT], AF.Copy),
          ("misc",), ("wbc",))
        C("vector", lambda E: E.tensor_tensor(sb["lh"][:], sb["lh"][:], sb["wbc"][:], ALU.mult),
          ("lh", "wbc"), ("hw", "lh"))
        C("scalar", lambda E: E.activation(sb["hw_hi"][:], sb["lh"][:], AF.Copy),
          ("hw",), ("hw_hi",))
        C("vector", lambda E: E.tensor_tensor(sb["hw_lo"][:], sb["lh"][:], sb["hw_hi"][:], ALU.subtract),
          ("hw", "hw_hi"), ("hw_lo",))

    def ffn_otile(k, sbi, o):
        cs = sbi * SBT
        jo = o % 2
        fb = f"ff{jo}"
        dma("sync", wla[:, 0:2048], wh_hi[o, :, 0:2048], (f"wbm{o}_0H", "wla0"), ("wla0",), fam="wla")
        dma("sync", wla[:, 4096:6144], wh_lo[o, :, 0:2048], (f"wbm{o}_0L", "wla2"), ("wla2",), fam="wla")
        dma("sync", wla[:, 2048:4096], wh_hi[o, :, 2048:4096], (f"wbm{o}_1H", "wla1"), ("wla1",), fam="wla")
        dma("sync", wla[:, 6144:8192], wh_lo[o, :, 2048:4096], (f"wbm{o}_1L", "wla3"), ("wla3",), fam="wla")
        dma("sync", sb[f"lbt_hi{jo}"][:], lbh_hi[:, o * 128:(o + 1) * 128], ("lbhH",), (f"lbt_hi{jo}",), fam="lbt")
        dma("sync", sb[f"lbt_lo{jo}"][:], lbh_lo[:, o * 128:(o + 1) * 128], ("lbhL",), (f"lbt_lo{jo}",), fam="lbt")
        nm = [0]
        for t in range(NKT):
            for (wk, xk) in (("hi", "hi"), ("hi", "lo"), ("lo", "hi")):
                off = (0 if wk == "hi" else 4096) + t * 128
                rk = "wla0" if off < 2048 else ("wla1" if off < 4096 else ("wla2" if off < 6144 else "wla3"))
                C("tensor", lambda E, off=off, t=t, xk=xk, s=(nm[0] == 0), fb=fb: E.matmul(
                    ps[fb][:], wla[:, off:off + 128],
                    sb["xn_hi" if xk == "hi" else "xn_lo"][:, t * SBT:(t + 1) * SBT],
                    start=s, stop=False),
                    (rk, f"xn{'h' if xk == 'hi' else 'l'}{t}"), (fb,))
                nm[0] += 1
        for pi, (wk, xk) in enumerate((("hi", "hi"), ("hi", "lo"), ("lo", "hi"))):
            C("tensor", lambda E, wk=wk, xk=xk, fb=fb, st=(pi == 2), jo=jo: E.matmul(
                ps[fb][:], sb[f"lbt_{wk}{jo}"][:], sb[f"hw_{xk}"][:],
                start=False, stop=st),
                (f"lbt_{wk}{jo}", f"hw_{xk}"), (fb,))
        C("scalar", lambda E, o=o, fb=fb, jo=jo: E.activation(
            sb[f"fft{jo}"][:], ps[fb][:], AF.Identity, bias=sb["bb"][:, o:o + 1]),
            (fb, "bb"), (f"fft{jo}",))
        dma("sync", sb[f"rr{jo}"][:], ag1i[k][o * 128:(o + 1) * 128, cs:cs + SBT],
            (f"ag1w{k}_{o}",), (f"rr{jo}",), fam=f"rr{jo}")
        C("vector", lambda E, jo=jo: E.tensor_tensor(sb[f"rr{jo}"][:], sb[f"rr{jo}"][:], sb["mbc"][:, 0:SBT], ALU.subtract),
          (f"rr{jo}", "mbc_v", "mbc"), (f"rr{jo}",))
        C("vector", lambda E, jo=jo: E.tensor_tensor(sb[f"rr{jo}"][:], sb[f"rr{jo}"][:], sb["rbc"][:, 0:SBT], ALU.mult),
          (f"rr{jo}", "rbc_v", "rbc"), (f"rr{jo}",))
        C("vector", lambda E, jo=jo: E.tensor_tensor(sb[f"fft{jo}"][:], sb[f"fft{jo}"][:], sb[f"rr{jo}"][:], ALU.add),
          (f"fft{jo}", f"rr{jo}"), (f"fft{jo}",))
        dma("sync", r2scr[o * 128:(o + 1) * 128, cs:cs + SBT], sb[f"fft{jo}"][:],
            (f"fft{jo}",), (f"r2_{o}_{sbi}",), fam=f"r2s{k % 2}")

    def ffn_fin(k):
        c0 = k * LC
        for o in range(NOT):
            jo = o % 2
            dma("sync", sb[f"r2l{jo}"][:], r2scr[o * 128:(o + 1) * 128, :],
                (f"r2_{o}_0", f"r2_{o}_1"), (f"xlr{jo}",), fam="yscl")
            C("scalar", lambda E, jo=jo: E.activation(sb["sq2b"][:], sb[f"r2l{jo}"][:], AF.Square),
              (f"xlr{jo}",), ("sq2b",))
            C("tensor", lambda E, jo=jo, s=(o == 0), t=(o == NOT - 1): E.matmul(
                ps["xp2"][0:1, :], sb["ones_c"][:, 0:1], sb[f"r2l{jo}"][:], start=s, stop=t),
                ("ones_c", f"xlr{jo}"), ("xp2", "xp2s"))
            C("tensor", lambda E, s=(o == 0), t=(o == NOT - 1): E.matmul(
                ps["xp2"][32:33, :], sb["ones_c"][:, 0:1], sb["sq2b"][:], start=s, stop=t),
                ("ones_c", "sq2b"), ("xp2", "xp2q"))
        C("scalar", lambda E: E.activation(sb["stat_sb"][0:1, :], ps["xp2"][0:1, :], AF.Copy),
          ("xp2s", "xp2q", "xp2"), ("stat_sb",))
        C("scalar", lambda E: E.activation(sb["stat_sb"][32:33, :], ps["xp2"][32:33, :], AF.Copy),
          ("xp2s", "xp2q", "xp2"), ("stat_sb",))
        dma("gpsimd", ag2i[k][0:1, :], sb["stat_sb"][0:1, :], ("stat_sb",), (f"ag2w{k}",), fam="ag2w")
        dma("gpsimd", ag2i[k][1:2, :], sb["stat_sb"][32:33, :], ("stat_sb",), (f"ag2w{k}b",), fam="ag2w")
        P.op("gpsimd", lambda E: E.collective_compute(
            "AllReduce", ALU.add, replica_groups=RG, ins=[ag2i[k]], outs=[ag2o[k]]),
            reads=(f"ag2w{k}", f"ag2w{k}b"), writes=(f"gath2_{k}",), kind="cc", name=f"ar2_{k}", fam=f"car2{k % 2}")
        stv = sb["stv"]
        dma("sync", stv[:, 0:LC], ag2o[k][0:1, :], (f"gath2_{k}",), ("stv0",), fam="stv")
        dma("sync", stv[:, LC:2 * LC], ag2o[k][1:2, :], (f"gath2_{k}",), ("stv1",), fam="stv")
        ln_stats("mbc_v", "rbc_v")
        for o in range(NOT):
            jo = o % 2
            dma("sync", sb[f"r2l{jo}"][:], r2scr[o * 128:(o + 1) * 128, :],
                (f"r2_{o}_0", f"r2_{o}_1"), (f"xlr{jo}",), fam="yscl")
            C("vector", lambda E, jo=jo: E.tensor_tensor(sb[f"r2l{jo}"][:], sb[f"r2l{jo}"][:], sb["mbc"][:], ALU.subtract),
              (f"xlr{jo}", "mbc_v", "mbc"), (f"xlr{jo}",))
            C("vector", lambda E, jo=jo: E.tensor_tensor(sb[f"r2l{jo}"][:], sb[f"r2l{jo}"][:], sb["rbc"][:], ALU.mult),
              (f"xlr{jo}", "rbc_v", "rbc"), (f"xlr{jo}",))
            dma("sync", out_c[o * 128:(o + 1) * 128, c0:c0 + LC], sb[f"r2l{jo}"][:],
                (f"xlr{jo}",), (f"outw{k}_{o}",), fam="out")

    SERIAL = SERIAL_EMIT
    for k in range(NCH):
        if DEBUG and k == 1:
            dma("sync", dbg_cr, sb["carry"][:],
                tuple(f"carry{cc2}" for cc2 in range(NDT * NST)), ("dbgcr",), fam="dbg")
        scan_side(k)
        for h in range(2):
            bcast_half(k, h)
            if k > 0 and not SERIAL:
                ffn_prep(k - 1, h)
            dtile(k, h, 0)
            for i in range(NDT):
                if i + 1 < NDT:
                    dtile(k, h, i + 1)
                for q in range(2):
                    quarter(k, h, i, q)
                dtile_end(k, h, i)
                if k > 0 and not SERIAL:
                    ffn_otile(k - 1, h, i)
                elif k == 0:
                    emit_wcast(2)
            if h == 1 and k > 0 and not SERIAL:
                ffn_fin(k - 1)
        ag1(k)
    emit_wcast(64)
    if SERIAL:
        for k in range(NCH):
            for sbi in range(2):
                ffn_prep(k, sbi)
                for o in range(NOT):
                    ffn_otile(k, sbi, o)
            ffn_fin(k)
    else:
        for sbi in range(2):
            ffn_prep(NCH - 1, sbi)
            for o in range(NOT):
                ffn_otile(NCH - 1, sbi, o)
        ffn_fin(NCH - 1)

    P.op("vector", lambda E: E.engine_nop(),
         reads=tuple(f"outw{k}_{o}" for k in range(NCH) for o in range(NOT)),
         writes=(), kind="c", name="fin")

    P.finalize()
    P.check_deadlock()
    with nc.Block() as block:
        @block.sync
        def _(E):
            P.emit_engine("sync", E, chans)

        @block.vector
        def _(E):
            P.emit_engine("vector", E, chans)

        @block.scalar
        def _(E):
            P.emit_engine("scalar", E, chans)

        @block.gpsimd
        def _(E):
            P.emit_engine("gpsimd", E, chans)

        @block.tensor
        def _(E):
            P.emit_engine("tensor", E, chans)
    stack.close()
    return nc


def make_inputs(ins):
    x = np.ascontiguousarray(ins["L0_attn_x"])
    A_log = np.asarray(ins["L0_attn_A_log"])
    Dv = np.asarray(ins["L0_attn_D"])
    xpw = np.asarray(ins["L0_attn_x_proj_w"])
    dtw = np.asarray(ins["L0_attn_dt_proj_w"])
    dtb = np.asarray(ins["L0_attn_dt_proj_b"])
    Wb = np.asarray(ins["L0_ffn_W_base"])
    bb = np.asarray(ins["L0_ffn_b_base"])
    Wr = np.asarray(ins["L0_ffn_W_router"])
    br = np.asarray(ins["L0_ffn_b_router"])
    lA = np.asarray(ins["L0_ffn_lora_A"])
    lB = np.asarray(ins["L0_ffn_lora_B"])

    idn = np.eye(128, dtype=np.float32)
    sel8 = np.zeros((8, 128), np.float32)
    for ee in range(8):
        sel8[ee, ee * 16:(ee + 1) * 16] = 1.0
    xpwT = np.ascontiguousarray(xpw.T)
    lap = np.ascontiguousarray(lA.transpose(1, 0, 2).reshape(D, 128))
    lbp = np.ascontiguousarray(lB.reshape(128, D))
    WrT = np.ascontiguousarray(Wr.T)

    maps = []
    for c in range(8):
        b, h = c // 2, c % 2
        own = slice(h * DH, (h + 1) * DH)
        xb = x[b].T
        xt_c = np.concatenate([xb[own], xb[(1 - h) * DH:(2 - h) * DH]], axis=0)
        perm = np.concatenate([np.arange(h * DH, (h + 1) * DH), np.arange((1 - h) * DH, (2 - h) * DH)])
        xpw_c = xpwT[perm]
        xpw_t = np.ascontiguousarray(xpw_c.reshape(32, 128, XPR).transpose(1, 0, 2).reshape(128, 32 * XPR))
        dtw_c = np.ascontiguousarray(dtw[own].T)
        alog_c = np.ascontiguousarray(A_log[own].reshape(NDT, 128, NST).transpose(1, 0, 2).reshape(128, NDT * NST))
        d_c = np.ascontiguousarray(Dv[own].reshape(NDT, 128).T)
        dtb_c = np.ascontiguousarray(dtb[own].reshape(NDT, 128).T)
        Wslice = Wb[own, :]
        wbs_c = np.ascontiguousarray(
            Wslice.reshape(NOT, 128, NKT, 128).transpose(0, 3, 2, 1).reshape(NOT, 128, NKT * 128))
        bb_c = np.ascontiguousarray(bb[own].reshape(NOT, 128).T)
        wrt_c = np.ascontiguousarray(WrT.reshape(NKT, 128, 8).transpose(1, 0, 2).reshape(128, NKT * 8))
        las_c = np.ascontiguousarray(lap.reshape(NKT, 128, 128).transpose(1, 0, 2).reshape(128, NKT * 128))
        lbs_c = np.ascontiguousarray(lbp[:, own])
        maps.append({
            "xt": xt_c, "xpw": xpw_t, "dtwi": dtw_c, "alog": alog_c,
            "dvec": d_c, "dtbi": dtb_c, "wbs": wbs_c, "bbt": bb_c,
            "wrti": wrt_c, "brt": br.reshape(8, 1).astype(np.float32),
            "las": las_c, "lbsi": lbs_c, "idni": idn, "sel8i": sel8,
        })
    return maps


_NC_CACHE = [None]


def kernel(**inputs):
    maps = make_inputs(inputs)
    if _NC_CACHE[0] is None:
        _NC_CACHE[0] = build_nc()
    nc = _NC_CACHE[0]
    res = run_bass_kernel_spmd(nc, maps, core_ids=list(range(8)))
    out = np.empty((B, L, D), np.float32)
    for c in range(8):
        b, h = c // 2, c % 2
        oc = res.results[c]["out_c"]
        out[b, :, h * DH:(h + 1) * DH] = oc.T
    return out




# version 05
# speedup vs baseline: 1.0069x; 1.0069x over previous
import numpy as np
import concourse.bass as bass
import concourse.mybir as mybir
from concourse.bass_utils import run_bass_kernel_spmd

F32 = mybir.dt.float32
BF16 = mybir.dt.bfloat16
AF = mybir.ActivationFunctionType
ALU = mybir.AluOpType

B, L, D = 4, 2048, 4096
DH, NST, DTR, XPR = 2048, 16, 128, 160
NCH, LC = 4, 512
SBT = 256
NDT = 16
OH, NOT, NKT = 2048, 16, 32
EPS = 1e-5
RG = [[0, 1], [2, 3], [4, 5], [6, 7]]
SORT8 = [(0, 1), (2, 3), (4, 5), (6, 7), (0, 2), (1, 3), (4, 6), (5, 7),
         (1, 2), (5, 6), (0, 4), (3, 7), (1, 5), (2, 6), (1, 4), (3, 6),
         (2, 4), (3, 5), (3, 4)]
DEBUG = False
SERIAL_EMIT = False


class Prog:
    ENGINES = ("sync", "vector", "scalar", "gpsimd", "tensor")

    def __init__(self):
        self.ops = []
        self.last_w = {}
        self.readers = {}
        self.fams = {}
        self.fam_R = {}

    def op(self, eng, fn, reads=(), writes=(), kind="c", name="", fam=None, R=0):
        deps = set()
        for kk in reads:
            if kk in self.last_w:
                deps.add(self.last_w[kk])
        for kk in writes:
            for r in self.readers.get(kk, ()):
                deps.add(r)
            if kk in self.last_w:
                deps.add(self.last_w[kk])
        i = len(self.ops)
        if kind in ("d", "cc"):
            assert fam is not None, name
            self.fams.setdefault(fam, []).append(i)
            self.fam_R[fam] = max(self.fam_R.get(fam, 0), R)
        self.ops.append(dict(eng=eng, fn=fn, deps=deps, kind=kind, name=name, fam=fam))
        for kk in reads:
            self.readers.setdefault(kk, []).append(i)
        for kk in writes:
            self.last_w[kk] = i
            self.readers[kk] = []
        return i

    def finalize(self):
        ops = self.ops
        self.need_inc = [False] * len(ops)
        for o in ops:
            for d in o["deps"]:
                p = ops[d]
                if p["kind"] == "c" and p["eng"] == o["eng"]:
                    continue
                if p["kind"] == "c":
                    self.need_inc[d] = True
        cnt = {}
        self.val = [0] * len(ops)
        for i, o in enumerate(ops):
            if o["kind"] == "c" and self.need_inc[i]:
                ch = (o["eng"], "c")
                cnt[ch] = cnt.get(ch, 0) + 1
                self.val[i] = cnt[ch]
        self.fam_idx = {}
        for fam, lst in self.fams.items():
            for q, gi in enumerate(lst):
                self.fam_idx[gi] = q

    def check_deadlock(self):
        import bisect
        ops = self.ops
        streams = {en: [i for i in range(len(ops)) if ops[i]["eng"] == en] for en in self.ENGINES}
        pos = {en: 0 for en in self.ENGINES}
        cnt = {}

        def waits_of(i):
            o = ops[i]
            ws = []
            for d in o["deps"]:
                p = ops[d]
                if p["kind"] == "c":
                    if p["eng"] == o["eng"]:
                        continue
                    ws.append((("c", p["eng"]), self.val[d]))
                else:
                    fam = p["fam"]
                    lst = self.fams[fam]
                    cx = max(bisect.bisect_left(lst, i), self.fam_idx[d] + 1)
                    tgt = min(cx + self.fam_R[fam], len(lst))
                    if ops[i]["kind"] == p["kind"] and ops[i].get("fam") == fam:
                        tgt = min(tgt, self.fam_idx[i])
                    tgt = max(tgt, self.fam_idx[d] + 1)
                    step = 16 if p["kind"] == "d" else 1
                    ws.append((("f", fam), tgt * step))
            return ws

        progressed = True
        while progressed:
            progressed = False
            for en in self.ENGINES:
                while pos[en] < len(streams[en]):
                    i = streams[en][pos[en]]
                    if any(cnt.get(ch, 0) < v for ch, v in waits_of(i)):
                        break
                    o = ops[i]
                    if o["kind"] == "c":
                        if self.need_inc[i]:
                            ch = ("c", en)
                            cnt[ch] = cnt.get(ch, 0) + 1
                    else:
                        step = 16 if o["kind"] == "d" else 1
                        ch = ("f", o["fam"])
                        cnt[ch] = cnt.get(ch, 0) + step
                    pos[en] += 1
                    progressed = True
        stuck = {en: pos[en] for en in self.ENGINES if pos[en] < len(streams[en])}
        if stuck:
            for en, p in stuck.items():
                i = streams[en][p]
                o = ops[i]
                print(f"DEADLOCK: {en} pos {p}/{len(streams[en])} op {i} kind={o['kind']} fam={o['fam']} name={o['name']}")
                for d in sorted(o["deps"]):
                    pp = ops[d]
                    print(f"   dep {d}: eng={pp['eng']} kind={pp['kind']} fam={pp['fam']} name={pp['name']} val={self.val[d]}")
                print("   waits:", waits_of(i), "cur:", {ch: cnt.get(ch, 0) for ch, v in waits_of(i)})
            raise RuntimeError("deadlock detected")
        print("deadlock check: OK")

    def emit_engine(self, ename, eng, chans):
        ops = self.ops
        import bisect
        waited = {}
        for i, o in enumerate(ops):
            if o["eng"] != ename:
                continue
            for d in sorted(o["deps"]):
                p = ops[d]
                if p["kind"] == "c":
                    if p["eng"] == ename:
                        continue
                    ch = ("c", p["eng"])
                    v = self.val[d]
                    assert v > 0, (i, o["name"], d, p["name"])
                    sem = chans[(p["eng"], "c")]
                else:
                    fam = p["fam"]
                    lst = self.fams[fam]
                    cx = bisect.bisect_left(lst, i)
                    cx = max(cx, self.fam_idx[d] + 1)
                    tgt = min(cx + self.fam_R[fam], len(lst))
                    if o["kind"] == p["kind"] and o.get("fam") == fam:
                        tgt = min(tgt, self.fam_idx[i])
                    tgt = max(tgt, self.fam_idx[d] + 1)
                    step = 16 if p["kind"] == "d" else 1
                    ch = ("f", fam)
                    v = tgt * step
                    sem = chans[("fam", fam)]
                if waited.get(ch, 0) >= v:
                    continue
                eng.wait_ge(sem, v)
                waited[ch] = v
            inst = o["fn"](eng)
            if o["kind"] == "c":
                if self.need_inc[i]:
                    inst.then_inc(chans[(ename, "c")], 1)
            else:
                step = 16 if o["kind"] == "d" else 1
                inst.then_inc(chans[("fam", o["fam"])], step)


def build_nc():
    nc = bass.Bass("TRN2", target_bir_lowering=False)
    P = Prog()

    xt = nc.dram_tensor("xt", [D, L], F32, kind="ExternalInput").ap()
    xpw = nc.dram_tensor("xpw", [128, 32 * XPR], F32, kind="ExternalInput").ap()
    dtwi = nc.dram_tensor("dtwi", [128, DH], F32, kind="ExternalInput").ap()
    alog = nc.dram_tensor("alog", [128, NDT * NST], F32, kind="ExternalInput").ap()
    dvec = nc.dram_tensor("dvec", [128, NDT], F32, kind="ExternalInput").ap()
    dtbi = nc.dram_tensor("dtbi", [128, NDT], F32, kind="ExternalInput").ap()
    wbs = nc.dram_tensor("wbs", [NOT, 128, NKT * 128], F32, kind="ExternalInput").ap()
    bbt = nc.dram_tensor("bbt", [128, NOT], F32, kind="ExternalInput").ap()
    wrti = nc.dram_tensor("wrti", [128, NKT * 8], F32, kind="ExternalInput").ap()
    brt = nc.dram_tensor("brt", [8, 1], F32, kind="ExternalInput").ap()
    las = nc.dram_tensor("las", [128, NKT * 128], F32, kind="ExternalInput").ap()
    lbsi = nc.dram_tensor("lbsi", [128, OH], F32, kind="ExternalInput").ap()
    idni = nc.dram_tensor("idni", [128, 128], F32, kind="ExternalInput").ap()
    sel8i = nc.dram_tensor("sel8i", [8, 128], F32, kind="ExternalInput").ap()
    out_c = nc.dram_tensor("out_c", [OH, L], F32, kind="ExternalOutput").ap()
    if DEBUG:
        dbg_res = nc.dram_tensor("dbg_res", [DH, L], F32, kind="ExternalOutput").ap()
        dbg_xn = nc.dram_tensor("dbg_xn", [D, L], F32, kind="ExternalOutput").ap()
        dbg_cr = nc.dram_tensor("dbg_cr", [128, NDT * NST], F32, kind="ExternalOutput").ap()
        dbg_bp = nc.dram_tensor("dbg_bp", [128, 8 * LC], F32, kind="ExternalOutput").ap()
        dbg_cp = nc.dram_tensor("dbg_cp", [128, 8 * LC], F32, kind="ExternalOutput").ap()
        dbg_dt = nc.dram_tensor("dbg_dt", [128, LC], F32, kind="ExternalOutput").ap()
        dbg_v = nc.dram_tensor("dbg_v", [128, LC], F32, kind="ExternalOutput").ap()

    xph_hi = nc.dram_tensor("xph_hi", [128, 32 * XPR], BF16).ap()
    xph_lo = nc.dram_tensor("xph_lo", [128, 32 * XPR], BF16).ap()
    lbh_hi = nc.dram_tensor("lbh_hi", [128, OH], BF16).ap()
    lbh_lo = nc.dram_tensor("lbh_lo", [128, OH], BF16).ap()
    lah_hi = nc.dram_tensor("lah_hi", [128, NKT * 128], BF16).ap()
    lah_lo = nc.dram_tensor("lah_lo", [128, NKT * 128], BF16).ap()
    wh_hi = nc.dram_tensor("wh_hi", [NOT, 128, NKT * 128], BF16).ap()
    wh_lo = nc.dram_tensor("wh_lo", [NOT, 128, NKT * 128], BF16).ap()
    yscr = nc.dram_tensor("yscr", [DH, LC], F32).ap()
    bchbm = nc.dram_tensor("bchbm", [32, LC], F32).ap()
    r2scr = nc.dram_tensor("r2scr", [OH, LC], F32).ap()
    ag1i = [nc.dram_tensor(f"ag1i{k}", [DH, LC], F32).ap() for k in range(NCH)]
    ag1o = [nc.dram_tensor(f"ag1o{k}", [2 * DH, LC], F32).ap() for k in range(NCH)]
    ag2i = [nc.dram_tensor(f"ag2i{k}", [2, LC], F32).ap() for k in range(NCH)]
    ag2o = [nc.dram_tensor(f"ag2o{k}", [2, LC], F32).ap() for k in range(NCH)]

    from contextlib import ExitStack
    stack = ExitStack()
    e = stack.enter_context

    sb = {}
    def S(name, shape, dt=F32):
        sb[name] = e(nc.sbuf_tensor(name, shape, dt))
        return sb[name]

    S("ones_r", [1, 128]); S("ones_c", [128, 1])
    S("idn", [128, 128]); S("sel8", [8, 128])
    S("A", [128, NDT * NST]); S("onepD", [128, NDT]); S("dtb", [128, NDT])
    S("bb", [128, NOT]); S("br", [8, 1])
    S("dtw_hi", [128, DH], BF16); S("dtw_lo", [128, DH], BF16)
    S("wr_hi", [128, NKT * 8], BF16); S("wr_lo", [128, NKT * 8], BF16)
    S("carry", [128, NDT * NST])
    S("xpw_hi", [128, 4 * XPR], BF16); S("xpw_lo", [128, 4 * XPR], BF16)
    S("Bp", [128, 8 * LC]); S("Cp", [128, 8 * LC])
    S("dAz0", [128, 4 * LC]); S("dAz1", [128, 4 * LC])
    S("dBx0", [128, 4 * LC]); S("dBx1", [128, 4 * LC])
    S("hp", [128, 4 * LC])
    for j in range(2):
        S(f"xhi{j}", [128, LC], BF16); S(f"xlo{j}", [128, LC], BF16)
        S(f"dtt{j}", [128, LC]); S(f"xot{j}", [128, LC]); S(f"vt{j}", [128, LC])
        S(f"yh{j}", [128, LC]); S(f"res{j}", [128, LC])
    S("ysc", [128, LC]); S("zero4", [128, 4]); S("tc4", [128, 4])
    S("zz", [128, 4 * LC])
    S("lbt_hi0", [128, 128], BF16); S("lbt_lo0", [128, 128], BF16)
    S("lbt_hi1", [128, 128], BF16); S("lbt_lo1", [128, 128], BF16)
    S("dtin_hi", [128, LC], BF16); S("dtin_lo", [128, LC], BF16)
    S("bc_c", [32, LC])
    S("stat_sb", [64, LC])
    S("xn_hi", [128, NKT * SBT], BF16); S("xn_lo", [128, NKT * SBT], BF16)
    S("wla", [128, 8192], BF16)
    for j in range(2):
        S(f"gt{j}", [128, SBT])
        S(f"fft{j}", [128, SBT]); S(f"rr{j}", [128, SBT])
        S(f"r2l{j}", [128, LC])
    S("sq2", [128, SBT])
    S("sq2b", [128, SBT])
    S("stv", [1, 3 * LC])
    S("mbc", [128, LC]); S("rbc", [128, LC])
    S("rt8", [8, SBT]); S("lgt", [128, 16])
    S("sortp", [128, 48])
    S("wfin", [128, 16]); S("wt8", [8, SBT]); S("wbc", [128, SBT])
    S("lh", [128, SBT])
    S("hw_hi", [128, SBT], BF16); S("hw_lo", [128, SBT], BF16)

    wla = sb["wla"]
    wla_f32 = wla[:].bitcast(F32)
    wf32 = wla_f32[:, 0:2048]
    cst_hi = wla[:, 4096:6144]
    cst_lo = wla[:, 6144:8192]
    WQ = ("wla0", "wla1", "wla2", "wla3")

    ps = {}
    def PSUM(name, shape):
        ps[name] = e(nc.psum_tensor(name, shape, F32))
    PSUM("xp1", [128, LC])
    PSUM("xp2", [64, LC])
    PSUM("bcb", [128, LC])
    PSUM("st", [64, LC])
    PSUM("rtl", [128, SBT])
    PSUM("misc", [128, LC])
    PSUM("ff0", [128, SBT]); PSUM("ff1", [128, SBT])

    chans = {}
    for en in Prog.ENGINES:
        chans[(en, "c")] = e(nc.semaphore(f"c_{en}"))
    FAMS = ["consts", "xphw", "lahw", "wbmw", "wla", "xlr0", "xlr1", "xot0", "xot1", "xpw",
            "bp", "cp", "gt0", "gt1", "rr0", "rr1", "yscl", "bchbm", "stv", "out",
            "yscr0", "yscr1", "r2s0", "r2s1", "agw0", "agw1", "arw", "ag2w", "lbhw", "lbt",
            "cag0", "cag1", "car0", "car1", "car20", "car21", "dbg"]
    for f in FAMS:
        chans[("fam", f)] = e(nc.semaphore(f"f_{f}"))

    def dma(eng, out_ap, in_ap, reads, writes, name="", fam="consts", R=0):
        P.op(eng, lambda E, o=out_ap, i=in_ap: E.dma_start(out=o, in_=i),
             reads=reads, writes=writes, kind="d", name=name, fam=fam, R=R)

    C = P.op

    C("vector", lambda E: E.memset(sb["ones_r"][:], 1.0), (), ("ones_r",))
    C("vector", lambda E: E.memset(sb["ones_c"][:], 1.0), (), ("ones_c",))
    C("vector", lambda E: E.memset(sb["zero4"][:], 0.0), (), ("zero4",))
    dma("sync", sb["idn"][:], idni, (), ("idn",))
    dma("sync", sb["sel8"][:], sel8i, (), ("sel8",))
    dma("sync", sb["A"][:], alog, (), ("A",))
    C("scalar", lambda E: E.activation(sb["A"][:], sb["A"][:], AF.Exp), ("A",), ("A",))
    C("vector", lambda E: E.tensor_single_scalar(sb["A"][:], sb["A"][:], -1.0, ALU.mult), ("A",), ("A",))
    dma("sync", sb["onepD"][:], dvec, (), ("onepD",))
    C("vector", lambda E: E.tensor_single_scalar(sb["onepD"][:], sb["onepD"][:], 1.0, ALU.add), ("onepD",), ("onepD",))
    dma("sync", sb["dtb"][:], dtbi, (), ("dtb",))
    dma("sync", sb["bb"][:], bbt, (), ("bb",))
    dma("sync", sb["br"][:], brt, (), ("br",))

    def stage_cast(src_ap, cols, hi_dst, hi_key, lo_dst, lo_key):
        dma("sync", wf32[:, 0:cols], src_ap, ("wla0", "wla1"), ("wla0", "wla1"), fam="wla")
        C("scalar", lambda E, c=cols, h=hi_dst: E.activation(h, wf32[:, 0:c], AF.Copy),
          ("wla0", "wla1"), (hi_key,))
        C("vector", lambda E, c=cols, h=hi_dst, l=lo_dst: E.tensor_tensor(l, wf32[:, 0:c], h, ALU.subtract),
          ("wla0", "wla1", hi_key), (lo_key,))

    stage_cast(dtwi, DH, sb["dtw_hi"][:], "dtw_hi", sb["dtw_lo"][:], "dtw_lo")
    stage_cast(wrti, NKT * 8, sb["wr_hi"][:], "wr_hi", sb["wr_lo"][:], "wr_lo")

    def cast_to_hbm(nm, src_ap, dst_hi, dst_lo, total):
        off = 0
        while off < total:
            w = min(2048, total - off)
            dma("sync", wf32[:, 0:w], src_ap[:, off:off + w], ("wla0", "wla1"), ("wla0", "wla1"), fam="wla")
            C("scalar", lambda E, w=w: E.activation(cst_hi[:, 0:w], wf32[:, 0:w], AF.Copy),
              ("wla0", "wla1"), ("wla2",))
            C("vector", lambda E, w=w: E.tensor_tensor(cst_lo[:, 0:w], wf32[:, 0:w], cst_hi[:, 0:w], ALU.subtract),
              ("wla0", "wla1", "wla2"), ("wla3",))
            dma("sync", dst_hi[:, off:off + w], cst_hi[:, 0:w], ("wla2",), (f"{nm}H",), fam=nm + "w")
            dma("sync", dst_lo[:, off:off + w], cst_lo[:, 0:w], ("wla3",), (f"{nm}L",), fam=nm + "w")
            off += w

    cast_to_hbm("xph", xpw, xph_hi, xph_lo, 32 * XPR)
    cast_to_hbm("lah", las, lah_hi, lah_lo, NKT * 128)
    cast_to_hbm("lbh", lbsi, lbh_hi, lbh_lo, OH)

    wblocks = [(o, hf) for o in range(NOT) for hf in range(2)]
    widx = [0]

    def emit_wcast(nblk):
        for _ in range(nblk):
            if widx[0] >= len(wblocks):
                return
            o, hf = wblocks[widx[0]]
            widx[0] += 1
            sl = slice(hf * 2048, (hf + 1) * 2048)
            dma("sync", wf32[:], wbs[o, :, sl], ("wla0", "wla1"), ("wla0", "wla1"), fam="wla")
            C("scalar", lambda E: E.activation(cst_hi[:], wf32[:], AF.Copy),
              ("wla0", "wla1"), ("wla2",))
            C("vector", lambda E: E.tensor_tensor(cst_lo[:], wf32[:], cst_hi[:], ALU.subtract),
              ("wla0", "wla1", "wla2"), ("wla3",))
            dma("sync", wh_hi[o, :, sl], cst_hi[:], ("wla2",), (f"wbm{o}_{hf}H",), fam="wbmw")
            dma("sync", wh_lo[o, :, sl], cst_lo[:], ("wla3",), (f"wbm{o}_{hf}L",), fam="wbmw")

    def scan_side(k):
        c0 = k * LC
        n_mm = [0]
        for pc in range(8):
            poff = pc * 4 * XPR
            dma("sync", sb["xpw_hi"][:], xph_hi[:, poff:poff + 4 * XPR], ("xphH", "xpw_hi"), ("xpw_hi",), fam="xpw")
            dma("sync", sb["xpw_lo"][:], xph_lo[:, poff:poff + 4 * XPR], ("xphL", "xpw_lo"), ("xpw_lo",), fam="xpw")
            for tl in range(4):
                t = pc * 4 + tl
                j = t % 2
                dma("scalar", sb[f"r2l{j}"][:], xt[t * 128:(t + 1) * 128, c0:c0 + LC], (), (f"xlr{j}",), fam=f"xlr{j}")
                C("scalar", lambda E, j=j: E.activation(sb[f"xhi{j}"][:], sb[f"r2l{j}"][:], AF.Copy),
                  (f"xlr{j}",), (f"xhi{j}",))
                C("vector", lambda E, j=j: E.tensor_tensor(sb[f"xlo{j}"][:], sb[f"r2l{j}"][:], sb[f"xhi{j}"][:], ALU.subtract),
                  (f"xlr{j}", f"xhi{j}"), (f"xlo{j}",))
                for (wk, xk) in (("hi", "hi"), ("hi", "lo"), ("lo", "hi")):
                    s0 = (n_mm[0] == 0)
                    s1 = (n_mm[0] == 95)
                    C("tensor", lambda E, wk=wk, xk=xk, j=j, tl=tl, s0=s0, s1=s1: E.matmul(
                        ps["xp1"][:], sb[f"xpw_{wk}"][:, tl * XPR:tl * XPR + 128], sb[f"x{xk}{j}"][:],
                        start=s0, stop=s1),
                        (f"xpw_{wk}", f"x{xk}{j}"), ("xp1",))
                    C("tensor", lambda E, wk=wk, xk=xk, j=j, tl=tl, s0=s0, s1=s1: E.matmul(
                        ps["bcb"][0:32, :], sb[f"xpw_{wk}"][:, tl * XPR + 128:(tl + 1) * XPR], sb[f"x{xk}{j}"][:],
                        start=s0, stop=s1),
                        (f"xpw_{wk}", f"x{xk}{j}"), ("bcb",))
                    n_mm[0] += 1
        C("scalar", lambda E: E.activation(sb["dtin_hi"][:], ps["xp1"][:], AF.Copy),
          ("xp1",), ("dtin_hi",))
        C("vector", lambda E: E.tensor_tensor(sb["dtin_lo"][:], ps["xp1"][:], sb["dtin_hi"][:], ALU.subtract),
          ("xp1", "dtin_hi"), ("dtin_lo",))
        C("scalar", lambda E: E.activation(sb["bc_c"][:], ps["bcb"][0:32, :], AF.Copy),
          ("bcb",), ("bc_c",))
        dma("scalar", bchbm, sb["bc_c"][:], ("bc_c",), ("bchbm",), fam="bchbm")

    def bcast_half(k, h):
        srcB = bchbm[h * 8:(h + 1) * 8, :].flatten().unsqueeze(0).broadcast_to([128, 8 * LC])
        srcC = bchbm[16 + h * 8:16 + (h + 1) * 8, :].flatten().unsqueeze(0).broadcast_to([128, 8 * LC])
        dma("gpsimd", sb["Bp"][:], srcB, ("bchbm",), tuple(f"Bp{z}" for z in range(8)), fam="bp")
        dma("gpsimd", sb["Cp"][:], srcC, ("bchbm",), tuple(f"Cp{z}" for z in range(8)), fam="cp")
        if DEBUG and k == 1 and h == 0:
            dma("sync", dbg_bp, sb["Bp"][:], tuple(f"Bp{z}" for z in range(8)), ("dbgbp",), fam="dbg")
            dma("sync", dbg_cp, sb["Cp"][:], tuple(f"Cp{z}" for z in range(8)), ("dbgcp",), fam="dbg")

    def dtile(k, h, i):
        j = i % 2
        for pi, (wk, xk) in enumerate((("hi", "hi"), ("hi", "lo"), ("lo", "hi"))):
            C("tensor", lambda E, wk=wk, xk=xk, i=i, s=(pi == 0), t=(pi == 2): E.matmul(
                ps["xp1"][:], sb[f"dtw_{wk}"][:, i * 128:(i + 1) * 128], sb[f"dtin_{xk}"][:],
                start=s, stop=t),
                (f"dtw_{wk}", f"dtin_{xk}"), ("xp1",))
        C("scalar", lambda E, i=i, j=j: E.activation(sb[f"dtt{j}"][:], ps["xp1"][:], AF.Exp, bias=sb["dtb"][:, i:i + 1]),
          ("xp1", "dtb"), (f"dtt{j}",))
        C("scalar", lambda E, j=j: E.activation(sb[f"dtt{j}"][:], sb[f"dtt{j}"][:], AF.Ln, bias=1.0),
          (f"dtt{j}",), (f"dtt{j}",))
        dma("scalar", sb[f"xot{j}"][:], xt[i * 128:(i + 1) * 128, k * LC:(k + 1) * LC], (), (f"xot{j}",), fam=f"xot{j}")
        C("gpsimd", lambda E, j=j: E.tensor_tensor(sb[f"vt{j}"][:], sb[f"dtt{j}"][:], sb[f"xot{j}"][:], ALU.mult),
          (f"dtt{j}", f"xot{j}"), (f"vt{j}",))
        if DEBUG and k == 1 and h == 0 and i == 0:
            dma("sync", dbg_dt, sb[f"dtt{j}"][:], (f"dtt{j}",), ("dbgdt",), fam="dbg")
            dma("sync", dbg_v, sb[f"vt{j}"][:], (f"vt{j}",), ("dbgv",), fam="dbg")

    ucnt = [0]

    def quarter(k, h, i, q):
        j = i % 2
        ub = ucnt[0] % 2
        ucnt[0] += 1
        dA = f"dAz{ub}"
        dB = f"dBx{ub}"
        for jj in range(4):
            n = h * 8 + q * 4 + jj
            col = i * NST + n
            C("scalar", lambda E, jj=jj, col=col, j=j, dA=dA: E.activation(
                sb[dA][:, jj * LC:(jj + 1) * LC], sb[f"dtt{j}"][:], AF.Exp,
                scale=sb["A"][:, col:col + 1]),
                (f"dtt{j}", "A"), (dA,))
        vb = sb[f"vt{j}"][:].unsqueeze(1).broadcast_to([128, 4, LC])
        C("gpsimd", lambda E, q=q, dB=dB, vb=vb: E.tensor_tensor(
            sb[dB][:].rearrange("p (n l) -> p n l", n=4),
            sb["Bp"][:, q * 4 * LC:(q + 1) * 4 * LC].rearrange("p (n l) -> p n l", n=4),
            vb, ALU.mult),
            tuple([f"vt{j}"] + [f"Bp{q * 4 + z}" for z in range(4)]), (dB,))
        colb0 = i * NST + h * 8 + q * 4
        if k > 0:
            dA_f = sb[dA][:].rearrange("p (n l) -> p n l", n=4)[:, :, 0]
            dB_f = sb[dB][:].rearrange("p (n l) -> p n l", n=4)[:, :, 0]
            C("gpsimd", lambda E, dA_f=dA_f, colb0=colb0: E.tensor_tensor(
                sb["tc4"][:], dA_f, sb["carry"][:, colb0:colb0 + 4], ALU.mult),
                tuple([dA] + [f"carry{colb0 + z}" for z in range(4)]), ("tc4",))
            C("gpsimd", lambda E, dB_f=dB_f: E.tensor_tensor(
                dB_f, dB_f, sb["tc4"][:], ALU.add),
                (dB, "tc4"), (dB,))
        for jj in range(4):
            rd = [dA, dB]
            C("vector", lambda E, jj=jj, dA=dA, dB=dB: E.tensor_tensor_scan(
                sb["hp"][:, jj * LC:(jj + 1) * LC],
                sb[dA][:, jj * LC:(jj + 1) * LC],
                sb[dB][:, jj * LC:(jj + 1) * LC],
                0.0, ALU.mult, ALU.add),
                tuple(rd), ("hp",))
        C("vector", lambda E, q=q: E.tensor_tensor(
            sb["zz"][:], sb["hp"][:], sb["Cp"][:, q * 4 * LC:(q + 1) * 4 * LC], ALU.mult),
            tuple(["hp"] + [f"Cp{q * 4 + z}" for z in range(4)]), ("zz",))
        colb = i * NST + h * 8 + q * 4
        C("vector", lambda E, colb=colb: E.tensor_tensor(
            sb["carry"][:, colb:colb + 4],
            sb["hp"][:].rearrange("p (n l) -> p n l", n=4)[:, :, LC - 1],
            sb["zero4"][:], ALU.add),
            ("hp", "zero4"), tuple(f"carry{colb + z}" for z in range(4)))
        C("gpsimd", lambda E: E.tensor_tensor(
            sb["zz"][:, 0:2 * LC], sb["zz"][:, 0:2 * LC], sb["zz"][:, 2 * LC:4 * LC], ALU.add),
            ("zz",), ("zz",))
        j2 = i % 2
        if q == 0:
            C("gpsimd", lambda E, j2=j2: E.tensor_tensor(
                sb[f"yh{j2}"][:], sb["zz"][:, 0:LC], sb["zz"][:, LC:2 * LC], ALU.add),
                ("zz",), (f"yh{j2}",))
        else:
            C("gpsimd", lambda E: E.tensor_tensor(
                sb["zz"][:, 2 * LC:3 * LC], sb["zz"][:, 0:LC], sb["zz"][:, LC:2 * LC], ALU.add),
                ("zz",), ("zz",))
            C("gpsimd", lambda E, j2=j2: E.tensor_tensor(
                sb[f"yh{j2}"][:], sb[f"yh{j2}"][:], sb["zz"][:, 2 * LC:3 * LC], ALU.add),
                ("zz", f"yh{j2}"), (f"yh{j2}",))

    def dtile_end(k, h, i):
        j = i % 2
        if h == 0:
            dma("gpsimd", yscr[i * 128:(i + 1) * 128, :], sb[f"yh{j}"][:],
                (f"yh{j}",), (f"yscr{i}",), fam=f"yscr{k % 2}")
        else:
            dma("scalar", sb["ysc"][:], yscr[i * 128:(i + 1) * 128, :],
                (f"yscr{i}",), ("ysc",), fam="yscl")
            C("vector", lambda E, j=j: E.tensor_tensor(sb[f"yh{j}"][:], sb[f"yh{j}"][:], sb["ysc"][:], ALU.add),
              (f"yh{j}", "ysc"), (f"yh{j}",))
            C("vector", lambda E, j=j, i=i: E.scalar_tensor_tensor(
                sb[f"res{j}"][:], sb[f"xot{j}"][:], sb["onepD"][:, i:i + 1], sb[f"yh{j}"][:],
                ALU.mult, ALU.add),
                (f"xot{j}", "onepD", f"yh{j}"), (f"res{j}",))
            dma("scalar", ag1i[k][i * 128:(i + 1) * 128, :], sb[f"res{j}"][:],
                (f"res{j}",), (f"ag1w{k}_{i}",), fam=f"agw{k % 2}")
            if DEBUG:
                dma("gpsimd", dbg_res[i * 128:(i + 1) * 128, k * LC:(k + 1) * LC], sb[f"res{j}"][:],
                    (f"res{j}",), (f"dbgres{k}_{i}",), fam="dbg")

    def ag1(k):
        reads = tuple(f"ag1w{k}_{i}" for i in range(NDT))
        P.op("gpsimd", lambda E: E.collective_compute(
            "AllGather", ALU.bypass, replica_groups=RG, ins=[ag1i[k]], outs=[ag1o[k]]),
            reads=reads, writes=(f"gath{k}",), kind="cc", name=f"ag1_{k}", fam=f"cag{k % 2}")

    def ln_stats(m_name, r_name, width=LC):
        stv = sb["stv"]
        s0 = stv[:, 0:width]; s1 = stv[:, LC:LC + width]; s2 = stv[:, 2 * LC:2 * LC + width]
        C("vector", lambda E: E.tensor_single_scalar(s0, s0, 1.0 / D, ALU.mult),
          ("stv0",), ("stv0",))
        C("tensor", lambda E, w=width: E.matmul(ps["misc"][:, 0:w], sb["ones_r"][0:1, :], s0[0:1, :], start=True, stop=True),
          ("stv0", "ones_r"), ("misc",))
        C("scalar", lambda E, w=width: E.activation(sb["mbc"][:, 0:w], ps["misc"][:, 0:w], AF.Copy),
          ("misc",), (m_name, "mbc"))
        C("vector", lambda E: E.tensor_single_scalar(s1, s1, 1.0 / D, ALU.mult),
          ("stv1",), ("stv1",))
        C("vector", lambda E: E.tensor_tensor(s2, s0, s0, ALU.mult),
          ("stv0",), ("stv2",))
        C("vector", lambda E: E.tensor_tensor(s1, s1, s2, ALU.subtract),
          ("stv1", "stv2"), ("stv1",))
        C("vector", lambda E: E.tensor_single_scalar(s1, s1, EPS, ALU.add),
          ("stv1",), ("stv1",))
        C("scalar", lambda E: E.activation(s2, s1, AF.Ln),
          ("stv1",), ("stv2",))
        C("scalar", lambda E: E.activation(s2, s2, AF.Exp, scale=-0.5),
          ("stv2",), ("stv2",))
        C("vector", lambda E: E.tensor_tensor(s0, s2, s2, ALU.mult),
          ("stv2",), ("stv0",))
        C("vector", lambda E: E.tensor_tensor(s0, s0, s1, ALU.mult),
          ("stv0", "stv1"), ("stv0",))
        C("vector", lambda E: E.tensor_scalar(s0, s0, -0.5, 1.5, ALU.mult, ALU.add),
          ("stv0",), ("stv0",))
        C("vector", lambda E: E.tensor_tensor(s2, s2, s0, ALU.mult),
          ("stv2", "stv0"), ("stv2",))
        C("tensor", lambda E, w=width: E.matmul(ps["misc"][:, 0:w], sb["ones_r"][0:1, :], s2[0:1, :], start=True, stop=True),
          ("stv2", "ones_r"), ("misc",))
        C("scalar", lambda E, w=width: E.activation(sb["rbc"][:, 0:w], ps["misc"][:, 0:w], AF.Copy),
          ("misc",), (r_name, "rbc"))

    def sparsemax_ops(sbi):
        pool = sb["sortp"]
        W2 = 2
        def sl(s):
            return pool[:, s * W2:(s + 1) * W2]
        slots = {}
        free = list(range(24))
        for ee in range(8):
            s0 = free.pop(0)
            slots[ee] = s0
            C("vector", lambda E, ee=ee, s0=s0: E.tensor_copy(sl(s0), sb["lgt"][:, ee * W2:(ee + 1) * W2]),
              ("lgt",), (f"sp{s0}",))
        for (i, jx) in SORT8:
            si, sj = slots[i], slots[jx]
            a = free.pop(0); b2 = free.pop(0)
            C("vector", lambda E, si=si, sj=sj, a=a: E.tensor_tensor(sl(a), sl(si), sl(sj), ALU.max),
              (f"sp{si}", f"sp{sj}"), (f"sp{a}",))
            C("vector", lambda E, si=si, sj=sj, b2=b2: E.tensor_tensor(sl(b2), sl(si), sl(sj), ALU.min),
              (f"sp{si}", f"sp{sj}"), (f"sp{b2}",))
            free.append(si); free.append(sj)
            slots[i], slots[jx] = a, b2
        cum = {}
        c0s = free.pop(0)
        cum[0] = c0s
        C("vector", lambda E, s=slots[0], c=c0s: E.tensor_copy(sl(c), sl(s)),
          (f"sp{slots[0]}",), (f"sp{c0s}",))
        for ee in range(1, 8):
            cs = free.pop(0)
            C("vector", lambda E, p=cum[ee - 1], z=slots[ee], cs=cs: E.tensor_tensor(sl(cs), sl(p), sl(z), ALU.add),
              (f"sp{cum[ee - 1]}", f"sp{slots[ee]}"), (f"sp{cs}",))
            cum[ee] = cs
        tsl = free.pop(0); msl = free.pop(0); ksl = free.pop(0)
        for ee in range(8):
            C("vector", lambda E, z=slots[ee], c=cum[ee], t=tsl, ee=ee: E.scalar_tensor_tensor(
                sl(t), sl(z), float(ee + 1), sl(c), ALU.mult, ALU.subtract),
                (f"sp{slots[ee]}", f"sp{cum[ee]}"), (f"sp{tsl}",))
            C("vector", lambda E, t=tsl, m=msl: E.tensor_single_scalar(sl(m), sl(t), -1.0, ALU.is_gt),
              (f"sp{tsl}",), (f"sp{msl}",))
            if ee == 0:
                C("vector", lambda E, m=msl, ks=ksl: E.tensor_copy(sl(ks), sl(m)),
                  (f"sp{msl}",), (f"sp{ksl}",))
            else:
                C("vector", lambda E, m=msl, ks=ksl: E.tensor_tensor(sl(ks), sl(ks), sl(m), ALU.add),
                  (f"sp{msl}", f"sp{ksl}"), (f"sp{ksl}",))
        acc, eq = tsl, msl
        t2 = free[0]
        for ee in range(8):
            C("vector", lambda E, ks=ksl, eqs=eq, ee=ee: E.tensor_single_scalar(sl(eqs), sl(ks), float(ee + 1), ALU.is_equal),
              (f"sp{ksl}",), (f"sp{eq}",))
            if ee == 0:
                C("vector", lambda E, c=cum[ee], eqs=eq, a=acc: E.tensor_tensor(sl(a), sl(c), sl(eqs), ALU.mult),
                  (f"sp{cum[ee]}", f"sp{eq}"), (f"sp{acc}",))
            else:
                C("vector", lambda E, c=cum[ee], eqs=eq, t2=t2: E.tensor_tensor(sl(t2), sl(c), sl(eqs), ALU.mult),
                  (f"sp{cum[ee]}", f"sp{eq}"), (f"sp{t2}",))
                C("vector", lambda E, a=acc, t2=t2: E.tensor_tensor(sl(a), sl(a), sl(t2), ALU.add),
                  (f"sp{acc}", f"sp{t2}"), (f"sp{acc}",))
        C("vector", lambda E, ks=ksl, eqs=eq: E.reciprocal(sl(eqs), sl(ks)),
          (f"sp{ksl}",), (f"sp{eq}",))
        C("vector", lambda E, a=acc: E.tensor_single_scalar(sl(a), sl(a), -1.0, ALU.add),
          (f"sp{acc}",), (f"sp{acc}",))
        C("vector", lambda E, a=acc, eqs=eq: E.tensor_tensor(sl(a), sl(a), sl(eqs), ALU.mult),
          (f"sp{acc}", f"sp{eq}"), (f"sp{acc}",))
        for ee in range(8):
            C("vector", lambda E, ee=ee, a=acc: E.tensor_tensor(
                sb["wfin"][:, ee * W2:(ee + 1) * W2], sb["lgt"][:, ee * W2:(ee + 1) * W2], sl(a), ALU.subtract),
                (f"sp{acc}", "lgt"), (f"wf{ee}",))
            C("vector", lambda E, ee=ee: E.tensor_single_scalar(
                sb["wfin"][:, ee * W2:(ee + 1) * W2], sb["wfin"][:, ee * W2:(ee + 1) * W2], 0.0, ALU.max),
                (f"wf{ee}",), (f"wf{ee}", "wfin"))

    def ffn_prep(k, sbi):
        cs = sbi * SBT
        for t in range(NKT):
            j = t % 2
            row = t * 128
            dma("scalar", sb[f"gt{j}"][:], ag1o[k][row:row + 128, cs:cs + SBT], (f"gath{k}",), (f"gt{j}",), fam=f"gt{j}")
            C("scalar", lambda E, j=j: E.activation(sb["sq2"][:], sb[f"gt{j}"][:], AF.Square),
              (f"gt{j}",), ("sq2",))
            C("tensor", lambda E, j=j, s=(t == 0), st=(t == NKT - 1): E.matmul(
                ps["st"][0:1, 0:SBT], sb["ones_c"][:, 0:1], sb[f"gt{j}"][:], start=s, stop=st),
                ("ones_c", f"gt{j}"), ("st", "st0"))
            C("tensor", lambda E, s=(t == 0), st=(t == NKT - 1): E.matmul(
                ps["st"][32:33, 0:SBT], sb["ones_c"][:, 0:1], sb["sq2"][:], start=s, stop=st),
                ("ones_c", "sq2"), ("st", "st1"))
        stv = sb["stv"]
        C("scalar", lambda E: E.activation(stv[0:1, 0:SBT], ps["st"][0:1, 0:SBT], AF.Copy),
          ("st0", "st1", "st"), ("stv0",))
        C("scalar", lambda E: E.activation(stv[0:1, LC:LC + SBT], ps["st"][32:33, 0:SBT], AF.Copy),
          ("st0", "st1", "st"), ("stv1",))
        ln_stats("mbc_v", "rbc_v", width=SBT)
        for t in range(NKT):
            j = t % 2
            row = t * 128
            dma("scalar", sb[f"gt{j}"][:], ag1o[k][row:row + 128, cs:cs + SBT], (f"gath{k}",), (f"gt{j}",), fam=f"gt{j}")
            C("vector", lambda E, j=j: E.tensor_tensor(sb[f"gt{j}"][:], sb[f"gt{j}"][:], sb["mbc"][:, 0:SBT], ALU.subtract),
              (f"gt{j}", "mbc_v", "mbc"), (f"gt{j}",))
            C("vector", lambda E, j=j: E.tensor_tensor(sb[f"gt{j}"][:], sb[f"gt{j}"][:], sb["rbc"][:, 0:SBT], ALU.mult),
              (f"gt{j}", "rbc_v", "rbc"), (f"gt{j}",))
            C("scalar", lambda E, j=j, t=t: E.activation(sb["xn_hi"][:, t * SBT:(t + 1) * SBT], sb[f"gt{j}"][:], AF.Copy),
              (f"gt{j}",), (f"xnh{t}",))
            C("vector", lambda E, j=j, t=t: E.tensor_tensor(
                sb["xn_lo"][:, t * SBT:(t + 1) * SBT], sb[f"gt{j}"][:], sb["xn_hi"][:, t * SBT:(t + 1) * SBT], ALU.subtract),
                (f"gt{j}", f"xnh{t}"), (f"xnl{t}",))
            if DEBUG:
                dma("sync", dbg_xn[t * 128:(t + 1) * 128, k * LC + cs:k * LC + cs + SBT], sb[f"gt{j}"][:],
                    (f"gt{j}",), (f"dbgxn{k}_{sbi}_{t}",), fam="dbg")
        nm = [0]
        for t in range(NKT):
            for (wk, xk) in (("hi", "hi"), ("hi", "lo"), ("lo", "hi")):
                C("tensor", lambda E, t=t, wk=wk, xk=xk, s=(nm[0] == 0), st=(nm[0] == 95): E.matmul(
                    ps["rtl"][0:8, :], sb[f"wr_{wk}"][:, t * 8:(t + 1) * 8],
                    sb["xn_hi" if xk == "hi" else "xn_lo"][:, t * SBT:(t + 1) * SBT],
                    start=s, stop=st),
                    (f"wr_{wk}", f"xn{'h' if xk == 'hi' else 'l'}{t}"), ("rtl",))
                nm[0] += 1
        C("scalar", lambda E: E.activation(sb["rt8"][:], ps["rtl"][0:8, :], AF.Identity, bias=sb["br"][:, 0:1]),
          ("rtl", "br"), ("rt8",))
        dma("sync", wla[:, 0:4096], lah_hi, ("lahH", "wla0", "wla1"), ("wla0", "wla1"), fam="wla")
        dma("sync", wla[:, 4096:8192], lah_lo, ("lahL", "wla2", "wla3"), ("wla2", "wla3"), fam="wla")
        nm = [0]
        for t in range(NKT):
            for (wk, xk) in (("hi", "hi"), ("hi", "lo"), ("lo", "hi")):
                off = t * 128 if wk == "hi" else 4096 + t * 128
                rk = ("wla0", "wla1") if wk == "hi" else ("wla2", "wla3")
                C("tensor", lambda E, off=off, t=t, xk=xk, s=(nm[0] == 0), st=(nm[0] == 95): E.matmul(
                    ps["rtl"][:], wla[:, off:off + 128],
                    sb["xn_hi" if xk == "hi" else "xn_lo"][:, t * SBT:(t + 1) * SBT],
                    start=s, stop=st),
                    rk + (f"xn{'h' if xk == 'hi' else 'l'}{t}", "rt8"), ("rtl",))
                nm[0] += 1
        C("scalar", lambda E: E.activation(sb["lh"][:], ps["rtl"][:], AF.Copy),
          ("rtl",), ("lh",))
        lgt_v = sb["lgt"][:].rearrange("p (e g) -> p e g", g=2)
        for g in range(2):
            C("tensor", lambda E, g=g: E.transpose(ps["misc"][:, g * 8:(g + 1) * 8],
                                                   sb["rt8"][:, g * 128:(g + 1) * 128], sb["idn"][0:8, 0:8]),
              ("rt8", "idn"), ("misc",))
            C("scalar", lambda E, g=g, lv=lgt_v: E.activation(lv[:, :, g], ps["misc"][:, g * 8:(g + 1) * 8], AF.Copy),
              ("misc",), ("lgt",))
        sparsemax_ops(sbi)
        wf_keys = tuple(f"wf{ee}" for ee in range(8))
        wfin_v = sb["wfin"][:].rearrange("p (e g) -> p e g", g=2)
        for g in range(2):
            C("tensor", lambda E, g=g, wv=wfin_v: E.transpose(ps["misc"][0:8, 128 + g * 128:128 + (g + 1) * 128],
                                                              wv[:, :, g], sb["idn"][:]),
              wf_keys + ("idn",), ("misc",))
            C("scalar", lambda E, g=g: E.activation(sb["wt8"][:, g * 128:(g + 1) * 128],
                                                    ps["misc"][0:8, 128 + g * 128:128 + (g + 1) * 128], AF.Copy),
              ("misc",), ("wt8",))
        C("tensor", lambda E: E.matmul(ps["misc"][:, 0:SBT], sb["sel8"][:], sb["wt8"][:], start=True, stop=True),
          ("sel8", "wt8"), ("misc",))
        C("scalar", lambda E: E.activation(sb["wbc"][:], ps["misc"][:, 0:SBT], AF.Copy),
          ("misc",), ("wbc",))
        C("vector", lambda E: E.tensor_tensor(sb["lh"][:], sb["lh"][:], sb["wbc"][:], ALU.mult),
          ("lh", "wbc"), ("hw", "lh"))
        C("scalar", lambda E: E.activation(sb["hw_hi"][:], sb["lh"][:], AF.Copy),
          ("hw",), ("hw_hi",))
        C("vector", lambda E: E.tensor_tensor(sb["hw_lo"][:], sb["lh"][:], sb["hw_hi"][:], ALU.subtract),
          ("hw", "hw_hi"), ("hw_lo",))

    pend_stats = []

    def emit_stats():
        while pend_stats:
            o, sbi, jo, sqn, cs = pend_stats.pop(0)
            C("tensor", lambda E, jo=jo, cs=cs, s=(o == 0), t=(o == NOT - 1): E.matmul(
                ps["xp2"][0:1, cs:cs + SBT], sb["ones_c"][:, 0:1], sb[f"fft{jo}"][:, 0:SBT], start=s, stop=t),
                ("ones_c", f"fft{jo}"), ("xp2", "xp2s"))
            C("tensor", lambda E, sqn=sqn, cs=cs, s=(o == 0), t=(o == NOT - 1): E.matmul(
                ps["xp2"][32:33, cs:cs + SBT], sb["ones_c"][:, 0:1], sb[sqn][:], start=s, stop=t),
                ("ones_c", sqn), ("xp2", "xp2q"))

    def ffn_otile(k, sbi, o):
        emit_stats()
        cs = sbi * SBT
        jo = o % 2
        fb = f"ff{jo}"
        dma("sync", wla[:, 0:2048], wh_hi[o, :, 0:2048], (f"wbm{o}_0H", "wla0"), ("wla0",), fam="wla")
        dma("sync", wla[:, 4096:6144], wh_lo[o, :, 0:2048], (f"wbm{o}_0L", "wla2"), ("wla2",), fam="wla")
        dma("sync", wla[:, 2048:4096], wh_hi[o, :, 2048:4096], (f"wbm{o}_1H", "wla1"), ("wla1",), fam="wla")
        dma("sync", wla[:, 6144:8192], wh_lo[o, :, 2048:4096], (f"wbm{o}_1L", "wla3"), ("wla3",), fam="wla")
        dma("sync", sb[f"lbt_hi{jo}"][:], lbh_hi[:, o * 128:(o + 1) * 128], ("lbhH",), (f"lbt_hi{jo}",), fam="lbt")
        dma("sync", sb[f"lbt_lo{jo}"][:], lbh_lo[:, o * 128:(o + 1) * 128], ("lbhL",), (f"lbt_lo{jo}",), fam="lbt")
        nm = [0]
        for t in range(NKT):
            for (wk, xk) in (("hi", "hi"), ("hi", "lo"), ("lo", "hi")):
                off = (0 if wk == "hi" else 4096) + t * 128
                rk = "wla0" if off < 2048 else ("wla1" if off < 4096 else ("wla2" if off < 6144 else "wla3"))
                C("tensor", lambda E, off=off, t=t, xk=xk, s=(nm[0] == 0), fb=fb: E.matmul(
                    ps[fb][:], wla[:, off:off + 128],
                    sb["xn_hi" if xk == "hi" else "xn_lo"][:, t * SBT:(t + 1) * SBT],
                    start=s, stop=False),
                    (rk, f"xn{'h' if xk == 'hi' else 'l'}{t}"), (fb,))
                nm[0] += 1
        for pi, (wk, xk) in enumerate((("hi", "hi"), ("hi", "lo"), ("lo", "hi"))):
            C("tensor", lambda E, wk=wk, xk=xk, fb=fb, st=(pi == 2), jo=jo: E.matmul(
                ps[fb][:], sb[f"lbt_{wk}{jo}"][:], sb[f"hw_{xk}"][:],
                start=False, stop=st),
                (f"lbt_{wk}{jo}", f"hw_{xk}"), (fb,))
        C("scalar", lambda E, o=o, fb=fb, jo=jo: E.activation(
            sb[f"fft{jo}"][:], ps[fb][:], AF.Identity, bias=sb["bb"][:, o:o + 1]),
            (fb, "bb"), (f"fft{jo}",))
        dma("scalar", sb[f"rr{jo}"][:], ag1i[k][o * 128:(o + 1) * 128, cs:cs + SBT],
            (f"ag1w{k}_{o}",), (f"rr{jo}",), fam=f"rr{jo}")
        C("vector", lambda E, jo=jo: E.tensor_tensor(sb[f"rr{jo}"][:], sb[f"rr{jo}"][:], sb["mbc"][:, 0:SBT], ALU.subtract),
          (f"rr{jo}", "mbc_v", "mbc"), (f"rr{jo}",))
        C("vector", lambda E, jo=jo: E.tensor_tensor(sb[f"rr{jo}"][:], sb[f"rr{jo}"][:], sb["rbc"][:, 0:SBT], ALU.mult),
          (f"rr{jo}", "rbc_v", "rbc"), (f"rr{jo}",))
        C("vector", lambda E, jo=jo: E.tensor_tensor(sb[f"fft{jo}"][:], sb[f"fft{jo}"][:], sb[f"rr{jo}"][:], ALU.add),
          (f"fft{jo}", f"rr{jo}"), (f"fft{jo}",))
        dma("scalar", r2scr[o * 128:(o + 1) * 128, cs:cs + SBT], sb[f"fft{jo}"][:],
            (f"fft{jo}",), (f"r2_{o}_{sbi}",), fam=f"r2s{k % 2}")
        sqn = "sq2" if jo == 0 else "sq2b"
        C("scalar", lambda E, jo=jo, sqn=sqn: E.activation(sb[sqn][:], sb[f"fft{jo}"][:], AF.Square),
          (f"fft{jo}",), (sqn,))
        pend_stats.append((o, sbi, jo, sqn, cs))

    def ffn_fin(k):
        c0 = k * LC
        emit_stats()
        C("scalar", lambda E: E.activation(sb["stat_sb"][0:1, :], ps["xp2"][0:1, :], AF.Copy),
          ("xp2s", "xp2q", "xp2"), ("stat_sb",))
        C("scalar", lambda E: E.activation(sb["stat_sb"][32:33, :], ps["xp2"][32:33, :], AF.Copy),
          ("xp2s", "xp2q", "xp2"), ("stat_sb",))
        dma("scalar", ag2i[k][0:1, :], sb["stat_sb"][0:1, :], ("stat_sb",), (f"ag2w{k}",), fam="ag2w")
        dma("scalar", ag2i[k][1:2, :], sb["stat_sb"][32:33, :], ("stat_sb",), (f"ag2w{k}b",), fam="ag2w")
        P.op("gpsimd", lambda E: E.collective_compute(
            "AllReduce", ALU.add, replica_groups=RG, ins=[ag2i[k]], outs=[ag2o[k]]),
            reads=(f"ag2w{k}", f"ag2w{k}b"), writes=(f"gath2_{k}",), kind="cc", name=f"ar2_{k}", fam=f"car2{k % 2}")
        stv = sb["stv"]
        dma("scalar", stv[:, 0:LC], ag2o[k][0:1, :], (f"gath2_{k}",), ("stv0",), fam="stv")
        dma("scalar", stv[:, LC:2 * LC], ag2o[k][1:2, :], (f"gath2_{k}",), ("stv1",), fam="stv")
        ln_stats("mbc_v", "rbc_v")
        for o in range(NOT):
            jo = o % 2
            dma("scalar", sb[f"r2l{jo}"][:], r2scr[o * 128:(o + 1) * 128, :],
                (f"r2_{o}_0", f"r2_{o}_1"), (f"xlr{jo}",), fam="yscl")
            C("vector", lambda E, jo=jo: E.tensor_tensor(sb[f"r2l{jo}"][:], sb[f"r2l{jo}"][:], sb["mbc"][:], ALU.subtract),
              (f"xlr{jo}", "mbc_v", "mbc"), (f"xlr{jo}",))
            C("vector", lambda E, jo=jo: E.tensor_tensor(sb[f"r2l{jo}"][:], sb[f"r2l{jo}"][:], sb["rbc"][:], ALU.mult),
              (f"xlr{jo}", "rbc_v", "rbc"), (f"xlr{jo}",))
            dma("scalar", out_c[o * 128:(o + 1) * 128, c0:c0 + LC], sb[f"r2l{jo}"][:],
                (f"xlr{jo}",), (f"outw{k}_{o}",), fam="out")

    SERIAL = SERIAL_EMIT
    for k in range(NCH):
        if DEBUG and k == 1:
            dma("sync", dbg_cr, sb["carry"][:],
                tuple(f"carry{cc2}" for cc2 in range(NDT * NST)), ("dbgcr",), fam="dbg")
        scan_side(k)
        for h in range(2):
            bcast_half(k, h)
            if k > 0 and not SERIAL:
                ffn_prep(k - 1, h)
            dtile(k, h, 0)
            for i in range(NDT):
                if i + 1 < NDT:
                    dtile(k, h, i + 1)
                for q in range(2):
                    quarter(k, h, i, q)
                dtile_end(k, h, i)
                if k > 0 and not SERIAL:
                    ffn_otile(k - 1, h, i)
                elif k == 0:
                    emit_wcast(2)
            if h == 1 and k > 0 and not SERIAL:
                ffn_fin(k - 1)
        ag1(k)
    emit_wcast(64)
    if SERIAL:
        for k in range(NCH):
            for sbi in range(2):
                ffn_prep(k, sbi)
                for o in range(NOT):
                    ffn_otile(k, sbi, o)
            ffn_fin(k)
    else:
        for sbi in range(2):
            ffn_prep(NCH - 1, sbi)
            for o in range(NOT):
                ffn_otile(NCH - 1, sbi, o)
        ffn_fin(NCH - 1)

    P.op("vector", lambda E: E.engine_nop(),
         reads=tuple(f"outw{k}_{o}" for k in range(NCH) for o in range(NOT)),
         writes=(), kind="c", name="fin")

    P.finalize()
    P.check_deadlock()
    with nc.Block() as block:
        @block.sync
        def _(E):
            P.emit_engine("sync", E, chans)

        @block.vector
        def _(E):
            P.emit_engine("vector", E, chans)

        @block.scalar
        def _(E):
            P.emit_engine("scalar", E, chans)

        @block.gpsimd
        def _(E):
            P.emit_engine("gpsimd", E, chans)

        @block.tensor
        def _(E):
            P.emit_engine("tensor", E, chans)
    stack.close()
    return nc


def make_inputs(ins):
    x = np.ascontiguousarray(ins["L0_attn_x"])
    A_log = np.asarray(ins["L0_attn_A_log"])
    Dv = np.asarray(ins["L0_attn_D"])
    xpw = np.asarray(ins["L0_attn_x_proj_w"])
    dtw = np.asarray(ins["L0_attn_dt_proj_w"])
    dtb = np.asarray(ins["L0_attn_dt_proj_b"])
    Wb = np.asarray(ins["L0_ffn_W_base"])
    bb = np.asarray(ins["L0_ffn_b_base"])
    Wr = np.asarray(ins["L0_ffn_W_router"])
    br = np.asarray(ins["L0_ffn_b_router"])
    lA = np.asarray(ins["L0_ffn_lora_A"])
    lB = np.asarray(ins["L0_ffn_lora_B"])

    idn = np.eye(128, dtype=np.float32)
    sel8 = np.zeros((8, 128), np.float32)
    for ee in range(8):
        sel8[ee, ee * 16:(ee + 1) * 16] = 1.0
    xpwT = np.ascontiguousarray(xpw.T)
    lap = np.ascontiguousarray(lA.transpose(1, 0, 2).reshape(D, 128))
    lbp = np.ascontiguousarray(lB.reshape(128, D))
    WrT = np.ascontiguousarray(Wr.T)

    maps = []
    for c in range(8):
        b, h = c // 2, c % 2
        own = slice(h * DH, (h + 1) * DH)
        xb = x[b].T
        xt_c = np.concatenate([xb[own], xb[(1 - h) * DH:(2 - h) * DH]], axis=0)
        perm = np.concatenate([np.arange(h * DH, (h + 1) * DH), np.arange((1 - h) * DH, (2 - h) * DH)])
        xpw_c = xpwT[perm]
        xpw_t = np.ascontiguousarray(xpw_c.reshape(32, 128, XPR).transpose(1, 0, 2).reshape(128, 32 * XPR))
        dtw_c = np.ascontiguousarray(dtw[own].T)
        alog_c = np.ascontiguousarray(A_log[own].reshape(NDT, 128, NST).transpose(1, 0, 2).reshape(128, NDT * NST))
        d_c = np.ascontiguousarray(Dv[own].reshape(NDT, 128).T)
        dtb_c = np.ascontiguousarray(dtb[own].reshape(NDT, 128).T)
        Wslice = Wb[own, :]
        wbs_c = np.ascontiguousarray(
            Wslice.reshape(NOT, 128, NKT, 128).transpose(0, 3, 2, 1).reshape(NOT, 128, NKT * 128))
        bb_c = np.ascontiguousarray(bb[own].reshape(NOT, 128).T)
        wrt_c = np.ascontiguousarray(WrT.reshape(NKT, 128, 8).transpose(1, 0, 2).reshape(128, NKT * 8))
        las_c = np.ascontiguousarray(lap.reshape(NKT, 128, 128).transpose(1, 0, 2).reshape(128, NKT * 128))
        lbs_c = np.ascontiguousarray(lbp[:, own])
        maps.append({
            "xt": xt_c, "xpw": xpw_t, "dtwi": dtw_c, "alog": alog_c,
            "dvec": d_c, "dtbi": dtb_c, "wbs": wbs_c, "bbt": bb_c,
            "wrti": wrt_c, "brt": br.reshape(8, 1).astype(np.float32),
            "las": las_c, "lbsi": lbs_c, "idni": idn, "sel8i": sel8,
        })
    return maps


_NC_CACHE = [None]


def kernel(**inputs):
    maps = make_inputs(inputs)
    if _NC_CACHE[0] is None:
        _NC_CACHE[0] = build_nc()
    nc = _NC_CACHE[0]
    res = run_bass_kernel_spmd(nc, maps, core_ids=list(range(8)))
    out = np.empty((B, L, D), np.float32)
    for c in range(8):
        b, h = c // 2, c % 2
        oc = res.results[c]["out_c"]
        out[b, :, h * DH:(h + 1) * DH] = oc.T
    return out




# version 06
# speedup vs baseline: 1.1503x; 1.1424x over previous
import numpy as np
import concourse.bass as bass
import concourse.mybir as mybir
from concourse.bass_utils import run_bass_kernel_spmd

F32 = mybir.dt.float32
BF16 = mybir.dt.bfloat16
AF = mybir.ActivationFunctionType
ALU = mybir.AluOpType

B, L, D = 4, 2048, 4096
DH, NST, DTR, XPR = 2048, 16, 128, 160
NCH, LC = 4, 512
SBT = 256
NDT = 16
OH, NOT, NKT = 2048, 16, 32
EPS = 1e-5
RG = [[0, 1], [2, 3], [4, 5], [6, 7]]
SORT8 = [(0, 1), (2, 3), (4, 5), (6, 7), (0, 2), (1, 3), (4, 6), (5, 7),
         (1, 2), (5, 6), (0, 4), (3, 7), (1, 5), (2, 6), (1, 4), (3, 6),
         (2, 4), (3, 5), (3, 4)]
DEBUG = False
SERIAL_EMIT = False
LAST_PROG = None


class Prog:
    ENGINES = ("sync", "vector", "scalar", "gpsimd", "tensor")

    def __init__(self):
        self.ops = []
        self.last_w = {}
        self.readers = {}
        self.fams = {}
        self.fam_R = {}

    def op(self, eng, fn, reads=(), writes=(), kind="c", name="", fam=None, R=0):
        import sys as _sys
        fr = _sys._getframe(1)
        if fr.f_code.co_name in ("dma", "C"):
            fr = fr.f_back
        lbl = f"{fr.f_code.co_name}:{fr.f_lineno}"
        deps = set()
        for kk in reads:
            if kk in self.last_w:
                deps.add(self.last_w[kk])
        for kk in writes:
            for r in self.readers.get(kk, ()):
                deps.add(r)
            if kk in self.last_w:
                deps.add(self.last_w[kk])
        i = len(self.ops)
        if kind in ("d", "cc"):
            assert fam is not None, name
            self.fams.setdefault(fam, []).append(i)
            self.fam_R[fam] = max(self.fam_R.get(fam, 0), R)
        self.ops.append(dict(eng=eng, fn=fn, deps=deps, kind=kind, name=name, fam=fam, lbl=lbl))
        for kk in reads:
            self.readers.setdefault(kk, []).append(i)
        for kk in writes:
            self.last_w[kk] = i
            self.readers[kk] = []
        return i

    def finalize(self):
        ops = self.ops
        self.need_inc = [False] * len(ops)
        for o in ops:
            for d in o["deps"]:
                p = ops[d]
                if p["kind"] == "c" and p["eng"] == o["eng"]:
                    continue
                if p["kind"] == "c":
                    self.need_inc[d] = True
        cnt = {}
        self.val = [0] * len(ops)
        for i, o in enumerate(ops):
            if o["kind"] == "c" and self.need_inc[i]:
                ch = (o["eng"], "c")
                cnt[ch] = cnt.get(ch, 0) + 1
                self.val[i] = cnt[ch]
        self.fam_idx = {}
        for fam, lst in self.fams.items():
            for q, gi in enumerate(lst):
                self.fam_idx[gi] = q

    def check_deadlock(self):
        import bisect
        ops = self.ops
        streams = {en: [i for i in range(len(ops)) if ops[i]["eng"] == en] for en in self.ENGINES}
        pos = {en: 0 for en in self.ENGINES}
        cnt = {}

        def waits_of(i):
            o = ops[i]
            ws = []
            for d in o["deps"]:
                p = ops[d]
                if p["kind"] == "c":
                    if p["eng"] == o["eng"]:
                        continue
                    ws.append((("c", p["eng"]), self.val[d]))
                else:
                    fam = p["fam"]
                    lst = self.fams[fam]
                    cx = max(bisect.bisect_left(lst, i), self.fam_idx[d] + 1)
                    tgt = min(cx + self.fam_R[fam], len(lst))
                    if ops[i]["kind"] == p["kind"] and ops[i].get("fam") == fam:
                        tgt = min(tgt, self.fam_idx[i])
                    tgt = max(tgt, self.fam_idx[d] + 1)
                    step = 16 if p["kind"] == "d" else 1
                    ws.append((("f", fam), tgt * step))
            return ws

        progressed = True
        while progressed:
            progressed = False
            for en in self.ENGINES:
                while pos[en] < len(streams[en]):
                    i = streams[en][pos[en]]
                    if any(cnt.get(ch, 0) < v for ch, v in waits_of(i)):
                        break
                    o = ops[i]
                    if o["kind"] == "c":
                        if self.need_inc[i]:
                            ch = ("c", en)
                            cnt[ch] = cnt.get(ch, 0) + 1
                    else:
                        step = 16 if o["kind"] == "d" else 1
                        ch = ("f", o["fam"])
                        cnt[ch] = cnt.get(ch, 0) + step
                    pos[en] += 1
                    progressed = True
        stuck = {en: pos[en] for en in self.ENGINES if pos[en] < len(streams[en])}
        if stuck:
            for en, p in stuck.items():
                i = streams[en][p]
                o = ops[i]
                print(f"DEADLOCK: {en} pos {p}/{len(streams[en])} op {i} kind={o['kind']} fam={o['fam']} name={o['name']}")
                for d in sorted(o["deps"]):
                    pp = ops[d]
                    print(f"   dep {d}: eng={pp['eng']} kind={pp['kind']} fam={pp['fam']} name={pp['name']} val={self.val[d]}")
                print("   waits:", waits_of(i), "cur:", {ch: cnt.get(ch, 0) for ch, v in waits_of(i)})
            raise RuntimeError("deadlock detected")
        print("deadlock check: OK")

    def emit_engine(self, ename, eng, chans):
        ops = self.ops
        import bisect
        waited = {}
        for i, o in enumerate(ops):
            if o["eng"] != ename:
                continue
            for d in sorted(o["deps"]):
                p = ops[d]
                if p["kind"] == "c":
                    if p["eng"] == ename:
                        continue
                    ch = ("c", p["eng"])
                    v = self.val[d]
                    assert v > 0, (i, o["name"], d, p["name"])
                    sem = chans[(p["eng"], "c")]
                else:
                    fam = p["fam"]
                    lst = self.fams[fam]
                    cx = bisect.bisect_left(lst, i)
                    cx = max(cx, self.fam_idx[d] + 1)
                    tgt = min(cx + self.fam_R[fam], len(lst))
                    if o["kind"] == p["kind"] and o.get("fam") == fam:
                        tgt = min(tgt, self.fam_idx[i])
                    tgt = max(tgt, self.fam_idx[d] + 1)
                    step = 16 if p["kind"] == "d" else 1
                    ch = ("f", fam)
                    v = tgt * step
                    sem = chans[("fam", fam)]
                if waited.get(ch, 0) >= v:
                    continue
                eng.wait_ge(sem, v)
                waited[ch] = v
            inst = o["fn"](eng)
            if o["kind"] == "c":
                if self.need_inc[i]:
                    inst.then_inc(chans[(ename, "c")], 1)
            else:
                step = 16 if o["kind"] == "d" else 1
                inst.then_inc(chans[("fam", o["fam"])], step)


def build_nc():
    nc = bass.Bass("TRN2", target_bir_lowering=False)
    P = Prog()

    xt = nc.dram_tensor("xt", [D, L], F32, kind="ExternalInput").ap()
    xpw = nc.dram_tensor("xpw", [128, 32 * XPR], F32, kind="ExternalInput").ap()
    dtwi = nc.dram_tensor("dtwi", [128, DH], F32, kind="ExternalInput").ap()
    alog = nc.dram_tensor("alog", [128, NDT * NST], F32, kind="ExternalInput").ap()
    dvec = nc.dram_tensor("dvec", [128, NDT], F32, kind="ExternalInput").ap()
    dtbi = nc.dram_tensor("dtbi", [128, NDT], F32, kind="ExternalInput").ap()
    wbs = nc.dram_tensor("wbs", [NOT, 128, NKT * 128], F32, kind="ExternalInput").ap()
    bbt = nc.dram_tensor("bbt", [128, NOT], F32, kind="ExternalInput").ap()
    wrti = nc.dram_tensor("wrti", [128, NKT * 8], F32, kind="ExternalInput").ap()
    brt = nc.dram_tensor("brt", [8, 1], F32, kind="ExternalInput").ap()
    las = nc.dram_tensor("las", [128, NKT * 128], F32, kind="ExternalInput").ap()
    lbsi = nc.dram_tensor("lbsi", [128, OH], F32, kind="ExternalInput").ap()
    idni = nc.dram_tensor("idni", [128, 128], F32, kind="ExternalInput").ap()
    sel8i = nc.dram_tensor("sel8i", [8, 128], F32, kind="ExternalInput").ap()
    out_c = nc.dram_tensor("out_c", [OH, L], F32, kind="ExternalOutput").ap()
    if DEBUG:
        dbg_res = nc.dram_tensor("dbg_res", [DH, L], F32, kind="ExternalOutput").ap()
        dbg_xn = nc.dram_tensor("dbg_xn", [D, L], F32, kind="ExternalOutput").ap()
        dbg_cr = nc.dram_tensor("dbg_cr", [128, NDT * NST], F32, kind="ExternalOutput").ap()
        dbg_st = nc.dram_tensor("dbg_st", [NCH, 2, LC], F32, kind="ExternalOutput").ap()
        dbg_bp = nc.dram_tensor("dbg_bp", [128, 8 * LC], F32, kind="ExternalOutput").ap()
        dbg_cp = nc.dram_tensor("dbg_cp", [128, 8 * LC], F32, kind="ExternalOutput").ap()
        dbg_dt = nc.dram_tensor("dbg_dt", [128, LC], F32, kind="ExternalOutput").ap()
        dbg_v = nc.dram_tensor("dbg_v", [128, LC], F32, kind="ExternalOutput").ap()

    xph_hi = nc.dram_tensor("xph_hi", [128, 32 * XPR], BF16).ap()
    xph_lo = nc.dram_tensor("xph_lo", [128, 32 * XPR], BF16).ap()
    lbh_hi = nc.dram_tensor("lbh_hi", [128, OH], BF16).ap()
    lbh_lo = nc.dram_tensor("lbh_lo", [128, OH], BF16).ap()
    lah_hi = nc.dram_tensor("lah_hi", [128, NKT * 128], BF16).ap()
    lah_lo = nc.dram_tensor("lah_lo", [128, NKT * 128], BF16).ap()
    wh_hi = nc.dram_tensor("wh_hi", [NOT, 128, NKT * 128], BF16).ap()
    wh_lo = nc.dram_tensor("wh_lo", [NOT, 128, NKT * 128], BF16).ap()
    yscr = nc.dram_tensor("yscr", [DH, LC], F32).ap()
    bchbm = nc.dram_tensor("bchbm", [32, LC], F32).ap()
    r2scr = nc.dram_tensor("r2scr", [OH, LC], F32).ap()
    ag1i = [nc.dram_tensor(f"ag1i{k}", [DH + 2, LC], F32).ap() for k in range(NCH)]
    ag1o = [nc.dram_tensor(f"ag1o{k}", [2 * (DH + 2), LC], F32).ap() for k in range(NCH)]
    ag2i = [nc.dram_tensor(f"ag2i{k}", [2, LC], F32).ap() for k in range(NCH)]
    ag2o = [nc.dram_tensor(f"ag2o{k}", [2, LC], F32).ap() for k in range(NCH)]

    from contextlib import ExitStack
    stack = ExitStack()
    e = stack.enter_context

    sb = {}
    def S(name, shape, dt=F32):
        sb[name] = e(nc.sbuf_tensor(name, shape, dt))
        return sb[name]

    S("ones_r", [1, 128]); S("ones_c", [128, 1])
    S("idn", [128, 128]); S("sel8", [8, 128])
    S("A", [128, NDT * NST]); S("onepD", [128, NDT]); S("dtb", [128, NDT])
    S("bb", [128, NOT]); S("br", [8, 1])
    S("dtw_hi", [128, DH], BF16); S("dtw_lo", [128, DH], BF16)
    S("wr_hi", [128, NKT * 8], BF16); S("wr_lo", [128, NKT * 8], BF16)
    S("carry", [128, NDT * NST])
    for j in range(2):
        S(f"xpw_hi{j}", [128, 2 * XPR], BF16); S(f"xpw_lo{j}", [128, 2 * XPR], BF16)
    S("Bp", [128, 8 * LC]); S("Cp", [128, 8 * LC])
    S("dAz0", [128, 4 * LC]); S("dAz1", [128, 4 * LC])
    S("dBx0", [128, 4 * LC]); S("dBx1", [128, 4 * LC])
    S("hp", [128, 4 * LC])
    for j in range(2):
        S(f"xhi{j}", [128, LC], BF16); S(f"xlo{j}", [128, LC], BF16)
        S(f"dtt{j}", [128, LC]); S(f"xot{j}", [128, LC]); S(f"vt{j}", [128, LC])
        S(f"yh{j}", [128, LC]); S(f"res{j}", [128, LC])
    S("ysc", [128, LC]); S("zero4", [128, 4]); S("tc4", [128, 4])
    S("zz", [128, 4 * LC])
    S("lbt_hi0", [128, 128], BF16); S("lbt_lo0", [128, 128], BF16)
    S("lbt_hi1", [128, 128], BF16); S("lbt_lo1", [128, 128], BF16)
    S("dtin_hi", [128, LC], BF16); S("dtin_lo", [128, LC], BF16)
    S("bc_c", [32, LC])
    S("stat_sb", [64, LC])
    S("xn_hi", [128, NKT * SBT], BF16); S("xn_lo", [128, NKT * SBT], BF16)
    S("wla", [128, 8192], BF16)
    for j in range(2):
        S(f"gt{j}", [128, SBT])
        S(f"rr{j}", [128, SBT])
        S(f"r2l{j}", [128, LC])
    for j in range(4):
        S(f"fft{j}", [128, SBT])
    S("sq2", [128, SBT])
    S("sq2b", [128, SBT])
    S("stv", [1, 3 * LC])
    S("mbc", [128, LC]); S("rbc", [128, LC])
    S("rt8", [8, SBT]); S("lgt", [128, 16])
    S("sortp", [128, 48])
    S("wfin", [128, 16]); S("wt8", [8, SBT]); S("wbc", [128, SBT])
    S("lh", [128, SBT])
    S("hw_hi", [128, SBT], BF16); S("hw_lo", [128, SBT], BF16)

    wla = sb["wla"]
    wla_f32 = wla[:].bitcast(F32)
    wf32 = wla_f32[:, 0:2048]
    cst_hi = wla[:, 4096:6144]
    cst_lo = wla[:, 6144:8192]
    WQ = ("wla0", "wla1", "wla2", "wla3")

    ps = {}
    def PSUM(name, shape):
        ps[name] = e(nc.psum_tensor(name, shape, F32))
    PSUM("xp1", [128, LC])
    PSUM("xp2", [64, LC])
    PSUM("bcb", [128, LC])
    PSUM("st", [64, LC])
    PSUM("rtl", [128, SBT])
    PSUM("misc", [128, LC])
    PSUM("ff0", [128, SBT]); PSUM("ff1", [128, SBT])

    chans = {}
    for en in Prog.ENGINES:
        chans[(en, "c")] = e(nc.semaphore(f"c_{en}"))
    FAMS = ["consts", "xphw", "lahw", "wbmw", "wla", "xlr0", "xlr1", "xot0", "xot1", "xpw",
            "bp", "cp", "gt0", "gt1", "rr0", "rr1", "yscl", "bchbm", "stv", "out",
            "yscr0", "yscr1", "r2s0", "r2s1", "agw0", "agw1", "arw", "ag2w",
            "cag0", "cag1", "car0", "car1", "car20", "car21", "dbg",
            "xpwh0", "xpwh1", "xpwl0", "xpwl1", "lbt0", "lbt1", "finl0", "finl1", "lbhw"]
    for f in FAMS:
        chans[("fam", f)] = e(nc.semaphore(f"f_{f}"))

    def dma(eng, out_ap, in_ap, reads, writes, name="", fam="consts", R=0):
        P.op(eng, lambda E, o=out_ap, i=in_ap: E.dma_start(out=o, in_=i),
             reads=reads, writes=writes, kind="d", name=name, fam=fam, R=R)

    C = P.op

    C("vector", lambda E: E.memset(sb["ones_r"][:], 1.0), (), ("ones_r",))
    C("vector", lambda E: E.memset(sb["ones_c"][:], 1.0), (), ("ones_c",))
    C("vector", lambda E: E.memset(sb["zero4"][:], 0.0), (), ("zero4",))
    dma("sync", sb["idn"][:], idni, (), ("idn",))
    dma("sync", sb["sel8"][:], sel8i, (), ("sel8",))
    dma("sync", sb["A"][:], alog, (), ("A",))
    C("scalar", lambda E: E.activation(sb["A"][:], sb["A"][:], AF.Exp), ("A",), ("A",))
    C("vector", lambda E: E.tensor_single_scalar(sb["A"][:], sb["A"][:], -1.0, ALU.mult), ("A",), ("A",))
    dma("sync", sb["onepD"][:], dvec, (), ("onepD",))
    C("vector", lambda E: E.tensor_single_scalar(sb["onepD"][:], sb["onepD"][:], 1.0, ALU.add), ("onepD",), ("onepD",))
    dma("sync", sb["dtb"][:], dtbi, (), ("dtb",))
    dma("sync", sb["bb"][:], bbt, (), ("bb",))
    dma("sync", sb["br"][:], brt, (), ("br",))

    def stage_cast(src_ap, cols, hi_dst, hi_key, lo_dst, lo_key):
        dma("sync", wf32[:, 0:cols], src_ap, ("wla0", "wla1"), ("wla0", "wla1"), fam="wla")
        C("scalar", lambda E, c=cols, h=hi_dst: E.activation(h, wf32[:, 0:c], AF.Copy),
          ("wla0", "wla1"), (hi_key,))
        C("vector", lambda E, c=cols, h=hi_dst, l=lo_dst: E.tensor_tensor(l, wf32[:, 0:c], h, ALU.subtract),
          ("wla0", "wla1", hi_key), (lo_key,))

    stage_cast(dtwi, DH, sb["dtw_hi"][:], "dtw_hi", sb["dtw_lo"][:], "dtw_lo")
    stage_cast(wrti, NKT * 8, sb["wr_hi"][:], "wr_hi", sb["wr_lo"][:], "wr_lo")

    def cast_to_hbm(nm, src_ap, dst_hi, dst_lo, total):
        off = 0
        while off < total:
            w = min(2048, total - off)
            dma("sync", wf32[:, 0:w], src_ap[:, off:off + w], ("wla0", "wla1"), ("wla0", "wla1"), fam="wla")
            C("scalar", lambda E, w=w: E.activation(cst_hi[:, 0:w], wf32[:, 0:w], AF.Copy),
              ("wla0", "wla1"), ("wla2",))
            C("vector", lambda E, w=w: E.tensor_tensor(cst_lo[:, 0:w], wf32[:, 0:w], cst_hi[:, 0:w], ALU.subtract),
              ("wla0", "wla1", "wla2"), ("wla3",))
            dma("sync", dst_hi[:, off:off + w], cst_hi[:, 0:w], ("wla2",), (f"{nm}H",), fam=nm + "w")
            dma("sync", dst_lo[:, off:off + w], cst_lo[:, 0:w], ("wla3",), (f"{nm}L",), fam=nm + "w")
            off += w

    cast_to_hbm("xph", xpw, xph_hi, xph_lo, 32 * XPR)
    cast_to_hbm("lah", las, lah_hi, lah_lo, NKT * 128)
    cast_to_hbm("lbh", lbsi, lbh_hi, lbh_lo, OH)

    wblocks = [(o, hf) for o in range(NOT) for hf in range(2)]
    widx = [0]

    def emit_wcast(nblk):
        for _ in range(nblk):
            if widx[0] >= len(wblocks):
                return
            o, hf = wblocks[widx[0]]
            widx[0] += 1
            sl = slice(hf * 2048, (hf + 1) * 2048)
            dma("sync", wf32[:], wbs[o, :, sl], ("wla0", "wla1"), ("wla0", "wla1"), fam="wla")
            C("scalar", lambda E: E.activation(cst_hi[:], wf32[:], AF.Copy),
              ("wla0", "wla1"), ("wla2",))
            C("vector", lambda E: E.tensor_tensor(cst_lo[:], wf32[:], cst_hi[:], ALU.subtract),
              ("wla0", "wla1", "wla2"), ("wla3",))
            dma("sync", wh_hi[o, :, sl], cst_hi[:], ("wla2",), (f"wbm{o}_{hf}H",), fam="wbmw")
            dma("sync", wh_lo[o, :, sl], cst_lo[:], ("wla3",), (f"wbm{o}_{hf}L",), fam="wbmw")

    def scan_side(k):
        c0 = k * LC
        n_mm = [0]
        for pc in range(16):
            pb = pc % 2
            poff = pc * 2 * XPR
            dma("sync", sb[f"xpw_hi{pb}"][:], xph_hi[:, poff:poff + 2 * XPR], (f"xpw_hi{pb}",), (f"xpw_hi{pb}",), fam=f"xpwh{pb}")
            dma("sync", sb[f"xpw_lo{pb}"][:], xph_lo[:, poff:poff + 2 * XPR], (f"xpw_lo{pb}",), (f"xpw_lo{pb}",), fam=f"xpwl{pb}")
            for tl in range(2):
                t = pc * 2 + tl
                j = t % 2
                dma("scalar", sb[f"r2l{j}"][:], xt[t * 128:(t + 1) * 128, c0:c0 + LC], (), (f"xlr{j}",), fam=f"xlr{j}")
                C("scalar", lambda E, j=j: E.activation(sb[f"xhi{j}"][:], sb[f"r2l{j}"][:], AF.Copy),
                  (f"xlr{j}",), (f"xhi{j}",))
                C("vector", lambda E, j=j: E.tensor_tensor(sb[f"xlo{j}"][:], sb[f"r2l{j}"][:], sb[f"xhi{j}"][:], ALU.subtract),
                  (f"xlr{j}", f"xhi{j}"), (f"xlo{j}",))
                for (wk, xk) in (("hi", "hi"), ("hi", "lo"), ("lo", "hi")):
                    s0 = (n_mm[0] == 0)
                    s1 = (n_mm[0] == 95)
                    C("tensor", lambda E, wk=wk, xk=xk, j=j, tl=tl, pb=pb, s0=s0, s1=s1: E.matmul(
                        ps["xp1"][:], sb[f"xpw_{wk}{pb}"][:, tl * XPR:tl * XPR + 128], sb[f"x{xk}{j}"][:],
                        start=s0, stop=s1),
                        (f"xpw_{wk}{pb}", f"x{xk}{j}"), ("xp1",))
                    C("tensor", lambda E, wk=wk, xk=xk, j=j, tl=tl, pb=pb, s0=s0, s1=s1: E.matmul(
                        ps["bcb"][0:32, :], sb[f"xpw_{wk}{pb}"][:, tl * XPR + 128:(tl + 1) * XPR], sb[f"x{xk}{j}"][:],
                        start=s0, stop=s1),
                        (f"xpw_{wk}{pb}", f"x{xk}{j}"), ("bcb",))
                    n_mm[0] += 1
        C("scalar", lambda E: E.activation(sb["dtin_hi"][:], ps["xp1"][:], AF.Copy),
          ("xp1",), ("dtin_hi",))
        C("vector", lambda E: E.tensor_tensor(sb["dtin_lo"][:], ps["xp1"][:], sb["dtin_hi"][:], ALU.subtract),
          ("xp1", "dtin_hi"), ("dtin_lo",))
        C("scalar", lambda E: E.activation(sb["bc_c"][:], ps["bcb"][0:32, :], AF.Copy),
          ("bcb",), ("bc_c",))
        dma("scalar", bchbm, sb["bc_c"][:], ("bc_c",), ("bchbm",), fam="bchbm")

    def bcast_half(k, h):
        srcB = bchbm[h * 8:(h + 1) * 8, :].flatten().unsqueeze(0).broadcast_to([128, 8 * LC])
        srcC = bchbm[16 + h * 8:16 + (h + 1) * 8, :].flatten().unsqueeze(0).broadcast_to([128, 8 * LC])
        dma("gpsimd", sb["Bp"][:], srcB, ("bchbm",), tuple(f"Bp{z}" for z in range(8)), fam="bp")
        dma("gpsimd", sb["Cp"][:], srcC, ("bchbm",), tuple(f"Cp{z}" for z in range(8)), fam="cp")
        if DEBUG and k == 1 and h == 0:
            dma("sync", dbg_bp, sb["Bp"][:], tuple(f"Bp{z}" for z in range(8)), ("dbgbp",), fam="dbg")
            dma("sync", dbg_cp, sb["Cp"][:], tuple(f"Cp{z}" for z in range(8)), ("dbgcp",), fam="dbg")

    def dtile(k, h, i):
        j = i % 2
        for pi, (wk, xk) in enumerate((("hi", "hi"), ("hi", "lo"), ("lo", "hi"))):
            C("tensor", lambda E, wk=wk, xk=xk, i=i, s=(pi == 0), t=(pi == 2): E.matmul(
                ps["xp1"][:], sb[f"dtw_{wk}"][:, i * 128:(i + 1) * 128], sb[f"dtin_{xk}"][:],
                start=s, stop=t),
                (f"dtw_{wk}", f"dtin_{xk}"), ("xp1",))
        C("scalar", lambda E, i=i, j=j: E.activation(sb[f"dtt{j}"][:], ps["xp1"][:], AF.Exp, bias=sb["dtb"][:, i:i + 1]),
          ("xp1", "dtb"), (f"dtt{j}",))
        C("scalar", lambda E, j=j: E.activation(sb[f"dtt{j}"][:], sb[f"dtt{j}"][:], AF.Ln, bias=1.0),
          (f"dtt{j}",), (f"dtt{j}",))
        dma("scalar", sb[f"xot{j}"][:], xt[i * 128:(i + 1) * 128, k * LC:(k + 1) * LC], (), (f"xot{j}",), fam=f"xot{j}")
        C("gpsimd", lambda E, j=j: E.tensor_tensor(sb[f"vt{j}"][:], sb[f"dtt{j}"][:], sb[f"xot{j}"][:], ALU.mult),
          (f"dtt{j}", f"xot{j}"), (f"vt{j}",))
        if DEBUG and k == 1 and h == 0 and i == 0:
            dma("sync", dbg_dt, sb[f"dtt{j}"][:], (f"dtt{j}",), ("dbgdt",), fam="dbg")
            dma("sync", dbg_v, sb[f"vt{j}"][:], (f"vt{j}",), ("dbgv",), fam="dbg")

    ucnt = [0]

    def quarter(k, h, i, q):
        j = i % 2
        ub = ucnt[0] % 2
        ucnt[0] += 1
        dA = f"dAz{ub}"
        dB = f"dBx{ub}"
        for jj in range(4):
            n = h * 8 + q * 4 + jj
            col = i * NST + n
            C("scalar", lambda E, jj=jj, col=col, j=j, dA=dA: E.activation(
                sb[dA][:, jj * LC:(jj + 1) * LC], sb[f"dtt{j}"][:], AF.Exp,
                scale=sb["A"][:, col:col + 1]),
                (f"dtt{j}", "A"), (dA,))
        vb = sb[f"vt{j}"][:].unsqueeze(1).broadcast_to([128, 4, LC])
        C("gpsimd", lambda E, q=q, dB=dB, vb=vb: E.tensor_tensor(
            sb[dB][:].rearrange("p (n l) -> p n l", n=4),
            sb["Bp"][:, q * 4 * LC:(q + 1) * 4 * LC].rearrange("p (n l) -> p n l", n=4),
            vb, ALU.mult),
            tuple([f"vt{j}"] + [f"Bp{q * 4 + z}" for z in range(4)]), (dB,))
        colb0 = i * NST + h * 8 + q * 4
        if k > 0:
            dA_f = sb[dA][:].rearrange("p (n l) -> p n l", n=4)[:, :, 0]
            dB_f = sb[dB][:].rearrange("p (n l) -> p n l", n=4)[:, :, 0]
            C("gpsimd", lambda E, dA_f=dA_f, colb0=colb0: E.tensor_tensor(
                sb["tc4"][:], dA_f, sb["carry"][:, colb0:colb0 + 4], ALU.mult),
                tuple([dA] + [f"carry{colb0 + z}" for z in range(4)]), ("tc4",))
            C("gpsimd", lambda E, dB_f=dB_f: E.tensor_tensor(
                dB_f, dB_f, sb["tc4"][:], ALU.add),
                (dB, "tc4"), (dB,))
        for jj in range(4):
            rd = [dA, dB]
            C("vector", lambda E, jj=jj, dA=dA, dB=dB: E.tensor_tensor_scan(
                sb["hp"][:, jj * LC:(jj + 1) * LC],
                sb[dA][:, jj * LC:(jj + 1) * LC],
                sb[dB][:, jj * LC:(jj + 1) * LC],
                0.0, ALU.mult, ALU.add),
                tuple(rd), ("hp",))
        C("vector", lambda E, q=q: E.tensor_tensor(
            sb["zz"][:], sb["hp"][:], sb["Cp"][:, q * 4 * LC:(q + 1) * 4 * LC], ALU.mult),
            tuple(["hp"] + [f"Cp{q * 4 + z}" for z in range(4)]), ("zz",))
        colb = i * NST + h * 8 + q * 4
        C("vector", lambda E, colb=colb: E.tensor_tensor(
            sb["carry"][:, colb:colb + 4],
            sb["hp"][:].rearrange("p (n l) -> p n l", n=4)[:, :, LC - 1],
            sb["zero4"][:], ALU.add),
            ("hp", "zero4"), tuple(f"carry{colb + z}" for z in range(4)))
        C("vector", lambda E: E.tensor_tensor(
            sb["zz"][:, 0:2 * LC], sb["zz"][:, 0:2 * LC], sb["zz"][:, 2 * LC:4 * LC], ALU.add),
            ("zz",), ("zz",))
        j2 = i % 2
        if q == 0:
            C("vector", lambda E, j2=j2: E.tensor_tensor(
                sb[f"yh{j2}"][:], sb["zz"][:, 0:LC], sb["zz"][:, LC:2 * LC], ALU.add),
                ("zz",), (f"yh{j2}",))
        else:
            C("vector", lambda E: E.tensor_tensor(
                sb["zz"][:, 2 * LC:3 * LC], sb["zz"][:, 0:LC], sb["zz"][:, LC:2 * LC], ALU.add),
                ("zz",), ("zz",))
            C("vector", lambda E, j2=j2: E.tensor_tensor(
                sb[f"yh{j2}"][:], sb[f"yh{j2}"][:], sb["zz"][:, 2 * LC:3 * LC], ALU.add),
                ("zz", f"yh{j2}"), (f"yh{j2}",))

    def dtile_end(k, h, i):
        j = i % 2
        if h == 0:
            dma("gpsimd", yscr[i * 128:(i + 1) * 128, :], sb[f"yh{j}"][:],
                (f"yh{j}",), (f"yscr{i}",), fam=f"yscr{k % 2}")
        else:
            dma("scalar", sb["ysc"][:], yscr[i * 128:(i + 1) * 128, :],
                (f"yscr{i}",), ("ysc",), fam="yscl")
            C("vector", lambda E, j=j: E.tensor_tensor(sb[f"yh{j}"][:], sb[f"yh{j}"][:], sb["ysc"][:], ALU.add),
              (f"yh{j}", "ysc"), (f"yh{j}",))
            C("vector", lambda E, j=j, i=i: E.scalar_tensor_tensor(
                sb[f"res{j}"][:], sb[f"xot{j}"][:], sb["onepD"][:, i:i + 1], sb[f"yh{j}"][:],
                ALU.mult, ALU.add),
                (f"xot{j}", "onepD", f"yh{j}"), (f"res{j}",))
            dma("gpsimd", ag1i[k][i * 128:(i + 1) * 128, :], sb[f"res{j}"][:],
                (f"res{j}",), (f"ag1w{k}_{i}",), fam=f"agw{k % 2}")
            C("scalar", lambda E, j=j: E.activation(sb[f"yh{j}"][:], sb[f"res{j}"][:], AF.Square),
              (f"res{j}",), (f"yh{j}",))
            pend_res.append((i, j))
            if DEBUG:
                dma("gpsimd", dbg_res[i * 128:(i + 1) * 128, k * LC:(k + 1) * LC], sb[f"res{j}"][:],
                    (f"res{j}",), (f"dbgres{k}_{i}",), fam="dbg")

    def ag1(k):
        while pend_res:
            emit_one_res()
        C("scalar", lambda E: E.activation(sb["stat_sb"][0:1, :], ps["st"][0:1, :], AF.Copy),
          ("st0", "st1", "st"), ("stat1a", "stat_sb"))
        C("scalar", lambda E: E.activation(sb["stat_sb"][32:33, :], ps["st"][32:33, :], AF.Copy),
          ("st0", "st1", "st"), ("stat1b", "stat_sb"))
        dma("scalar", ag1i[k][DH:DH + 1, :], sb["stat_sb"][0:1, :], ("stat1a",), (f"ag1w{k}_s0",), fam=f"agw{k % 2}")
        dma("scalar", ag1i[k][DH + 1:DH + 2, :], sb["stat_sb"][32:33, :], ("stat1b",), (f"ag1w{k}_s1",), fam=f"agw{k % 2}")
        reads = tuple(f"ag1w{k}_{i}" for i in range(NDT)) + (f"ag1w{k}_s0", f"ag1w{k}_s1")
        P.op("gpsimd", lambda E: E.collective_compute(
            "AllGather", ALU.bypass, replica_groups=RG, ins=[ag1i[k]], outs=[ag1o[k]]),
            reads=reads, writes=(f"gath{k}",), kind="cc", name=f"ag1_{k}", fam=f"cag{k % 2}")

    def ln_stats(m_name, r_name, width=LC):
        stv = sb["stv"]
        s0 = stv[:, 0:width]; s1 = stv[:, LC:LC + width]; s2 = stv[:, 2 * LC:2 * LC + width]
        C("vector", lambda E: E.tensor_single_scalar(s0, s0, 1.0 / D, ALU.mult),
          ("stv0",), ("stv0",))
        C("tensor", lambda E, w=width: E.matmul(ps["misc"][:, 0:w], sb["ones_r"][0:1, :], s0[0:1, :], start=True, stop=True),
          ("stv0", "ones_r"), ("misc",))
        C("scalar", lambda E, w=width: E.activation(sb["mbc"][:, 0:w], ps["misc"][:, 0:w], AF.Copy),
          ("misc",), (m_name, "mbc"))
        C("vector", lambda E: E.tensor_single_scalar(s1, s1, 1.0 / D, ALU.mult),
          ("stv1",), ("stv1",))
        C("vector", lambda E: E.tensor_tensor(s2, s0, s0, ALU.mult),
          ("stv0",), ("stv2",))
        C("vector", lambda E: E.tensor_tensor(s1, s1, s2, ALU.subtract),
          ("stv1", "stv2"), ("stv1",))
        C("vector", lambda E: E.tensor_single_scalar(s1, s1, EPS, ALU.add),
          ("stv1",), ("stv1",))
        C("scalar", lambda E: E.activation(s2, s1, AF.Ln),
          ("stv1",), ("stv2",))
        C("scalar", lambda E: E.activation(s2, s2, AF.Exp, scale=-0.5),
          ("stv2",), ("stv2",))
        C("vector", lambda E: E.tensor_tensor(s0, s2, s2, ALU.mult),
          ("stv2",), ("stv0",))
        C("vector", lambda E: E.tensor_tensor(s0, s0, s1, ALU.mult),
          ("stv0", "stv1"), ("stv0",))
        C("vector", lambda E: E.tensor_scalar(s0, s0, -0.5, 1.5, ALU.mult, ALU.add),
          ("stv0",), ("stv0",))
        C("vector", lambda E: E.tensor_tensor(s2, s2, s0, ALU.mult),
          ("stv2", "stv0"), ("stv2",))
        C("tensor", lambda E, w=width: E.matmul(ps["misc"][:, 0:w], sb["ones_r"][0:1, :], s2[0:1, :], start=True, stop=True),
          ("stv2", "ones_r"), ("misc",))
        C("scalar", lambda E, w=width: E.activation(sb["rbc"][:, 0:w], ps["misc"][:, 0:w], AF.Copy),
          ("misc",), (r_name, "rbc"))

    def sparsemax_ops(sbi):
        pool = sb["sortp"]
        W2 = 2
        def sl(s):
            return pool[:, s * W2:(s + 1) * W2]
        slots = {}
        free = list(range(24))
        for ee in range(8):
            s0 = free.pop(0)
            slots[ee] = s0
            C("vector", lambda E, ee=ee, s0=s0: E.tensor_copy(sl(s0), sb["lgt"][:, ee * W2:(ee + 1) * W2]),
              ("lgt",), (f"sp{s0}",))
        for (i, jx) in SORT8:
            si, sj = slots[i], slots[jx]
            a = free.pop(0); b2 = free.pop(0)
            C("vector", lambda E, si=si, sj=sj, a=a: E.tensor_tensor(sl(a), sl(si), sl(sj), ALU.max),
              (f"sp{si}", f"sp{sj}"), (f"sp{a}",))
            C("vector", lambda E, si=si, sj=sj, b2=b2: E.tensor_tensor(sl(b2), sl(si), sl(sj), ALU.min),
              (f"sp{si}", f"sp{sj}"), (f"sp{b2}",))
            free.append(si); free.append(sj)
            slots[i], slots[jx] = a, b2
        cum = {}
        c0s = free.pop(0)
        cum[0] = c0s
        C("vector", lambda E, s=slots[0], c=c0s: E.tensor_copy(sl(c), sl(s)),
          (f"sp{slots[0]}",), (f"sp{c0s}",))
        for ee in range(1, 8):
            cs = free.pop(0)
            C("vector", lambda E, p=cum[ee - 1], z=slots[ee], cs=cs: E.tensor_tensor(sl(cs), sl(p), sl(z), ALU.add),
              (f"sp{cum[ee - 1]}", f"sp{slots[ee]}"), (f"sp{cs}",))
            cum[ee] = cs
        tsl = free.pop(0); msl = free.pop(0); ksl = free.pop(0)
        for ee in range(8):
            C("vector", lambda E, z=slots[ee], c=cum[ee], t=tsl, ee=ee: E.scalar_tensor_tensor(
                sl(t), sl(z), float(ee + 1), sl(c), ALU.mult, ALU.subtract),
                (f"sp{slots[ee]}", f"sp{cum[ee]}"), (f"sp{tsl}",))
            C("vector", lambda E, t=tsl, m=msl: E.tensor_single_scalar(sl(m), sl(t), -1.0, ALU.is_gt),
              (f"sp{tsl}",), (f"sp{msl}",))
            if ee == 0:
                C("vector", lambda E, m=msl, ks=ksl: E.tensor_copy(sl(ks), sl(m)),
                  (f"sp{msl}",), (f"sp{ksl}",))
            else:
                C("vector", lambda E, m=msl, ks=ksl: E.tensor_tensor(sl(ks), sl(ks), sl(m), ALU.add),
                  (f"sp{msl}", f"sp{ksl}"), (f"sp{ksl}",))
        acc, eq = tsl, msl
        t2 = free[0]
        for ee in range(8):
            C("vector", lambda E, ks=ksl, eqs=eq, ee=ee: E.tensor_single_scalar(sl(eqs), sl(ks), float(ee + 1), ALU.is_equal),
              (f"sp{ksl}",), (f"sp{eq}",))
            if ee == 0:
                C("vector", lambda E, c=cum[ee], eqs=eq, a=acc: E.tensor_tensor(sl(a), sl(c), sl(eqs), ALU.mult),
                  (f"sp{cum[ee]}", f"sp{eq}"), (f"sp{acc}",))
            else:
                C("vector", lambda E, c=cum[ee], eqs=eq, t2=t2: E.tensor_tensor(sl(t2), sl(c), sl(eqs), ALU.mult),
                  (f"sp{cum[ee]}", f"sp{eq}"), (f"sp{t2}",))
                C("vector", lambda E, a=acc, t2=t2: E.tensor_tensor(sl(a), sl(a), sl(t2), ALU.add),
                  (f"sp{acc}", f"sp{t2}"), (f"sp{acc}",))
        C("vector", lambda E, ks=ksl, eqs=eq: E.reciprocal(sl(eqs), sl(ks)),
          (f"sp{ksl}",), (f"sp{eq}",))
        C("vector", lambda E, a=acc: E.tensor_single_scalar(sl(a), sl(a), -1.0, ALU.add),
          (f"sp{acc}",), (f"sp{acc}",))
        C("vector", lambda E, a=acc, eqs=eq: E.tensor_tensor(sl(a), sl(a), sl(eqs), ALU.mult),
          (f"sp{acc}", f"sp{eq}"), (f"sp{acc}",))
        for ee in range(8):
            C("vector", lambda E, ee=ee, a=acc: E.tensor_tensor(
                sb["wfin"][:, ee * W2:(ee + 1) * W2], sb["lgt"][:, ee * W2:(ee + 1) * W2], sl(a), ALU.subtract),
                (f"sp{acc}", "lgt"), (f"wf{ee}",))
            C("vector", lambda E, ee=ee: E.tensor_single_scalar(
                sb["wfin"][:, ee * W2:(ee + 1) * W2], sb["wfin"][:, ee * W2:(ee + 1) * W2], 0.0, ALU.max),
                (f"wf{ee}",), (f"wf{ee}", "wfin"))

    def ffn_prep(k, sbi):
        cs = sbi * SBT
        stv = sb["stv"]
        if sbi == 0:
            dma("scalar", stv[0:1, 0:LC], ag1o[k][DH:DH + 1, :], (f"gath{k}",), ("stv0",), fam="stv")
            dma("scalar", stv[0:1, LC:2 * LC], ag1o[k][DH + 1:DH + 2, :], (f"gath{k}",), ("stv1",), fam="stv")
            dma("scalar", stv[0:1, 2 * LC:3 * LC], ag1o[k][2 * DH + 2:2 * DH + 3, :], (f"gath{k}",), ("stv2",), fam="stv")
            C("vector", lambda E: E.tensor_tensor(stv[0:1, 0:LC], stv[0:1, 0:LC], stv[0:1, 2 * LC:3 * LC], ALU.add),
              ("stv0", "stv2"), ("stv0",))
            dma("scalar", stv[0:1, 2 * LC:3 * LC], ag1o[k][2 * DH + 3:2 * DH + 4, :], (f"gath{k}",), ("stv2",), fam="stv")
            C("vector", lambda E: E.tensor_tensor(stv[0:1, LC:2 * LC], stv[0:1, LC:2 * LC], stv[0:1, 2 * LC:3 * LC], ALU.add),
              ("stv1", "stv2"), ("stv1",))
            if DEBUG:
                dma("scalar", dbg_st[k, 0:1, :], stv[0:1, 0:LC], ("stv0",), (f"dbgst{k}a",), fam="dbg")
                dma("scalar", dbg_st[k, 1:2, :], stv[0:1, LC:2 * LC], ("stv1",), (f"dbgst{k}b",), fam="dbg")
            ln_stats("mbc_v", "rbc_v", width=LC)
        for t in range(NKT):
            j = t % 2
            row = t * 128 + (0 if t < NDT else 2)
            dma("scalar", sb[f"gt{j}"][:], ag1o[k][row:row + 128, cs:cs + SBT], (f"gath{k}",), (f"gt{j}",), fam=f"gt{j}")
            C("vector", lambda E, j=j, cs=cs: E.tensor_tensor(sb[f"gt{j}"][:], sb[f"gt{j}"][:], sb["mbc"][:, cs:cs + SBT], ALU.subtract),
              (f"gt{j}", "mbc_v", "mbc"), (f"gt{j}",))
            C("vector", lambda E, j=j, cs=cs: E.tensor_tensor(sb[f"gt{j}"][:], sb[f"gt{j}"][:], sb["rbc"][:, cs:cs + SBT], ALU.mult),
              (f"gt{j}", "rbc_v", "rbc"), (f"gt{j}",))
            C("scalar", lambda E, j=j, t=t: E.activation(sb["xn_hi"][:, t * SBT:(t + 1) * SBT], sb[f"gt{j}"][:], AF.Copy),
              (f"gt{j}",), (f"xnh{t}",))
            C("vector", lambda E, j=j, t=t: E.tensor_tensor(
                sb["xn_lo"][:, t * SBT:(t + 1) * SBT], sb[f"gt{j}"][:], sb["xn_hi"][:, t * SBT:(t + 1) * SBT], ALU.subtract),
                (f"gt{j}", f"xnh{t}"), (f"xnl{t}",))
            if DEBUG:
                dma("sync", dbg_xn[t * 128:(t + 1) * 128, k * LC + cs:k * LC + cs + SBT], sb[f"gt{j}"][:],
                    (f"gt{j}",), (f"dbgxn{k}_{sbi}_{t}",), fam="dbg")
        nm = [0]
        for t in range(NKT):
            for (wk, xk) in (("hi", "hi"), ("hi", "lo"), ("lo", "hi")):
                C("tensor", lambda E, t=t, wk=wk, xk=xk, s=(nm[0] == 0), st=(nm[0] == 95): E.matmul(
                    ps["rtl"][0:8, :], sb[f"wr_{wk}"][:, t * 8:(t + 1) * 8],
                    sb["xn_hi" if xk == "hi" else "xn_lo"][:, t * SBT:(t + 1) * SBT],
                    start=s, stop=st),
                    (f"wr_{wk}", f"xn{'h' if xk == 'hi' else 'l'}{t}"), ("rtl",))
                nm[0] += 1
        C("scalar", lambda E: E.activation(sb["rt8"][:], ps["rtl"][0:8, :], AF.Identity, bias=sb["br"][:, 0:1]),
          ("rtl", "br"), ("rt8",))
        dma("sync", wla[:, 0:4096], lah_hi, ("lahH", "wla0", "wla1"), ("wla0", "wla1"), fam="wla")
        dma("sync", wla[:, 4096:8192], lah_lo, ("lahL", "wla2", "wla3"), ("wla2", "wla3"), fam="wla")
        nm = [0]
        for t in range(NKT):
            for (wk, xk) in (("hi", "hi"), ("hi", "lo"), ("lo", "hi")):
                off = t * 128 if wk == "hi" else 4096 + t * 128
                rk = ("wla0", "wla1") if wk == "hi" else ("wla2", "wla3")
                C("tensor", lambda E, off=off, t=t, xk=xk, s=(nm[0] == 0), st=(nm[0] == 95): E.matmul(
                    ps["rtl"][:], wla[:, off:off + 128],
                    sb["xn_hi" if xk == "hi" else "xn_lo"][:, t * SBT:(t + 1) * SBT],
                    start=s, stop=st),
                    rk + (f"xn{'h' if xk == 'hi' else 'l'}{t}", "rt8"), ("rtl",))
                nm[0] += 1
        C("scalar", lambda E: E.activation(sb["lh"][:], ps["rtl"][:], AF.Copy),
          ("rtl",), ("lh",))
        lgt_v = sb["lgt"][:].rearrange("p (e g) -> p e g", g=2)
        for g in range(2):
            C("tensor", lambda E, g=g: E.transpose(ps["misc"][:, g * 8:(g + 1) * 8],
                                                   sb["rt8"][:, g * 128:(g + 1) * 128], sb["idn"][0:8, 0:8]),
              ("rt8", "idn"), ("misc",))
            C("scalar", lambda E, g=g, lv=lgt_v: E.activation(lv[:, :, g], ps["misc"][:, g * 8:(g + 1) * 8], AF.Copy),
              ("misc",), ("lgt",))
        sparsemax_ops(sbi)
        wf_keys = tuple(f"wf{ee}" for ee in range(8))
        wfin_v = sb["wfin"][:].rearrange("p (e g) -> p e g", g=2)
        for g in range(2):
            C("tensor", lambda E, g=g, wv=wfin_v: E.transpose(ps["misc"][0:8, 128 + g * 128:128 + (g + 1) * 128],
                                                              wv[:, :, g], sb["idn"][:]),
              wf_keys + ("idn",), ("misc",))
            C("scalar", lambda E, g=g: E.activation(sb["wt8"][:, g * 128:(g + 1) * 128],
                                                    ps["misc"][0:8, 128 + g * 128:128 + (g + 1) * 128], AF.Copy),
              ("misc",), ("wt8",))
        C("tensor", lambda E: E.matmul(ps["misc"][:, 0:SBT], sb["sel8"][:], sb["wt8"][:], start=True, stop=True),
          ("sel8", "wt8"), ("misc",))
        C("scalar", lambda E: E.activation(sb["wbc"][:], ps["misc"][:, 0:SBT], AF.Copy),
          ("misc",), ("wbc",))
        C("vector", lambda E: E.tensor_tensor(sb["lh"][:], sb["lh"][:], sb["wbc"][:], ALU.mult),
          ("lh", "wbc"), ("hw", "lh"))
        C("scalar", lambda E: E.activation(sb["hw_hi"][:], sb["lh"][:], AF.Copy),
          ("hw",), ("hw_hi",))
        C("vector", lambda E: E.tensor_tensor(sb["hw_lo"][:], sb["lh"][:], sb["hw_hi"][:], ALU.subtract),
          ("hw", "hw_hi"), ("hw_lo",))

    pend_stats = []
    pend_res = []

    def emit_res_stats():
        while pend_res:
            i, j = pend_res.pop(0)
            C("tensor", lambda E, j=j, s=(i == 0), t=(i == NDT - 1): E.matmul(
                ps["st"][0:1, :], sb["ones_c"][:, 0:1], sb[f"res{j}"][:], start=s, stop=t),
                ("ones_c", f"res{j}"), ("st", "st0"))
            C("tensor", lambda E, j=j, s=(i == 0), t=(i == NDT - 1): E.matmul(
                ps["st"][32:33, :], sb["ones_c"][:, 0:1], sb[f"yh{j}"][:], start=s, stop=t),
                ("ones_c", f"yh{j}"), ("st", "st1"))

    def emit_one_res():
        i, j = pend_res.pop(0)
        C("tensor", lambda E, j=j, s=(i == 0), t=(i == NDT - 1): E.matmul(
            ps["st"][0:1, :], sb["ones_c"][:, 0:1], sb[f"res{j}"][:], start=s, stop=t),
            ("ones_c", f"res{j}"), ("st", "st0"))
        C("tensor", lambda E, j=j, s=(i == 0), t=(i == NDT - 1): E.matmul(
            ps["st"][32:33, :], sb["ones_c"][:, 0:1], sb[f"yh{j}"][:], start=s, stop=t),
            ("ones_c", f"yh{j}"), ("st", "st1"))

    def emit_ff_stats(keep=2):
        while len(pend_stats) > keep:
            o, sbi, jo, sqn, cs = pend_stats.pop(0)
            C("tensor", lambda E, jo=jo, cs=cs, s=(o == 0), t=(o == NOT - 1): E.matmul(
                ps["xp2"][0:1, cs:cs + SBT], sb["ones_c"][:, 0:1], sb[f"fft{jo}"][:, 0:SBT], start=s, stop=t),
                ("ones_c", f"fft{jo}"), ("xp2", "xp2s"))
            C("tensor", lambda E, sqn=sqn, cs=cs, s=(o == 0), t=(o == NOT - 1): E.matmul(
                ps["xp2"][32:33, cs:cs + SBT], sb["ones_c"][:, 0:1], sb[sqn][:], start=s, stop=t),
                ("ones_c", sqn), ("xp2", "xp2q"))

    def ffn_otile(k, sbi, o):
        emit_ff_stats(keep=1)
        while len(pend_res) > 1:
            emit_one_res()
        cs = sbi * SBT
        jo = o % 2
        fb = f"ff{jo}"
        dma("sync", wla[:, 0:2048], wh_hi[o, :, 0:2048], (f"wbm{o}_0H", "wla0"), ("wla0",), fam="wla")
        dma("sync", wla[:, 4096:6144], wh_lo[o, :, 0:2048], (f"wbm{o}_0L", "wla2"), ("wla2",), fam="wla")
        dma("sync", wla[:, 2048:4096], wh_hi[o, :, 2048:4096], (f"wbm{o}_1H", "wla1"), ("wla1",), fam="wla")
        dma("sync", wla[:, 6144:8192], wh_lo[o, :, 2048:4096], (f"wbm{o}_1L", "wla3"), ("wla3",), fam="wla")
        dma("sync", sb[f"lbt_hi{jo}"][:], lbh_hi[:, o * 128:(o + 1) * 128], ("lbhH",), (f"lbt_hi{jo}",), fam=f"lbt{jo}")
        dma("sync", sb[f"lbt_lo{jo}"][:], lbh_lo[:, o * 128:(o + 1) * 128], ("lbhL",), (f"lbt_lo{jo}",), fam=f"lbt{jo}")
        nm = [0]
        for t in range(NKT):
            for (wk, xk) in (("hi", "hi"), ("hi", "lo"), ("lo", "hi")):
                off = (0 if wk == "hi" else 4096) + t * 128
                rk = "wla0" if off < 2048 else ("wla1" if off < 4096 else ("wla2" if off < 6144 else "wla3"))
                C("tensor", lambda E, off=off, t=t, xk=xk, s=(nm[0] == 0), fb=fb: E.matmul(
                    ps[fb][:], wla[:, off:off + 128],
                    sb["xn_hi" if xk == "hi" else "xn_lo"][:, t * SBT:(t + 1) * SBT],
                    start=s, stop=False),
                    (rk, f"xn{'h' if xk == 'hi' else 'l'}{t}"), (fb,))
                nm[0] += 1
        for pi, (wk, xk) in enumerate((("hi", "hi"), ("hi", "lo"), ("lo", "hi"))):
            C("tensor", lambda E, wk=wk, xk=xk, fb=fb, st=(pi == 2), jo=jo: E.matmul(
                ps[fb][:], sb[f"lbt_{wk}{jo}"][:], sb[f"hw_{xk}"][:],
                start=False, stop=st),
                (f"lbt_{wk}{jo}", f"hw_{xk}"), (fb,))
        jo4 = o % 4
        C("scalar", lambda E, o=o, fb=fb, jo4=jo4: E.activation(
            sb[f"fft{jo4}"][:], ps[fb][:], AF.Identity, bias=sb["bb"][:, o:o + 1]),
            (fb, "bb"), (f"fft{jo4}",))
        dma("scalar", sb[f"rr{jo}"][:], ag1i[k][o * 128:(o + 1) * 128, cs:cs + SBT],
            (f"ag1w{k}_{o}",), (f"rr{jo}",), fam=f"rr{jo}")
        C("vector", lambda E, jo=jo, cs=cs: E.tensor_tensor(sb[f"rr{jo}"][:], sb[f"rr{jo}"][:], sb["mbc"][:, cs:cs + SBT], ALU.subtract),
          (f"rr{jo}", "mbc_v", "mbc"), (f"rr{jo}",))
        C("vector", lambda E, jo=jo, cs=cs: E.tensor_tensor(sb[f"rr{jo}"][:], sb[f"rr{jo}"][:], sb["rbc"][:, cs:cs + SBT], ALU.mult),
          (f"rr{jo}", "rbc_v", "rbc"), (f"rr{jo}",))
        C("vector", lambda E, jo4=jo4: E.tensor_tensor(sb[f"fft{jo4}"][:], sb[f"fft{jo4}"][:], sb[f"rr{jo}"][:], ALU.add),
          (f"fft{jo4}", f"rr{jo}"), (f"fft{jo4}",))
        dma("gpsimd", r2scr[o * 128:(o + 1) * 128, cs:cs + SBT], sb[f"fft{jo4}"][:],
            (f"fft{jo4}",), (f"r2_{o}_{sbi}",), fam=f"r2s{k % 2}")
        sqn = "sq2" if jo == 0 else "sq2b"
        C("scalar", lambda E, jo4=jo4, sqn=sqn: E.activation(sb[sqn][:], sb[f"fft{jo4}"][:], AF.Square),
          (f"fft{jo4}",), (sqn,))
        pend_stats.append((o, sbi, jo4, sqn, cs))

    def ffn_fin(k):
        c0 = k * LC
        emit_ff_stats(keep=0)
        C("scalar", lambda E: E.activation(sb["stat_sb"][0:1, :], ps["xp2"][0:1, :], AF.Copy),
          ("xp2s", "xp2q", "xp2"), ("stat_sb",))
        C("scalar", lambda E: E.activation(sb["stat_sb"][32:33, :], ps["xp2"][32:33, :], AF.Copy),
          ("xp2s", "xp2q", "xp2"), ("stat_sb",))
        dma("scalar", ag2i[k][0:1, :], sb["stat_sb"][0:1, :], ("stat_sb",), (f"ag2w{k}",), fam="ag2w")
        dma("scalar", ag2i[k][1:2, :], sb["stat_sb"][32:33, :], ("stat_sb",), (f"ag2w{k}b",), fam="ag2w")
        P.op("gpsimd", lambda E: E.collective_compute(
            "AllReduce", ALU.add, replica_groups=RG, ins=[ag2i[k]], outs=[ag2o[k]]),
            reads=(f"ag2w{k}", f"ag2w{k}b"), writes=(f"gath2_{k}",), kind="cc", name=f"ar2_{k}", fam=f"car2{k % 2}")
        stv = sb["stv"]
        dma("scalar", stv[:, 0:LC], ag2o[k][0:1, :], (f"gath2_{k}",), ("stv0",), fam="stv")
        dma("scalar", stv[:, LC:2 * LC], ag2o[k][1:2, :], (f"gath2_{k}",), ("stv1",), fam="stv")
        ln_stats("mbc_v", "rbc_v")
        for o in range(NOT):
            jo = o % 2
            dma("scalar", sb[f"r2l{jo}"][:], r2scr[o * 128:(o + 1) * 128, :],
                (f"r2_{o}_0", f"r2_{o}_1"), (f"xlr{jo}",), fam=f"finl{jo}")
            C("vector", lambda E, jo=jo: E.tensor_tensor(sb[f"r2l{jo}"][:], sb[f"r2l{jo}"][:], sb["mbc"][:], ALU.subtract),
              (f"xlr{jo}", "mbc_v", "mbc"), (f"xlr{jo}",))
            C("vector", lambda E, jo=jo: E.tensor_tensor(sb[f"r2l{jo}"][:], sb[f"r2l{jo}"][:], sb["rbc"][:], ALU.mult),
              (f"xlr{jo}", "rbc_v", "rbc"), (f"xlr{jo}",))
            dma("scalar", out_c[o * 128:(o + 1) * 128, c0:c0 + LC], sb[f"r2l{jo}"][:],
                (f"xlr{jo}",), (f"outw{k}_{o}",), fam="out")

    SERIAL = SERIAL_EMIT
    for k in range(NCH):
        if DEBUG and k == 1:
            dma("sync", dbg_cr, sb["carry"][:],
                tuple(f"carry{cc2}" for cc2 in range(NDT * NST)), ("dbgcr",), fam="dbg")
        scan_side(k)
        for h in range(2):
            bcast_half(k, h)
            if k > 0 and not SERIAL:
                ffn_prep(k - 1, h)
            dtile(k, h, 0)
            for i in range(NDT):
                if i + 1 < NDT:
                    dtile(k, h, i + 1)
                for q in range(2):
                    quarter(k, h, i, q)
                dtile_end(k, h, i)
                if k > 0 and not SERIAL:
                    ffn_otile(k - 1, h, i)
                elif k == 0:
                    while len(pend_res) > 1:
                        emit_one_res()
                    emit_wcast(2)
            if h == 1 and k > 0 and not SERIAL:
                ffn_fin(k - 1)
        ag1(k)
    emit_wcast(64)
    if SERIAL:
        for k in range(NCH):
            for sbi in range(2):
                ffn_prep(k, sbi)
                for o in range(NOT):
                    ffn_otile(k, sbi, o)
            ffn_fin(k)
    else:
        for sbi in range(2):
            ffn_prep(NCH - 1, sbi)
            for o in range(NOT):
                ffn_otile(NCH - 1, sbi, o)
        ffn_fin(NCH - 1)

    P.op("vector", lambda E: E.engine_nop(),
         reads=tuple(f"outw{k}_{o}" for k in range(NCH) for o in range(NOT)),
         writes=(), kind="c", name="fin")

    P.finalize()
    global LAST_PROG
    LAST_PROG = P
    P.check_deadlock()
    with nc.Block() as block:
        @block.sync
        def _(E):
            P.emit_engine("sync", E, chans)

        @block.vector
        def _(E):
            P.emit_engine("vector", E, chans)

        @block.scalar
        def _(E):
            P.emit_engine("scalar", E, chans)

        @block.gpsimd
        def _(E):
            P.emit_engine("gpsimd", E, chans)

        @block.tensor
        def _(E):
            P.emit_engine("tensor", E, chans)
    stack.close()
    return nc


def make_inputs(ins):
    x = np.ascontiguousarray(ins["L0_attn_x"])
    A_log = np.asarray(ins["L0_attn_A_log"])
    Dv = np.asarray(ins["L0_attn_D"])
    xpw = np.asarray(ins["L0_attn_x_proj_w"])
    dtw = np.asarray(ins["L0_attn_dt_proj_w"])
    dtb = np.asarray(ins["L0_attn_dt_proj_b"])
    Wb = np.asarray(ins["L0_ffn_W_base"])
    bb = np.asarray(ins["L0_ffn_b_base"])
    Wr = np.asarray(ins["L0_ffn_W_router"])
    br = np.asarray(ins["L0_ffn_b_router"])
    lA = np.asarray(ins["L0_ffn_lora_A"])
    lB = np.asarray(ins["L0_ffn_lora_B"])

    idn = np.eye(128, dtype=np.float32)
    sel8 = np.zeros((8, 128), np.float32)
    for ee in range(8):
        sel8[ee, ee * 16:(ee + 1) * 16] = 1.0
    xpwT = np.ascontiguousarray(xpw.T)
    lap = np.ascontiguousarray(lA.transpose(1, 0, 2).reshape(D, 128))
    lbp = np.ascontiguousarray(lB.reshape(128, D))
    WrT = np.ascontiguousarray(Wr.T)

    maps = []
    for c in range(8):
        b, h = c // 2, c % 2
        own = slice(h * DH, (h + 1) * DH)
        xb = x[b].T
        xt_c = np.concatenate([xb[own], xb[(1 - h) * DH:(2 - h) * DH]], axis=0)
        perm = np.concatenate([np.arange(h * DH, (h + 1) * DH), np.arange((1 - h) * DH, (2 - h) * DH)])
        xpw_c = xpwT[perm]
        xpw_t = np.ascontiguousarray(xpw_c.reshape(32, 128, XPR).transpose(1, 0, 2).reshape(128, 32 * XPR))
        dtw_c = np.ascontiguousarray(dtw[own].T)
        alog_c = np.ascontiguousarray(A_log[own].reshape(NDT, 128, NST).transpose(1, 0, 2).reshape(128, NDT * NST))
        d_c = np.ascontiguousarray(Dv[own].reshape(NDT, 128).T)
        dtb_c = np.ascontiguousarray(dtb[own].reshape(NDT, 128).T)
        Wslice = Wb[own, :]
        wbs_c = np.ascontiguousarray(
            Wslice.reshape(NOT, 128, NKT, 128).transpose(0, 3, 2, 1).reshape(NOT, 128, NKT * 128))
        bb_c = np.ascontiguousarray(bb[own].reshape(NOT, 128).T)
        wrt_c = np.ascontiguousarray(WrT.reshape(NKT, 128, 8).transpose(1, 0, 2).reshape(128, NKT * 8))
        las_c = np.ascontiguousarray(lap.reshape(NKT, 128, 128).transpose(1, 0, 2).reshape(128, NKT * 128))
        lbs_c = np.ascontiguousarray(lbp[:, own])
        maps.append({
            "xt": xt_c, "xpw": xpw_t, "dtwi": dtw_c, "alog": alog_c,
            "dvec": d_c, "dtbi": dtb_c, "wbs": wbs_c, "bbt": bb_c,
            "wrti": wrt_c, "brt": br.reshape(8, 1).astype(np.float32),
            "las": las_c, "lbsi": lbs_c, "idni": idn, "sel8i": sel8,
        })
    return maps


_NC_CACHE = [None]


def kernel(**inputs):
    maps = make_inputs(inputs)
    if _NC_CACHE[0] is None:
        _NC_CACHE[0] = build_nc()
    nc = _NC_CACHE[0]
    res = run_bass_kernel_spmd(nc, maps, core_ids=list(range(8)))
    out = np.empty((B, L, D), np.float32)
    for c in range(8):
        b, h = c // 2, c % 2
        oc = res.results[c]["out_c"]
        out[b, :, h * DH:(h + 1) * DH] = oc.T
    return out




# version 07
# speedup vs baseline: 1.1541x; 1.0033x over previous
import numpy as np
import concourse.bass as bass
import concourse.mybir as mybir
from concourse.bass_utils import run_bass_kernel_spmd

F32 = mybir.dt.float32
BF16 = mybir.dt.bfloat16
AF = mybir.ActivationFunctionType
ALU = mybir.AluOpType

B, L, D = 4, 2048, 4096
DH, NST, DTR, XPR = 2048, 16, 128, 160
NCH, LC = 4, 512
SBT = 256
NDT = 16
OH, NOT, NKT = 2048, 16, 32
EPS = 1e-5
RG = [[0, 1], [2, 3], [4, 5], [6, 7]]
SORT8 = [(0, 1), (2, 3), (4, 5), (6, 7), (0, 2), (1, 3), (4, 6), (5, 7),
         (1, 2), (5, 6), (0, 4), (3, 7), (1, 5), (2, 6), (1, 4), (3, 6),
         (2, 4), (3, 5), (3, 4)]
DEBUG = False
SERIAL_EMIT = False
LAST_PROG = None


class Prog:
    ENGINES = ("sync", "vector", "scalar", "gpsimd", "tensor")

    def __init__(self):
        self.ops = []
        self.last_w = {}
        self.readers = {}
        self.fams = {}
        self.fam_R = {}

    def op(self, eng, fn, reads=(), writes=(), kind="c", name="", fam=None, R=0):
        import sys as _sys
        fr = _sys._getframe(1)
        if fr.f_code.co_name in ("dma", "C"):
            fr = fr.f_back
        lbl = f"{fr.f_code.co_name}:{fr.f_lineno}"
        deps = set()
        for kk in reads:
            if kk in self.last_w:
                deps.add(self.last_w[kk])
        for kk in writes:
            for r in self.readers.get(kk, ()):
                deps.add(r)
            if kk in self.last_w:
                deps.add(self.last_w[kk])
        i = len(self.ops)
        if kind in ("d", "cc"):
            assert fam is not None, name
            self.fams.setdefault(fam, []).append(i)
            self.fam_R[fam] = max(self.fam_R.get(fam, 0), R)
        self.ops.append(dict(eng=eng, fn=fn, deps=deps, kind=kind, name=name, fam=fam, lbl=lbl))
        for kk in reads:
            self.readers.setdefault(kk, []).append(i)
        for kk in writes:
            self.last_w[kk] = i
            self.readers[kk] = []
        return i

    def finalize(self):
        ops = self.ops
        self.need_inc = [False] * len(ops)
        for o in ops:
            for d in o["deps"]:
                p = ops[d]
                if p["kind"] == "c" and p["eng"] == o["eng"]:
                    continue
                if p["kind"] == "c":
                    self.need_inc[d] = True
        cnt = {}
        self.val = [0] * len(ops)
        for i, o in enumerate(ops):
            if o["kind"] == "c" and self.need_inc[i]:
                ch = (o["eng"], "c")
                cnt[ch] = cnt.get(ch, 0) + 1
                self.val[i] = cnt[ch]
        self.fam_idx = {}
        for fam, lst in self.fams.items():
            for q, gi in enumerate(lst):
                self.fam_idx[gi] = q

    def check_deadlock(self):
        import bisect
        ops = self.ops
        streams = {en: [i for i in range(len(ops)) if ops[i]["eng"] == en] for en in self.ENGINES}
        pos = {en: 0 for en in self.ENGINES}
        cnt = {}

        def waits_of(i):
            o = ops[i]
            ws = []
            for d in o["deps"]:
                p = ops[d]
                if p["kind"] == "c":
                    if p["eng"] == o["eng"]:
                        continue
                    ws.append((("c", p["eng"]), self.val[d]))
                else:
                    fam = p["fam"]
                    lst = self.fams[fam]
                    cx = max(bisect.bisect_left(lst, i), self.fam_idx[d] + 1)
                    tgt = min(cx + self.fam_R[fam], len(lst))
                    if ops[i]["kind"] == p["kind"] and ops[i].get("fam") == fam:
                        tgt = min(tgt, self.fam_idx[i])
                    tgt = max(tgt, self.fam_idx[d] + 1)
                    step = 16 if p["kind"] == "d" else 1
                    ws.append((("f", fam), tgt * step))
            return ws

        progressed = True
        while progressed:
            progressed = False
            for en in self.ENGINES:
                while pos[en] < len(streams[en]):
                    i = streams[en][pos[en]]
                    if any(cnt.get(ch, 0) < v for ch, v in waits_of(i)):
                        break
                    o = ops[i]
                    if o["kind"] == "c":
                        if self.need_inc[i]:
                            ch = ("c", en)
                            cnt[ch] = cnt.get(ch, 0) + 1
                    else:
                        step = 16 if o["kind"] == "d" else 1
                        ch = ("f", o["fam"])
                        cnt[ch] = cnt.get(ch, 0) + step
                    pos[en] += 1
                    progressed = True
        stuck = {en: pos[en] for en in self.ENGINES if pos[en] < len(streams[en])}
        if stuck:
            for en, p in stuck.items():
                i = streams[en][p]
                o = ops[i]
                print(f"DEADLOCK: {en} pos {p}/{len(streams[en])} op {i} kind={o['kind']} fam={o['fam']} name={o['name']}")
                for d in sorted(o["deps"]):
                    pp = ops[d]
                    print(f"   dep {d}: eng={pp['eng']} kind={pp['kind']} fam={pp['fam']} name={pp['name']} val={self.val[d]}")
                print("   waits:", waits_of(i), "cur:", {ch: cnt.get(ch, 0) for ch, v in waits_of(i)})
            raise RuntimeError("deadlock detected")
        print("deadlock check: OK")

    def emit_engine(self, ename, eng, chans):
        ops = self.ops
        import bisect
        waited = {}
        for i, o in enumerate(ops):
            if o["eng"] != ename:
                continue
            for d in sorted(o["deps"]):
                p = ops[d]
                if p["kind"] == "c":
                    if p["eng"] == ename:
                        continue
                    ch = ("c", p["eng"])
                    v = self.val[d]
                    assert v > 0, (i, o["name"], d, p["name"])
                    sem = chans[(p["eng"], "c")]
                else:
                    fam = p["fam"]
                    lst = self.fams[fam]
                    cx = bisect.bisect_left(lst, i)
                    cx = max(cx, self.fam_idx[d] + 1)
                    tgt = min(cx + self.fam_R[fam], len(lst))
                    if o["kind"] == p["kind"] and o.get("fam") == fam:
                        tgt = min(tgt, self.fam_idx[i])
                    tgt = max(tgt, self.fam_idx[d] + 1)
                    step = 16 if p["kind"] == "d" else 1
                    ch = ("f", fam)
                    v = tgt * step
                    sem = chans[("fam", fam)]
                if waited.get(ch, 0) >= v:
                    continue
                eng.wait_ge(sem, v)
                waited[ch] = v
            inst = o["fn"](eng)
            if o["kind"] == "c":
                if self.need_inc[i]:
                    inst.then_inc(chans[(ename, "c")], 1)
            else:
                step = 16 if o["kind"] == "d" else 1
                inst.then_inc(chans[("fam", o["fam"])], step)


def build_nc():
    nc = bass.Bass("TRN2", target_bir_lowering=False)
    P = Prog()

    xt = nc.dram_tensor("xt", [D, L], F32, kind="ExternalInput").ap()
    xpw = nc.dram_tensor("xpw", [128, 32 * XPR], F32, kind="ExternalInput").ap()
    dtwi = nc.dram_tensor("dtwi", [128, DH], F32, kind="ExternalInput").ap()
    alog = nc.dram_tensor("alog", [128, NDT * NST], F32, kind="ExternalInput").ap()
    dvec = nc.dram_tensor("dvec", [128, NDT], F32, kind="ExternalInput").ap()
    dtbi = nc.dram_tensor("dtbi", [128, NDT], F32, kind="ExternalInput").ap()
    wbs = nc.dram_tensor("wbs", [NOT, 128, NKT * 128], F32, kind="ExternalInput").ap()
    bbt = nc.dram_tensor("bbt", [128, NOT], F32, kind="ExternalInput").ap()
    wrti = nc.dram_tensor("wrti", [128, NKT * 8], F32, kind="ExternalInput").ap()
    brt = nc.dram_tensor("brt", [8, 1], F32, kind="ExternalInput").ap()
    las = nc.dram_tensor("las", [128, NKT * 128], F32, kind="ExternalInput").ap()
    lbsi = nc.dram_tensor("lbsi", [128, OH], F32, kind="ExternalInput").ap()
    idni = nc.dram_tensor("idni", [128, 128], F32, kind="ExternalInput").ap()
    sel8i = nc.dram_tensor("sel8i", [8, 128], F32, kind="ExternalInput").ap()
    out_c = nc.dram_tensor("out_c", [OH, L], F32, kind="ExternalOutput").ap()
    if DEBUG:
        dbg_res = nc.dram_tensor("dbg_res", [DH, L], F32, kind="ExternalOutput").ap()
        dbg_xn = nc.dram_tensor("dbg_xn", [D, L], F32, kind="ExternalOutput").ap()
        dbg_cr = nc.dram_tensor("dbg_cr", [128, NDT * NST], F32, kind="ExternalOutput").ap()
        dbg_st = nc.dram_tensor("dbg_st", [NCH, 2, LC], F32, kind="ExternalOutput").ap()
        dbg_bp = nc.dram_tensor("dbg_bp", [128, 8 * LC], F32, kind="ExternalOutput").ap()
        dbg_cp = nc.dram_tensor("dbg_cp", [128, 8 * LC], F32, kind="ExternalOutput").ap()
        dbg_dt = nc.dram_tensor("dbg_dt", [128, LC], F32, kind="ExternalOutput").ap()
        dbg_v = nc.dram_tensor("dbg_v", [128, LC], F32, kind="ExternalOutput").ap()

    xph_hi = nc.dram_tensor("xph_hi", [128, 32 * XPR], BF16).ap()
    xph_lo = nc.dram_tensor("xph_lo", [128, 32 * XPR], BF16).ap()
    lbh_hi = nc.dram_tensor("lbh_hi", [128, OH], BF16).ap()
    lbh_lo = nc.dram_tensor("lbh_lo", [128, OH], BF16).ap()
    lah_hi = nc.dram_tensor("lah_hi", [128, NKT * 128], BF16).ap()
    lah_lo = nc.dram_tensor("lah_lo", [128, NKT * 128], BF16).ap()
    wh_hi = nc.dram_tensor("wh_hi", [NOT, 128, NKT * 128], BF16).ap()
    wh_lo = nc.dram_tensor("wh_lo", [NOT, 128, NKT * 128], BF16).ap()
    yscr = nc.dram_tensor("yscr", [DH, LC], F32).ap()
    bchbm = nc.dram_tensor("bchbm", [32, LC], F32).ap()
    r2scr = nc.dram_tensor("r2scr", [OH, LC], F32).ap()
    ag1i = [nc.dram_tensor(f"ag1i{k}", [DH + 2, LC], F32).ap() for k in range(NCH)]
    ag1o = [nc.dram_tensor(f"ag1o{k}", [2 * (DH + 2), LC], F32).ap() for k in range(NCH)]
    ag2i = [nc.dram_tensor(f"ag2i{k}", [2, LC], F32).ap() for k in range(NCH)]
    ag2o = [nc.dram_tensor(f"ag2o{k}", [2, LC], F32).ap() for k in range(NCH)]

    from contextlib import ExitStack
    stack = ExitStack()
    e = stack.enter_context

    sb = {}
    def S(name, shape, dt=F32):
        sb[name] = e(nc.sbuf_tensor(name, shape, dt))
        return sb[name]

    S("ones_r", [1, 128]); S("ones_c", [128, 1])
    S("idn", [128, 128]); S("sel8", [8, 128])
    S("A", [128, NDT * NST]); S("onepD", [128, NDT]); S("dtb", [128, NDT])
    S("bb", [128, NOT]); S("br", [8, 1])
    S("dtw_hi", [128, DH], BF16); S("dtw_lo", [128, DH], BF16)
    S("wr_hi", [128, NKT * 8], BF16); S("wr_lo", [128, NKT * 8], BF16)
    S("carry", [128, NDT * NST])
    for j in range(2):
        S(f"xpw_hi{j}", [128, 2 * XPR], BF16); S(f"xpw_lo{j}", [128, 2 * XPR], BF16)
    S("Bp", [128, 8 * LC]); S("Cp", [128, 8 * LC])
    S("dAz0", [128, 4 * LC]); S("dAz1", [128, 4 * LC])
    S("dBx0", [128, 4 * LC]); S("dBx1", [128, 4 * LC])
    S("hp", [128, 4 * LC])
    for j in range(2):
        S(f"xhi{j}", [128, LC], BF16); S(f"xlo{j}", [128, LC], BF16)
        S(f"dtt{j}", [128, LC]); S(f"xot{j}", [128, LC]); S(f"vt{j}", [128, LC])
    for j in range(4):
        S(f"yh{j}", [128, LC]); S(f"res{j}", [128, LC])
    S("ysc", [128, LC]); S("zero4", [128, 4]); S("tc4", [128, 4])
    S("lbt_hi0", [128, 128], BF16); S("lbt_lo0", [128, 128], BF16)
    S("lbt_hi1", [128, 128], BF16); S("lbt_lo1", [128, 128], BF16)
    S("dtin_hi", [128, LC], BF16); S("dtin_lo", [128, LC], BF16)
    S("bc_c", [32, LC])
    S("stat_sb", [64, LC])
    S("xn_hi", [128, NKT * SBT], BF16); S("xn_lo", [128, NKT * SBT], BF16)
    S("wla", [128, 8192], BF16)
    for j in range(2):
        S(f"gt{j}", [128, SBT])
        S(f"rr{j}", [128, SBT])
        S(f"r2l{j}", [128, LC])
    for j in range(4):
        S(f"fft{j}", [128, SBT])
    S("sq2", [128, SBT])
    S("sq2b", [128, SBT])
    S("stv", [1, 3 * LC])
    S("mbc", [128, LC]); S("rbc", [128, LC])
    S("rt8", [8, SBT]); S("lgt", [128, 16])
    S("sortp", [128, 48])
    S("wfin", [128, 16]); S("wt8", [8, SBT]); S("wbc", [128, SBT])
    S("lh", [128, SBT])
    S("hw_hi", [128, SBT], BF16); S("hw_lo", [128, SBT], BF16)

    wla = sb["wla"]
    wla_f32 = wla[:].bitcast(F32)
    wf32 = wla_f32[:, 0:2048]
    cst_hi = wla[:, 4096:6144]
    cst_lo = wla[:, 6144:8192]
    WQ = ("wla0", "wla1", "wla2", "wla3")

    ps = {}
    def PSUM(name, shape):
        ps[name] = e(nc.psum_tensor(name, shape, F32))
    PSUM("xp1", [128, LC])
    PSUM("xp2", [64, LC])
    PSUM("bcb", [128, LC])
    PSUM("st", [64, LC])
    PSUM("rtl", [128, SBT])
    PSUM("misc", [128, LC])
    PSUM("ff0", [128, SBT]); PSUM("ff1", [128, SBT])

    chans = {}
    for en in Prog.ENGINES:
        chans[(en, "c")] = e(nc.semaphore(f"c_{en}"))
    FAMS = ["consts", "xphw", "lahw", "wbmw", "wla", "xlr0", "xlr1", "xot0", "xot1", "xpw",
            "bp", "cp", "gt0", "gt1", "rr0", "rr1", "yscl", "bchbm", "stv", "out",
            "yscr0", "yscr1", "r2s0", "r2s1", "agw0", "agw1", "arw", "ag2w",
            "cag0", "cag1", "car0", "car1", "car20", "car21", "dbg",
            "xpwh0", "xpwh1", "xpwl0", "xpwl1", "lbt0", "lbt1", "finl0", "finl1", "lbhw"]
    for f in FAMS:
        chans[("fam", f)] = e(nc.semaphore(f"f_{f}"))

    def dma(eng, out_ap, in_ap, reads, writes, name="", fam="consts", R=0):
        P.op(eng, lambda E, o=out_ap, i=in_ap: E.dma_start(out=o, in_=i),
             reads=reads, writes=writes, kind="d", name=name, fam=fam, R=R)

    C = P.op

    C("vector", lambda E: E.memset(sb["ones_r"][:], 1.0), (), ("ones_r",))
    C("vector", lambda E: E.memset(sb["ones_c"][:], 1.0), (), ("ones_c",))
    C("vector", lambda E: E.memset(sb["zero4"][:], 0.0), (), ("zero4",))
    dma("sync", sb["idn"][:], idni, (), ("idn",))
    dma("sync", sb["sel8"][:], sel8i, (), ("sel8",))
    dma("sync", sb["A"][:], alog, (), ("A",))
    C("scalar", lambda E: E.activation(sb["A"][:], sb["A"][:], AF.Exp), ("A",), ("A",))
    C("vector", lambda E: E.tensor_single_scalar(sb["A"][:], sb["A"][:], -1.0, ALU.mult), ("A",), ("A",))
    dma("sync", sb["onepD"][:], dvec, (), ("onepD",))
    C("vector", lambda E: E.tensor_single_scalar(sb["onepD"][:], sb["onepD"][:], 1.0, ALU.add), ("onepD",), ("onepD",))
    dma("sync", sb["dtb"][:], dtbi, (), ("dtb",))
    dma("sync", sb["bb"][:], bbt, (), ("bb",))
    dma("sync", sb["br"][:], brt, (), ("br",))

    def stage_cast(src_ap, cols, hi_dst, hi_key, lo_dst, lo_key):
        dma("sync", wf32[:, 0:cols], src_ap, ("wla0", "wla1"), ("wla0", "wla1"), fam="wla")
        C("scalar", lambda E, c=cols, h=hi_dst: E.activation(h, wf32[:, 0:c], AF.Copy),
          ("wla0", "wla1"), (hi_key,))
        C("vector", lambda E, c=cols, h=hi_dst, l=lo_dst: E.tensor_tensor(l, wf32[:, 0:c], h, ALU.subtract),
          ("wla0", "wla1", hi_key), (lo_key,))

    stage_cast(dtwi, DH, sb["dtw_hi"][:], "dtw_hi", sb["dtw_lo"][:], "dtw_lo")
    stage_cast(wrti, NKT * 8, sb["wr_hi"][:], "wr_hi", sb["wr_lo"][:], "wr_lo")

    def cast_to_hbm(nm, src_ap, dst_hi, dst_lo, total):
        off = 0
        while off < total:
            w = min(2048, total - off)
            dma("sync", wf32[:, 0:w], src_ap[:, off:off + w], ("wla0", "wla1"), ("wla0", "wla1"), fam="wla")
            C("scalar", lambda E, w=w: E.activation(cst_hi[:, 0:w], wf32[:, 0:w], AF.Copy),
              ("wla0", "wla1"), ("wla2",))
            C("vector", lambda E, w=w: E.tensor_tensor(cst_lo[:, 0:w], wf32[:, 0:w], cst_hi[:, 0:w], ALU.subtract),
              ("wla0", "wla1", "wla2"), ("wla3",))
            dma("sync", dst_hi[:, off:off + w], cst_hi[:, 0:w], ("wla2",), (f"{nm}H",), fam=nm + "w")
            dma("sync", dst_lo[:, off:off + w], cst_lo[:, 0:w], ("wla3",), (f"{nm}L",), fam=nm + "w")
            off += w

    cast_to_hbm("xph", xpw, xph_hi, xph_lo, 32 * XPR)
    cast_to_hbm("lah", las, lah_hi, lah_lo, NKT * 128)
    cast_to_hbm("lbh", lbsi, lbh_hi, lbh_lo, OH)

    wblocks = [(o, hf) for o in range(NOT) for hf in range(2)]
    widx = [0]

    def emit_wcast(nblk):
        for _ in range(nblk):
            if widx[0] >= len(wblocks):
                return
            o, hf = wblocks[widx[0]]
            widx[0] += 1
            sl = slice(hf * 2048, (hf + 1) * 2048)
            dma("sync", wf32[:], wbs[o, :, sl], ("wla0", "wla1"), ("wla0", "wla1"), fam="wla")
            C("scalar", lambda E: E.activation(cst_hi[:], wf32[:], AF.Copy),
              ("wla0", "wla1"), ("wla2",))
            C("vector", lambda E: E.tensor_tensor(cst_lo[:], wf32[:], cst_hi[:], ALU.subtract),
              ("wla0", "wla1", "wla2"), ("wla3",))
            dma("sync", wh_hi[o, :, sl], cst_hi[:], ("wla2",), (f"wbm{o}_{hf}H",), fam="wbmw")
            dma("sync", wh_lo[o, :, sl], cst_lo[:], ("wla3",), (f"wbm{o}_{hf}L",), fam="wbmw")

    def scan_side(k):
        c0 = k * LC
        n_mm = [0]
        for pc in range(16):
            pb = pc % 2
            poff = pc * 2 * XPR
            dma("sync", sb[f"xpw_hi{pb}"][:], xph_hi[:, poff:poff + 2 * XPR], (f"xpw_hi{pb}",), (f"xpw_hi{pb}",), fam=f"xpwh{pb}")
            dma("sync", sb[f"xpw_lo{pb}"][:], xph_lo[:, poff:poff + 2 * XPR], (f"xpw_lo{pb}",), (f"xpw_lo{pb}",), fam=f"xpwl{pb}")
            for tl in range(2):
                t = pc * 2 + tl
                j = t % 2
                dma("sync" if j == 0 else "scalar", sb[f"r2l{j}"][:], xt[t * 128:(t + 1) * 128, c0:c0 + LC], (), (f"xlr{j}",), fam=f"xlr{j}")
                C("scalar", lambda E, j=j: E.activation(sb[f"xhi{j}"][:], sb[f"r2l{j}"][:], AF.Copy),
                  (f"xlr{j}",), (f"xhi{j}",))
                C("vector", lambda E, j=j: E.tensor_tensor(sb[f"xlo{j}"][:], sb[f"r2l{j}"][:], sb[f"xhi{j}"][:], ALU.subtract),
                  (f"xlr{j}", f"xhi{j}"), (f"xlo{j}",))
                for (wk, xk) in (("hi", "hi"), ("hi", "lo"), ("lo", "hi")):
                    s0 = (n_mm[0] == 0)
                    s1 = (n_mm[0] == 95)
                    C("tensor", lambda E, wk=wk, xk=xk, j=j, tl=tl, pb=pb, s0=s0, s1=s1: E.matmul(
                        ps["xp1"][:], sb[f"xpw_{wk}{pb}"][:, tl * XPR:tl * XPR + 128], sb[f"x{xk}{j}"][:],
                        start=s0, stop=s1),
                        (f"xpw_{wk}{pb}", f"x{xk}{j}"), ("xp1",))
                    C("tensor", lambda E, wk=wk, xk=xk, j=j, tl=tl, pb=pb, s0=s0, s1=s1: E.matmul(
                        ps["bcb"][0:32, :], sb[f"xpw_{wk}{pb}"][:, tl * XPR + 128:(tl + 1) * XPR], sb[f"x{xk}{j}"][:],
                        start=s0, stop=s1),
                        (f"xpw_{wk}{pb}", f"x{xk}{j}"), ("bcb",))
                    n_mm[0] += 1
        C("scalar", lambda E: E.activation(sb["dtin_hi"][:], ps["xp1"][:], AF.Copy),
          ("xp1",), ("dtin_hi",))
        C("vector", lambda E: E.tensor_tensor(sb["dtin_lo"][:], ps["xp1"][:], sb["dtin_hi"][:], ALU.subtract),
          ("xp1", "dtin_hi"), ("dtin_lo",))
        C("scalar", lambda E: E.activation(sb["bc_c"][:], ps["bcb"][0:32, :], AF.Copy),
          ("bcb",), ("bc_c",))
        dma("scalar", bchbm, sb["bc_c"][:], ("bc_c",), ("bchbm",), fam="bchbm")

    def bcast_half(k, h):
        srcB = bchbm[h * 8:(h + 1) * 8, :].flatten().unsqueeze(0).broadcast_to([128, 8 * LC])
        srcC = bchbm[16 + h * 8:16 + (h + 1) * 8, :].flatten().unsqueeze(0).broadcast_to([128, 8 * LC])
        dma("gpsimd", sb["Bp"][:], srcB, ("bchbm",), tuple(f"Bp{z}" for z in range(8)), fam="bp")
        dma("gpsimd", sb["Cp"][:], srcC, ("bchbm",), tuple(f"Cp{z}" for z in range(8)), fam="cp")
        if DEBUG and k == 1 and h == 0:
            dma("sync", dbg_bp, sb["Bp"][:], tuple(f"Bp{z}" for z in range(8)), ("dbgbp",), fam="dbg")
            dma("sync", dbg_cp, sb["Cp"][:], tuple(f"Cp{z}" for z in range(8)), ("dbgcp",), fam="dbg")

    def dtile(k, h, i):
        j = i % 2
        for pi, (wk, xk) in enumerate((("hi", "hi"), ("hi", "lo"), ("lo", "hi"))):
            C("tensor", lambda E, wk=wk, xk=xk, i=i, s=(pi == 0), t=(pi == 2): E.matmul(
                ps["xp1"][:], sb[f"dtw_{wk}"][:, i * 128:(i + 1) * 128], sb[f"dtin_{xk}"][:],
                start=s, stop=t),
                (f"dtw_{wk}", f"dtin_{xk}"), ("xp1",))
        C("scalar", lambda E, i=i, j=j: E.activation(sb[f"dtt{j}"][:], ps["xp1"][:], AF.Exp, bias=sb["dtb"][:, i:i + 1]),
          ("xp1", "dtb"), (f"dtt{j}",))
        C("scalar", lambda E, j=j: E.activation(sb[f"dtt{j}"][:], sb[f"dtt{j}"][:], AF.Ln, bias=1.0),
          (f"dtt{j}",), (f"dtt{j}",))
        dma("scalar", sb[f"xot{j}"][:], xt[i * 128:(i + 1) * 128, k * LC:(k + 1) * LC], (), (f"xot{j}",), fam=f"xot{j}")
        C("gpsimd", lambda E, j=j: E.tensor_tensor(sb[f"vt{j}"][:], sb[f"dtt{j}"][:], sb[f"xot{j}"][:], ALU.mult),
          (f"dtt{j}", f"xot{j}"), (f"vt{j}",))
        if DEBUG and k == 1 and h == 0 and i == 0:
            dma("sync", dbg_dt, sb[f"dtt{j}"][:], (f"dtt{j}",), ("dbgdt",), fam="dbg")
            dma("sync", dbg_v, sb[f"vt{j}"][:], (f"vt{j}",), ("dbgv",), fam="dbg")

    ucnt = [0]

    def quarter(k, h, i, q):
        j = i % 2
        ub = ucnt[0] % 2
        ucnt[0] += 1
        dA = f"dAz{ub}"
        dB = f"dBx{ub}"
        for jj in range(4):
            n = h * 8 + q * 4 + jj
            col = i * NST + n
            C("scalar", lambda E, jj=jj, col=col, j=j, dA=dA: E.activation(
                sb[dA][:, jj * LC:(jj + 1) * LC], sb[f"dtt{j}"][:], AF.Exp,
                scale=sb["A"][:, col:col + 1]),
                (f"dtt{j}", "A"), (dA,))
        vb = sb[f"vt{j}"][:].unsqueeze(1).broadcast_to([128, 4, LC])
        C("gpsimd", lambda E, q=q, dB=dB, vb=vb: E.tensor_tensor(
            sb[dB][:].rearrange("p (n l) -> p n l", n=4),
            sb["Bp"][:, q * 4 * LC:(q + 1) * 4 * LC].rearrange("p (n l) -> p n l", n=4),
            vb, ALU.mult),
            tuple([f"vt{j}"] + [f"Bp{q * 4 + z}" for z in range(4)]), (dB,))
        colb0 = i * NST + h * 8 + q * 4
        if k > 0:
            dA_f = sb[dA][:].rearrange("p (n l) -> p n l", n=4)[:, :, 0]
            dB_f = sb[dB][:].rearrange("p (n l) -> p n l", n=4)[:, :, 0]
            C("gpsimd", lambda E, dA_f=dA_f, colb0=colb0: E.tensor_tensor(
                sb["tc4"][:], dA_f, sb["carry"][:, colb0:colb0 + 4], ALU.mult),
                tuple([dA] + [f"carry{colb0 + z}" for z in range(4)]), ("tc4",))
            C("gpsimd", lambda E, dB_f=dB_f: E.tensor_tensor(
                dB_f, dB_f, sb["tc4"][:], ALU.add),
                (dB, "tc4"), (dB,))
        for jj in range(4):
            rd = [dA, dB]
            C("vector", lambda E, jj=jj, dA=dA, dB=dB: E.tensor_tensor_scan(
                sb["hp"][:, jj * LC:(jj + 1) * LC],
                sb[dA][:, jj * LC:(jj + 1) * LC],
                sb[dB][:, jj * LC:(jj + 1) * LC],
                0.0, ALU.mult, ALU.add),
                tuple(rd), ("hp",))
        colb = i * NST + h * 8 + q * 4
        C("vector", lambda E, colb=colb: E.tensor_tensor(
            sb["carry"][:, colb:colb + 3],
            sb["hp"][:, 0:3 * LC].rearrange("p (n l) -> p n l", n=3)[:, :, LC - 1],
            sb["zero4"][:, 0:3], ALU.add),
            ("hp", "zero4"), tuple(f"carry{colb + z}" for z in range(3)))
        C("vector", lambda E, colb=colb: E.tensor_tensor(
            sb["carry"][:, colb + 3:colb + 4],
            sb["hp"][:, 4 * LC - 1:4 * LC],
            sb["zero4"][:, 0:1], ALU.add),
            ("hp", "zero4"), (f"carry{colb + 3}",))
        C("vector", lambda E, q=q: E.tensor_tensor(
            sb["hp"][:], sb["hp"][:], sb["Cp"][:, q * 4 * LC:(q + 1) * 4 * LC], ALU.mult),
            tuple(["hp"] + [f"Cp{q * 4 + z}" for z in range(4)]), ("hp",))
        C("vector", lambda E: E.tensor_tensor(
            sb["hp"][:, 0:2 * LC], sb["hp"][:, 0:2 * LC], sb["hp"][:, 2 * LC:4 * LC], ALU.add),
            ("hp",), ("hp",))
        j4 = i % 4
        if q == 0:
            C("vector", lambda E, j4=j4: E.tensor_tensor(
                sb[f"yh{j4}"][:], sb["hp"][:, 0:LC], sb["hp"][:, LC:2 * LC], ALU.add),
                ("hp",), (f"yh{j4}",))
        else:
            C("vector", lambda E: E.tensor_tensor(
                sb["hp"][:, 2 * LC:3 * LC], sb["hp"][:, 0:LC], sb["hp"][:, LC:2 * LC], ALU.add),
                ("hp",), ("hp",))
            C("vector", lambda E, j4=j4: E.tensor_tensor(
                sb[f"yh{j4}"][:], sb[f"yh{j4}"][:], sb["hp"][:, 2 * LC:3 * LC], ALU.add),
                ("hp", f"yh{j4}"), (f"yh{j4}",))

    pend_stores = []

    def flush_stores(keep=0):
        while len(pend_stores) > keep:
            pend_stores.pop(0)()

    def dtile_end(k, h, i):
        flush_stores(keep=2)
        j = i % 2
        j4 = i % 4
        if h == 0:
            pend_stores.append(lambda i=i, j4=j4, k=k: dma(
                "gpsimd", yscr[i * 128:(i + 1) * 128, :], sb[f"yh{j4}"][:],
                (f"yh{j4}",), (f"yscr{i}",), fam=f"yscr{k % 2}"))
        else:
            dma("scalar", sb["ysc"][:], yscr[i * 128:(i + 1) * 128, :],
                (f"yscr{i}",), ("ysc",), fam="yscl")
            C("vector", lambda E, j4=j4: E.tensor_tensor(sb[f"yh{j4}"][:], sb[f"yh{j4}"][:], sb["ysc"][:], ALU.add),
              (f"yh{j4}", "ysc"), (f"yh{j4}",))
            C("vector", lambda E, j=j, j4=j4, i=i: E.scalar_tensor_tensor(
                sb[f"res{j4}"][:], sb[f"xot{j}"][:], sb["onepD"][:, i:i + 1], sb[f"yh{j4}"][:],
                ALU.mult, ALU.add),
                (f"xot{j}", "onepD", f"yh{j4}"), (f"res{j4}",))
            pend_stores.append(lambda i=i, j4=j4, k=k: dma(
                "gpsimd", ag1i[k][i * 128:(i + 1) * 128, :], sb[f"res{j4}"][:],
                (f"res{j4}",), (f"ag1w{k}_{i}",), fam=f"agw{k % 2}"))
            C("scalar", lambda E, j4=j4: E.activation(sb[f"yh{j4}"][:], sb[f"res{j4}"][:], AF.Square),
              (f"res{j4}",), (f"yh{j4}",))
            pend_res.append((i, j4))
            if DEBUG:
                pend_stores.append(lambda i=i, j4=j4, k=k: dma(
                    "gpsimd", dbg_res[i * 128:(i + 1) * 128, k * LC:(k + 1) * LC], sb[f"res{j4}"][:],
                    (f"res{j4}",), (f"dbgres{k}_{i}",), fam="dbg"))

    def ag1(k):
        flush_stores()
        while pend_res:
            emit_one_res()
        C("scalar", lambda E: E.activation(sb["stat_sb"][0:1, :], ps["st"][0:1, :], AF.Copy),
          ("st0", "st1", "st"), ("stat1a", "stat_sb"))
        C("scalar", lambda E: E.activation(sb["stat_sb"][32:33, :], ps["st"][32:33, :], AF.Copy),
          ("st0", "st1", "st"), ("stat1b", "stat_sb"))
        dma("scalar", ag1i[k][DH:DH + 1, :], sb["stat_sb"][0:1, :], ("stat1a",), (f"ag1w{k}_s0",), fam=f"agw{k % 2}")
        dma("scalar", ag1i[k][DH + 1:DH + 2, :], sb["stat_sb"][32:33, :], ("stat1b",), (f"ag1w{k}_s1",), fam=f"agw{k % 2}")
        reads = tuple(f"ag1w{k}_{i}" for i in range(NDT)) + (f"ag1w{k}_s0", f"ag1w{k}_s1")
        P.op("gpsimd", lambda E: E.collective_compute(
            "AllGather", ALU.bypass, replica_groups=RG, ins=[ag1i[k]], outs=[ag1o[k]]),
            reads=reads, writes=(f"gath{k}",), kind="cc", name=f"ag1_{k}", fam=f"cag{k % 2}")

    def ln_stats(m_name, r_name, width=LC):
        stv = sb["stv"]
        s0 = stv[:, 0:width]; s1 = stv[:, LC:LC + width]; s2 = stv[:, 2 * LC:2 * LC + width]
        C("vector", lambda E: E.tensor_single_scalar(s0, s0, 1.0 / D, ALU.mult),
          ("stv0",), ("stv0",))
        C("tensor", lambda E, w=width: E.matmul(ps["misc"][:, 0:w], sb["ones_r"][0:1, :], s0[0:1, :], start=True, stop=True),
          ("stv0", "ones_r"), ("misc",))
        C("scalar", lambda E, w=width: E.activation(sb["mbc"][:, 0:w], ps["misc"][:, 0:w], AF.Copy),
          ("misc",), (m_name, "mbc"))
        C("vector", lambda E: E.tensor_single_scalar(s1, s1, 1.0 / D, ALU.mult),
          ("stv1",), ("stv1",))
        C("vector", lambda E: E.tensor_tensor(s2, s0, s0, ALU.mult),
          ("stv0",), ("stv2",))
        C("vector", lambda E: E.tensor_tensor(s1, s1, s2, ALU.subtract),
          ("stv1", "stv2"), ("stv1",))
        C("vector", lambda E: E.tensor_single_scalar(s1, s1, EPS, ALU.add),
          ("stv1",), ("stv1",))
        C("scalar", lambda E: E.activation(s2, s1, AF.Ln),
          ("stv1",), ("stv2",))
        C("scalar", lambda E: E.activation(s2, s2, AF.Exp, scale=-0.5),
          ("stv2",), ("stv2",))
        C("vector", lambda E: E.tensor_tensor(s0, s2, s2, ALU.mult),
          ("stv2",), ("stv0",))
        C("vector", lambda E: E.tensor_tensor(s0, s0, s1, ALU.mult),
          ("stv0", "stv1"), ("stv0",))
        C("vector", lambda E: E.tensor_scalar(s0, s0, -0.5, 1.5, ALU.mult, ALU.add),
          ("stv0",), ("stv0",))
        C("vector", lambda E: E.tensor_tensor(s2, s2, s0, ALU.mult),
          ("stv2", "stv0"), ("stv2",))
        C("tensor", lambda E, w=width: E.matmul(ps["misc"][:, 0:w], sb["ones_r"][0:1, :], s2[0:1, :], start=True, stop=True),
          ("stv2", "ones_r"), ("misc",))
        C("scalar", lambda E, w=width: E.activation(sb["rbc"][:, 0:w], ps["misc"][:, 0:w], AF.Copy),
          ("misc",), (r_name, "rbc"))

    def sparsemax_ops(sbi):
        pool = sb["sortp"]
        W2 = 2
        def sl(s):
            return pool[:, s * W2:(s + 1) * W2]
        slots = {}
        free = list(range(24))
        for ee in range(8):
            s0 = free.pop(0)
            slots[ee] = s0
            C("vector", lambda E, ee=ee, s0=s0: E.tensor_copy(sl(s0), sb["lgt"][:, ee * W2:(ee + 1) * W2]),
              ("lgt",), (f"sp{s0}",))
        for (i, jx) in SORT8:
            si, sj = slots[i], slots[jx]
            a = free.pop(0); b2 = free.pop(0)
            C("vector", lambda E, si=si, sj=sj, a=a: E.tensor_tensor(sl(a), sl(si), sl(sj), ALU.max),
              (f"sp{si}", f"sp{sj}"), (f"sp{a}",))
            C("vector", lambda E, si=si, sj=sj, b2=b2: E.tensor_tensor(sl(b2), sl(si), sl(sj), ALU.min),
              (f"sp{si}", f"sp{sj}"), (f"sp{b2}",))
            free.append(si); free.append(sj)
            slots[i], slots[jx] = a, b2
        cum = {}
        c0s = free.pop(0)
        cum[0] = c0s
        C("vector", lambda E, s=slots[0], c=c0s: E.tensor_copy(sl(c), sl(s)),
          (f"sp{slots[0]}",), (f"sp{c0s}",))
        for ee in range(1, 8):
            cs = free.pop(0)
            C("vector", lambda E, p=cum[ee - 1], z=slots[ee], cs=cs: E.tensor_tensor(sl(cs), sl(p), sl(z), ALU.add),
              (f"sp{cum[ee - 1]}", f"sp{slots[ee]}"), (f"sp{cs}",))
            cum[ee] = cs
        tsl = free.pop(0); msl = free.pop(0); ksl = free.pop(0)
        for ee in range(8):
            C("vector", lambda E, z=slots[ee], c=cum[ee], t=tsl, ee=ee: E.scalar_tensor_tensor(
                sl(t), sl(z), float(ee + 1), sl(c), ALU.mult, ALU.subtract),
                (f"sp{slots[ee]}", f"sp{cum[ee]}"), (f"sp{tsl}",))
            C("vector", lambda E, t=tsl, m=msl: E.tensor_single_scalar(sl(m), sl(t), -1.0, ALU.is_gt),
              (f"sp{tsl}",), (f"sp{msl}",))
            if ee == 0:
                C("vector", lambda E, m=msl, ks=ksl: E.tensor_copy(sl(ks), sl(m)),
                  (f"sp{msl}",), (f"sp{ksl}",))
            else:
                C("vector", lambda E, m=msl, ks=ksl: E.tensor_tensor(sl(ks), sl(ks), sl(m), ALU.add),
                  (f"sp{msl}", f"sp{ksl}"), (f"sp{ksl}",))
        acc, eq = tsl, msl
        t2 = free[0]
        for ee in range(8):
            C("vector", lambda E, ks=ksl, eqs=eq, ee=ee: E.tensor_single_scalar(sl(eqs), sl(ks), float(ee + 1), ALU.is_equal),
              (f"sp{ksl}",), (f"sp{eq}",))
            if ee == 0:
                C("vector", lambda E, c=cum[ee], eqs=eq, a=acc: E.tensor_tensor(sl(a), sl(c), sl(eqs), ALU.mult),
                  (f"sp{cum[ee]}", f"sp{eq}"), (f"sp{acc}",))
            else:
                C("vector", lambda E, c=cum[ee], eqs=eq, t2=t2: E.tensor_tensor(sl(t2), sl(c), sl(eqs), ALU.mult),
                  (f"sp{cum[ee]}", f"sp{eq}"), (f"sp{t2}",))
                C("vector", lambda E, a=acc, t2=t2: E.tensor_tensor(sl(a), sl(a), sl(t2), ALU.add),
                  (f"sp{acc}", f"sp{t2}"), (f"sp{acc}",))
        C("vector", lambda E, ks=ksl, eqs=eq: E.reciprocal(sl(eqs), sl(ks)),
          (f"sp{ksl}",), (f"sp{eq}",))
        C("vector", lambda E, a=acc: E.tensor_single_scalar(sl(a), sl(a), -1.0, ALU.add),
          (f"sp{acc}",), (f"sp{acc}",))
        C("vector", lambda E, a=acc, eqs=eq: E.tensor_tensor(sl(a), sl(a), sl(eqs), ALU.mult),
          (f"sp{acc}", f"sp{eq}"), (f"sp{acc}",))
        for ee in range(8):
            C("vector", lambda E, ee=ee, a=acc: E.tensor_tensor(
                sb["wfin"][:, ee * W2:(ee + 1) * W2], sb["lgt"][:, ee * W2:(ee + 1) * W2], sl(a), ALU.subtract),
                (f"sp{acc}", "lgt"), (f"wf{ee}",))
            C("vector", lambda E, ee=ee: E.tensor_single_scalar(
                sb["wfin"][:, ee * W2:(ee + 1) * W2], sb["wfin"][:, ee * W2:(ee + 1) * W2], 0.0, ALU.max),
                (f"wf{ee}",), (f"wf{ee}", "wfin"))

    def ffn_prep(k, sbi):
        cs = sbi * SBT
        stv = sb["stv"]
        if sbi == 0:
            dma("scalar", stv[0:1, 0:LC], ag1o[k][DH:DH + 1, :], (f"gath{k}",), ("stv0",), fam="stv")
            dma("scalar", stv[0:1, LC:2 * LC], ag1o[k][DH + 1:DH + 2, :], (f"gath{k}",), ("stv1",), fam="stv")
            dma("scalar", stv[0:1, 2 * LC:3 * LC], ag1o[k][2 * DH + 2:2 * DH + 3, :], (f"gath{k}",), ("stv2",), fam="stv")
            C("vector", lambda E: E.tensor_tensor(stv[0:1, 0:LC], stv[0:1, 0:LC], stv[0:1, 2 * LC:3 * LC], ALU.add),
              ("stv0", "stv2"), ("stv0",))
            dma("scalar", stv[0:1, 2 * LC:3 * LC], ag1o[k][2 * DH + 3:2 * DH + 4, :], (f"gath{k}",), ("stv2",), fam="stv")
            C("vector", lambda E: E.tensor_tensor(stv[0:1, LC:2 * LC], stv[0:1, LC:2 * LC], stv[0:1, 2 * LC:3 * LC], ALU.add),
              ("stv1", "stv2"), ("stv1",))
            if DEBUG:
                dma("scalar", dbg_st[k, 0:1, :], stv[0:1, 0:LC], ("stv0",), (f"dbgst{k}a",), fam="dbg")
                dma("scalar", dbg_st[k, 1:2, :], stv[0:1, LC:2 * LC], ("stv1",), (f"dbgst{k}b",), fam="dbg")
            ln_stats("mbc_v", "rbc_v", width=LC)
        for t in range(NKT):
            j = t % 2
            row = t * 128 + (0 if t < NDT else 2)
            dma("scalar", sb[f"gt{j}"][:], ag1o[k][row:row + 128, cs:cs + SBT], (f"gath{k}",), (f"gt{j}",), fam=f"gt{j}")
            C("vector", lambda E, j=j, cs=cs: E.tensor_tensor(sb[f"gt{j}"][:], sb[f"gt{j}"][:], sb["mbc"][:, cs:cs + SBT], ALU.subtract),
              (f"gt{j}", "mbc_v", "mbc"), (f"gt{j}",))
            C("vector", lambda E, j=j, cs=cs: E.tensor_tensor(sb[f"gt{j}"][:], sb[f"gt{j}"][:], sb["rbc"][:, cs:cs + SBT], ALU.mult),
              (f"gt{j}", "rbc_v", "rbc"), (f"gt{j}",))
            C("scalar", lambda E, j=j, t=t: E.activation(sb["xn_hi"][:, t * SBT:(t + 1) * SBT], sb[f"gt{j}"][:], AF.Copy),
              (f"gt{j}",), (f"xnh{t}",))
            C("vector", lambda E, j=j, t=t: E.tensor_tensor(
                sb["xn_lo"][:, t * SBT:(t + 1) * SBT], sb[f"gt{j}"][:], sb["xn_hi"][:, t * SBT:(t + 1) * SBT], ALU.subtract),
                (f"gt{j}", f"xnh{t}"), (f"xnl{t}",))
            if DEBUG:
                dma("sync", dbg_xn[t * 128:(t + 1) * 128, k * LC + cs:k * LC + cs + SBT], sb[f"gt{j}"][:],
                    (f"gt{j}",), (f"dbgxn{k}_{sbi}_{t}",), fam="dbg")
        nm = [0]
        for t in range(NKT):
            for (wk, xk) in (("hi", "hi"), ("hi", "lo"), ("lo", "hi")):
                C("tensor", lambda E, t=t, wk=wk, xk=xk, s=(nm[0] == 0), st=(nm[0] == 95): E.matmul(
                    ps["rtl"][0:8, :], sb[f"wr_{wk}"][:, t * 8:(t + 1) * 8],
                    sb["xn_hi" if xk == "hi" else "xn_lo"][:, t * SBT:(t + 1) * SBT],
                    start=s, stop=st),
                    (f"wr_{wk}", f"xn{'h' if xk == 'hi' else 'l'}{t}"), ("rtl",))
                nm[0] += 1
        C("scalar", lambda E: E.activation(sb["rt8"][:], ps["rtl"][0:8, :], AF.Identity, bias=sb["br"][:, 0:1]),
          ("rtl", "br"), ("rt8",))
        dma("sync", wla[:, 0:4096], lah_hi, ("lahH", "wla0", "wla1"), ("wla0", "wla1"), fam="wla")
        dma("sync", wla[:, 4096:8192], lah_lo, ("lahL", "wla2", "wla3"), ("wla2", "wla3"), fam="wla")
        nm = [0]
        for t in range(NKT):
            for (wk, xk) in (("hi", "hi"), ("hi", "lo"), ("lo", "hi")):
                off = t * 128 if wk == "hi" else 4096 + t * 128
                rk = ("wla0", "wla1") if wk == "hi" else ("wla2", "wla3")
                C("tensor", lambda E, off=off, t=t, xk=xk, s=(nm[0] == 0), st=(nm[0] == 95): E.matmul(
                    ps["rtl"][:], wla[:, off:off + 128],
                    sb["xn_hi" if xk == "hi" else "xn_lo"][:, t * SBT:(t + 1) * SBT],
                    start=s, stop=st),
                    rk + (f"xn{'h' if xk == 'hi' else 'l'}{t}", "rt8"), ("rtl",))
                nm[0] += 1
        C("scalar", lambda E: E.activation(sb["lh"][:], ps["rtl"][:], AF.Copy),
          ("rtl",), ("lh",))
        lgt_v = sb["lgt"][:].rearrange("p (e g) -> p e g", g=2)
        for g in range(2):
            C("tensor", lambda E, g=g: E.transpose(ps["misc"][:, g * 8:(g + 1) * 8],
                                                   sb["rt8"][:, g * 128:(g + 1) * 128], sb["idn"][0:8, 0:8]),
              ("rt8", "idn"), ("misc",))
            C("scalar", lambda E, g=g, lv=lgt_v: E.activation(lv[:, :, g], ps["misc"][:, g * 8:(g + 1) * 8], AF.Copy),
              ("misc",), ("lgt",))
        sparsemax_ops(sbi)
        wf_keys = tuple(f"wf{ee}" for ee in range(8))
        wfin_v = sb["wfin"][:].rearrange("p (e g) -> p e g", g=2)
        for g in range(2):
            C("tensor", lambda E, g=g, wv=wfin_v: E.transpose(ps["misc"][0:8, 128 + g * 128:128 + (g + 1) * 128],
                                                              wv[:, :, g], sb["idn"][:]),
              wf_keys + ("idn",), ("misc",))
            C("scalar", lambda E, g=g: E.activation(sb["wt8"][:, g * 128:(g + 1) * 128],
                                                    ps["misc"][0:8, 128 + g * 128:128 + (g + 1) * 128], AF.Copy),
              ("misc",), ("wt8",))
        C("tensor", lambda E: E.matmul(ps["misc"][:, 0:SBT], sb["sel8"][:], sb["wt8"][:], start=True, stop=True),
          ("sel8", "wt8"), ("misc",))
        C("scalar", lambda E: E.activation(sb["wbc"][:], ps["misc"][:, 0:SBT], AF.Copy),
          ("misc",), ("wbc",))
        C("vector", lambda E: E.tensor_tensor(sb["lh"][:], sb["lh"][:], sb["wbc"][:], ALU.mult),
          ("lh", "wbc"), ("hw", "lh"))
        C("scalar", lambda E: E.activation(sb["hw_hi"][:], sb["lh"][:], AF.Copy),
          ("hw",), ("hw_hi",))
        C("vector", lambda E: E.tensor_tensor(sb["hw_lo"][:], sb["lh"][:], sb["hw_hi"][:], ALU.subtract),
          ("hw", "hw_hi"), ("hw_lo",))

    pend_stats = []
    pend_res = []

    def emit_res_stats():
        while pend_res:
            i, j = pend_res.pop(0)
            C("tensor", lambda E, j=j, s=(i == 0), t=(i == NDT - 1): E.matmul(
                ps["st"][0:1, :], sb["ones_c"][:, 0:1], sb[f"res{j}"][:], start=s, stop=t),
                ("ones_c", f"res{j}"), ("st", "st0"))
            C("tensor", lambda E, j=j, s=(i == 0), t=(i == NDT - 1): E.matmul(
                ps["st"][32:33, :], sb["ones_c"][:, 0:1], sb[f"yh{j}"][:], start=s, stop=t),
                ("ones_c", f"yh{j}"), ("st", "st1"))

    def emit_one_res():
        i, j = pend_res.pop(0)
        C("tensor", lambda E, j=j, s=(i == 0), t=(i == NDT - 1): E.matmul(
            ps["st"][0:1, :], sb["ones_c"][:, 0:1], sb[f"res{j}"][:], start=s, stop=t),
            ("ones_c", f"res{j}"), ("st", "st0"))
        C("tensor", lambda E, j=j, s=(i == 0), t=(i == NDT - 1): E.matmul(
            ps["st"][32:33, :], sb["ones_c"][:, 0:1], sb[f"yh{j}"][:], start=s, stop=t),
            ("ones_c", f"yh{j}"), ("st", "st1"))

    def emit_ff_stats(keep=2):
        while len(pend_stats) > keep:
            o, sbi, jo, sqn, cs = pend_stats.pop(0)
            C("tensor", lambda E, jo=jo, cs=cs, s=(o == 0), t=(o == NOT - 1): E.matmul(
                ps["xp2"][0:1, cs:cs + SBT], sb["ones_c"][:, 0:1], sb[f"fft{jo}"][:, 0:SBT], start=s, stop=t),
                ("ones_c", f"fft{jo}"), ("xp2", "xp2s"))
            C("tensor", lambda E, sqn=sqn, cs=cs, s=(o == 0), t=(o == NOT - 1): E.matmul(
                ps["xp2"][32:33, cs:cs + SBT], sb["ones_c"][:, 0:1], sb[sqn][:], start=s, stop=t),
                ("ones_c", sqn), ("xp2", "xp2q"))

    def ffn_otile(k, sbi, o):
        flush_stores(keep=2)
        emit_ff_stats(keep=1)
        while len(pend_res) > 2:
            emit_one_res()
        cs = sbi * SBT
        jo = o % 2
        fb = f"ff{jo}"
        dma("sync", wla[:, 0:2048], wh_hi[o, :, 0:2048], (f"wbm{o}_0H", "wla0"), ("wla0",), fam="wla")
        dma("sync", wla[:, 4096:6144], wh_lo[o, :, 0:2048], (f"wbm{o}_0L", "wla2"), ("wla2",), fam="wla")
        dma("sync", wla[:, 2048:4096], wh_hi[o, :, 2048:4096], (f"wbm{o}_1H", "wla1"), ("wla1",), fam="wla")
        dma("sync", wla[:, 6144:8192], wh_lo[o, :, 2048:4096], (f"wbm{o}_1L", "wla3"), ("wla3",), fam="wla")
        dma("sync", sb[f"lbt_hi{jo}"][:], lbh_hi[:, o * 128:(o + 1) * 128], ("lbhH",), (f"lbt_hi{jo}",), fam=f"lbt{jo}")
        dma("sync", sb[f"lbt_lo{jo}"][:], lbh_lo[:, o * 128:(o + 1) * 128], ("lbhL",), (f"lbt_lo{jo}",), fam=f"lbt{jo}")
        nm = [0]
        for t in range(NKT):
            for (wk, xk) in (("hi", "hi"), ("hi", "lo"), ("lo", "hi")):
                off = (0 if wk == "hi" else 4096) + t * 128
                rk = "wla0" if off < 2048 else ("wla1" if off < 4096 else ("wla2" if off < 6144 else "wla3"))
                C("tensor", lambda E, off=off, t=t, xk=xk, s=(nm[0] == 0), fb=fb: E.matmul(
                    ps[fb][:], wla[:, off:off + 128],
                    sb["xn_hi" if xk == "hi" else "xn_lo"][:, t * SBT:(t + 1) * SBT],
                    start=s, stop=False),
                    (rk, f"xn{'h' if xk == 'hi' else 'l'}{t}"), (fb,))
                nm[0] += 1
        for pi, (wk, xk) in enumerate((("hi", "hi"), ("hi", "lo"), ("lo", "hi"))):
            C("tensor", lambda E, wk=wk, xk=xk, fb=fb, st=(pi == 2), jo=jo: E.matmul(
                ps[fb][:], sb[f"lbt_{wk}{jo}"][:], sb[f"hw_{xk}"][:],
                start=False, stop=st),
                (f"lbt_{wk}{jo}", f"hw_{xk}"), (fb,))
        jo4 = o % 4
        C("scalar", lambda E, o=o, fb=fb, jo4=jo4: E.activation(
            sb[f"fft{jo4}"][:], ps[fb][:], AF.Identity, bias=sb["bb"][:, o:o + 1]),
            (fb, "bb"), (f"fft{jo4}",))
        dma("scalar", sb[f"rr{jo}"][:], ag1i[k][o * 128:(o + 1) * 128, cs:cs + SBT],
            (f"ag1w{k}_{o}",), (f"rr{jo}",), fam=f"rr{jo}")
        C("vector", lambda E, jo=jo, cs=cs: E.tensor_tensor(sb[f"rr{jo}"][:], sb[f"rr{jo}"][:], sb["mbc"][:, cs:cs + SBT], ALU.subtract),
          (f"rr{jo}", "mbc_v", "mbc"), (f"rr{jo}",))
        C("vector", lambda E, jo=jo, cs=cs: E.tensor_tensor(sb[f"rr{jo}"][:], sb[f"rr{jo}"][:], sb["rbc"][:, cs:cs + SBT], ALU.mult),
          (f"rr{jo}", "rbc_v", "rbc"), (f"rr{jo}",))
        C("vector", lambda E, jo4=jo4: E.tensor_tensor(sb[f"fft{jo4}"][:], sb[f"fft{jo4}"][:], sb[f"rr{jo}"][:], ALU.add),
          (f"fft{jo4}", f"rr{jo}"), (f"fft{jo4}",))
        pend_stores.append(lambda o=o, cs=cs, jo4=jo4, k=k, sbi=sbi: dma(
            "gpsimd", r2scr[o * 128:(o + 1) * 128, cs:cs + SBT], sb[f"fft{jo4}"][:],
            (f"fft{jo4}",), (f"r2_{o}_{sbi}",), fam=f"r2s{k % 2}"))
        sqn = "sq2" if jo == 0 else "sq2b"
        C("scalar", lambda E, jo4=jo4, sqn=sqn: E.activation(sb[sqn][:], sb[f"fft{jo4}"][:], AF.Square),
          (f"fft{jo4}",), (sqn,))
        pend_stats.append((o, sbi, jo4, sqn, cs))

    def ffn_fin(k):
        c0 = k * LC
        flush_stores()
        emit_ff_stats(keep=0)
        C("scalar", lambda E: E.activation(sb["stat_sb"][0:1, :], ps["xp2"][0:1, :], AF.Copy),
          ("xp2s", "xp2q", "xp2"), ("stat_sb",))
        C("scalar", lambda E: E.activation(sb["stat_sb"][32:33, :], ps["xp2"][32:33, :], AF.Copy),
          ("xp2s", "xp2q", "xp2"), ("stat_sb",))
        dma("scalar", ag2i[k][0:1, :], sb["stat_sb"][0:1, :], ("stat_sb",), (f"ag2w{k}",), fam="ag2w")
        dma("scalar", ag2i[k][1:2, :], sb["stat_sb"][32:33, :], ("stat_sb",), (f"ag2w{k}b",), fam="ag2w")
        P.op("gpsimd", lambda E: E.collective_compute(
            "AllReduce", ALU.add, replica_groups=RG, ins=[ag2i[k]], outs=[ag2o[k]]),
            reads=(f"ag2w{k}", f"ag2w{k}b"), writes=(f"gath2_{k}",), kind="cc", name=f"ar2_{k}", fam=f"car2{k % 2}")
        stv = sb["stv"]
        dma("scalar", stv[:, 0:LC], ag2o[k][0:1, :], (f"gath2_{k}",), ("stv0",), fam="stv")
        dma("scalar", stv[:, LC:2 * LC], ag2o[k][1:2, :], (f"gath2_{k}",), ("stv1",), fam="stv")
        ln_stats("mbc_v", "rbc_v")
        for o in range(NOT):
            jo = o % 2
            dma("scalar", sb[f"r2l{jo}"][:], r2scr[o * 128:(o + 1) * 128, :],
                (f"r2_{o}_0", f"r2_{o}_1"), (f"xlr{jo}",), fam=f"finl{jo}")
            C("vector", lambda E, jo=jo: E.tensor_tensor(sb[f"r2l{jo}"][:], sb[f"r2l{jo}"][:], sb["mbc"][:], ALU.subtract),
              (f"xlr{jo}", "mbc_v", "mbc"), (f"xlr{jo}",))
            C("vector", lambda E, jo=jo: E.tensor_tensor(sb[f"r2l{jo}"][:], sb[f"r2l{jo}"][:], sb["rbc"][:], ALU.mult),
              (f"xlr{jo}", "rbc_v", "rbc"), (f"xlr{jo}",))
            dma("scalar", out_c[o * 128:(o + 1) * 128, c0:c0 + LC], sb[f"r2l{jo}"][:],
                (f"xlr{jo}",), (f"outw{k}_{o}",), fam="out")

    SERIAL = SERIAL_EMIT
    for k in range(NCH):
        if DEBUG and k == 1:
            dma("sync", dbg_cr, sb["carry"][:],
                tuple(f"carry{cc2}" for cc2 in range(NDT * NST)), ("dbgcr",), fam="dbg")
        scan_side(k)
        for h in range(2):
            bcast_half(k, h)
            if k > 0 and not SERIAL:
                ffn_prep(k - 1, h)
            dtile(k, h, 0)
            for i in range(NDT):
                if i + 1 < NDT:
                    dtile(k, h, i + 1)
                for q in range(2):
                    quarter(k, h, i, q)
                dtile_end(k, h, i)
                if k > 0 and not SERIAL:
                    ffn_otile(k - 1, h, i)
                elif k == 0:
                    while len(pend_res) > 2:
                        emit_one_res()
                    emit_wcast(2)
            if h == 1 and k > 0 and not SERIAL:
                ffn_fin(k - 1)
        ag1(k)
    emit_wcast(64)
    if SERIAL:
        for k in range(NCH):
            for sbi in range(2):
                ffn_prep(k, sbi)
                for o in range(NOT):
                    ffn_otile(k, sbi, o)
            ffn_fin(k)
    else:
        for sbi in range(2):
            ffn_prep(NCH - 1, sbi)
            for o in range(NOT):
                ffn_otile(NCH - 1, sbi, o)
        ffn_fin(NCH - 1)

    P.op("vector", lambda E: E.engine_nop(),
         reads=tuple(f"outw{k}_{o}" for k in range(NCH) for o in range(NOT)),
         writes=(), kind="c", name="fin")

    P.finalize()
    global LAST_PROG
    LAST_PROG = P
    P.check_deadlock()
    with nc.Block() as block:
        @block.sync
        def _(E):
            P.emit_engine("sync", E, chans)

        @block.vector
        def _(E):
            P.emit_engine("vector", E, chans)

        @block.scalar
        def _(E):
            P.emit_engine("scalar", E, chans)

        @block.gpsimd
        def _(E):
            P.emit_engine("gpsimd", E, chans)

        @block.tensor
        def _(E):
            P.emit_engine("tensor", E, chans)
    stack.close()
    return nc


def make_inputs(ins):
    x = np.ascontiguousarray(ins["L0_attn_x"])
    A_log = np.asarray(ins["L0_attn_A_log"])
    Dv = np.asarray(ins["L0_attn_D"])
    xpw = np.asarray(ins["L0_attn_x_proj_w"])
    dtw = np.asarray(ins["L0_attn_dt_proj_w"])
    dtb = np.asarray(ins["L0_attn_dt_proj_b"])
    Wb = np.asarray(ins["L0_ffn_W_base"])
    bb = np.asarray(ins["L0_ffn_b_base"])
    Wr = np.asarray(ins["L0_ffn_W_router"])
    br = np.asarray(ins["L0_ffn_b_router"])
    lA = np.asarray(ins["L0_ffn_lora_A"])
    lB = np.asarray(ins["L0_ffn_lora_B"])

    idn = np.eye(128, dtype=np.float32)
    sel8 = np.zeros((8, 128), np.float32)
    for ee in range(8):
        sel8[ee, ee * 16:(ee + 1) * 16] = 1.0
    xpwT = np.ascontiguousarray(xpw.T)
    lap = np.ascontiguousarray(lA.transpose(1, 0, 2).reshape(D, 128))
    lbp = np.ascontiguousarray(lB.reshape(128, D))
    WrT = np.ascontiguousarray(Wr.T)

    maps = []
    for c in range(8):
        b, h = c // 2, c % 2
        own = slice(h * DH, (h + 1) * DH)
        xb = x[b].T
        xt_c = np.concatenate([xb[own], xb[(1 - h) * DH:(2 - h) * DH]], axis=0)
        perm = np.concatenate([np.arange(h * DH, (h + 1) * DH), np.arange((1 - h) * DH, (2 - h) * DH)])
        xpw_c = xpwT[perm]
        xpw_t = np.ascontiguousarray(xpw_c.reshape(32, 128, XPR).transpose(1, 0, 2).reshape(128, 32 * XPR))
        dtw_c = np.ascontiguousarray(dtw[own].T)
        alog_c = np.ascontiguousarray(A_log[own].reshape(NDT, 128, NST).transpose(1, 0, 2).reshape(128, NDT * NST))
        d_c = np.ascontiguousarray(Dv[own].reshape(NDT, 128).T)
        dtb_c = np.ascontiguousarray(dtb[own].reshape(NDT, 128).T)
        Wslice = Wb[own, :]
        wbs_c = np.ascontiguousarray(
            Wslice.reshape(NOT, 128, NKT, 128).transpose(0, 3, 2, 1).reshape(NOT, 128, NKT * 128))
        bb_c = np.ascontiguousarray(bb[own].reshape(NOT, 128).T)
        wrt_c = np.ascontiguousarray(WrT.reshape(NKT, 128, 8).transpose(1, 0, 2).reshape(128, NKT * 8))
        las_c = np.ascontiguousarray(lap.reshape(NKT, 128, 128).transpose(1, 0, 2).reshape(128, NKT * 128))
        lbs_c = np.ascontiguousarray(lbp[:, own])
        maps.append({
            "xt": xt_c, "xpw": xpw_t, "dtwi": dtw_c, "alog": alog_c,
            "dvec": d_c, "dtbi": dtb_c, "wbs": wbs_c, "bbt": bb_c,
            "wrti": wrt_c, "brt": br.reshape(8, 1).astype(np.float32),
            "las": las_c, "lbsi": lbs_c, "idni": idn, "sel8i": sel8,
        })
    return maps


_NC_CACHE = [None]


def kernel(**inputs):
    maps = make_inputs(inputs)
    if _NC_CACHE[0] is None:
        _NC_CACHE[0] = build_nc()
    nc = _NC_CACHE[0]
    res = run_bass_kernel_spmd(nc, maps, core_ids=list(range(8)))
    out = np.empty((B, L, D), np.float32)
    for c in range(8):
        b, h = c // 2, c % 2
        oc = res.results[c]["out_c"]
        out[b, :, h * DH:(h + 1) * DH] = oc.T
    return out




# version 08
# speedup vs baseline: 1.2190x; 1.0562x over previous
import numpy as np
import concourse.bass as bass
import concourse.mybir as mybir
from concourse.bass_utils import run_bass_kernel_spmd

F32 = mybir.dt.float32
BF16 = mybir.dt.bfloat16
AF = mybir.ActivationFunctionType
ALU = mybir.AluOpType

B, L, D = 4, 2048, 4096
DH, NST, DTR, XPR = 2048, 16, 128, 160
NCH, LC = 4, 512
SBT = 256
NDT = 16
OH, NOT, NKT = 2048, 16, 32
EPS = 1e-5
RG = [[0, 1], [2, 3], [4, 5], [6, 7]]
SORT8 = [(0, 1), (2, 3), (4, 5), (6, 7), (0, 2), (1, 3), (4, 6), (5, 7),
         (1, 2), (5, 6), (0, 4), (3, 7), (1, 5), (2, 6), (1, 4), (3, 6),
         (2, 4), (3, 5), (3, 4)]
DEBUG = False
SERIAL_EMIT = False
LAST_PROG = None


class Prog:
    ENGINES = ("sync", "vector", "scalar", "gpsimd", "tensor")

    def __init__(self):
        self.ops = []
        self.last_w = {}
        self.readers = {}
        self.fams = {}
        self.fam_R = {}

    def op(self, eng, fn, reads=(), writes=(), kind="c", name="", fam=None, R=0):
        import sys as _sys
        fr = _sys._getframe(1)
        if fr.f_code.co_name in ("dma", "C"):
            fr = fr.f_back
        lbl = f"{fr.f_code.co_name}:{fr.f_lineno}"
        deps = set()
        for kk in reads:
            if kk in self.last_w:
                deps.add(self.last_w[kk])
        for kk in writes:
            for r in self.readers.get(kk, ()):
                deps.add(r)
            if kk in self.last_w:
                deps.add(self.last_w[kk])
        i = len(self.ops)
        if kind in ("d", "cc"):
            assert fam is not None, name
            self.fams.setdefault(fam, []).append(i)
            self.fam_R[fam] = max(self.fam_R.get(fam, 0), R)
        self.ops.append(dict(eng=eng, fn=fn, deps=deps, kind=kind, name=name, fam=fam, lbl=lbl))
        for kk in reads:
            self.readers.setdefault(kk, []).append(i)
        for kk in writes:
            self.last_w[kk] = i
            self.readers[kk] = []
        return i

    def finalize(self):
        ops = self.ops
        self.need_inc = [False] * len(ops)
        for o in ops:
            for d in o["deps"]:
                p = ops[d]
                if p["kind"] == "c" and p["eng"] == o["eng"]:
                    continue
                if p["kind"] == "c":
                    self.need_inc[d] = True
        cnt = {}
        self.val = [0] * len(ops)
        for i, o in enumerate(ops):
            if o["kind"] == "c" and self.need_inc[i]:
                ch = (o["eng"], "c")
                cnt[ch] = cnt.get(ch, 0) + 1
                self.val[i] = cnt[ch]
        self.fam_idx = {}
        for fam, lst in self.fams.items():
            for q, gi in enumerate(lst):
                self.fam_idx[gi] = q

    def check_deadlock(self):
        import bisect
        ops = self.ops
        streams = {en: [i for i in range(len(ops)) if ops[i]["eng"] == en] for en in self.ENGINES}
        pos = {en: 0 for en in self.ENGINES}
        cnt = {}

        def waits_of(i):
            o = ops[i]
            ws = []
            for d in o["deps"]:
                p = ops[d]
                if p["kind"] == "c":
                    if p["eng"] == o["eng"]:
                        continue
                    ws.append((("c", p["eng"]), self.val[d]))
                else:
                    fam = p["fam"]
                    lst = self.fams[fam]
                    cx = max(bisect.bisect_left(lst, i), self.fam_idx[d] + 1)
                    tgt = min(cx + self.fam_R[fam], len(lst))
                    if ops[i]["kind"] == p["kind"] and ops[i].get("fam") == fam:
                        tgt = min(tgt, self.fam_idx[i])
                    tgt = max(tgt, self.fam_idx[d] + 1)
                    step = 16 if p["kind"] == "d" else 1
                    ws.append((("f", fam), tgt * step))
            return ws

        progressed = True
        while progressed:
            progressed = False
            for en in self.ENGINES:
                while pos[en] < len(streams[en]):
                    i = streams[en][pos[en]]
                    if any(cnt.get(ch, 0) < v for ch, v in waits_of(i)):
                        break
                    o = ops[i]
                    if o["kind"] == "c":
                        if self.need_inc[i]:
                            ch = ("c", en)
                            cnt[ch] = cnt.get(ch, 0) + 1
                    else:
                        step = 16 if o["kind"] == "d" else 1
                        ch = ("f", o["fam"])
                        cnt[ch] = cnt.get(ch, 0) + step
                    pos[en] += 1
                    progressed = True
        stuck = {en: pos[en] for en in self.ENGINES if pos[en] < len(streams[en])}
        if stuck:
            for en, p in stuck.items():
                i = streams[en][p]
                o = ops[i]
                print(f"DEADLOCK: {en} pos {p}/{len(streams[en])} op {i} kind={o['kind']} fam={o['fam']} name={o['name']}")
                for d in sorted(o["deps"]):
                    pp = ops[d]
                    print(f"   dep {d}: eng={pp['eng']} kind={pp['kind']} fam={pp['fam']} name={pp['name']} val={self.val[d]}")
                print("   waits:", waits_of(i), "cur:", {ch: cnt.get(ch, 0) for ch, v in waits_of(i)})
            raise RuntimeError("deadlock detected")
        print("deadlock check: OK")

    def emit_engine(self, ename, eng, chans):
        ops = self.ops
        import bisect
        waited = {}
        for i, o in enumerate(ops):
            if o["eng"] != ename:
                continue
            for d in sorted(o["deps"]):
                p = ops[d]
                if p["kind"] == "c":
                    if p["eng"] == ename:
                        continue
                    ch = ("c", p["eng"])
                    v = self.val[d]
                    assert v > 0, (i, o["name"], d, p["name"])
                    sem = chans[(p["eng"], "c")]
                else:
                    fam = p["fam"]
                    lst = self.fams[fam]
                    cx = bisect.bisect_left(lst, i)
                    cx = max(cx, self.fam_idx[d] + 1)
                    tgt = min(cx + self.fam_R[fam], len(lst))
                    if o["kind"] == p["kind"] and o.get("fam") == fam:
                        tgt = min(tgt, self.fam_idx[i])
                    tgt = max(tgt, self.fam_idx[d] + 1)
                    step = 16 if p["kind"] == "d" else 1
                    ch = ("f", fam)
                    v = tgt * step
                    sem = chans[("fam", fam)]
                if waited.get(ch, 0) >= v:
                    continue
                eng.wait_ge(sem, v)
                waited[ch] = v
            inst = o["fn"](eng)
            if o["kind"] == "c":
                if self.need_inc[i]:
                    inst.then_inc(chans[(ename, "c")], 1)
            else:
                step = 16 if o["kind"] == "d" else 1
                inst.then_inc(chans[("fam", o["fam"])], step)


def build_nc():
    nc = bass.Bass("TRN2", target_bir_lowering=False)
    P = Prog()

    xt = nc.dram_tensor("xt", [D, L], F32, kind="ExternalInput").ap()
    xpw = nc.dram_tensor("xpw", [128, 32 * XPR], F32, kind="ExternalInput").ap()
    dtwi = nc.dram_tensor("dtwi", [128, DH], F32, kind="ExternalInput").ap()
    alog = nc.dram_tensor("alog", [128, NDT * NST], F32, kind="ExternalInput").ap()
    dvec = nc.dram_tensor("dvec", [128, NDT], F32, kind="ExternalInput").ap()
    dtbi = nc.dram_tensor("dtbi", [128, NDT], F32, kind="ExternalInput").ap()
    wbs = nc.dram_tensor("wbs", [NOT, 128, NKT * 128], F32, kind="ExternalInput").ap()
    bbt = nc.dram_tensor("bbt", [128, NOT], F32, kind="ExternalInput").ap()
    wrti = nc.dram_tensor("wrti", [128, NKT * 8], F32, kind="ExternalInput").ap()
    brt = nc.dram_tensor("brt", [8, 1], F32, kind="ExternalInput").ap()
    las = nc.dram_tensor("las", [128, NKT * 128], F32, kind="ExternalInput").ap()
    lbsi = nc.dram_tensor("lbsi", [128, OH], F32, kind="ExternalInput").ap()
    idni = nc.dram_tensor("idni", [128, 128], F32, kind="ExternalInput").ap()
    sel8i = nc.dram_tensor("sel8i", [8, 128], F32, kind="ExternalInput").ap()
    out_c = nc.dram_tensor("out_c", [OH, L], F32, kind="ExternalOutput").ap()
    if DEBUG:
        dbg_res = nc.dram_tensor("dbg_res", [DH, L], F32, kind="ExternalOutput").ap()
        dbg_xn = nc.dram_tensor("dbg_xn", [D, L], F32, kind="ExternalOutput").ap()
        dbg_cr = nc.dram_tensor("dbg_cr", [128, NDT * NST], F32, kind="ExternalOutput").ap()
        dbg_st = nc.dram_tensor("dbg_st", [NCH, 2, LC], F32, kind="ExternalOutput").ap()
        dbg_bp = nc.dram_tensor("dbg_bp", [128, 8 * LC], F32, kind="ExternalOutput").ap()
        dbg_cp = nc.dram_tensor("dbg_cp", [128, 8 * LC], F32, kind="ExternalOutput").ap()
        dbg_dt = nc.dram_tensor("dbg_dt", [128, LC], F32, kind="ExternalOutput").ap()
        dbg_v = nc.dram_tensor("dbg_v", [128, LC], F32, kind="ExternalOutput").ap()

    xph_hi = nc.dram_tensor("xph_hi", [128, 32 * XPR], BF16).ap()
    xph_lo = nc.dram_tensor("xph_lo", [128, 32 * XPR], BF16).ap()
    lbh_hi = nc.dram_tensor("lbh_hi", [128, OH], BF16).ap()
    lbh_lo = nc.dram_tensor("lbh_lo", [128, OH], BF16).ap()
    lah_hi = nc.dram_tensor("lah_hi", [128, NKT * 128], BF16).ap()
    lah_lo = nc.dram_tensor("lah_lo", [128, NKT * 128], BF16).ap()
    wh_hi = nc.dram_tensor("wh_hi", [NOT, 128, NKT * 128], BF16).ap()
    wh_lo = nc.dram_tensor("wh_lo", [NOT, 128, NKT * 128], BF16).ap()
    yscr = nc.dram_tensor("yscr", [DH, LC], F32).ap()
    bchbm = nc.dram_tensor("bchbm", [32, LC], F32).ap()
    r2scr = nc.dram_tensor("r2scr", [OH, LC], F32).ap()
    ag1i = [nc.dram_tensor(f"ag1i{k}", [DH + 2, LC], F32).ap() for k in range(NCH)]
    ag1o = [nc.dram_tensor(f"ag1o{k}", [2 * (DH + 2), LC], F32).ap() for k in range(NCH)]
    ag2i = [nc.dram_tensor(f"ag2i{k}", [2, LC], F32).ap() for k in range(NCH)]
    ag2o = [nc.dram_tensor(f"ag2o{k}", [2, LC], F32).ap() for k in range(NCH)]

    from contextlib import ExitStack
    stack = ExitStack()
    e = stack.enter_context

    sb = {}
    def S(name, shape, dt=F32):
        sb[name] = e(nc.sbuf_tensor(name, shape, dt))
        return sb[name]

    S("ones_r", [1, 128]); S("ones_c", [128, 1])
    S("idn", [128, 128]); S("sel8", [8, 128])
    S("A", [128, NDT * NST]); S("onepD", [128, NDT]); S("dtb", [128, NDT])
    S("bb", [128, NOT]); S("br", [8, 1])
    S("dtw_hi", [128, DH], BF16); S("dtw_lo", [128, DH], BF16)
    S("wr_hi", [128, NKT * 8], BF16); S("wr_lo", [128, NKT * 8], BF16)
    S("carry", [128, NDT * NST])
    for j in range(2):
        S(f"xpw_hi{j}", [128, 2 * XPR], BF16); S(f"xpw_lo{j}", [128, 2 * XPR], BF16)
    S("Bp", [128, 8 * LC]); S("Cp", [128, 8 * LC])
    S("dAz0", [128, 4 * LC]); S("dAz1", [128, 4 * LC])
    S("dBx0", [128, 4 * LC]); S("dBx1", [128, 4 * LC])
    S("hp", [128, 4 * LC])
    for j in range(2):
        S(f"xhi{j}", [128, LC], BF16); S(f"xlo{j}", [128, LC], BF16)
        S(f"dtt{j}", [128, LC]); S(f"xot{j}", [128, LC]); S(f"vt{j}", [128, LC])
    for j in range(4):
        S(f"yh{j}", [128, LC]); S(f"res{j}", [128, LC])
    S("ysc", [128, LC]); S("zero4", [128, 4]); S("tc4", [128, 4])
    S("lbt_hi0", [128, 128], BF16); S("lbt_lo0", [128, 128], BF16)
    S("lbt_hi1", [128, 128], BF16); S("lbt_lo1", [128, 128], BF16)
    S("dtin_hi", [128, LC], BF16); S("dtin_lo", [128, LC], BF16)
    S("bc_c", [32, LC])
    S("stat_sb", [64, LC])
    S("xn_hi", [128, NKT * SBT], BF16); S("xn_lo", [128, NKT * SBT], BF16)
    S("wla", [128, 8192], BF16)
    for j in range(2):
        S(f"gt{j}", [128, SBT])
        S(f"rr{j}", [128, SBT])
        S(f"r2l{j}", [128, LC])
    for j in range(4):
        S(f"fft{j}", [128, SBT])
    S("sq2", [128, SBT])
    S("sq2b", [128, SBT])
    S("stv", [1, 3 * LC])
    S("mbc", [128, LC]); S("rbc", [128, LC])
    S("rt8", [8, SBT]); S("lgt", [128, 16])
    S("sortp", [128, 48])
    S("wfin", [128, 16]); S("wt8", [8, SBT]); S("wbc", [128, SBT])
    S("lh", [128, SBT])
    S("hw_hi", [128, SBT], BF16); S("hw_lo", [128, SBT], BF16)

    wla = sb["wla"]
    wla_f32 = wla[:].bitcast(F32)
    wf32 = wla_f32[:, 0:2048]
    cst_hi = wla[:, 4096:6144]
    cst_lo = wla[:, 6144:8192]
    WQ = ("wla0", "wla1", "wla2", "wla3")

    ps = {}
    def PSUM(name, shape):
        ps[name] = e(nc.psum_tensor(name, shape, F32))
    PSUM("xp1", [128, LC])
    PSUM("xp1b", [128, LC])
    PSUM("bst", [128, LC])
    PSUM("xp2", [64, LC])
    PSUM("rtl", [128, SBT])
    PSUM("misc", [128, LC])
    PSUM("ff0", [128, SBT]); PSUM("ff1", [128, SBT])

    chans = {}
    for en in Prog.ENGINES:
        chans[(en, "c")] = e(nc.semaphore(f"c_{en}"))
    FAMS = ["consts", "xphw", "lahw", "wbmw", "wla", "xlr0", "xlr1", "xot0", "xot1", "xpw",
            "bp", "cp", "gt0", "gt1", "rr0", "rr1", "yscl", "bchbm", "stv", "out",
            "yscr0", "yscr1", "r2s0", "r2s1", "agw0", "agw1", "arw", "ag2w",
            "cag0", "cag1", "car0", "car1", "car20", "car21", "dbg",
            "xpwh0", "xpwh1", "xpwl0", "xpwl1", "lbt0", "lbt1", "finl0", "finl1", "lbhw"]
    for f in FAMS:
        chans[("fam", f)] = e(nc.semaphore(f"f_{f}"))

    def dma(eng, out_ap, in_ap, reads, writes, name="", fam="consts", R=0):
        P.op(eng, lambda E, o=out_ap, i=in_ap: E.dma_start(out=o, in_=i),
             reads=reads, writes=writes, kind="d", name=name, fam=fam, R=R)

    C = P.op

    C("vector", lambda E: E.memset(sb["ones_r"][:], 1.0), (), ("ones_r",))
    C("vector", lambda E: E.memset(sb["ones_c"][:], 1.0), (), ("ones_c",))
    C("vector", lambda E: E.memset(sb["zero4"][:], 0.0), (), ("zero4",))
    dma("sync", sb["idn"][:], idni, (), ("idn",))
    dma("sync", sb["sel8"][:], sel8i, (), ("sel8",))
    dma("sync", sb["A"][:], alog, (), ("A",))
    C("scalar", lambda E: E.activation(sb["A"][:], sb["A"][:], AF.Exp), ("A",), ("A",))
    C("vector", lambda E: E.tensor_single_scalar(sb["A"][:], sb["A"][:], -1.0, ALU.mult), ("A",), ("A",))
    dma("sync", sb["onepD"][:], dvec, (), ("onepD",))
    C("vector", lambda E: E.tensor_single_scalar(sb["onepD"][:], sb["onepD"][:], 1.0, ALU.add), ("onepD",), ("onepD",))
    dma("sync", sb["dtb"][:], dtbi, (), ("dtb",))
    dma("sync", sb["bb"][:], bbt, (), ("bb",))
    dma("sync", sb["br"][:], brt, (), ("br",))

    def stage_cast(src_ap, cols, hi_dst, hi_key, lo_dst, lo_key):
        dma("sync", wf32[:, 0:cols], src_ap, ("wla0", "wla1"), ("wla0", "wla1"), fam="wla")
        C("scalar", lambda E, c=cols, h=hi_dst: E.activation(h, wf32[:, 0:c], AF.Copy),
          ("wla0", "wla1"), (hi_key,))
        C("vector", lambda E, c=cols, h=hi_dst, l=lo_dst: E.tensor_tensor(l, wf32[:, 0:c], h, ALU.subtract),
          ("wla0", "wla1", hi_key), (lo_key,))

    stage_cast(dtwi, DH, sb["dtw_hi"][:], "dtw_hi", sb["dtw_lo"][:], "dtw_lo")
    stage_cast(wrti, NKT * 8, sb["wr_hi"][:], "wr_hi", sb["wr_lo"][:], "wr_lo")

    def cast_to_hbm(nm, src_ap, dst_hi, dst_lo, total):
        off = 0
        while off < total:
            w = min(2048, total - off)
            dma("sync", wf32[:, 0:w], src_ap[:, off:off + w], ("wla0", "wla1"), ("wla0", "wla1"), fam="wla")
            C("scalar", lambda E, w=w: E.activation(cst_hi[:, 0:w], wf32[:, 0:w], AF.Copy),
              ("wla0", "wla1"), ("wla2",))
            C("vector", lambda E, w=w: E.tensor_tensor(cst_lo[:, 0:w], wf32[:, 0:w], cst_hi[:, 0:w], ALU.subtract),
              ("wla0", "wla1", "wla2"), ("wla3",))
            dma("sync", dst_hi[:, off:off + w], cst_hi[:, 0:w], ("wla2",), (f"{nm}H",), fam=nm + "w")
            dma("sync", dst_lo[:, off:off + w], cst_lo[:, 0:w], ("wla3",), (f"{nm}L",), fam=nm + "w")
            off += w

    cast_to_hbm("xph", xpw, xph_hi, xph_lo, 32 * XPR)
    cast_to_hbm("lah", las, lah_hi, lah_lo, NKT * 128)
    cast_to_hbm("lbh", lbsi, lbh_hi, lbh_lo, OH)

    wblocks = [(o, hf) for o in range(NOT) for hf in range(2)]
    widx = [0]

    def emit_wcast(nblk):
        for _ in range(nblk):
            if widx[0] >= len(wblocks):
                return
            o, hf = wblocks[widx[0]]
            widx[0] += 1
            sl = slice(hf * 2048, (hf + 1) * 2048)
            dma("sync", wf32[:], wbs[o, :, sl], ("wla0", "wla1"), ("wla0", "wla1"), fam="wla")
            C("scalar", lambda E: E.activation(cst_hi[:], wf32[:], AF.Copy),
              ("wla0", "wla1"), ("wla2",))
            C("vector", lambda E: E.tensor_tensor(cst_lo[:], wf32[:], cst_hi[:], ALU.subtract),
              ("wla0", "wla1", "wla2"), ("wla3",))
            dma("sync", wh_hi[o, :, sl], cst_hi[:], ("wla2",), (f"wbm{o}_{hf}H",), fam="wbmw")
            dma("sync", wh_lo[o, :, sl], cst_lo[:], ("wla3",), (f"wbm{o}_{hf}L",), fam="wbmw")

    def scan_side(k):
        c0 = k * LC
        n_mm = [0]
        for pc in range(16):
            pb = pc % 2
            poff = pc * 2 * XPR
            dma("sync", sb[f"xpw_hi{pb}"][:], xph_hi[:, poff:poff + 2 * XPR], (f"xpw_hi{pb}",), (f"xpw_hi{pb}",), fam=f"xpwh{pb}")
            dma("sync", sb[f"xpw_lo{pb}"][:], xph_lo[:, poff:poff + 2 * XPR], (f"xpw_lo{pb}",), (f"xpw_lo{pb}",), fam=f"xpwl{pb}")
            for tl in range(2):
                t = pc * 2 + tl
                j = t % 2
                dma("sync" if j == 0 else "scalar", sb[f"r2l{j}"][:], xt[t * 128:(t + 1) * 128, c0:c0 + LC], (), (f"xlr{j}",), fam=f"xlr{j}")
                C("scalar", lambda E, j=j: E.activation(sb[f"xhi{j}"][:], sb[f"r2l{j}"][:], AF.Copy),
                  (f"xlr{j}",), (f"xhi{j}",))
                C("vector", lambda E, j=j: E.tensor_tensor(sb[f"xlo{j}"][:], sb[f"r2l{j}"][:], sb[f"xhi{j}"][:], ALU.subtract),
                  (f"xlr{j}", f"xhi{j}"), (f"xlo{j}",))
                for (wk, xk) in (("hi", "hi"), ("hi", "lo"), ("lo", "hi")):
                    s0 = (n_mm[0] == 0)
                    s1 = (n_mm[0] == 95)
                    C("tensor", lambda E, wk=wk, xk=xk, j=j, tl=tl, pb=pb, s0=s0, s1=s1: E.matmul(
                        ps["xp1"][:], sb[f"xpw_{wk}{pb}"][:, tl * XPR:tl * XPR + 128], sb[f"x{xk}{j}"][:],
                        start=s0, stop=s1),
                        (f"xpw_{wk}{pb}", f"x{xk}{j}"), ("xp1",))
                    C("tensor", lambda E, wk=wk, xk=xk, j=j, tl=tl, pb=pb, s0=s0, s1=s1: E.matmul(
                        ps["bst"][0:32, :], sb[f"xpw_{wk}{pb}"][:, tl * XPR + 128:(tl + 1) * XPR], sb[f"x{xk}{j}"][:],
                        start=s0, stop=s1),
                        (f"xpw_{wk}{pb}", f"x{xk}{j}"), ("bcb",))
                    n_mm[0] += 1
        C("scalar", lambda E: E.activation(sb["dtin_hi"][:], ps["xp1"][:], AF.Copy),
          ("xp1",), ("dtin_hi",))
        C("vector", lambda E: E.tensor_tensor(sb["dtin_lo"][:], ps["xp1"][:], sb["dtin_hi"][:], ALU.subtract),
          ("xp1", "dtin_hi"), ("dtin_lo",))
        C("scalar", lambda E: E.activation(sb["bc_c"][:], ps["bst"][0:32, :], AF.Copy),
          ("bcb",), ("bc_c",))
        dma("scalar", bchbm, sb["bc_c"][:], ("bc_c",), ("bchbm",), fam="bchbm")

    def bcast_half(k, h):
        srcB = bchbm[h * 8:(h + 1) * 8, :].flatten().unsqueeze(0).broadcast_to([128, 8 * LC])
        srcC = bchbm[16 + h * 8:16 + (h + 1) * 8, :].flatten().unsqueeze(0).broadcast_to([128, 8 * LC])
        dma("gpsimd", sb["Bp"][:], srcB, ("bchbm",), tuple(f"Bp{z}" for z in range(8)), fam="bp")
        dma("gpsimd", sb["Cp"][:], srcC, ("bchbm",), tuple(f"Cp{z}" for z in range(8)), fam="cp")
        if DEBUG and k == 1 and h == 0:
            dma("sync", dbg_bp, sb["Bp"][:], tuple(f"Bp{z}" for z in range(8)), ("dbgbp",), fam="dbg")
            dma("sync", dbg_cp, sb["Cp"][:], tuple(f"Cp{z}" for z in range(8)), ("dbgcp",), fam="dbg")

    def dtile(k, h, i):
        j = i % 2
        pn = "xp1" if i % 2 == 0 else "xp1b"
        for pi, (wk, xk) in enumerate((("hi", "hi"), ("hi", "lo"), ("lo", "hi"))):
            C("tensor", lambda E, wk=wk, xk=xk, i=i, pn=pn, s=(pi == 0), t=(pi == 2): E.matmul(
                ps[pn][:], sb[f"dtw_{wk}"][:, i * 128:(i + 1) * 128], sb[f"dtin_{xk}"][:],
                start=s, stop=t),
                (f"dtw_{wk}", f"dtin_{xk}"), (pn,))
        C("scalar", lambda E, i=i, j=j, pn=pn: E.activation(sb[f"dtt{j}"][:], ps[pn][:], AF.Exp, bias=sb["dtb"][:, i:i + 1]),
          (pn, "dtb"), (f"dtt{j}",))
        C("scalar", lambda E, j=j: E.activation(sb[f"dtt{j}"][:], sb[f"dtt{j}"][:], AF.Ln, bias=1.0),
          (f"dtt{j}",), (f"dtt{j}",))
        dma("scalar", sb[f"xot{j}"][:], xt[i * 128:(i + 1) * 128, k * LC:(k + 1) * LC], (), (f"xot{j}",), fam=f"xot{j}")
        C("gpsimd", lambda E, j=j: E.tensor_tensor(sb[f"vt{j}"][:], sb[f"dtt{j}"][:], sb[f"xot{j}"][:], ALU.mult),
          (f"dtt{j}", f"xot{j}"), (f"vt{j}",))
        if DEBUG and k == 1 and h == 0 and i == 0:
            dma("sync", dbg_dt, sb[f"dtt{j}"][:], (f"dtt{j}",), ("dbgdt",), fam="dbg")
            dma("sync", dbg_v, sb[f"vt{j}"][:], (f"vt{j}",), ("dbgv",), fam="dbg")

    ucnt = [0]

    def quarter(k, h, i, q):
        j = i % 2
        ub = ucnt[0] % 2
        ucnt[0] += 1
        dA = f"dAz{ub}"
        dB = f"dBx{ub}"
        for jj in range(4):
            n = h * 8 + q * 4 + jj
            col = i * NST + n
            C("scalar", lambda E, jj=jj, col=col, j=j, dA=dA: E.activation(
                sb[dA][:, jj * LC:(jj + 1) * LC], sb[f"dtt{j}"][:], AF.Exp,
                scale=sb["A"][:, col:col + 1]),
                (f"dtt{j}", "A"), (dA,))
        vb = sb[f"vt{j}"][:].unsqueeze(1).broadcast_to([128, 4, LC])
        C("gpsimd", lambda E, q=q, dB=dB, vb=vb: E.tensor_tensor(
            sb[dB][:].rearrange("p (n l) -> p n l", n=4),
            sb["Bp"][:, q * 4 * LC:(q + 1) * 4 * LC].rearrange("p (n l) -> p n l", n=4),
            vb, ALU.mult),
            tuple([f"vt{j}"] + [f"Bp{q * 4 + z}" for z in range(4)]), (dB,))
        colb0 = i * NST + h * 8 + q * 4
        if k > 0:
            dA_f = sb[dA][:].rearrange("p (n l) -> p n l", n=4)[:, :, 0]
            dB_f = sb[dB][:].rearrange("p (n l) -> p n l", n=4)[:, :, 0]
            C("gpsimd", lambda E, dA_f=dA_f, colb0=colb0: E.tensor_tensor(
                sb["tc4"][:], dA_f, sb["carry"][:, colb0:colb0 + 4], ALU.mult),
                tuple([dA] + [f"carry{colb0 + z}" for z in range(4)]), ("tc4",))
            C("gpsimd", lambda E, dB_f=dB_f: E.tensor_tensor(
                dB_f, dB_f, sb["tc4"][:], ALU.add),
                (dB, "tc4"), (dB,))
        for jj in range(4):
            rd = [dA, dB]
            C("vector", lambda E, jj=jj, dA=dA, dB=dB: E.tensor_tensor_scan(
                sb["hp"][:, jj * LC:(jj + 1) * LC],
                sb[dA][:, jj * LC:(jj + 1) * LC],
                sb[dB][:, jj * LC:(jj + 1) * LC],
                0.0, ALU.mult, ALU.add),
                tuple(rd), ("hp",))
        colb = i * NST + h * 8 + q * 4
        C("vector", lambda E, colb=colb: E.tensor_tensor(
            sb["carry"][:, colb:colb + 3],
            sb["hp"][:, 0:3 * LC].rearrange("p (n l) -> p n l", n=3)[:, :, LC - 1],
            sb["zero4"][:, 0:3], ALU.add),
            ("hp", "zero4"), tuple(f"carry{colb + z}" for z in range(3)))
        C("vector", lambda E, colb=colb: E.tensor_tensor(
            sb["carry"][:, colb + 3:colb + 4],
            sb["hp"][:, 4 * LC - 1:4 * LC],
            sb["zero4"][:, 0:1], ALU.add),
            ("hp", "zero4"), (f"carry{colb + 3}",))
        C("vector", lambda E, q=q: E.tensor_tensor(
            sb["hp"][:], sb["hp"][:], sb["Cp"][:, q * 4 * LC:(q + 1) * 4 * LC], ALU.mult),
            tuple(["hp"] + [f"Cp{q * 4 + z}" for z in range(4)]), ("hp",))
        C("vector", lambda E: E.tensor_tensor(
            sb["hp"][:, 0:2 * LC], sb["hp"][:, 0:2 * LC], sb["hp"][:, 2 * LC:4 * LC], ALU.add),
            ("hp",), ("hp",))
        j4 = i % 4
        if q == 0:
            C("vector", lambda E, j4=j4: E.tensor_tensor(
                sb[f"yh{j4}"][:], sb["hp"][:, 0:LC], sb["hp"][:, LC:2 * LC], ALU.add),
                ("hp",), (f"yh{j4}",))
        else:
            C("vector", lambda E: E.tensor_tensor(
                sb["hp"][:, 2 * LC:3 * LC], sb["hp"][:, 0:LC], sb["hp"][:, LC:2 * LC], ALU.add),
                ("hp",), ("hp",))
            C("vector", lambda E, j4=j4: E.tensor_tensor(
                sb[f"yh{j4}"][:], sb[f"yh{j4}"][:], sb["hp"][:, 2 * LC:3 * LC], ALU.add),
                ("hp", f"yh{j4}"), (f"yh{j4}",))

    pend_stores = []

    def flush_stores(keep=0):
        while len(pend_stores) > keep:
            pend_stores.pop(0)()

    def dtile_end(k, h, i):
        flush_stores(keep=2)
        j = i % 2
        j4 = i % 4
        if h == 0:
            pend_stores.append(lambda i=i, j4=j4, k=k: dma(
                "gpsimd", yscr[i * 128:(i + 1) * 128, :], sb[f"yh{j4}"][:],
                (f"yh{j4}",), (f"yscr{i}",), fam=f"yscr{k % 2}"))
        else:
            dma("scalar", sb["ysc"][:], yscr[i * 128:(i + 1) * 128, :],
                (f"yscr{i}",), ("ysc",), fam="yscl")
            C("vector", lambda E, j4=j4: E.tensor_tensor(sb[f"yh{j4}"][:], sb[f"yh{j4}"][:], sb["ysc"][:], ALU.add),
              (f"yh{j4}", "ysc"), (f"yh{j4}",))
            C("vector", lambda E, j=j, j4=j4, i=i: E.scalar_tensor_tensor(
                sb[f"res{j4}"][:], sb[f"xot{j}"][:], sb["onepD"][:, i:i + 1], sb[f"yh{j4}"][:],
                ALU.mult, ALU.add),
                (f"xot{j}", "onepD", f"yh{j4}"), (f"res{j4}",))
            pend_stores.append(lambda i=i, j4=j4, k=k: dma(
                "gpsimd", ag1i[k][i * 128:(i + 1) * 128, :], sb[f"res{j4}"][:],
                (f"res{j4}",), (f"ag1w{k}_{i}",), fam=f"agw{k % 2}"))
            pend_stores.append(lambda j4=j4: C(
                "scalar", lambda E, j4=j4: E.activation(sb[f"yh{j4}"][:], sb[f"res{j4}"][:], AF.Square),
                (f"res{j4}",), (f"yh{j4}",)))
            pend_res.append((i, j4))
            if DEBUG:
                pend_stores.append(lambda i=i, j4=j4, k=k: dma(
                    "gpsimd", dbg_res[i * 128:(i + 1) * 128, k * LC:(k + 1) * LC], sb[f"res{j4}"][:],
                    (f"res{j4}",), (f"dbgres{k}_{i}",), fam="dbg"))

    def ag1(k):
        flush_stores()
        while pend_res:
            emit_one_res()
        C("scalar", lambda E: E.activation(sb["stat_sb"][0:1, :], ps["bst"][32:33, :], AF.Copy),
          ("st0", "st1", "st"), ("stat1a", "stat_sb"))
        C("scalar", lambda E: E.activation(sb["stat_sb"][32:33, :], ps["bst"][64:65, :], AF.Copy),
          ("st0", "st1", "st"), ("stat1b", "stat_sb"))
        dma("scalar", ag1i[k][DH:DH + 1, :], sb["stat_sb"][0:1, :], ("stat1a",), (f"ag1w{k}_s0",), fam=f"agw{k % 2}")
        dma("scalar", ag1i[k][DH + 1:DH + 2, :], sb["stat_sb"][32:33, :], ("stat1b",), (f"ag1w{k}_s1",), fam=f"agw{k % 2}")
        reads = tuple(f"ag1w{k}_{i}" for i in range(NDT)) + (f"ag1w{k}_s0", f"ag1w{k}_s1")
        P.op("gpsimd", lambda E: E.collective_compute(
            "AllGather", ALU.bypass, replica_groups=RG, ins=[ag1i[k]], outs=[ag1o[k]]),
            reads=reads, writes=(f"gath{k}",), kind="cc", name=f"ag1_{k}", fam=f"cag{k % 2}")

    def ln_stats(m_name, r_name, width=LC):
        stv = sb["stv"]
        s0 = stv[:, 0:width]; s1 = stv[:, LC:LC + width]; s2 = stv[:, 2 * LC:2 * LC + width]
        C("vector", lambda E: E.tensor_single_scalar(s0, s0, 1.0 / D, ALU.mult),
          ("stv0",), ("stv0",))
        C("tensor", lambda E, w=width: E.matmul(ps["misc"][:, 0:w], sb["ones_r"][0:1, :], s0[0:1, :], start=True, stop=True),
          ("stv0", "ones_r"), ("misc",))
        C("scalar", lambda E, w=width: E.activation(sb["mbc"][:, 0:w], ps["misc"][:, 0:w], AF.Copy),
          ("misc",), (m_name, "mbc"))
        C("vector", lambda E: E.tensor_single_scalar(s1, s1, 1.0 / D, ALU.mult),
          ("stv1",), ("stv1",))
        C("vector", lambda E: E.tensor_tensor(s2, s0, s0, ALU.mult),
          ("stv0",), ("stv2",))
        C("vector", lambda E: E.tensor_tensor(s1, s1, s2, ALU.subtract),
          ("stv1", "stv2"), ("stv1",))
        C("vector", lambda E: E.tensor_single_scalar(s1, s1, EPS, ALU.add),
          ("stv1",), ("stv1",))
        C("scalar", lambda E: E.activation(s2, s1, AF.Ln),
          ("stv1",), ("stv2",))
        C("scalar", lambda E: E.activation(s2, s2, AF.Exp, scale=-0.5),
          ("stv2",), ("stv2",))
        C("vector", lambda E: E.tensor_tensor(s0, s2, s2, ALU.mult),
          ("stv2",), ("stv0",))
        C("vector", lambda E: E.tensor_tensor(s0, s0, s1, ALU.mult),
          ("stv0", "stv1"), ("stv0",))
        C("vector", lambda E: E.tensor_scalar(s0, s0, -0.5, 1.5, ALU.mult, ALU.add),
          ("stv0",), ("stv0",))
        C("vector", lambda E: E.tensor_tensor(s2, s2, s0, ALU.mult),
          ("stv2", "stv0"), ("stv2",))
        C("tensor", lambda E, w=width: E.matmul(ps["misc"][:, 0:w], sb["ones_r"][0:1, :], s2[0:1, :], start=True, stop=True),
          ("stv2", "ones_r"), ("misc",))
        C("scalar", lambda E, w=width: E.activation(sb["rbc"][:, 0:w], ps["misc"][:, 0:w], AF.Copy),
          ("misc",), (r_name, "rbc"))

    def sparsemax_ops(sbi):
        pool = sb["sortp"]
        W2 = 2
        def sl(s):
            return pool[:, s * W2:(s + 1) * W2]
        slots = {}
        free = list(range(24))
        for ee in range(8):
            s0 = free.pop(0)
            slots[ee] = s0
            C("vector", lambda E, ee=ee, s0=s0: E.tensor_copy(sl(s0), sb["lgt"][:, ee * W2:(ee + 1) * W2]),
              ("lgt",), (f"sp{s0}",))
        for (i, jx) in SORT8:
            si, sj = slots[i], slots[jx]
            a = free.pop(0); b2 = free.pop(0)
            C("vector", lambda E, si=si, sj=sj, a=a: E.tensor_tensor(sl(a), sl(si), sl(sj), ALU.max),
              (f"sp{si}", f"sp{sj}"), (f"sp{a}",))
            C("vector", lambda E, si=si, sj=sj, b2=b2: E.tensor_tensor(sl(b2), sl(si), sl(sj), ALU.min),
              (f"sp{si}", f"sp{sj}"), (f"sp{b2}",))
            free.append(si); free.append(sj)
            slots[i], slots[jx] = a, b2
        cum = {}
        c0s = free.pop(0)
        cum[0] = c0s
        C("vector", lambda E, s=slots[0], c=c0s: E.tensor_copy(sl(c), sl(s)),
          (f"sp{slots[0]}",), (f"sp{c0s}",))
        for ee in range(1, 8):
            cs = free.pop(0)
            C("vector", lambda E, p=cum[ee - 1], z=slots[ee], cs=cs: E.tensor_tensor(sl(cs), sl(p), sl(z), ALU.add),
              (f"sp{cum[ee - 1]}", f"sp{slots[ee]}"), (f"sp{cs}",))
            cum[ee] = cs
        tsl = free.pop(0); msl = free.pop(0); ksl = free.pop(0)
        for ee in range(8):
            C("vector", lambda E, z=slots[ee], c=cum[ee], t=tsl, ee=ee: E.scalar_tensor_tensor(
                sl(t), sl(z), float(ee + 1), sl(c), ALU.mult, ALU.subtract),
                (f"sp{slots[ee]}", f"sp{cum[ee]}"), (f"sp{tsl}",))
            C("vector", lambda E, t=tsl, m=msl: E.tensor_single_scalar(sl(m), sl(t), -1.0, ALU.is_gt),
              (f"sp{tsl}",), (f"sp{msl}",))
            if ee == 0:
                C("vector", lambda E, m=msl, ks=ksl: E.tensor_copy(sl(ks), sl(m)),
                  (f"sp{msl}",), (f"sp{ksl}",))
            else:
                C("vector", lambda E, m=msl, ks=ksl: E.tensor_tensor(sl(ks), sl(ks), sl(m), ALU.add),
                  (f"sp{msl}", f"sp{ksl}"), (f"sp{ksl}",))
        acc, eq = tsl, msl
        t2 = free[0]
        for ee in range(8):
            C("vector", lambda E, ks=ksl, eqs=eq, ee=ee: E.tensor_single_scalar(sl(eqs), sl(ks), float(ee + 1), ALU.is_equal),
              (f"sp{ksl}",), (f"sp{eq}",))
            if ee == 0:
                C("vector", lambda E, c=cum[ee], eqs=eq, a=acc: E.tensor_tensor(sl(a), sl(c), sl(eqs), ALU.mult),
                  (f"sp{cum[ee]}", f"sp{eq}"), (f"sp{acc}",))
            else:
                C("vector", lambda E, c=cum[ee], eqs=eq, t2=t2: E.tensor_tensor(sl(t2), sl(c), sl(eqs), ALU.mult),
                  (f"sp{cum[ee]}", f"sp{eq}"), (f"sp{t2}",))
                C("vector", lambda E, a=acc, t2=t2: E.tensor_tensor(sl(a), sl(a), sl(t2), ALU.add),
                  (f"sp{acc}", f"sp{t2}"), (f"sp{acc}",))
        C("vector", lambda E, ks=ksl, eqs=eq: E.reciprocal(sl(eqs), sl(ks)),
          (f"sp{ksl}",), (f"sp{eq}",))
        C("vector", lambda E, a=acc: E.tensor_single_scalar(sl(a), sl(a), -1.0, ALU.add),
          (f"sp{acc}",), (f"sp{acc}",))
        C("vector", lambda E, a=acc, eqs=eq: E.tensor_tensor(sl(a), sl(a), sl(eqs), ALU.mult),
          (f"sp{acc}", f"sp{eq}"), (f"sp{acc}",))
        for ee in range(8):
            C("vector", lambda E, ee=ee, a=acc: E.tensor_tensor(
                sb["wfin"][:, ee * W2:(ee + 1) * W2], sb["lgt"][:, ee * W2:(ee + 1) * W2], sl(a), ALU.subtract),
                (f"sp{acc}", "lgt"), (f"wf{ee}",))
            C("vector", lambda E, ee=ee: E.tensor_single_scalar(
                sb["wfin"][:, ee * W2:(ee + 1) * W2], sb["wfin"][:, ee * W2:(ee + 1) * W2], 0.0, ALU.max),
                (f"wf{ee}",), (f"wf{ee}", "wfin"))

    def ffn_prep(k, sbi):
        cs = sbi * SBT
        stv = sb["stv"]
        if sbi == 0:
            dma("scalar", stv[0:1, 0:LC], ag1o[k][DH:DH + 1, :], (f"gath{k}",), ("stv0",), fam="stv")
            dma("scalar", stv[0:1, LC:2 * LC], ag1o[k][DH + 1:DH + 2, :], (f"gath{k}",), ("stv1",), fam="stv")
            dma("scalar", stv[0:1, 2 * LC:3 * LC], ag1o[k][2 * DH + 2:2 * DH + 3, :], (f"gath{k}",), ("stv2",), fam="stv")
            C("vector", lambda E: E.tensor_tensor(stv[0:1, 0:LC], stv[0:1, 0:LC], stv[0:1, 2 * LC:3 * LC], ALU.add),
              ("stv0", "stv2"), ("stv0",))
            dma("scalar", stv[0:1, 2 * LC:3 * LC], ag1o[k][2 * DH + 3:2 * DH + 4, :], (f"gath{k}",), ("stv2",), fam="stv")
            C("vector", lambda E: E.tensor_tensor(stv[0:1, LC:2 * LC], stv[0:1, LC:2 * LC], stv[0:1, 2 * LC:3 * LC], ALU.add),
              ("stv1", "stv2"), ("stv1",))
            if DEBUG:
                dma("scalar", dbg_st[k, 0:1, :], stv[0:1, 0:LC], ("stv0",), (f"dbgst{k}a",), fam="dbg")
                dma("scalar", dbg_st[k, 1:2, :], stv[0:1, LC:2 * LC], ("stv1",), (f"dbgst{k}b",), fam="dbg")
            ln_stats("mbc_v", "rbc_v", width=LC)
        for t in range(NKT):
            j = t % 2
            row = t * 128 + (0 if t < NDT else 2)
            dma("scalar", sb[f"gt{j}"][:], ag1o[k][row:row + 128, cs:cs + SBT], (f"gath{k}",), (f"gt{j}",), fam=f"gt{j}")
            C("vector", lambda E, j=j, cs=cs: E.tensor_tensor(sb[f"gt{j}"][:], sb[f"gt{j}"][:], sb["mbc"][:, cs:cs + SBT], ALU.subtract),
              (f"gt{j}", "mbc_v", "mbc"), (f"gt{j}",))
            C("vector", lambda E, j=j, cs=cs: E.tensor_tensor(sb[f"gt{j}"][:], sb[f"gt{j}"][:], sb["rbc"][:, cs:cs + SBT], ALU.mult),
              (f"gt{j}", "rbc_v", "rbc"), (f"gt{j}",))
            C("scalar", lambda E, j=j, t=t: E.activation(sb["xn_hi"][:, t * SBT:(t + 1) * SBT], sb[f"gt{j}"][:], AF.Copy),
              (f"gt{j}",), (f"xnh{t}",))
            C("vector", lambda E, j=j, t=t: E.tensor_tensor(
                sb["xn_lo"][:, t * SBT:(t + 1) * SBT], sb[f"gt{j}"][:], sb["xn_hi"][:, t * SBT:(t + 1) * SBT], ALU.subtract),
                (f"gt{j}", f"xnh{t}"), (f"xnl{t}",))
            if DEBUG:
                dma("sync", dbg_xn[t * 128:(t + 1) * 128, k * LC + cs:k * LC + cs + SBT], sb[f"gt{j}"][:],
                    (f"gt{j}",), (f"dbgxn{k}_{sbi}_{t}",), fam="dbg")
        nm = [0]
        for t in range(NKT):
            for (wk, xk) in (("hi", "hi"), ("hi", "lo"), ("lo", "hi")):
                C("tensor", lambda E, t=t, wk=wk, xk=xk, s=(nm[0] == 0), st=(nm[0] == 95): E.matmul(
                    ps["rtl"][0:8, :], sb[f"wr_{wk}"][:, t * 8:(t + 1) * 8],
                    sb["xn_hi" if xk == "hi" else "xn_lo"][:, t * SBT:(t + 1) * SBT],
                    start=s, stop=st),
                    (f"wr_{wk}", f"xn{'h' if xk == 'hi' else 'l'}{t}"), ("rtl",))
                nm[0] += 1
        C("scalar", lambda E: E.activation(sb["rt8"][:], ps["rtl"][0:8, :], AF.Identity, bias=sb["br"][:, 0:1]),
          ("rtl", "br"), ("rt8",))
        dma("sync", wla[:, 0:4096], lah_hi, ("lahH", "wla0", "wla1"), ("wla0", "wla1"), fam="wla")
        dma("sync", wla[:, 4096:8192], lah_lo, ("lahL", "wla2", "wla3"), ("wla2", "wla3"), fam="wla")
        nm = [0]
        for t in range(NKT):
            for (wk, xk) in (("hi", "hi"), ("hi", "lo"), ("lo", "hi")):
                off = t * 128 if wk == "hi" else 4096 + t * 128
                rk = ("wla0", "wla1") if wk == "hi" else ("wla2", "wla3")
                C("tensor", lambda E, off=off, t=t, xk=xk, s=(nm[0] == 0), st=(nm[0] == 95): E.matmul(
                    ps["rtl"][:], wla[:, off:off + 128],
                    sb["xn_hi" if xk == "hi" else "xn_lo"][:, t * SBT:(t + 1) * SBT],
                    start=s, stop=st),
                    rk + (f"xn{'h' if xk == 'hi' else 'l'}{t}", "rt8"), ("rtl",))
                nm[0] += 1
        C("scalar", lambda E: E.activation(sb["lh"][:], ps["rtl"][:], AF.Copy),
          ("rtl",), ("lh",))
        lgt_v = sb["lgt"][:].rearrange("p (e g) -> p e g", g=2)
        for g in range(2):
            C("tensor", lambda E, g=g: E.transpose(ps["misc"][:, g * 8:(g + 1) * 8],
                                                   sb["rt8"][:, g * 128:(g + 1) * 128], sb["idn"][0:8, 0:8]),
              ("rt8", "idn"), ("misc",))
            C("scalar", lambda E, g=g, lv=lgt_v: E.activation(lv[:, :, g], ps["misc"][:, g * 8:(g + 1) * 8], AF.Copy),
              ("misc",), ("lgt",))
        sparsemax_ops(sbi)
        wf_keys = tuple(f"wf{ee}" for ee in range(8))
        wfin_v = sb["wfin"][:].rearrange("p (e g) -> p e g", g=2)
        for g in range(2):
            C("tensor", lambda E, g=g, wv=wfin_v: E.transpose(ps["misc"][0:8, 128 + g * 128:128 + (g + 1) * 128],
                                                              wv[:, :, g], sb["idn"][:]),
              wf_keys + ("idn",), ("misc",))
            C("scalar", lambda E, g=g: E.activation(sb["wt8"][:, g * 128:(g + 1) * 128],
                                                    ps["misc"][0:8, 128 + g * 128:128 + (g + 1) * 128], AF.Copy),
              ("misc",), ("wt8",))
        C("tensor", lambda E: E.matmul(ps["misc"][:, 0:SBT], sb["sel8"][:], sb["wt8"][:], start=True, stop=True),
          ("sel8", "wt8"), ("misc",))
        C("scalar", lambda E: E.activation(sb["wbc"][:], ps["misc"][:, 0:SBT], AF.Copy),
          ("misc",), ("wbc",))
        C("vector", lambda E: E.tensor_tensor(sb["lh"][:], sb["lh"][:], sb["wbc"][:], ALU.mult),
          ("lh", "wbc"), ("hw", "lh"))
        C("scalar", lambda E: E.activation(sb["hw_hi"][:], sb["lh"][:], AF.Copy),
          ("hw",), ("hw_hi",))
        C("vector", lambda E: E.tensor_tensor(sb["hw_lo"][:], sb["lh"][:], sb["hw_hi"][:], ALU.subtract),
          ("hw", "hw_hi"), ("hw_lo",))

    pend_stats = []
    pend_res = []

    def emit_res_stats():
        while pend_res:
            i, j = pend_res.pop(0)
            C("tensor", lambda E, j=j, s=(i == 0), t=(i == NDT - 1): E.matmul(
                ps["bst"][32:33, :], sb["ones_c"][:, 0:1], sb[f"res{j}"][:], start=s, stop=t),
                ("ones_c", f"res{j}"), ("st", "st0"))
            C("tensor", lambda E, j=j, s=(i == 0), t=(i == NDT - 1): E.matmul(
                ps["bst"][64:65, :], sb["ones_c"][:, 0:1], sb[f"yh{j}"][:], start=s, stop=t),
                ("ones_c", f"yh{j}"), ("st", "st1"))

    def emit_one_res():
        i, j = pend_res.pop(0)
        C("tensor", lambda E, j=j, s=(i == 0), t=(i == NDT - 1): E.matmul(
            ps["bst"][32:33, :], sb["ones_c"][:, 0:1], sb[f"res{j}"][:], start=s, stop=t),
            ("ones_c", f"res{j}"), ("st", "st0"))
        C("tensor", lambda E, j=j, s=(i == 0), t=(i == NDT - 1): E.matmul(
            ps["bst"][64:65, :], sb["ones_c"][:, 0:1], sb[f"yh{j}"][:], start=s, stop=t),
            ("ones_c", f"yh{j}"), ("st", "st1"))

    def emit_ff_stats(keep=2):
        while len(pend_stats) > keep:
            o, sbi, jo, sqn, cs = pend_stats.pop(0)
            C("tensor", lambda E, jo=jo, cs=cs, s=(o == 0), t=(o == NOT - 1): E.matmul(
                ps["xp2"][0:1, cs:cs + SBT], sb["ones_c"][:, 0:1], sb[f"fft{jo}"][:, 0:SBT], start=s, stop=t),
                ("ones_c", f"fft{jo}"), ("xp2", "xp2s"))
            C("tensor", lambda E, sqn=sqn, cs=cs, s=(o == 0), t=(o == NOT - 1): E.matmul(
                ps["xp2"][32:33, cs:cs + SBT], sb["ones_c"][:, 0:1], sb[sqn][:], start=s, stop=t),
                ("ones_c", sqn), ("xp2", "xp2q"))

    def ffn_otile(k, sbi, o):
        flush_stores(keep=2)
        emit_ff_stats(keep=1)
        while len(pend_res) > 2:
            emit_one_res()
        cs = sbi * SBT
        jo = o % 2
        fb = f"ff{jo}"
        dma("sync", wla[:, 0:2048], wh_hi[o, :, 0:2048], (f"wbm{o}_0H", "wla0"), ("wla0",), fam="wla")
        dma("sync", wla[:, 4096:6144], wh_lo[o, :, 0:2048], (f"wbm{o}_0L", "wla2"), ("wla2",), fam="wla")
        dma("sync", wla[:, 2048:4096], wh_hi[o, :, 2048:4096], (f"wbm{o}_1H", "wla1"), ("wla1",), fam="wla")
        dma("sync", wla[:, 6144:8192], wh_lo[o, :, 2048:4096], (f"wbm{o}_1L", "wla3"), ("wla3",), fam="wla")
        dma("sync", sb[f"lbt_hi{jo}"][:], lbh_hi[:, o * 128:(o + 1) * 128], ("lbhH",), (f"lbt_hi{jo}",), fam=f"lbt{jo}")
        dma("sync", sb[f"lbt_lo{jo}"][:], lbh_lo[:, o * 128:(o + 1) * 128], ("lbhL",), (f"lbt_lo{jo}",), fam=f"lbt{jo}")
        nm = [0]
        for t in range(NKT):
            for (wk, xk) in (("hi", "hi"), ("hi", "lo"), ("lo", "hi")):
                off = (0 if wk == "hi" else 4096) + t * 128
                rk = "wla0" if off < 2048 else ("wla1" if off < 4096 else ("wla2" if off < 6144 else "wla3"))
                C("tensor", lambda E, off=off, t=t, xk=xk, s=(nm[0] == 0), fb=fb: E.matmul(
                    ps[fb][:], wla[:, off:off + 128],
                    sb["xn_hi" if xk == "hi" else "xn_lo"][:, t * SBT:(t + 1) * SBT],
                    start=s, stop=False),
                    (rk, f"xn{'h' if xk == 'hi' else 'l'}{t}"), (fb,))
                nm[0] += 1
        for pi, (wk, xk) in enumerate((("hi", "hi"), ("hi", "lo"), ("lo", "hi"))):
            C("tensor", lambda E, wk=wk, xk=xk, fb=fb, st=(pi == 2), jo=jo: E.matmul(
                ps[fb][:], sb[f"lbt_{wk}{jo}"][:], sb[f"hw_{xk}"][:],
                start=False, stop=st),
                (f"lbt_{wk}{jo}", f"hw_{xk}"), (fb,))
        jo4 = o % 4
        C("scalar", lambda E, o=o, fb=fb, jo4=jo4: E.activation(
            sb[f"fft{jo4}"][:], ps[fb][:], AF.Identity, bias=sb["bb"][:, o:o + 1]),
            (fb, "bb"), (f"fft{jo4}",))
        dma("scalar", sb[f"rr{jo}"][:], ag1i[k][o * 128:(o + 1) * 128, cs:cs + SBT],
            (f"ag1w{k}_{o}",), (f"rr{jo}",), fam=f"rr{jo}")
        C("vector", lambda E, jo=jo, cs=cs: E.tensor_tensor(sb[f"rr{jo}"][:], sb[f"rr{jo}"][:], sb["mbc"][:, cs:cs + SBT], ALU.subtract),
          (f"rr{jo}", "mbc_v", "mbc"), (f"rr{jo}",))
        C("vector", lambda E, jo=jo, cs=cs: E.tensor_tensor(sb[f"rr{jo}"][:], sb[f"rr{jo}"][:], sb["rbc"][:, cs:cs + SBT], ALU.mult),
          (f"rr{jo}", "rbc_v", "rbc"), (f"rr{jo}",))
        C("vector", lambda E, jo4=jo4: E.tensor_tensor(sb[f"fft{jo4}"][:], sb[f"fft{jo4}"][:], sb[f"rr{jo}"][:], ALU.add),
          (f"fft{jo4}", f"rr{jo}"), (f"fft{jo4}",))
        pend_stores.append(lambda o=o, cs=cs, jo4=jo4, k=k, sbi=sbi: dma(
            "gpsimd", r2scr[o * 128:(o + 1) * 128, cs:cs + SBT], sb[f"fft{jo4}"][:],
            (f"fft{jo4}",), (f"r2_{o}_{sbi}",), fam=f"r2s{k % 2}"))
        sqn = "sq2" if jo == 0 else "sq2b"
        pend_stores.append(lambda jo4=jo4, sqn=sqn: C(
            "scalar", lambda E, jo4=jo4, sqn=sqn: E.activation(sb[sqn][:], sb[f"fft{jo4}"][:], AF.Square),
            (f"fft{jo4}",), (sqn,)))
        pend_stats.append((o, sbi, jo4, sqn, cs))

    def ffn_fin(k):
        c0 = k * LC
        flush_stores()
        emit_ff_stats(keep=0)
        C("scalar", lambda E: E.activation(sb["stat_sb"][0:1, :], ps["xp2"][0:1, :], AF.Copy),
          ("xp2s", "xp2q", "xp2"), ("stat_sb",))
        C("scalar", lambda E: E.activation(sb["stat_sb"][32:33, :], ps["xp2"][32:33, :], AF.Copy),
          ("xp2s", "xp2q", "xp2"), ("stat_sb",))
        dma("scalar", ag2i[k][0:1, :], sb["stat_sb"][0:1, :], ("stat_sb",), (f"ag2w{k}",), fam="ag2w")
        dma("scalar", ag2i[k][1:2, :], sb["stat_sb"][32:33, :], ("stat_sb",), (f"ag2w{k}b",), fam="ag2w")
        P.op("gpsimd", lambda E: E.collective_compute(
            "AllReduce", ALU.add, replica_groups=RG, ins=[ag2i[k]], outs=[ag2o[k]]),
            reads=(f"ag2w{k}", f"ag2w{k}b"), writes=(f"gath2_{k}",), kind="cc", name=f"ar2_{k}", fam=f"car2{k % 2}")
        stv = sb["stv"]
        dma("scalar", stv[:, 0:LC], ag2o[k][0:1, :], (f"gath2_{k}",), ("stv0",), fam="stv")
        dma("scalar", stv[:, LC:2 * LC], ag2o[k][1:2, :], (f"gath2_{k}",), ("stv1",), fam="stv")
        ln_stats("mbc_v", "rbc_v")
        for o in range(NOT):
            jo = o % 2
            dma("scalar", sb[f"r2l{jo}"][:], r2scr[o * 128:(o + 1) * 128, :],
                (f"r2_{o}_0", f"r2_{o}_1"), (f"xlr{jo}",), fam=f"finl{jo}")
            C("vector", lambda E, jo=jo: E.tensor_tensor(sb[f"r2l{jo}"][:], sb[f"r2l{jo}"][:], sb["mbc"][:], ALU.subtract),
              (f"xlr{jo}", "mbc_v", "mbc"), (f"xlr{jo}",))
            C("vector", lambda E, jo=jo: E.tensor_tensor(sb[f"r2l{jo}"][:], sb[f"r2l{jo}"][:], sb["rbc"][:], ALU.mult),
              (f"xlr{jo}", "rbc_v", "rbc"), (f"xlr{jo}",))
            dma("scalar", out_c[o * 128:(o + 1) * 128, c0:c0 + LC], sb[f"r2l{jo}"][:],
                (f"xlr{jo}",), (f"outw{k}_{o}",), fam="out")

    SERIAL = SERIAL_EMIT
    for k in range(NCH):
        if DEBUG and k == 1:
            dma("sync", dbg_cr, sb["carry"][:],
                tuple(f"carry{cc2}" for cc2 in range(NDT * NST)), ("dbgcr",), fam="dbg")
        scan_side(k)
        for h in range(2):
            bcast_half(k, h)
            if k > 0 and not SERIAL:
                ffn_prep(k - 1, h)
            dtile(k, h, 0)
            for i in range(NDT):
                if i + 1 < NDT:
                    dtile(k, h, i + 1)
                for q in range(2):
                    quarter(k, h, i, q)
                dtile_end(k, h, i)
                if k > 0 and not SERIAL:
                    ffn_otile(k - 1, h, i)
                elif k == 0:
                    while len(pend_res) > 2:
                        emit_one_res()
                    emit_wcast(2)
            if h == 1 and k > 0 and not SERIAL:
                ffn_fin(k - 1)
        ag1(k)
    emit_wcast(64)
    if SERIAL:
        for k in range(NCH):
            for sbi in range(2):
                ffn_prep(k, sbi)
                for o in range(NOT):
                    ffn_otile(k, sbi, o)
            ffn_fin(k)
    else:
        for sbi in range(2):
            ffn_prep(NCH - 1, sbi)
            for o in range(NOT):
                ffn_otile(NCH - 1, sbi, o)
        ffn_fin(NCH - 1)

    P.op("vector", lambda E: E.engine_nop(),
         reads=tuple(f"outw{k}_{o}" for k in range(NCH) for o in range(NOT)),
         writes=(), kind="c", name="fin")

    P.finalize()
    global LAST_PROG
    LAST_PROG = P
    P.check_deadlock()
    with nc.Block() as block:
        @block.sync
        def _(E):
            P.emit_engine("sync", E, chans)

        @block.vector
        def _(E):
            P.emit_engine("vector", E, chans)

        @block.scalar
        def _(E):
            P.emit_engine("scalar", E, chans)

        @block.gpsimd
        def _(E):
            P.emit_engine("gpsimd", E, chans)

        @block.tensor
        def _(E):
            P.emit_engine("tensor", E, chans)
    stack.close()
    return nc


def make_inputs(ins):
    x = np.ascontiguousarray(ins["L0_attn_x"])
    A_log = np.asarray(ins["L0_attn_A_log"])
    Dv = np.asarray(ins["L0_attn_D"])
    xpw = np.asarray(ins["L0_attn_x_proj_w"])
    dtw = np.asarray(ins["L0_attn_dt_proj_w"])
    dtb = np.asarray(ins["L0_attn_dt_proj_b"])
    Wb = np.asarray(ins["L0_ffn_W_base"])
    bb = np.asarray(ins["L0_ffn_b_base"])
    Wr = np.asarray(ins["L0_ffn_W_router"])
    br = np.asarray(ins["L0_ffn_b_router"])
    lA = np.asarray(ins["L0_ffn_lora_A"])
    lB = np.asarray(ins["L0_ffn_lora_B"])

    idn = np.eye(128, dtype=np.float32)
    sel8 = np.zeros((8, 128), np.float32)
    for ee in range(8):
        sel8[ee, ee * 16:(ee + 1) * 16] = 1.0
    xpwT = np.ascontiguousarray(xpw.T)
    lap = np.ascontiguousarray(lA.transpose(1, 0, 2).reshape(D, 128))
    lbp = np.ascontiguousarray(lB.reshape(128, D))
    WrT = np.ascontiguousarray(Wr.T)

    maps = []
    for c in range(8):
        b, h = c // 2, c % 2
        own = slice(h * DH, (h + 1) * DH)
        xb = x[b].T
        xt_c = np.concatenate([xb[own], xb[(1 - h) * DH:(2 - h) * DH]], axis=0)
        perm = np.concatenate([np.arange(h * DH, (h + 1) * DH), np.arange((1 - h) * DH, (2 - h) * DH)])
        xpw_c = xpwT[perm]
        xpw_t = np.ascontiguousarray(xpw_c.reshape(32, 128, XPR).transpose(1, 0, 2).reshape(128, 32 * XPR))
        dtw_c = np.ascontiguousarray(dtw[own].T)
        alog_c = np.ascontiguousarray(A_log[own].reshape(NDT, 128, NST).transpose(1, 0, 2).reshape(128, NDT * NST))
        d_c = np.ascontiguousarray(Dv[own].reshape(NDT, 128).T)
        dtb_c = np.ascontiguousarray(dtb[own].reshape(NDT, 128).T)
        Wslice = Wb[own, :]
        wbs_c = np.ascontiguousarray(
            Wslice.reshape(NOT, 128, NKT, 128).transpose(0, 3, 2, 1).reshape(NOT, 128, NKT * 128))
        bb_c = np.ascontiguousarray(bb[own].reshape(NOT, 128).T)
        wrt_c = np.ascontiguousarray(WrT.reshape(NKT, 128, 8).transpose(1, 0, 2).reshape(128, NKT * 8))
        las_c = np.ascontiguousarray(lap.reshape(NKT, 128, 128).transpose(1, 0, 2).reshape(128, NKT * 128))
        lbs_c = np.ascontiguousarray(lbp[:, own])
        maps.append({
            "xt": xt_c, "xpw": xpw_t, "dtwi": dtw_c, "alog": alog_c,
            "dvec": d_c, "dtbi": dtb_c, "wbs": wbs_c, "bbt": bb_c,
            "wrti": wrt_c, "brt": br.reshape(8, 1).astype(np.float32),
            "las": las_c, "lbsi": lbs_c, "idni": idn, "sel8i": sel8,
        })
    return maps


_NC_CACHE = [None]


def kernel(**inputs):
    maps = make_inputs(inputs)
    if _NC_CACHE[0] is None:
        _NC_CACHE[0] = build_nc()
    nc = _NC_CACHE[0]
    res = run_bass_kernel_spmd(nc, maps, core_ids=list(range(8)))
    out = np.empty((B, L, D), np.float32)
    for c in range(8):
        b, h = c // 2, c % 2
        oc = res.results[c]["out_c"]
        out[b, :, h * DH:(h + 1) * DH] = oc.T
    return out




# version 09
# speedup vs baseline: 1.2412x; 1.0183x over previous
import numpy as np
import concourse.bass as bass
import concourse.mybir as mybir
from concourse.bass_utils import run_bass_kernel_spmd

F32 = mybir.dt.float32
BF16 = mybir.dt.bfloat16
AF = mybir.ActivationFunctionType
ALU = mybir.AluOpType

B, L, D = 4, 2048, 4096
DH, NST, DTR, XPR = 2048, 16, 128, 160
NCH, LC = 4, 512
SBT = 256
NDT = 16
OH, NOT, NKT = 2048, 16, 32
EPS = 1e-5
RG = [[0, 1], [2, 3], [4, 5], [6, 7]]
SORT8 = [(0, 1), (2, 3), (4, 5), (6, 7), (0, 2), (1, 3), (4, 6), (5, 7),
         (1, 2), (5, 6), (0, 4), (3, 7), (1, 5), (2, 6), (1, 4), (3, 6),
         (2, 4), (3, 5), (3, 4)]
DEBUG = False
SERIAL_EMIT = False
LAST_PROG = None


class Prog:
    ENGINES = ("sync", "vector", "scalar", "gpsimd", "tensor")

    def __init__(self):
        self.ops = []
        self.last_w = {}
        self.readers = {}
        self.fams = {}
        self.fam_R = {}

    def op(self, eng, fn, reads=(), writes=(), kind="c", name="", fam=None, R=0):
        import sys as _sys
        fr = _sys._getframe(1)
        if fr.f_code.co_name in ("dma", "C"):
            fr = fr.f_back
        lbl = f"{fr.f_code.co_name}:{fr.f_lineno}"
        deps = set()
        for kk in reads:
            if kk in self.last_w:
                deps.add(self.last_w[kk])
        for kk in writes:
            for r in self.readers.get(kk, ()):
                deps.add(r)
            if kk in self.last_w:
                deps.add(self.last_w[kk])
        i = len(self.ops)
        if kind in ("d", "cc"):
            assert fam is not None, name
            self.fams.setdefault(fam, []).append(i)
            self.fam_R[fam] = max(self.fam_R.get(fam, 0), R)
        self.ops.append(dict(eng=eng, fn=fn, deps=deps, kind=kind, name=name, fam=fam, lbl=lbl))
        for kk in reads:
            self.readers.setdefault(kk, []).append(i)
        for kk in writes:
            self.last_w[kk] = i
            self.readers[kk] = []
        return i

    def finalize(self):
        ops = self.ops
        self.need_inc = [False] * len(ops)
        for o in ops:
            for d in o["deps"]:
                p = ops[d]
                if p["kind"] == "c" and p["eng"] == o["eng"]:
                    continue
                if p["kind"] == "c":
                    self.need_inc[d] = True
        cnt = {}
        self.val = [0] * len(ops)
        for i, o in enumerate(ops):
            if o["kind"] == "c" and self.need_inc[i]:
                ch = (o["eng"], "c")
                cnt[ch] = cnt.get(ch, 0) + 1
                self.val[i] = cnt[ch]
        self.fam_idx = {}
        for fam, lst in self.fams.items():
            for q, gi in enumerate(lst):
                self.fam_idx[gi] = q

    def check_deadlock(self):
        import bisect
        ops = self.ops
        streams = {en: [i for i in range(len(ops)) if ops[i]["eng"] == en] for en in self.ENGINES}
        pos = {en: 0 for en in self.ENGINES}
        cnt = {}

        def waits_of(i):
            o = ops[i]
            ws = []
            for d in o["deps"]:
                p = ops[d]
                if p["kind"] == "c":
                    if p["eng"] == o["eng"]:
                        continue
                    ws.append((("c", p["eng"]), self.val[d]))
                else:
                    fam = p["fam"]
                    lst = self.fams[fam]
                    cx = max(bisect.bisect_left(lst, i), self.fam_idx[d] + 1)
                    tgt = min(cx + self.fam_R[fam], len(lst))
                    if ops[i]["kind"] == p["kind"] and ops[i].get("fam") == fam:
                        tgt = min(tgt, self.fam_idx[i])
                    tgt = max(tgt, self.fam_idx[d] + 1)
                    step = 16 if p["kind"] == "d" else 1
                    ws.append((("f", fam), tgt * step))
            return ws

        progressed = True
        while progressed:
            progressed = False
            for en in self.ENGINES:
                while pos[en] < len(streams[en]):
                    i = streams[en][pos[en]]
                    if any(cnt.get(ch, 0) < v for ch, v in waits_of(i)):
                        break
                    o = ops[i]
                    if o["kind"] == "c":
                        if self.need_inc[i]:
                            ch = ("c", en)
                            cnt[ch] = cnt.get(ch, 0) + 1
                    else:
                        step = 16 if o["kind"] == "d" else 1
                        ch = ("f", o["fam"])
                        cnt[ch] = cnt.get(ch, 0) + step
                    pos[en] += 1
                    progressed = True
        stuck = {en: pos[en] for en in self.ENGINES if pos[en] < len(streams[en])}
        if stuck:
            for en, p in stuck.items():
                i = streams[en][p]
                o = ops[i]
                print(f"DEADLOCK: {en} pos {p}/{len(streams[en])} op {i} kind={o['kind']} fam={o['fam']} name={o['name']}")
                for d in sorted(o["deps"]):
                    pp = ops[d]
                    print(f"   dep {d}: eng={pp['eng']} kind={pp['kind']} fam={pp['fam']} name={pp['name']} val={self.val[d]}")
                print("   waits:", waits_of(i), "cur:", {ch: cnt.get(ch, 0) for ch, v in waits_of(i)})
            raise RuntimeError("deadlock detected")
        print("deadlock check: OK")

    def emit_engine(self, ename, eng, chans):
        ops = self.ops
        import bisect
        waited = {}
        for i, o in enumerate(ops):
            if o["eng"] != ename:
                continue
            for d in sorted(o["deps"]):
                p = ops[d]
                if p["kind"] == "c":
                    if p["eng"] == ename:
                        continue
                    ch = ("c", p["eng"])
                    v = self.val[d]
                    assert v > 0, (i, o["name"], d, p["name"])
                    sem = chans[(p["eng"], "c")]
                else:
                    fam = p["fam"]
                    lst = self.fams[fam]
                    cx = bisect.bisect_left(lst, i)
                    cx = max(cx, self.fam_idx[d] + 1)
                    tgt = min(cx + self.fam_R[fam], len(lst))
                    if o["kind"] == p["kind"] and o.get("fam") == fam:
                        tgt = min(tgt, self.fam_idx[i])
                    tgt = max(tgt, self.fam_idx[d] + 1)
                    step = 16 if p["kind"] == "d" else 1
                    ch = ("f", fam)
                    v = tgt * step
                    sem = chans[("fam", fam)]
                if waited.get(ch, 0) >= v:
                    continue
                eng.wait_ge(sem, v)
                waited[ch] = v
            inst = o["fn"](eng)
            if o["kind"] == "c":
                if self.need_inc[i]:
                    inst.then_inc(chans[(ename, "c")], 1)
            else:
                step = 16 if o["kind"] == "d" else 1
                inst.then_inc(chans[("fam", o["fam"])], step)


def build_nc():
    nc = bass.Bass("TRN2", target_bir_lowering=False)
    P = Prog()

    xt = nc.dram_tensor("xt", [D, L], F32, kind="ExternalInput").ap()
    xpw = nc.dram_tensor("xpw", [128, 32 * XPR], F32, kind="ExternalInput").ap()
    dtwi = nc.dram_tensor("dtwi", [128, DH], F32, kind="ExternalInput").ap()
    alog = nc.dram_tensor("alog", [128, NDT * NST], F32, kind="ExternalInput").ap()
    dvec = nc.dram_tensor("dvec", [128, NDT], F32, kind="ExternalInput").ap()
    dtbi = nc.dram_tensor("dtbi", [128, NDT], F32, kind="ExternalInput").ap()
    wbs = nc.dram_tensor("wbs", [NOT, 128, NKT * 128], F32, kind="ExternalInput").ap()
    bbt = nc.dram_tensor("bbt", [128, NOT], F32, kind="ExternalInput").ap()
    wrti = nc.dram_tensor("wrti", [128, NKT * 8], F32, kind="ExternalInput").ap()
    brt = nc.dram_tensor("brt", [8, 1], F32, kind="ExternalInput").ap()
    las = nc.dram_tensor("las", [128, NKT * 128], F32, kind="ExternalInput").ap()
    lbsi = nc.dram_tensor("lbsi", [128, OH], F32, kind="ExternalInput").ap()
    idni = nc.dram_tensor("idni", [128, 128], F32, kind="ExternalInput").ap()
    sel8i = nc.dram_tensor("sel8i", [8, 128], F32, kind="ExternalInput").ap()
    out_c = nc.dram_tensor("out_c", [OH, L], F32, kind="ExternalOutput").ap()
    if DEBUG:
        dbg_res = nc.dram_tensor("dbg_res", [DH, L], F32, kind="ExternalOutput").ap()
        dbg_xn = nc.dram_tensor("dbg_xn", [D, L], F32, kind="ExternalOutput").ap()
        dbg_cr = nc.dram_tensor("dbg_cr", [128, NDT * NST], F32, kind="ExternalOutput").ap()
        dbg_st = nc.dram_tensor("dbg_st", [NCH, 2, LC], F32, kind="ExternalOutput").ap()
        dbg_bp = nc.dram_tensor("dbg_bp", [128, 8 * LC], F32, kind="ExternalOutput").ap()
        dbg_cp = nc.dram_tensor("dbg_cp", [128, 8 * LC], F32, kind="ExternalOutput").ap()
        dbg_dt = nc.dram_tensor("dbg_dt", [128, LC], F32, kind="ExternalOutput").ap()
        dbg_v = nc.dram_tensor("dbg_v", [128, LC], F32, kind="ExternalOutput").ap()

    xph_hi = nc.dram_tensor("xph_hi", [128, 32 * XPR], BF16).ap()
    xph_lo = nc.dram_tensor("xph_lo", [128, 32 * XPR], BF16).ap()
    lbh_hi = nc.dram_tensor("lbh_hi", [128, OH], BF16).ap()
    lbh_lo = nc.dram_tensor("lbh_lo", [128, OH], BF16).ap()
    lah_hi = nc.dram_tensor("lah_hi", [128, NKT * 128], BF16).ap()
    lah_lo = nc.dram_tensor("lah_lo", [128, NKT * 128], BF16).ap()
    wh_hi = nc.dram_tensor("wh_hi", [NOT, 128, NKT * 128], BF16).ap()
    wh_lo = nc.dram_tensor("wh_lo", [NOT, 128, NKT * 128], BF16).ap()
    yscr = nc.dram_tensor("yscr", [DH, LC], F32).ap()
    bchbm = nc.dram_tensor("bchbm", [32, LC], F32).ap()
    r2scr = nc.dram_tensor("r2scr", [OH, LC], F32).ap()
    ag1i = [nc.dram_tensor(f"ag1i{k}", [DH + 2, LC], F32).ap() for k in range(NCH)]
    ag1o = [nc.dram_tensor(f"ag1o{k}", [2 * (DH + 2), LC], F32).ap() for k in range(NCH)]
    ag2i = [nc.dram_tensor(f"ag2i{k}", [2, LC], F32).ap() for k in range(NCH)]
    ag2o = [nc.dram_tensor(f"ag2o{k}", [2, LC], F32).ap() for k in range(NCH)]

    from contextlib import ExitStack
    stack = ExitStack()
    e = stack.enter_context

    sb = {}
    def S(name, shape, dt=F32):
        sb[name] = e(nc.sbuf_tensor(name, shape, dt))
        return sb[name]

    S("ones_r", [1, 128]); S("ones_c", [128, 1])
    S("idn", [128, 128]); S("sel8", [8, 128])
    S("A", [128, NDT * NST]); S("onepD", [128, NDT]); S("dtb", [128, NDT])
    S("bb", [128, NOT]); S("br", [8, 1])
    S("dtw_hi", [128, DH], BF16); S("dtw_lo", [128, DH], BF16)
    S("wr_hi", [128, NKT * 8], BF16); S("wr_lo", [128, NKT * 8], BF16)
    S("carry", [128, NDT * NST])
    for j in range(2):
        S(f"xpw_hi{j}", [128, 2 * XPR], BF16); S(f"xpw_lo{j}", [128, 2 * XPR], BF16)
    S("Bp", [128, 8 * LC]); S("Cp", [128, 8 * LC])
    S("dAz0", [128, 4 * LC]); S("dAz1", [128, 4 * LC])
    S("dBx0", [128, 4 * LC]); S("dBx1", [128, 4 * LC])
    S("hp", [128, 4 * LC])
    for j in range(2):
        S(f"xhi{j}", [128, LC], BF16); S(f"xlo{j}", [128, LC], BF16)
        S(f"dtt{j}", [128, LC]); S(f"xot{j}", [128, LC]); S(f"vt{j}", [128, LC])
    for j in range(4):
        S(f"yh{j}", [128, LC]); S(f"res{j}", [128, LC])
    S("ysc", [128, LC]); S("zero4", [128, 4]); S("tc4", [128, 4])
    S("lbt_hi0", [128, 128], BF16); S("lbt_lo0", [128, 128], BF16)
    S("lbt_hi1", [128, 128], BF16); S("lbt_lo1", [128, 128], BF16)
    S("dtin_hi", [128, LC], BF16); S("dtin_lo", [128, LC], BF16)
    S("bc_c", [32, LC])
    S("stat_sb", [64, LC])
    S("xn_hi", [128, NKT * SBT], BF16); S("xn_lo", [128, NKT * SBT], BF16)
    S("wla", [128, 8192], BF16)
    for j in range(2):
        S(f"gt{j}", [128, SBT])
        S(f"rr{j}", [128, SBT])
        S(f"r2l{j}", [128, LC])
    for j in range(4):
        S(f"fft{j}", [128, SBT])
    S("sq2", [128, SBT])
    S("sq2b", [128, SBT])
    S("stv", [1, 3 * LC])
    S("mbc", [128, LC]); S("rbc", [128, LC])
    S("rt8", [8, SBT]); S("lgt", [128, 16])
    S("sortp", [128, 48])
    S("wfin", [128, 16]); S("wt8", [8, SBT]); S("wbc", [128, SBT])
    S("lh", [128, SBT])
    S("hw_hi", [128, SBT], BF16); S("hw_lo", [128, SBT], BF16)

    wla = sb["wla"]
    wla_f32 = wla[:].bitcast(F32)
    wf32 = wla_f32[:, 0:2048]
    cst_hi = wla[:, 4096:6144]
    cst_lo = wla[:, 6144:8192]
    WQ = ("wla0", "wla1", "wla2", "wla3")

    ps = {}
    def PSUM(name, shape):
        ps[name] = e(nc.psum_tensor(name, shape, F32))
    PSUM("xp1", [128, LC])
    PSUM("xp1b", [128, LC])
    PSUM("bst", [128, LC])
    PSUM("xp2", [64, LC])
    PSUM("rtl", [128, SBT])
    PSUM("misc", [128, LC])
    PSUM("ff0", [128, SBT]); PSUM("ff1", [128, SBT])

    chans = {}
    for en in Prog.ENGINES:
        chans[(en, "c")] = e(nc.semaphore(f"c_{en}"))
    FAMS = ["consts", "xphw", "lahw", "wbmw", "wla", "xlr0", "xlr1", "xot0", "xot1", "xpw",
            "bp", "cp", "gt0", "gt1", "rr0", "rr1", "yscl", "bchbm", "stv", "out",
            "yscr0", "yscr1", "r2s0", "r2s1", "agw0", "agw1", "arw", "ag2w",
            "cag0", "cag1", "car0", "car1", "car20", "car21", "dbg",
            "xpwh0", "xpwh1", "xpwl0", "xpwl1", "lbt0", "lbt1", "finl0", "finl1", "lbhw"]
    for f in FAMS:
        chans[("fam", f)] = e(nc.semaphore(f"f_{f}"))

    def dma(eng, out_ap, in_ap, reads, writes, name="", fam="consts", R=0):
        P.op(eng, lambda E, o=out_ap, i=in_ap: E.dma_start(out=o, in_=i),
             reads=reads, writes=writes, kind="d", name=name, fam=fam, R=R)

    C = P.op

    C("vector", lambda E: E.memset(sb["ones_r"][:], 1.0), (), ("ones_r",))
    C("vector", lambda E: E.memset(sb["ones_c"][:], 1.0), (), ("ones_c",))
    C("vector", lambda E: E.memset(sb["zero4"][:], 0.0), (), ("zero4",))
    dma("sync", sb["idn"][:], idni, (), ("idn",))
    dma("sync", sb["sel8"][:], sel8i, (), ("sel8",))
    dma("sync", sb["A"][:], alog, (), ("A",))
    C("scalar", lambda E: E.activation(sb["A"][:], sb["A"][:], AF.Exp), ("A",), ("A",))
    C("vector", lambda E: E.tensor_single_scalar(sb["A"][:], sb["A"][:], -1.0, ALU.mult), ("A",), ("A",))
    dma("sync", sb["onepD"][:], dvec, (), ("onepD",))
    C("vector", lambda E: E.tensor_single_scalar(sb["onepD"][:], sb["onepD"][:], 1.0, ALU.add), ("onepD",), ("onepD",))
    dma("sync", sb["dtb"][:], dtbi, (), ("dtb",))
    dma("sync", sb["bb"][:], bbt, (), ("bb",))
    dma("sync", sb["br"][:], brt, (), ("br",))

    def stage_cast(src_ap, cols, hi_dst, hi_key, lo_dst, lo_key):
        dma("sync", wf32[:, 0:cols], src_ap, ("wla0", "wla1"), ("wla0", "wla1"), fam="wla")
        C("scalar", lambda E, c=cols, h=hi_dst: E.activation(h, wf32[:, 0:c], AF.Copy),
          ("wla0", "wla1"), (hi_key,))
        C("vector", lambda E, c=cols, h=hi_dst, l=lo_dst: E.tensor_tensor(l, wf32[:, 0:c], h, ALU.subtract),
          ("wla0", "wla1", hi_key), (lo_key,))

    stage_cast(dtwi, DH, sb["dtw_hi"][:], "dtw_hi", sb["dtw_lo"][:], "dtw_lo")
    stage_cast(wrti, NKT * 8, sb["wr_hi"][:], "wr_hi", sb["wr_lo"][:], "wr_lo")

    def cast_to_hbm(nm, src_ap, dst_hi, dst_lo, total):
        off = 0
        while off < total:
            w = min(2048, total - off)
            dma("sync", wf32[:, 0:w], src_ap[:, off:off + w], ("wla0", "wla1"), ("wla0", "wla1"), fam="wla")
            C("scalar", lambda E, w=w: E.activation(cst_hi[:, 0:w], wf32[:, 0:w], AF.Copy),
              ("wla0", "wla1"), ("wla2",))
            C("vector", lambda E, w=w: E.tensor_tensor(cst_lo[:, 0:w], wf32[:, 0:w], cst_hi[:, 0:w], ALU.subtract),
              ("wla0", "wla1", "wla2"), ("wla3",))
            dma("sync", dst_hi[:, off:off + w], cst_hi[:, 0:w], ("wla2",), (f"{nm}H",), fam=nm + "w")
            dma("sync", dst_lo[:, off:off + w], cst_lo[:, 0:w], ("wla3",), (f"{nm}L",), fam=nm + "w")
            off += w

    cast_to_hbm("xph", xpw, xph_hi, xph_lo, 32 * XPR)
    cast_to_hbm("lah", las, lah_hi, lah_lo, NKT * 128)
    cast_to_hbm("lbh", lbsi, lbh_hi, lbh_lo, OH)

    wblocks = [(o, hf) for o in range(NOT) for hf in range(2)]
    widx = [0]

    def emit_wcast(nblk):
        for _ in range(nblk):
            if widx[0] >= len(wblocks):
                return
            o, hf = wblocks[widx[0]]
            widx[0] += 1
            sl = slice(hf * 2048, (hf + 1) * 2048)
            dma("sync", wf32[:], wbs[o, :, sl], ("wla0", "wla1"), ("wla0", "wla1"), fam="wla")
            C("scalar", lambda E: E.activation(cst_hi[:], wf32[:], AF.Copy),
              ("wla0", "wla1"), ("wla2",))
            C("vector", lambda E: E.tensor_tensor(cst_lo[:], wf32[:], cst_hi[:], ALU.subtract),
              ("wla0", "wla1", "wla2"), ("wla3",))
            dma("sync", wh_hi[o, :, sl], cst_hi[:], ("wla2",), (f"wbm{o}_{hf}H",), fam="wbmw")
            dma("sync", wh_lo[o, :, sl], cst_lo[:], ("wla3",), (f"wbm{o}_{hf}L",), fam="wbmw")

    def scan_side(k):
        c0 = k * LC
        n_mm = [0]
        for pc in range(16):
            pb = pc % 2
            poff = pc * 2 * XPR
            dma("sync", sb[f"xpw_hi{pb}"][:], xph_hi[:, poff:poff + 2 * XPR], (f"xpw_hi{pb}",), (f"xpw_hi{pb}",), fam=f"xpwh{pb}")
            dma("sync", sb[f"xpw_lo{pb}"][:], xph_lo[:, poff:poff + 2 * XPR], (f"xpw_lo{pb}",), (f"xpw_lo{pb}",), fam=f"xpwl{pb}")
            for tl in range(2):
                t = pc * 2 + tl
                j = t % 2
                dma("sync" if j == 0 else "scalar", sb[f"r2l{j}"][:], xt[t * 128:(t + 1) * 128, c0:c0 + LC], (), (f"xlr{j}",), fam=f"xlr{j}")
                C("scalar", lambda E, j=j: E.activation(sb[f"xhi{j}"][:], sb[f"r2l{j}"][:], AF.Copy),
                  (f"xlr{j}",), (f"xhi{j}",))
                C("vector", lambda E, j=j: E.tensor_tensor(sb[f"xlo{j}"][:], sb[f"r2l{j}"][:], sb[f"xhi{j}"][:], ALU.subtract),
                  (f"xlr{j}", f"xhi{j}"), (f"xlo{j}",))
                for (wk, xk) in (("hi", "hi"), ("hi", "lo"), ("lo", "hi")):
                    s0 = (n_mm[0] == 0)
                    s1 = (n_mm[0] == 95)
                    C("tensor", lambda E, wk=wk, xk=xk, j=j, tl=tl, pb=pb, s0=s0, s1=s1: E.matmul(
                        ps["xp1"][:], sb[f"xpw_{wk}{pb}"][:, tl * XPR:tl * XPR + 128], sb[f"x{xk}{j}"][:],
                        start=s0, stop=s1),
                        (f"xpw_{wk}{pb}", f"x{xk}{j}"), ("xp1",))
                    C("tensor", lambda E, wk=wk, xk=xk, j=j, tl=tl, pb=pb, s0=s0, s1=s1: E.matmul(
                        ps["bst"][0:32, :], sb[f"xpw_{wk}{pb}"][:, tl * XPR + 128:(tl + 1) * XPR], sb[f"x{xk}{j}"][:],
                        start=s0, stop=s1),
                        (f"xpw_{wk}{pb}", f"x{xk}{j}"), ("bcb",))
                    n_mm[0] += 1
        C("scalar", lambda E: E.activation(sb["dtin_hi"][:], ps["xp1"][:], AF.Copy),
          ("xp1",), ("dtin_hi",))
        C("vector", lambda E: E.tensor_tensor(sb["dtin_lo"][:], ps["xp1"][:], sb["dtin_hi"][:], ALU.subtract),
          ("xp1", "dtin_hi"), ("dtin_lo",))
        C("scalar", lambda E: E.activation(sb["bc_c"][:], ps["bst"][0:32, :], AF.Copy),
          ("bcb",), ("bc_c",))
        dma("scalar", bchbm, sb["bc_c"][:], ("bc_c",), ("bchbm",), fam="bchbm")

    def bcast_half(k, h):
        srcB = bchbm[h * 8:(h + 1) * 8, :].flatten().unsqueeze(0).broadcast_to([128, 8 * LC])
        srcC = bchbm[16 + h * 8:16 + (h + 1) * 8, :].flatten().unsqueeze(0).broadcast_to([128, 8 * LC])
        dma("gpsimd", sb["Bp"][:], srcB, ("bchbm",), tuple(f"Bp{z}" for z in range(8)), fam="bp")
        dma("gpsimd", sb["Cp"][:], srcC, ("bchbm",), tuple(f"Cp{z}" for z in range(8)), fam="cp")
        if DEBUG and k == 1 and h == 0:
            dma("sync", dbg_bp, sb["Bp"][:], tuple(f"Bp{z}" for z in range(8)), ("dbgbp",), fam="dbg")
            dma("sync", dbg_cp, sb["Cp"][:], tuple(f"Cp{z}" for z in range(8)), ("dbgcp",), fam="dbg")

    def dtile(k, h, i):
        j = i % 2
        pn = "xp1" if i % 2 == 0 else "xp1b"
        for pi, (wk, xk) in enumerate((("hi", "hi"), ("hi", "lo"), ("lo", "hi"))):
            C("tensor", lambda E, wk=wk, xk=xk, i=i, pn=pn, s=(pi == 0), t=(pi == 2): E.matmul(
                ps[pn][:], sb[f"dtw_{wk}"][:, i * 128:(i + 1) * 128], sb[f"dtin_{xk}"][:],
                start=s, stop=t),
                (f"dtw_{wk}", f"dtin_{xk}"), (pn,))
        C("scalar", lambda E, i=i, j=j, pn=pn: E.activation(sb[f"dtt{j}"][:], ps[pn][:], AF.Exp, bias=sb["dtb"][:, i:i + 1]),
          (pn, "dtb"), (f"dtt{j}",))
        C("scalar", lambda E, j=j: E.activation(sb[f"dtt{j}"][:], sb[f"dtt{j}"][:], AF.Ln, bias=1.0),
          (f"dtt{j}",), (f"dtt{j}",))
        dma("sync", sb[f"xot{j}"][:], xt[i * 128:(i + 1) * 128, k * LC:(k + 1) * LC], (), (f"xot{j}",), fam=f"xot{j}")
        C("gpsimd", lambda E, j=j: E.tensor_tensor(sb[f"vt{j}"][:], sb[f"dtt{j}"][:], sb[f"xot{j}"][:], ALU.mult),
          (f"dtt{j}", f"xot{j}"), (f"vt{j}",))
        if DEBUG and k == 1 and h == 0 and i == 0:
            dma("sync", dbg_dt, sb[f"dtt{j}"][:], (f"dtt{j}",), ("dbgdt",), fam="dbg")
            dma("sync", dbg_v, sb[f"vt{j}"][:], (f"vt{j}",), ("dbgv",), fam="dbg")

    ucnt = [0]

    def quarter(k, h, i, q):
        j = i % 2
        ub = ucnt[0] % 2
        ucnt[0] += 1
        dA = f"dAz{ub}"
        dB = f"dBx{ub}"
        for jj in range(4):
            n = h * 8 + q * 4 + jj
            col = i * NST + n
            C("scalar", lambda E, jj=jj, col=col, j=j, dA=dA: E.activation(
                sb[dA][:, jj * LC:(jj + 1) * LC], sb[f"dtt{j}"][:], AF.Exp,
                scale=sb["A"][:, col:col + 1]),
                (f"dtt{j}", "A"), (dA,))
        vb = sb[f"vt{j}"][:].unsqueeze(1).broadcast_to([128, 4, LC])
        C("gpsimd", lambda E, q=q, dB=dB, vb=vb: E.tensor_tensor(
            sb[dB][:].rearrange("p (n l) -> p n l", n=4),
            sb["Bp"][:, q * 4 * LC:(q + 1) * 4 * LC].rearrange("p (n l) -> p n l", n=4),
            vb, ALU.mult),
            tuple([f"vt{j}"] + [f"Bp{q * 4 + z}" for z in range(4)]), (dB,))
        colb0 = i * NST + h * 8 + q * 4
        if k > 0:
            dA_f = sb[dA][:].rearrange("p (n l) -> p n l", n=4)[:, :, 0]
            dB_f = sb[dB][:].rearrange("p (n l) -> p n l", n=4)[:, :, 0]
            C("gpsimd", lambda E, dA_f=dA_f, colb0=colb0: E.tensor_tensor(
                sb["tc4"][:], dA_f, sb["carry"][:, colb0:colb0 + 4], ALU.mult),
                tuple([dA] + [f"carry{colb0 + z}" for z in range(4)]), ("tc4",))
            C("gpsimd", lambda E, dB_f=dB_f: E.tensor_tensor(
                dB_f, dB_f, sb["tc4"][:], ALU.add),
                (dB, "tc4"), (dB,))
        for jj in range(4):
            rd = [dA, dB]
            C("vector", lambda E, jj=jj, dA=dA, dB=dB: E.tensor_tensor_scan(
                sb["hp"][:, jj * LC:(jj + 1) * LC],
                sb[dA][:, jj * LC:(jj + 1) * LC],
                sb[dB][:, jj * LC:(jj + 1) * LC],
                0.0, ALU.mult, ALU.add),
                tuple(rd), ("hp",))
        colb = i * NST + h * 8 + q * 4
        C("vector", lambda E, colb=colb: E.tensor_tensor(
            sb["carry"][:, colb:colb + 3],
            sb["hp"][:, 0:3 * LC].rearrange("p (n l) -> p n l", n=3)[:, :, LC - 1],
            sb["zero4"][:, 0:3], ALU.add),
            ("hp", "zero4"), tuple(f"carry{colb + z}" for z in range(3)))
        C("vector", lambda E, colb=colb: E.tensor_tensor(
            sb["carry"][:, colb + 3:colb + 4],
            sb["hp"][:, 4 * LC - 1:4 * LC],
            sb["zero4"][:, 0:1], ALU.add),
            ("hp", "zero4"), (f"carry{colb + 3}",))
        C("vector", lambda E, q=q: E.tensor_tensor(
            sb["hp"][:], sb["hp"][:], sb["Cp"][:, q * 4 * LC:(q + 1) * 4 * LC], ALU.mult),
            tuple(["hp"] + [f"Cp{q * 4 + z}" for z in range(4)]), ("hp",))
        C("vector", lambda E: E.tensor_tensor(
            sb["hp"][:, 0:2 * LC], sb["hp"][:, 0:2 * LC], sb["hp"][:, 2 * LC:4 * LC], ALU.add),
            ("hp",), ("hp",))
        j4 = i % 4
        if q == 0:
            C("vector", lambda E, j4=j4: E.tensor_tensor(
                sb[f"yh{j4}"][:], sb["hp"][:, 0:LC], sb["hp"][:, LC:2 * LC], ALU.add),
                ("hp",), (f"yh{j4}",))
        else:
            C("vector", lambda E: E.tensor_tensor(
                sb["hp"][:, 2 * LC:3 * LC], sb["hp"][:, 0:LC], sb["hp"][:, LC:2 * LC], ALU.add),
                ("hp",), ("hp",))
            C("vector", lambda E, j4=j4: E.tensor_tensor(
                sb[f"yh{j4}"][:], sb[f"yh{j4}"][:], sb["hp"][:, 2 * LC:3 * LC], ALU.add),
                ("hp", f"yh{j4}"), (f"yh{j4}",))

    pend_stores = []
    pend_ffsq = []
    pend_rsq = []

    def flush_stores(keep=0):
        while len(pend_stores) > keep:
            pend_stores.pop(0)()

    def flush_q(q, keep=0):
        while len(q) > keep:
            q.pop(0)()

    def dtile_end(k, h, i):
        flush_q(pend_rsq, keep=0)
        flush_stores(keep=2)
        j = i % 2
        j4 = i % 4
        if h == 0:
            pend_stores.append(lambda i=i, j4=j4, k=k: dma(
                "gpsimd", yscr[i * 128:(i + 1) * 128, :], sb[f"yh{j4}"][:],
                (f"yh{j4}",), (f"yscr{i}",), fam=f"yscr{k % 2}"))
        else:
            dma("sync", sb["ysc"][:], yscr[i * 128:(i + 1) * 128, :],
                (f"yscr{i}",), ("ysc",), fam="yscl")
            C("vector", lambda E, j4=j4: E.tensor_tensor(sb[f"yh{j4}"][:], sb[f"yh{j4}"][:], sb["ysc"][:], ALU.add),
              (f"yh{j4}", "ysc"), (f"yh{j4}",))
            C("vector", lambda E, j=j, j4=j4, i=i: E.scalar_tensor_tensor(
                sb[f"res{j4}"][:], sb[f"xot{j}"][:], sb["onepD"][:, i:i + 1], sb[f"yh{j4}"][:],
                ALU.mult, ALU.add),
                (f"xot{j}", "onepD", f"yh{j4}"), (f"res{j4}",))
            pend_stores.append(lambda i=i, j4=j4, k=k: dma(
                "gpsimd", ag1i[k][i * 128:(i + 1) * 128, :], sb[f"res{j4}"][:],
                (f"res{j4}",), (f"ag1w{k}_{i}",), fam=f"agw{k % 2}"))
            pend_rsq.append(lambda j4=j4: C(
                "scalar", lambda E, j4=j4: E.activation(sb[f"yh{j4}"][:], sb[f"res{j4}"][:], AF.Square),
                (f"res{j4}",), (f"yh{j4}",)))
            pend_res.append((i, j4))
            if DEBUG:
                pend_stores.append(lambda i=i, j4=j4, k=k: dma(
                    "gpsimd", dbg_res[i * 128:(i + 1) * 128, k * LC:(k + 1) * LC], sb[f"res{j4}"][:],
                    (f"res{j4}",), (f"dbgres{k}_{i}",), fam="dbg"))

    def ag1(k):
        flush_q(pend_rsq, keep=0)
        flush_stores()
        while pend_res:
            emit_one_res()
        C("scalar", lambda E: E.activation(sb["stat_sb"][0:1, :], ps["bst"][32:33, :], AF.Copy),
          ("st0", "st1", "st"), ("stat1a", "stat_sb"))
        C("scalar", lambda E: E.activation(sb["stat_sb"][32:33, :], ps["bst"][64:65, :], AF.Copy),
          ("st0", "st1", "st"), ("stat1b", "stat_sb"))
        dma("scalar", ag1i[k][DH:DH + 1, :], sb["stat_sb"][0:1, :], ("stat1a",), (f"ag1w{k}_s0",), fam=f"agw{k % 2}")
        dma("scalar", ag1i[k][DH + 1:DH + 2, :], sb["stat_sb"][32:33, :], ("stat1b",), (f"ag1w{k}_s1",), fam=f"agw{k % 2}")
        reads = tuple(f"ag1w{k}_{i}" for i in range(NDT)) + (f"ag1w{k}_s0", f"ag1w{k}_s1")
        P.op("gpsimd", lambda E: E.collective_compute(
            "AllGather", ALU.bypass, replica_groups=RG, ins=[ag1i[k]], outs=[ag1o[k]]),
            reads=reads, writes=(f"gath{k}",), kind="cc", name=f"ag1_{k}", fam=f"cag{k % 2}")

    def ln_stats(m_name, r_name, width=LC):
        stv = sb["stv"]
        s0 = stv[:, 0:width]; s1 = stv[:, LC:LC + width]; s2 = stv[:, 2 * LC:2 * LC + width]
        C("vector", lambda E: E.tensor_single_scalar(s0, s0, 1.0 / D, ALU.mult),
          ("stv0",), ("stv0",))
        C("tensor", lambda E, w=width: E.matmul(ps["misc"][:, 0:w], sb["ones_r"][0:1, :], s0[0:1, :], start=True, stop=True),
          ("stv0", "ones_r"), ("misc",))
        C("scalar", lambda E, w=width: E.activation(sb["mbc"][:, 0:w], ps["misc"][:, 0:w], AF.Copy),
          ("misc",), (m_name, "mbc"))
        C("vector", lambda E: E.tensor_single_scalar(s1, s1, 1.0 / D, ALU.mult),
          ("stv1",), ("stv1",))
        C("vector", lambda E: E.tensor_tensor(s2, s0, s0, ALU.mult),
          ("stv0",), ("stv2",))
        C("vector", lambda E: E.tensor_tensor(s1, s1, s2, ALU.subtract),
          ("stv1", "stv2"), ("stv1",))
        C("vector", lambda E: E.tensor_single_scalar(s1, s1, EPS, ALU.add),
          ("stv1",), ("stv1",))
        C("scalar", lambda E: E.activation(s2, s1, AF.Ln),
          ("stv1",), ("stv2",))
        C("scalar", lambda E: E.activation(s2, s2, AF.Exp, scale=-0.5),
          ("stv2",), ("stv2",))
        C("vector", lambda E: E.tensor_tensor(s0, s2, s2, ALU.mult),
          ("stv2",), ("stv0",))
        C("vector", lambda E: E.tensor_tensor(s0, s0, s1, ALU.mult),
          ("stv0", "stv1"), ("stv0",))
        C("vector", lambda E: E.tensor_scalar(s0, s0, -0.5, 1.5, ALU.mult, ALU.add),
          ("stv0",), ("stv0",))
        C("vector", lambda E: E.tensor_tensor(s2, s2, s0, ALU.mult),
          ("stv2", "stv0"), ("stv2",))
        C("tensor", lambda E, w=width: E.matmul(ps["misc"][:, 0:w], sb["ones_r"][0:1, :], s2[0:1, :], start=True, stop=True),
          ("stv2", "ones_r"), ("misc",))
        C("scalar", lambda E, w=width: E.activation(sb["rbc"][:, 0:w], ps["misc"][:, 0:w], AF.Copy),
          ("misc",), (r_name, "rbc"))

    def sparsemax_ops(sbi):
        pool = sb["sortp"]
        W2 = 2
        def sl(s):
            return pool[:, s * W2:(s + 1) * W2]
        slots = {}
        free = list(range(24))
        for ee in range(8):
            s0 = free.pop(0)
            slots[ee] = s0
            C("vector", lambda E, ee=ee, s0=s0: E.tensor_copy(sl(s0), sb["lgt"][:, ee * W2:(ee + 1) * W2]),
              ("lgt",), (f"sp{s0}",))
        for (i, jx) in SORT8:
            si, sj = slots[i], slots[jx]
            a = free.pop(0); b2 = free.pop(0)
            C("vector", lambda E, si=si, sj=sj, a=a: E.tensor_tensor(sl(a), sl(si), sl(sj), ALU.max),
              (f"sp{si}", f"sp{sj}"), (f"sp{a}",))
            C("vector", lambda E, si=si, sj=sj, b2=b2: E.tensor_tensor(sl(b2), sl(si), sl(sj), ALU.min),
              (f"sp{si}", f"sp{sj}"), (f"sp{b2}",))
            free.append(si); free.append(sj)
            slots[i], slots[jx] = a, b2
        cum = {}
        c0s = free.pop(0)
        cum[0] = c0s
        C("vector", lambda E, s=slots[0], c=c0s: E.tensor_copy(sl(c), sl(s)),
          (f"sp{slots[0]}",), (f"sp{c0s}",))
        for ee in range(1, 8):
            cs = free.pop(0)
            C("vector", lambda E, p=cum[ee - 1], z=slots[ee], cs=cs: E.tensor_tensor(sl(cs), sl(p), sl(z), ALU.add),
              (f"sp{cum[ee - 1]}", f"sp{slots[ee]}"), (f"sp{cs}",))
            cum[ee] = cs
        tsl = free.pop(0); msl = free.pop(0); ksl = free.pop(0)
        for ee in range(8):
            C("vector", lambda E, z=slots[ee], c=cum[ee], t=tsl, ee=ee: E.scalar_tensor_tensor(
                sl(t), sl(z), float(ee + 1), sl(c), ALU.mult, ALU.subtract),
                (f"sp{slots[ee]}", f"sp{cum[ee]}"), (f"sp{tsl}",))
            C("vector", lambda E, t=tsl, m=msl: E.tensor_single_scalar(sl(m), sl(t), -1.0, ALU.is_gt),
              (f"sp{tsl}",), (f"sp{msl}",))
            if ee == 0:
                C("vector", lambda E, m=msl, ks=ksl: E.tensor_copy(sl(ks), sl(m)),
                  (f"sp{msl}",), (f"sp{ksl}",))
            else:
                C("vector", lambda E, m=msl, ks=ksl: E.tensor_tensor(sl(ks), sl(ks), sl(m), ALU.add),
                  (f"sp{msl}", f"sp{ksl}"), (f"sp{ksl}",))
        acc, eq = tsl, msl
        t2 = free[0]
        for ee in range(8):
            C("vector", lambda E, ks=ksl, eqs=eq, ee=ee: E.tensor_single_scalar(sl(eqs), sl(ks), float(ee + 1), ALU.is_equal),
              (f"sp{ksl}",), (f"sp{eq}",))
            if ee == 0:
                C("vector", lambda E, c=cum[ee], eqs=eq, a=acc: E.tensor_tensor(sl(a), sl(c), sl(eqs), ALU.mult),
                  (f"sp{cum[ee]}", f"sp{eq}"), (f"sp{acc}",))
            else:
                C("vector", lambda E, c=cum[ee], eqs=eq, t2=t2: E.tensor_tensor(sl(t2), sl(c), sl(eqs), ALU.mult),
                  (f"sp{cum[ee]}", f"sp{eq}"), (f"sp{t2}",))
                C("vector", lambda E, a=acc, t2=t2: E.tensor_tensor(sl(a), sl(a), sl(t2), ALU.add),
                  (f"sp{acc}", f"sp{t2}"), (f"sp{acc}",))
        C("vector", lambda E, ks=ksl, eqs=eq: E.reciprocal(sl(eqs), sl(ks)),
          (f"sp{ksl}",), (f"sp{eq}",))
        C("vector", lambda E, a=acc: E.tensor_single_scalar(sl(a), sl(a), -1.0, ALU.add),
          (f"sp{acc}",), (f"sp{acc}",))
        C("vector", lambda E, a=acc, eqs=eq: E.tensor_tensor(sl(a), sl(a), sl(eqs), ALU.mult),
          (f"sp{acc}", f"sp{eq}"), (f"sp{acc}",))
        for ee in range(8):
            C("vector", lambda E, ee=ee, a=acc: E.tensor_tensor(
                sb["wfin"][:, ee * W2:(ee + 1) * W2], sb["lgt"][:, ee * W2:(ee + 1) * W2], sl(a), ALU.subtract),
                (f"sp{acc}", "lgt"), (f"wf{ee}",))
            C("vector", lambda E, ee=ee: E.tensor_single_scalar(
                sb["wfin"][:, ee * W2:(ee + 1) * W2], sb["wfin"][:, ee * W2:(ee + 1) * W2], 0.0, ALU.max),
                (f"wf{ee}",), (f"wf{ee}", "wfin"))

    def ffn_prep(k, sbi):
        cs = sbi * SBT
        stv = sb["stv"]
        if sbi == 0:
            dma("scalar", stv[0:1, 0:LC], ag1o[k][DH:DH + 1, :], (f"gath{k}",), ("stv0",), fam="stv")
            dma("scalar", stv[0:1, LC:2 * LC], ag1o[k][DH + 1:DH + 2, :], (f"gath{k}",), ("stv1",), fam="stv")
            dma("scalar", stv[0:1, 2 * LC:3 * LC], ag1o[k][2 * DH + 2:2 * DH + 3, :], (f"gath{k}",), ("stv2",), fam="stv")
            C("vector", lambda E: E.tensor_tensor(stv[0:1, 0:LC], stv[0:1, 0:LC], stv[0:1, 2 * LC:3 * LC], ALU.add),
              ("stv0", "stv2"), ("stv0",))
            dma("scalar", stv[0:1, 2 * LC:3 * LC], ag1o[k][2 * DH + 3:2 * DH + 4, :], (f"gath{k}",), ("stv2",), fam="stv")
            C("vector", lambda E: E.tensor_tensor(stv[0:1, LC:2 * LC], stv[0:1, LC:2 * LC], stv[0:1, 2 * LC:3 * LC], ALU.add),
              ("stv1", "stv2"), ("stv1",))
            if DEBUG:
                dma("scalar", dbg_st[k, 0:1, :], stv[0:1, 0:LC], ("stv0",), (f"dbgst{k}a",), fam="dbg")
                dma("scalar", dbg_st[k, 1:2, :], stv[0:1, LC:2 * LC], ("stv1",), (f"dbgst{k}b",), fam="dbg")
            ln_stats("mbc_v", "rbc_v", width=LC)
        for t in range(NKT):
            j = t % 2
            row = t * 128 + (0 if t < NDT else 2)
            dma("sync", sb[f"gt{j}"][:], ag1o[k][row:row + 128, cs:cs + SBT], (f"gath{k}",), (f"gt{j}",), fam=f"gt{j}")
            C("vector", lambda E, j=j, cs=cs: E.tensor_tensor(sb[f"gt{j}"][:], sb[f"gt{j}"][:], sb["mbc"][:, cs:cs + SBT], ALU.subtract),
              (f"gt{j}", "mbc_v", "mbc"), (f"gt{j}",))
            C("vector", lambda E, j=j, cs=cs: E.tensor_tensor(sb[f"gt{j}"][:], sb[f"gt{j}"][:], sb["rbc"][:, cs:cs + SBT], ALU.mult),
              (f"gt{j}", "rbc_v", "rbc"), (f"gt{j}",))
            C("scalar", lambda E, j=j, t=t: E.activation(sb["xn_hi"][:, t * SBT:(t + 1) * SBT], sb[f"gt{j}"][:], AF.Copy),
              (f"gt{j}",), (f"xnh{t}",))
            C("vector", lambda E, j=j, t=t: E.tensor_tensor(
                sb["xn_lo"][:, t * SBT:(t + 1) * SBT], sb[f"gt{j}"][:], sb["xn_hi"][:, t * SBT:(t + 1) * SBT], ALU.subtract),
                (f"gt{j}", f"xnh{t}"), (f"xnl{t}",))
            if DEBUG:
                dma("sync", dbg_xn[t * 128:(t + 1) * 128, k * LC + cs:k * LC + cs + SBT], sb[f"gt{j}"][:],
                    (f"gt{j}",), (f"dbgxn{k}_{sbi}_{t}",), fam="dbg")
        nm = [0]
        for t in range(NKT):
            for (wk, xk) in (("hi", "hi"), ("hi", "lo"), ("lo", "hi")):
                C("tensor", lambda E, t=t, wk=wk, xk=xk, s=(nm[0] == 0), st=(nm[0] == 95): E.matmul(
                    ps["rtl"][0:8, :], sb[f"wr_{wk}"][:, t * 8:(t + 1) * 8],
                    sb["xn_hi" if xk == "hi" else "xn_lo"][:, t * SBT:(t + 1) * SBT],
                    start=s, stop=st),
                    (f"wr_{wk}", f"xn{'h' if xk == 'hi' else 'l'}{t}"), ("rtl",))
                nm[0] += 1
        C("scalar", lambda E: E.activation(sb["rt8"][:], ps["rtl"][0:8, :], AF.Identity, bias=sb["br"][:, 0:1]),
          ("rtl", "br"), ("rt8",))
        dma("sync", wla[:, 0:4096], lah_hi, ("lahH", "wla0", "wla1"), ("wla0", "wla1"), fam="wla")
        dma("sync", wla[:, 4096:8192], lah_lo, ("lahL", "wla2", "wla3"), ("wla2", "wla3"), fam="wla")
        nm = [0]
        for t in range(NKT):
            for (wk, xk) in (("hi", "hi"), ("hi", "lo"), ("lo", "hi")):
                off = t * 128 if wk == "hi" else 4096 + t * 128
                rk = ("wla0", "wla1") if wk == "hi" else ("wla2", "wla3")
                C("tensor", lambda E, off=off, t=t, xk=xk, s=(nm[0] == 0), st=(nm[0] == 95): E.matmul(
                    ps["rtl"][:], wla[:, off:off + 128],
                    sb["xn_hi" if xk == "hi" else "xn_lo"][:, t * SBT:(t + 1) * SBT],
                    start=s, stop=st),
                    rk + (f"xn{'h' if xk == 'hi' else 'l'}{t}", "rt8"), ("rtl",))
                nm[0] += 1
        C("scalar", lambda E: E.activation(sb["lh"][:], ps["rtl"][:], AF.Copy),
          ("rtl",), ("lh",))
        lgt_v = sb["lgt"][:].rearrange("p (e g) -> p e g", g=2)
        for g in range(2):
            C("tensor", lambda E, g=g: E.transpose(ps["misc"][:, g * 8:(g + 1) * 8],
                                                   sb["rt8"][:, g * 128:(g + 1) * 128], sb["idn"][0:8, 0:8]),
              ("rt8", "idn"), ("misc",))
            C("scalar", lambda E, g=g, lv=lgt_v: E.activation(lv[:, :, g], ps["misc"][:, g * 8:(g + 1) * 8], AF.Copy),
              ("misc",), ("lgt",))
        sparsemax_ops(sbi)
        wf_keys = tuple(f"wf{ee}" for ee in range(8))
        wfin_v = sb["wfin"][:].rearrange("p (e g) -> p e g", g=2)
        for g in range(2):
            C("tensor", lambda E, g=g, wv=wfin_v: E.transpose(ps["misc"][0:8, 128 + g * 128:128 + (g + 1) * 128],
                                                              wv[:, :, g], sb["idn"][:]),
              wf_keys + ("idn",), ("misc",))
            C("scalar", lambda E, g=g: E.activation(sb["wt8"][:, g * 128:(g + 1) * 128],
                                                    ps["misc"][0:8, 128 + g * 128:128 + (g + 1) * 128], AF.Copy),
              ("misc",), ("wt8",))
        C("tensor", lambda E: E.matmul(ps["misc"][:, 0:SBT], sb["sel8"][:], sb["wt8"][:], start=True, stop=True),
          ("sel8", "wt8"), ("misc",))
        C("scalar", lambda E: E.activation(sb["wbc"][:], ps["misc"][:, 0:SBT], AF.Copy),
          ("misc",), ("wbc",))
        C("vector", lambda E: E.tensor_tensor(sb["lh"][:], sb["lh"][:], sb["wbc"][:], ALU.mult),
          ("lh", "wbc"), ("hw", "lh"))
        C("scalar", lambda E: E.activation(sb["hw_hi"][:], sb["lh"][:], AF.Copy),
          ("hw",), ("hw_hi",))
        C("vector", lambda E: E.tensor_tensor(sb["hw_lo"][:], sb["lh"][:], sb["hw_hi"][:], ALU.subtract),
          ("hw", "hw_hi"), ("hw_lo",))

    pend_stats = []
    pend_res = []

    def emit_res_stats():
        while pend_res:
            i, j = pend_res.pop(0)
            C("tensor", lambda E, j=j, s=(i == 0), t=(i == NDT - 1): E.matmul(
                ps["bst"][32:33, :], sb["ones_c"][:, 0:1], sb[f"res{j}"][:], start=s, stop=t),
                ("ones_c", f"res{j}"), ("st", "st0"))
            C("tensor", lambda E, j=j, s=(i == 0), t=(i == NDT - 1): E.matmul(
                ps["bst"][64:65, :], sb["ones_c"][:, 0:1], sb[f"yh{j}"][:], start=s, stop=t),
                ("ones_c", f"yh{j}"), ("st", "st1"))

    def emit_one_res():
        i, j = pend_res.pop(0)
        C("tensor", lambda E, j=j, s=(i == 0), t=(i == NDT - 1): E.matmul(
            ps["bst"][32:33, :], sb["ones_c"][:, 0:1], sb[f"res{j}"][:], start=s, stop=t),
            ("ones_c", f"res{j}"), ("st", "st0"))
        C("tensor", lambda E, j=j, s=(i == 0), t=(i == NDT - 1): E.matmul(
            ps["bst"][64:65, :], sb["ones_c"][:, 0:1], sb[f"yh{j}"][:], start=s, stop=t),
            ("ones_c", f"yh{j}"), ("st", "st1"))

    def emit_ff_stats(keep=2):
        while len(pend_stats) > keep:
            o, sbi, jo, sqn, cs = pend_stats.pop(0)
            C("tensor", lambda E, jo=jo, cs=cs, s=(o == 0), t=(o == NOT - 1): E.matmul(
                ps["xp2"][0:1, cs:cs + SBT], sb["ones_c"][:, 0:1], sb[f"fft{jo}"][:, 0:SBT], start=s, stop=t),
                ("ones_c", f"fft{jo}"), ("xp2", "xp2s"))
            C("tensor", lambda E, sqn=sqn, cs=cs, s=(o == 0), t=(o == NOT - 1): E.matmul(
                ps["xp2"][32:33, cs:cs + SBT], sb["ones_c"][:, 0:1], sb[sqn][:], start=s, stop=t),
                ("ones_c", sqn), ("xp2", "xp2q"))

    def ffn_otile(k, sbi, o):
        flush_q(pend_ffsq, keep=0)
        emit_ff_stats(keep=1)
        flush_stores(keep=2)
        while len(pend_res) > 2:
            emit_one_res()
        cs = sbi * SBT
        jo = o % 2
        fb = f"ff{jo}"
        dma("scalar", sb[f"rr{jo}"][:], ag1i[k][o * 128:(o + 1) * 128, cs:cs + SBT],
            (f"ag1w{k}_{o}",), (f"rr{jo}",), fam=f"rr{jo}")
        dma("sync", wla[:, 0:2048], wh_hi[o, :, 0:2048], (f"wbm{o}_0H", "wla0"), ("wla0",), fam="wla")
        dma("sync", wla[:, 4096:6144], wh_lo[o, :, 0:2048], (f"wbm{o}_0L", "wla2"), ("wla2",), fam="wla")
        dma("sync", wla[:, 2048:4096], wh_hi[o, :, 2048:4096], (f"wbm{o}_1H", "wla1"), ("wla1",), fam="wla")
        dma("sync", wla[:, 6144:8192], wh_lo[o, :, 2048:4096], (f"wbm{o}_1L", "wla3"), ("wla3",), fam="wla")
        dma("sync", sb[f"lbt_hi{jo}"][:], lbh_hi[:, o * 128:(o + 1) * 128], ("lbhH",), (f"lbt_hi{jo}",), fam=f"lbt{jo}")
        dma("sync", sb[f"lbt_lo{jo}"][:], lbh_lo[:, o * 128:(o + 1) * 128], ("lbhL",), (f"lbt_lo{jo}",), fam=f"lbt{jo}")
        nm = [0]
        for t in range(NKT):
            for (wk, xk) in (("hi", "hi"), ("hi", "lo"), ("lo", "hi")):
                off = (0 if wk == "hi" else 4096) + t * 128
                rk = "wla0" if off < 2048 else ("wla1" if off < 4096 else ("wla2" if off < 6144 else "wla3"))
                C("tensor", lambda E, off=off, t=t, xk=xk, s=(nm[0] == 0), fb=fb: E.matmul(
                    ps[fb][:], wla[:, off:off + 128],
                    sb["xn_hi" if xk == "hi" else "xn_lo"][:, t * SBT:(t + 1) * SBT],
                    start=s, stop=False),
                    (rk, f"xn{'h' if xk == 'hi' else 'l'}{t}"), (fb,))
                nm[0] += 1
        for pi, (wk, xk) in enumerate((("hi", "hi"), ("hi", "lo"), ("lo", "hi"))):
            C("tensor", lambda E, wk=wk, xk=xk, fb=fb, st=(pi == 2), jo=jo: E.matmul(
                ps[fb][:], sb[f"lbt_{wk}{jo}"][:], sb[f"hw_{xk}"][:],
                start=False, stop=st),
                (f"lbt_{wk}{jo}", f"hw_{xk}"), (fb,))
        jo4 = o % 4
        C("scalar", lambda E, o=o, fb=fb, jo4=jo4: E.activation(
            sb[f"fft{jo4}"][:], ps[fb][:], AF.Identity, bias=sb["bb"][:, o:o + 1]),
            (fb, "bb"), (f"fft{jo4}",))
        C("vector", lambda E, jo=jo, cs=cs: E.tensor_tensor(sb[f"rr{jo}"][:], sb[f"rr{jo}"][:], sb["mbc"][:, cs:cs + SBT], ALU.subtract),
          (f"rr{jo}", "mbc_v", "mbc"), (f"rr{jo}",))
        C("vector", lambda E, jo=jo, cs=cs: E.tensor_tensor(sb[f"rr{jo}"][:], sb[f"rr{jo}"][:], sb["rbc"][:, cs:cs + SBT], ALU.mult),
          (f"rr{jo}", "rbc_v", "rbc"), (f"rr{jo}",))
        C("vector", lambda E, jo4=jo4: E.tensor_tensor(sb[f"fft{jo4}"][:], sb[f"fft{jo4}"][:], sb[f"rr{jo}"][:], ALU.add),
          (f"fft{jo4}", f"rr{jo}"), (f"fft{jo4}",))
        pend_stores.append(lambda o=o, cs=cs, jo4=jo4, k=k, sbi=sbi: dma(
            "gpsimd", r2scr[o * 128:(o + 1) * 128, cs:cs + SBT], sb[f"fft{jo4}"][:],
            (f"fft{jo4}",), (f"r2_{o}_{sbi}",), fam=f"r2s{k % 2}"))
        sqn = "sq2" if jo == 0 else "sq2b"
        pend_ffsq.append(lambda jo4=jo4, sqn=sqn: C(
            "scalar", lambda E, jo4=jo4, sqn=sqn: E.activation(sb[sqn][:], sb[f"fft{jo4}"][:], AF.Square),
            (f"fft{jo4}",), (sqn,)))
        pend_stats.append((o, sbi, jo4, sqn, cs))

    def ffn_fin(k):
        c0 = k * LC
        flush_q(pend_ffsq, keep=0)
        flush_stores()
        emit_ff_stats(keep=0)
        C("scalar", lambda E: E.activation(sb["stat_sb"][0:1, :], ps["xp2"][0:1, :], AF.Copy),
          ("xp2s", "xp2q", "xp2"), ("stat_sb",))
        C("scalar", lambda E: E.activation(sb["stat_sb"][32:33, :], ps["xp2"][32:33, :], AF.Copy),
          ("xp2s", "xp2q", "xp2"), ("stat_sb",))
        dma("scalar", ag2i[k][0:1, :], sb["stat_sb"][0:1, :], ("stat_sb",), (f"ag2w{k}",), fam="ag2w")
        dma("scalar", ag2i[k][1:2, :], sb["stat_sb"][32:33, :], ("stat_sb",), (f"ag2w{k}b",), fam="ag2w")
        P.op("gpsimd", lambda E: E.collective_compute(
            "AllReduce", ALU.add, replica_groups=RG, ins=[ag2i[k]], outs=[ag2o[k]]),
            reads=(f"ag2w{k}", f"ag2w{k}b"), writes=(f"gath2_{k}",), kind="cc", name=f"ar2_{k}", fam=f"car2{k % 2}")
        stv = sb["stv"]
        dma("scalar", stv[:, 0:LC], ag2o[k][0:1, :], (f"gath2_{k}",), ("stv0",), fam="stv")
        dma("scalar", stv[:, LC:2 * LC], ag2o[k][1:2, :], (f"gath2_{k}",), ("stv1",), fam="stv")
        ln_stats("mbc_v", "rbc_v")
        for o in range(NOT):
            jo = o % 2
            dma("scalar", sb[f"r2l{jo}"][:], r2scr[o * 128:(o + 1) * 128, :],
                (f"r2_{o}_0", f"r2_{o}_1"), (f"xlr{jo}",), fam=f"finl{jo}")
            C("vector", lambda E, jo=jo: E.tensor_tensor(sb[f"r2l{jo}"][:], sb[f"r2l{jo}"][:], sb["mbc"][:], ALU.subtract),
              (f"xlr{jo}", "mbc_v", "mbc"), (f"xlr{jo}",))
            C("vector", lambda E, jo=jo: E.tensor_tensor(sb[f"r2l{jo}"][:], sb[f"r2l{jo}"][:], sb["rbc"][:], ALU.mult),
              (f"xlr{jo}", "rbc_v", "rbc"), (f"xlr{jo}",))
            dma("scalar", out_c[o * 128:(o + 1) * 128, c0:c0 + LC], sb[f"r2l{jo}"][:],
                (f"xlr{jo}",), (f"outw{k}_{o}",), fam="out")

    SERIAL = SERIAL_EMIT
    for k in range(NCH):
        if DEBUG and k == 1:
            dma("sync", dbg_cr, sb["carry"][:],
                tuple(f"carry{cc2}" for cc2 in range(NDT * NST)), ("dbgcr",), fam="dbg")
        scan_side(k)
        for h in range(2):
            bcast_half(k, h)
            if k > 0 and not SERIAL:
                ffn_prep(k - 1, h)
            dtile(k, h, 0)
            for i in range(NDT):
                if i + 1 < NDT:
                    dtile(k, h, i + 1)
                for q in range(2):
                    quarter(k, h, i, q)
                dtile_end(k, h, i)
                if k > 0 and not SERIAL:
                    ffn_otile(k - 1, h, i)
                elif k == 0:
                    while len(pend_res) > 2:
                        emit_one_res()
                    emit_wcast(2)
            if h == 1 and k > 0 and not SERIAL:
                ffn_fin(k - 1)
        ag1(k)
    emit_wcast(64)
    if SERIAL:
        for k in range(NCH):
            for sbi in range(2):
                ffn_prep(k, sbi)
                for o in range(NOT):
                    ffn_otile(k, sbi, o)
            ffn_fin(k)
    else:
        for sbi in range(2):
            ffn_prep(NCH - 1, sbi)
            for o in range(NOT):
                ffn_otile(NCH - 1, sbi, o)
        ffn_fin(NCH - 1)

    P.op("vector", lambda E: E.engine_nop(),
         reads=tuple(f"outw{k}_{o}" for k in range(NCH) for o in range(NOT)),
         writes=(), kind="c", name="fin")

    P.finalize()
    global LAST_PROG
    LAST_PROG = P
    P.check_deadlock()
    with nc.Block() as block:
        @block.sync
        def _(E):
            P.emit_engine("sync", E, chans)

        @block.vector
        def _(E):
            P.emit_engine("vector", E, chans)

        @block.scalar
        def _(E):
            P.emit_engine("scalar", E, chans)

        @block.gpsimd
        def _(E):
            P.emit_engine("gpsimd", E, chans)

        @block.tensor
        def _(E):
            P.emit_engine("tensor", E, chans)
    stack.close()
    return nc


def make_inputs(ins):
    x = np.ascontiguousarray(ins["L0_attn_x"])
    A_log = np.asarray(ins["L0_attn_A_log"])
    Dv = np.asarray(ins["L0_attn_D"])
    xpw = np.asarray(ins["L0_attn_x_proj_w"])
    dtw = np.asarray(ins["L0_attn_dt_proj_w"])
    dtb = np.asarray(ins["L0_attn_dt_proj_b"])
    Wb = np.asarray(ins["L0_ffn_W_base"])
    bb = np.asarray(ins["L0_ffn_b_base"])
    Wr = np.asarray(ins["L0_ffn_W_router"])
    br = np.asarray(ins["L0_ffn_b_router"])
    lA = np.asarray(ins["L0_ffn_lora_A"])
    lB = np.asarray(ins["L0_ffn_lora_B"])

    idn = np.eye(128, dtype=np.float32)
    sel8 = np.zeros((8, 128), np.float32)
    for ee in range(8):
        sel8[ee, ee * 16:(ee + 1) * 16] = 1.0
    xpwT = np.ascontiguousarray(xpw.T)
    lap = np.ascontiguousarray(lA.transpose(1, 0, 2).reshape(D, 128))
    lbp = np.ascontiguousarray(lB.reshape(128, D))
    WrT = np.ascontiguousarray(Wr.T)

    maps = []
    for c in range(8):
        b, h = c // 2, c % 2
        own = slice(h * DH, (h + 1) * DH)
        xb = x[b].T
        xt_c = np.concatenate([xb[own], xb[(1 - h) * DH:(2 - h) * DH]], axis=0)
        perm = np.concatenate([np.arange(h * DH, (h + 1) * DH), np.arange((1 - h) * DH, (2 - h) * DH)])
        xpw_c = xpwT[perm]
        xpw_t = np.ascontiguousarray(xpw_c.reshape(32, 128, XPR).transpose(1, 0, 2).reshape(128, 32 * XPR))
        dtw_c = np.ascontiguousarray(dtw[own].T)
        alog_c = np.ascontiguousarray(A_log[own].reshape(NDT, 128, NST).transpose(1, 0, 2).reshape(128, NDT * NST))
        d_c = np.ascontiguousarray(Dv[own].reshape(NDT, 128).T)
        dtb_c = np.ascontiguousarray(dtb[own].reshape(NDT, 128).T)
        Wslice = Wb[own, :]
        wbs_c = np.ascontiguousarray(
            Wslice.reshape(NOT, 128, NKT, 128).transpose(0, 3, 2, 1).reshape(NOT, 128, NKT * 128))
        bb_c = np.ascontiguousarray(bb[own].reshape(NOT, 128).T)
        wrt_c = np.ascontiguousarray(WrT.reshape(NKT, 128, 8).transpose(1, 0, 2).reshape(128, NKT * 8))
        las_c = np.ascontiguousarray(lap.reshape(NKT, 128, 128).transpose(1, 0, 2).reshape(128, NKT * 128))
        lbs_c = np.ascontiguousarray(lbp[:, own])
        maps.append({
            "xt": xt_c, "xpw": xpw_t, "dtwi": dtw_c, "alog": alog_c,
            "dvec": d_c, "dtbi": dtb_c, "wbs": wbs_c, "bbt": bb_c,
            "wrti": wrt_c, "brt": br.reshape(8, 1).astype(np.float32),
            "las": las_c, "lbsi": lbs_c, "idni": idn, "sel8i": sel8,
        })
    return maps


_NC_CACHE = [None]


def kernel(**inputs):
    maps = make_inputs(inputs)
    if _NC_CACHE[0] is None:
        _NC_CACHE[0] = build_nc()
    nc = _NC_CACHE[0]
    res = run_bass_kernel_spmd(nc, maps, core_ids=list(range(8)))
    out = np.empty((B, L, D), np.float32)
    for c in range(8):
        b, h = c // 2, c % 2
        oc = res.results[c]["out_c"]
        out[b, :, h * DH:(h + 1) * DH] = oc.T
    return out




# version 10
# speedup vs baseline: 1.2820x; 1.0329x over previous
import numpy as np
import concourse.bass as bass
import concourse.mybir as mybir
from concourse.bass_utils import run_bass_kernel_spmd

F32 = mybir.dt.float32
BF16 = mybir.dt.bfloat16
AF = mybir.ActivationFunctionType
ALU = mybir.AluOpType

B, L, D = 4, 2048, 4096
DH, NST, DTR, XPR = 2048, 16, 128, 160
NCH, LC = 4, 512
SBT = 256
NDT = 16
OH, NOT, NKT = 2048, 16, 32
EPS = 1e-5
RG = [[0, 1], [2, 3], [4, 5], [6, 7]]
SORT8 = [(0, 1), (2, 3), (4, 5), (6, 7), (0, 2), (1, 3), (4, 6), (5, 7),
         (1, 2), (5, 6), (0, 4), (3, 7), (1, 5), (2, 6), (1, 4), (3, 6),
         (2, 4), (3, 5), (3, 4)]
DEBUG = False
SERIAL_EMIT = False
LAST_PROG = None


class Prog:
    ENGINES = ("sync", "vector", "scalar", "gpsimd", "tensor")

    def __init__(self):
        self.ops = []
        self.last_w = {}
        self.readers = {}
        self.fams = {}
        self.fam_R = {}

    def op(self, eng, fn, reads=(), writes=(), kind="c", name="", fam=None, R=0):
        import sys as _sys
        fr = _sys._getframe(1)
        if fr.f_code.co_name in ("dma", "C"):
            fr = fr.f_back
        lbl = f"{fr.f_code.co_name}:{fr.f_lineno}"
        deps = set()
        for kk in reads:
            if kk in self.last_w:
                deps.add(self.last_w[kk])
        for kk in writes:
            for r in self.readers.get(kk, ()):
                deps.add(r)
            if kk in self.last_w:
                deps.add(self.last_w[kk])
        i = len(self.ops)
        if kind in ("d", "cc"):
            assert fam is not None, name
            self.fams.setdefault(fam, []).append(i)
            self.fam_R[fam] = max(self.fam_R.get(fam, 0), R)
        self.ops.append(dict(eng=eng, fn=fn, deps=deps, kind=kind, name=name, fam=fam, lbl=lbl))
        for kk in reads:
            self.readers.setdefault(kk, []).append(i)
        for kk in writes:
            self.last_w[kk] = i
            self.readers[kk] = []
        return i

    def finalize(self):
        ops = self.ops
        self.need_inc = [False] * len(ops)
        for o in ops:
            for d in o["deps"]:
                p = ops[d]
                if p["kind"] == "c" and p["eng"] == o["eng"]:
                    continue
                if p["kind"] == "c":
                    self.need_inc[d] = True
        cnt = {}
        self.val = [0] * len(ops)
        for i, o in enumerate(ops):
            if o["kind"] == "c" and self.need_inc[i]:
                ch = (o["eng"], "c")
                cnt[ch] = cnt.get(ch, 0) + 1
                self.val[i] = cnt[ch]
        self.fam_idx = {}
        for fam, lst in self.fams.items():
            for q, gi in enumerate(lst):
                self.fam_idx[gi] = q

    def check_deadlock(self):
        import bisect
        ops = self.ops
        streams = {en: [i for i in range(len(ops)) if ops[i]["eng"] == en] for en in self.ENGINES}
        pos = {en: 0 for en in self.ENGINES}
        cnt = {}

        def waits_of(i):
            o = ops[i]
            ws = []
            for d in o["deps"]:
                p = ops[d]
                if p["kind"] == "c":
                    if p["eng"] == o["eng"]:
                        continue
                    ws.append((("c", p["eng"]), self.val[d]))
                else:
                    fam = p["fam"]
                    lst = self.fams[fam]
                    cx = max(bisect.bisect_left(lst, i), self.fam_idx[d] + 1)
                    tgt = min(cx + self.fam_R[fam], len(lst))
                    if ops[i]["kind"] == p["kind"] and ops[i].get("fam") == fam:
                        tgt = min(tgt, self.fam_idx[i])
                    tgt = max(tgt, self.fam_idx[d] + 1)
                    step = 16 if p["kind"] == "d" else 1
                    ws.append((("f", fam), tgt * step))
            return ws

        progressed = True
        while progressed:
            progressed = False
            for en in self.ENGINES:
                while pos[en] < len(streams[en]):
                    i = streams[en][pos[en]]
                    if any(cnt.get(ch, 0) < v for ch, v in waits_of(i)):
                        break
                    o = ops[i]
                    if o["kind"] == "c":
                        if self.need_inc[i]:
                            ch = ("c", en)
                            cnt[ch] = cnt.get(ch, 0) + 1
                    else:
                        step = 16 if o["kind"] == "d" else 1
                        ch = ("f", o["fam"])
                        cnt[ch] = cnt.get(ch, 0) + step
                    pos[en] += 1
                    progressed = True
        stuck = {en: pos[en] for en in self.ENGINES if pos[en] < len(streams[en])}
        if stuck:
            for en, p in stuck.items():
                i = streams[en][p]
                o = ops[i]
                print(f"DEADLOCK: {en} pos {p}/{len(streams[en])} op {i} kind={o['kind']} fam={o['fam']} name={o['name']}")
                for d in sorted(o["deps"]):
                    pp = ops[d]
                    print(f"   dep {d}: eng={pp['eng']} kind={pp['kind']} fam={pp['fam']} name={pp['name']} val={self.val[d]}")
                print("   waits:", waits_of(i), "cur:", {ch: cnt.get(ch, 0) for ch, v in waits_of(i)})
            raise RuntimeError("deadlock detected")
        print("deadlock check: OK")

    def emit_engine(self, ename, eng, chans):
        ops = self.ops
        import bisect
        waited = {}
        for i, o in enumerate(ops):
            if o["eng"] != ename:
                continue
            for d in sorted(o["deps"]):
                p = ops[d]
                if p["kind"] == "c":
                    if p["eng"] == ename:
                        continue
                    ch = ("c", p["eng"])
                    v = self.val[d]
                    assert v > 0, (i, o["name"], d, p["name"])
                    sem = chans[(p["eng"], "c")]
                else:
                    fam = p["fam"]
                    lst = self.fams[fam]
                    cx = bisect.bisect_left(lst, i)
                    cx = max(cx, self.fam_idx[d] + 1)
                    tgt = min(cx + self.fam_R[fam], len(lst))
                    if o["kind"] == p["kind"] and o.get("fam") == fam:
                        tgt = min(tgt, self.fam_idx[i])
                    tgt = max(tgt, self.fam_idx[d] + 1)
                    step = 16 if p["kind"] == "d" else 1
                    ch = ("f", fam)
                    v = tgt * step
                    sem = chans[("fam", fam)]
                if waited.get(ch, 0) >= v:
                    continue
                eng.wait_ge(sem, v)
                waited[ch] = v
            inst = o["fn"](eng)
            if o["kind"] == "c":
                if self.need_inc[i]:
                    inst.then_inc(chans[(ename, "c")], 1)
            else:
                step = 16 if o["kind"] == "d" else 1
                inst.then_inc(chans[("fam", o["fam"])], step)


def build_nc():
    nc = bass.Bass("TRN2", target_bir_lowering=False)
    P = Prog()

    xt = nc.dram_tensor("xt", [D, L], F32, kind="ExternalInput").ap()
    xpw = nc.dram_tensor("xpw", [128, 32 * XPR], F32, kind="ExternalInput").ap()
    dtwi = nc.dram_tensor("dtwi", [128, DH], F32, kind="ExternalInput").ap()
    alog = nc.dram_tensor("alog", [128, NDT * NST], F32, kind="ExternalInput").ap()
    dvec = nc.dram_tensor("dvec", [128, NDT], F32, kind="ExternalInput").ap()
    dtbi = nc.dram_tensor("dtbi", [128, NDT], F32, kind="ExternalInput").ap()
    wbs = nc.dram_tensor("wbs", [NOT, 128, NKT * 128], F32, kind="ExternalInput").ap()
    bbt = nc.dram_tensor("bbt", [128, NOT], F32, kind="ExternalInput").ap()
    wrti = nc.dram_tensor("wrti", [128, NKT * 8], F32, kind="ExternalInput").ap()
    brt = nc.dram_tensor("brt", [8, 1], F32, kind="ExternalInput").ap()
    las = nc.dram_tensor("las", [128, NKT * 128], F32, kind="ExternalInput").ap()
    lbsi = nc.dram_tensor("lbsi", [128, OH], F32, kind="ExternalInput").ap()
    idni = nc.dram_tensor("idni", [128, 128], F32, kind="ExternalInput").ap()
    sel8i = nc.dram_tensor("sel8i", [8, 128], F32, kind="ExternalInput").ap()
    out_c = nc.dram_tensor("out_c", [OH, L], F32, kind="ExternalOutput").ap()
    if DEBUG:
        dbg_res = nc.dram_tensor("dbg_res", [DH, L], F32, kind="ExternalOutput").ap()
        dbg_xn = nc.dram_tensor("dbg_xn", [D, L], F32, kind="ExternalOutput").ap()
        dbg_cr = nc.dram_tensor("dbg_cr", [128, NDT * NST], F32, kind="ExternalOutput").ap()
        dbg_st = nc.dram_tensor("dbg_st", [NCH, 2, LC], F32, kind="ExternalOutput").ap()
        dbg_bp = nc.dram_tensor("dbg_bp", [128, 8 * LC], F32, kind="ExternalOutput").ap()
        dbg_cp = nc.dram_tensor("dbg_cp", [128, 8 * LC], F32, kind="ExternalOutput").ap()
        dbg_dt = nc.dram_tensor("dbg_dt", [128, LC], F32, kind="ExternalOutput").ap()
        dbg_v = nc.dram_tensor("dbg_v", [128, LC], F32, kind="ExternalOutput").ap()

    xph_hi = nc.dram_tensor("xph_hi", [128, 32 * XPR], BF16).ap()
    xph_lo = nc.dram_tensor("xph_lo", [128, 32 * XPR], BF16).ap()
    lbh_hi = nc.dram_tensor("lbh_hi", [128, OH], BF16).ap()
    lbh_lo = nc.dram_tensor("lbh_lo", [128, OH], BF16).ap()
    lah_hi = nc.dram_tensor("lah_hi", [128, NKT * 128], BF16).ap()
    lah_lo = nc.dram_tensor("lah_lo", [128, NKT * 128], BF16).ap()
    wh_hi = nc.dram_tensor("wh_hi", [NOT, 128, NKT * 128], BF16).ap()
    wh_lo = nc.dram_tensor("wh_lo", [NOT, 128, NKT * 128], BF16).ap()
    yscr = nc.dram_tensor("yscr", [DH, LC], F32).ap()
    bchbm = nc.dram_tensor("bchbm", [32, LC], F32).ap()
    r2scr = nc.dram_tensor("r2scr", [OH, LC], F32).ap()
    ag1ia = [nc.dram_tensor(f"ag1ia{k}", [DH // 2, LC], F32).ap() for k in range(NCH)]
    ag1ib = [nc.dram_tensor(f"ag1ib{k}", [DH // 2, LC], F32).ap() for k in range(NCH)]
    ag1oa = [nc.dram_tensor(f"ag1oa{k}", [DH, LC], F32).ap() for k in range(NCH)]
    ag1ob = [nc.dram_tensor(f"ag1ob{k}", [DH, LC], F32).ap() for k in range(NCH)]
    ag1si = [nc.dram_tensor(f"ag1si{k}", [2, LC], F32).ap() for k in range(NCH)]
    ag1so = [nc.dram_tensor(f"ag1so{k}", [2, LC], F32).ap() for k in range(NCH)]
    ag2i = [nc.dram_tensor(f"ag2i{k}", [2, LC], F32).ap() for k in range(NCH)]
    ag2o = [nc.dram_tensor(f"ag2o{k}", [2, LC], F32).ap() for k in range(NCH)]

    from contextlib import ExitStack
    stack = ExitStack()
    e = stack.enter_context

    sb = {}
    def S(name, shape, dt=F32):
        sb[name] = e(nc.sbuf_tensor(name, shape, dt))
        return sb[name]

    S("ones_r", [1, 128]); S("ones_c", [128, 1])
    S("idn", [128, 128]); S("sel8", [8, 128])
    S("A", [128, NDT * NST]); S("onepD", [128, NDT]); S("dtb", [128, NDT])
    S("bb", [128, NOT]); S("br", [8, 1])
    S("dtw_hi", [128, DH], BF16); S("dtw_lo", [128, DH], BF16)
    S("wr_hi", [128, NKT * 8], BF16); S("wr_lo", [128, NKT * 8], BF16)
    S("carry", [128, NDT * NST])
    for j in range(2):
        S(f"xpw_hi{j}", [128, 2 * XPR], BF16); S(f"xpw_lo{j}", [128, 2 * XPR], BF16)
    S("Bp", [128, 8 * LC]); S("Cp", [128, 8 * LC])
    S("dAz0", [128, 4 * LC]); S("dAz1", [128, 4 * LC])
    S("dBx0", [128, 4 * LC]); S("dBx1", [128, 4 * LC])
    S("hp", [128, 4 * LC])
    for j in range(2):
        S(f"xhi{j}", [128, LC], BF16); S(f"xlo{j}", [128, LC], BF16)
        S(f"dtt{j}", [128, LC]); S(f"xot{j}", [128, LC]); S(f"vt{j}", [128, LC])
    for j in range(4):
        S(f"yh{j}", [128, LC]); S(f"res{j}", [128, LC])
    S("ysc", [128, LC]); S("zero4", [128, 4]); S("tc4", [128, 4])
    S("lbt_hi0", [128, 128], BF16); S("lbt_lo0", [128, 128], BF16)
    S("lbt_hi1", [128, 128], BF16); S("lbt_lo1", [128, 128], BF16)
    S("dtin_hi", [128, LC], BF16); S("dtin_lo", [128, LC], BF16)
    S("bc_c", [32, LC])
    S("stat_sb", [64, LC])
    S("xn_hi", [128, NKT * SBT], BF16); S("xn_lo", [128, NKT * SBT], BF16)
    S("wla", [128, 8192], BF16)
    for j in range(2):
        S(f"gt{j}", [128, SBT])
        S(f"rr{j}", [128, SBT])
        S(f"r2l{j}", [128, LC])
    for j in range(4):
        S(f"fft{j}", [128, SBT])
    S("sq2", [128, SBT])
    S("sq2b", [128, SBT])
    S("stv", [1, 3 * LC])
    S("mbc", [128, LC]); S("rbc", [128, LC])
    S("rt8", [8, SBT]); S("lgt", [128, 16])
    S("sortp", [128, 48])
    S("wfin", [128, 16]); S("wt8", [8, SBT]); S("wbc", [128, SBT])
    S("lh", [128, SBT])
    S("hw_hi", [128, SBT], BF16); S("hw_lo", [128, SBT], BF16)

    wla = sb["wla"]
    wla_f32 = wla[:].bitcast(F32)
    wf32 = wla_f32[:, 0:2048]
    cst_hi = wla[:, 4096:6144]
    cst_lo = wla[:, 6144:8192]
    WQ = ("wla0", "wla1", "wla2", "wla3")

    ps = {}
    def PSUM(name, shape):
        ps[name] = e(nc.psum_tensor(name, shape, F32))
    PSUM("xp1", [128, LC])
    PSUM("xp1b", [128, LC])
    PSUM("bst", [128, LC])
    PSUM("xp2", [64, LC])
    PSUM("rtl", [128, SBT])
    PSUM("misc", [128, LC])
    PSUM("ff0", [128, SBT]); PSUM("ff1", [128, SBT])

    chans = {}
    for en in Prog.ENGINES:
        chans[(en, "c")] = e(nc.semaphore(f"c_{en}"))
    FAMS = ["consts", "xphw", "lahw", "wbmw", "wla", "xlr0", "xlr1", "xot0", "xot1", "xpw",
            "bp", "cp", "gt0", "gt1", "rr0", "rr1", "yscl", "bchbm", "stv", "out",
            "yscr0", "yscr1", "r2s0", "r2s1", "agw0a", "agw0b", "agw1a", "agw1b", "ag1sw", "arw", "ag2w",
            "cag0", "cag1", "car0", "car1", "car20", "car21", "dbg",
            "xpwh0", "xpwh1", "xpwl0", "xpwl1", "lbt0", "lbt1", "finl0", "finl1", "lbhw"]
    for f in FAMS:
        chans[("fam", f)] = e(nc.semaphore(f"f_{f}"))

    def dma(eng, out_ap, in_ap, reads, writes, name="", fam="consts", R=0):
        P.op(eng, lambda E, o=out_ap, i=in_ap: E.dma_start(out=o, in_=i),
             reads=reads, writes=writes, kind="d", name=name, fam=fam, R=R)

    C = P.op

    C("vector", lambda E: E.memset(sb["ones_r"][:], 1.0), (), ("ones_r",))
    C("vector", lambda E: E.memset(sb["ones_c"][:], 1.0), (), ("ones_c",))
    C("vector", lambda E: E.memset(sb["zero4"][:], 0.0), (), ("zero4",))
    dma("sync", sb["idn"][:], idni, (), ("idn",))
    dma("sync", sb["sel8"][:], sel8i, (), ("sel8",))
    dma("sync", sb["A"][:], alog, (), ("A",))
    C("scalar", lambda E: E.activation(sb["A"][:], sb["A"][:], AF.Exp), ("A",), ("A",))
    C("vector", lambda E: E.tensor_single_scalar(sb["A"][:], sb["A"][:], -1.0, ALU.mult), ("A",), ("A",))
    dma("sync", sb["onepD"][:], dvec, (), ("onepD",))
    C("vector", lambda E: E.tensor_single_scalar(sb["onepD"][:], sb["onepD"][:], 1.0, ALU.add), ("onepD",), ("onepD",))
    dma("sync", sb["dtb"][:], dtbi, (), ("dtb",))
    dma("sync", sb["bb"][:], bbt, (), ("bb",))
    dma("sync", sb["br"][:], brt, (), ("br",))

    def stage_cast(src_ap, cols, hi_dst, hi_key, lo_dst, lo_key):
        dma("sync", wf32[:, 0:cols], src_ap, ("wla0", "wla1"), ("wla0", "wla1"), fam="wla")
        C("scalar", lambda E, c=cols, h=hi_dst: E.activation(h, wf32[:, 0:c], AF.Copy),
          ("wla0", "wla1"), (hi_key,))
        C("vector", lambda E, c=cols, h=hi_dst, l=lo_dst: E.tensor_tensor(l, wf32[:, 0:c], h, ALU.subtract),
          ("wla0", "wla1", hi_key), (lo_key,))

    stage_cast(dtwi, DH, sb["dtw_hi"][:], "dtw_hi", sb["dtw_lo"][:], "dtw_lo")
    stage_cast(wrti, NKT * 8, sb["wr_hi"][:], "wr_hi", sb["wr_lo"][:], "wr_lo")

    def cast_to_hbm(nm, src_ap, dst_hi, dst_lo, total):
        off = 0
        while off < total:
            w = min(2048, total - off)
            dma("sync", wf32[:, 0:w], src_ap[:, off:off + w], ("wla0", "wla1"), ("wla0", "wla1"), fam="wla")
            C("scalar", lambda E, w=w: E.activation(cst_hi[:, 0:w], wf32[:, 0:w], AF.Copy),
              ("wla0", "wla1"), ("wla2",))
            C("vector", lambda E, w=w: E.tensor_tensor(cst_lo[:, 0:w], wf32[:, 0:w], cst_hi[:, 0:w], ALU.subtract),
              ("wla0", "wla1", "wla2"), ("wla3",))
            dma("sync", dst_hi[:, off:off + w], cst_hi[:, 0:w], ("wla2",), (f"{nm}H",), fam=nm + "w")
            dma("sync", dst_lo[:, off:off + w], cst_lo[:, 0:w], ("wla3",), (f"{nm}L",), fam=nm + "w")
            off += w

    cast_to_hbm("xph", xpw, xph_hi, xph_lo, 32 * XPR)
    cast_to_hbm("lah", las, lah_hi, lah_lo, NKT * 128)
    cast_to_hbm("lbh", lbsi, lbh_hi, lbh_lo, OH)

    wblocks = [(o, hf) for o in range(NOT) for hf in range(2)]
    widx = [0]

    def emit_wcast(nblk):
        for _ in range(nblk):
            if widx[0] >= len(wblocks):
                return
            o, hf = wblocks[widx[0]]
            widx[0] += 1
            sl = slice(hf * 2048, (hf + 1) * 2048)
            dma("sync", wf32[:], wbs[o, :, sl], ("wla0", "wla1"), ("wla0", "wla1"), fam="wla")
            C("scalar", lambda E: E.activation(cst_hi[:], wf32[:], AF.Copy),
              ("wla0", "wla1"), ("wla2",))
            C("vector", lambda E: E.tensor_tensor(cst_lo[:], wf32[:], cst_hi[:], ALU.subtract),
              ("wla0", "wla1", "wla2"), ("wla3",))
            dma("sync", wh_hi[o, :, sl], cst_hi[:], ("wla2",), (f"wbm{o}_{hf}H",), fam="wbmw")
            dma("sync", wh_lo[o, :, sl], cst_lo[:], ("wla3",), (f"wbm{o}_{hf}L",), fam="wbmw")

    def scan_side(k):
        c0 = k * LC
        n_mm = [0]
        for pc in range(16):
            pb = pc % 2
            poff = pc * 2 * XPR
            dma("sync", sb[f"xpw_hi{pb}"][:], xph_hi[:, poff:poff + 2 * XPR], (f"xpw_hi{pb}",), (f"xpw_hi{pb}",), fam=f"xpwh{pb}")
            dma("sync", sb[f"xpw_lo{pb}"][:], xph_lo[:, poff:poff + 2 * XPR], (f"xpw_lo{pb}",), (f"xpw_lo{pb}",), fam=f"xpwl{pb}")
            for tl in range(2):
                t = pc * 2 + tl
                j = t % 2
                dma("sync" if j == 0 else "scalar", sb[f"r2l{j}"][:], xt[t * 128:(t + 1) * 128, c0:c0 + LC], (), (f"xlr{j}",), fam=f"xlr{j}")
                C("scalar", lambda E, j=j: E.activation(sb[f"xhi{j}"][:], sb[f"r2l{j}"][:], AF.Copy),
                  (f"xlr{j}",), (f"xhi{j}",))
                C("vector", lambda E, j=j: E.tensor_tensor(sb[f"xlo{j}"][:], sb[f"r2l{j}"][:], sb[f"xhi{j}"][:], ALU.subtract),
                  (f"xlr{j}", f"xhi{j}"), (f"xlo{j}",))
                for (wk, xk) in (("hi", "hi"), ("hi", "lo"), ("lo", "hi")):
                    s0 = (n_mm[0] == 0)
                    s1 = (n_mm[0] == 95)
                    C("tensor", lambda E, wk=wk, xk=xk, j=j, tl=tl, pb=pb, s0=s0, s1=s1: E.matmul(
                        ps["xp1"][:], sb[f"xpw_{wk}{pb}"][:, tl * XPR:tl * XPR + 128], sb[f"x{xk}{j}"][:],
                        start=s0, stop=s1),
                        (f"xpw_{wk}{pb}", f"x{xk}{j}"), ("xp1",))
                    C("tensor", lambda E, wk=wk, xk=xk, j=j, tl=tl, pb=pb, s0=s0, s1=s1: E.matmul(
                        ps["bst"][0:32, :], sb[f"xpw_{wk}{pb}"][:, tl * XPR + 128:(tl + 1) * XPR], sb[f"x{xk}{j}"][:],
                        start=s0, stop=s1),
                        (f"xpw_{wk}{pb}", f"x{xk}{j}"), ("bcb",))
                    n_mm[0] += 1
        C("scalar", lambda E: E.activation(sb["dtin_hi"][:], ps["xp1"][:], AF.Copy),
          ("xp1",), ("dtin_hi",))
        C("vector", lambda E: E.tensor_tensor(sb["dtin_lo"][:], ps["xp1"][:], sb["dtin_hi"][:], ALU.subtract),
          ("xp1", "dtin_hi"), ("dtin_lo",))
        C("scalar", lambda E: E.activation(sb["bc_c"][:], ps["bst"][0:32, :], AF.Copy),
          ("bcb",), ("bc_c",))
        dma("scalar", bchbm, sb["bc_c"][:], ("bc_c",), ("bchbm",), fam="bchbm")

    def bcast_half(k, h):
        srcB = bchbm[h * 8:(h + 1) * 8, :].flatten().unsqueeze(0).broadcast_to([128, 8 * LC])
        srcC = bchbm[16 + h * 8:16 + (h + 1) * 8, :].flatten().unsqueeze(0).broadcast_to([128, 8 * LC])
        dma("gpsimd", sb["Bp"][:], srcB, ("bchbm",), tuple(f"Bp{z}" for z in range(8)), fam="bp")
        dma("gpsimd", sb["Cp"][:], srcC, ("bchbm",), tuple(f"Cp{z}" for z in range(8)), fam="cp")
        if DEBUG and k == 1 and h == 0:
            dma("sync", dbg_bp, sb["Bp"][:], tuple(f"Bp{z}" for z in range(8)), ("dbgbp",), fam="dbg")
            dma("sync", dbg_cp, sb["Cp"][:], tuple(f"Cp{z}" for z in range(8)), ("dbgcp",), fam="dbg")

    def dtile(k, h, i):
        j = i % 2
        pn = "xp1" if i % 2 == 0 else "xp1b"
        for pi, (wk, xk) in enumerate((("hi", "hi"), ("hi", "lo"), ("lo", "hi"))):
            C("tensor", lambda E, wk=wk, xk=xk, i=i, pn=pn, s=(pi == 0), t=(pi == 2): E.matmul(
                ps[pn][:], sb[f"dtw_{wk}"][:, i * 128:(i + 1) * 128], sb[f"dtin_{xk}"][:],
                start=s, stop=t),
                (f"dtw_{wk}", f"dtin_{xk}"), (pn,))
        C("scalar", lambda E, i=i, j=j, pn=pn: E.activation(sb[f"dtt{j}"][:], ps[pn][:], AF.Exp, bias=sb["dtb"][:, i:i + 1]),
          (pn, "dtb"), (f"dtt{j}",))
        C("scalar", lambda E, j=j: E.activation(sb[f"dtt{j}"][:], sb[f"dtt{j}"][:], AF.Ln, bias=1.0),
          (f"dtt{j}",), (f"dtt{j}",))
        dma("sync", sb[f"xot{j}"][:], xt[i * 128:(i + 1) * 128, k * LC:(k + 1) * LC], (), (f"xot{j}",), fam=f"xot{j}")
        C("gpsimd", lambda E, j=j: E.tensor_tensor(sb[f"vt{j}"][:], sb[f"dtt{j}"][:], sb[f"xot{j}"][:], ALU.mult),
          (f"dtt{j}", f"xot{j}"), (f"vt{j}",))
        if DEBUG and k == 1 and h == 0 and i == 0:
            dma("sync", dbg_dt, sb[f"dtt{j}"][:], (f"dtt{j}",), ("dbgdt",), fam="dbg")
            dma("sync", dbg_v, sb[f"vt{j}"][:], (f"vt{j}",), ("dbgv",), fam="dbg")

    ucnt = [0]

    def quarter(k, h, i, q):
        j = i % 2
        ub = ucnt[0] % 2
        ucnt[0] += 1
        dA = f"dAz{ub}"
        dB = f"dBx{ub}"
        for jj in range(4):
            n = h * 8 + q * 4 + jj
            col = i * NST + n
            C("scalar", lambda E, jj=jj, col=col, j=j, dA=dA: E.activation(
                sb[dA][:, jj * LC:(jj + 1) * LC], sb[f"dtt{j}"][:], AF.Exp,
                scale=sb["A"][:, col:col + 1]),
                (f"dtt{j}", "A"), (dA,))
        vb = sb[f"vt{j}"][:].unsqueeze(1).broadcast_to([128, 4, LC])
        C("gpsimd", lambda E, q=q, dB=dB, vb=vb: E.tensor_tensor(
            sb[dB][:].rearrange("p (n l) -> p n l", n=4),
            sb["Bp"][:, q * 4 * LC:(q + 1) * 4 * LC].rearrange("p (n l) -> p n l", n=4),
            vb, ALU.mult),
            tuple([f"vt{j}"] + [f"Bp{q * 4 + z}" for z in range(4)]), (dB,))
        colb0 = i * NST + h * 8 + q * 4
        if k > 0:
            dA_f = sb[dA][:].rearrange("p (n l) -> p n l", n=4)[:, :, 0]
            dB_f = sb[dB][:].rearrange("p (n l) -> p n l", n=4)[:, :, 0]
            C("gpsimd", lambda E, dA_f=dA_f, colb0=colb0: E.tensor_tensor(
                sb["tc4"][:], dA_f, sb["carry"][:, colb0:colb0 + 4], ALU.mult),
                tuple([dA] + [f"carry{colb0 + z}" for z in range(4)]), ("tc4",))
            C("gpsimd", lambda E, dB_f=dB_f: E.tensor_tensor(
                dB_f, dB_f, sb["tc4"][:], ALU.add),
                (dB, "tc4"), (dB,))
        for jj in range(4):
            rd = [dA, dB]
            C("vector", lambda E, jj=jj, dA=dA, dB=dB: E.tensor_tensor_scan(
                sb["hp"][:, jj * LC:(jj + 1) * LC],
                sb[dA][:, jj * LC:(jj + 1) * LC],
                sb[dB][:, jj * LC:(jj + 1) * LC],
                0.0, ALU.mult, ALU.add),
                tuple(rd), ("hp",))
        colb = i * NST + h * 8 + q * 4
        C("vector", lambda E, colb=colb: E.tensor_tensor(
            sb["carry"][:, colb:colb + 3],
            sb["hp"][:, 0:3 * LC].rearrange("p (n l) -> p n l", n=3)[:, :, LC - 1],
            sb["zero4"][:, 0:3], ALU.add),
            ("hp", "zero4"), tuple(f"carry{colb + z}" for z in range(3)))
        C("vector", lambda E, colb=colb: E.tensor_tensor(
            sb["carry"][:, colb + 3:colb + 4],
            sb["hp"][:, 4 * LC - 1:4 * LC],
            sb["zero4"][:, 0:1], ALU.add),
            ("hp", "zero4"), (f"carry{colb + 3}",))
        C("vector", lambda E, q=q: E.tensor_tensor(
            sb["hp"][:], sb["hp"][:], sb["Cp"][:, q * 4 * LC:(q + 1) * 4 * LC], ALU.mult),
            tuple(["hp"] + [f"Cp{q * 4 + z}" for z in range(4)]), ("hp",))
        C("vector", lambda E: E.tensor_tensor(
            sb["hp"][:, 0:2 * LC], sb["hp"][:, 0:2 * LC], sb["hp"][:, 2 * LC:4 * LC], ALU.add),
            ("hp",), ("hp",))
        j4 = i % 4
        if q == 0:
            C("vector", lambda E, j4=j4: E.tensor_tensor(
                sb[f"yh{j4}"][:], sb["hp"][:, 0:LC], sb["hp"][:, LC:2 * LC], ALU.add),
                ("hp",), (f"yh{j4}",))
        else:
            C("vector", lambda E: E.tensor_tensor(
                sb["hp"][:, 2 * LC:3 * LC], sb["hp"][:, 0:LC], sb["hp"][:, LC:2 * LC], ALU.add),
                ("hp",), ("hp",))
            C("vector", lambda E, j4=j4: E.tensor_tensor(
                sb[f"yh{j4}"][:], sb[f"yh{j4}"][:], sb["hp"][:, 2 * LC:3 * LC], ALU.add),
                ("hp", f"yh{j4}"), (f"yh{j4}",))

    pend_stores = []
    pend_ffsq = []
    pend_rsq = []

    def flush_stores(keep=0):
        while len(pend_stores) > keep:
            pend_stores.pop(0)()

    def flush_q(q, keep=0):
        while len(q) > keep:
            q.pop(0)()

    def dtile_end(k, h, i):
        flush_q(pend_rsq, keep=0)
        flush_stores(keep=2)
        j = i % 2
        j4 = i % 4
        if h == 0:
            pend_stores.append(lambda i=i, j4=j4, k=k: dma(
                "gpsimd", yscr[i * 128:(i + 1) * 128, :], sb[f"yh{j4}"][:],
                (f"yh{j4}",), (f"yscr{i}",), fam=f"yscr{k % 2}"))
        else:
            dma("sync", sb["ysc"][:], yscr[i * 128:(i + 1) * 128, :],
                (f"yscr{i}",), ("ysc",), fam="yscl")
            C("vector", lambda E, j4=j4: E.tensor_tensor(sb[f"yh{j4}"][:], sb[f"yh{j4}"][:], sb["ysc"][:], ALU.add),
              (f"yh{j4}", "ysc"), (f"yh{j4}",))
            C("vector", lambda E, j=j, j4=j4, i=i: E.scalar_tensor_tensor(
                sb[f"res{j4}"][:], sb[f"xot{j}"][:], sb["onepD"][:, i:i + 1], sb[f"yh{j4}"][:],
                ALU.mult, ALU.add),
                (f"xot{j}", "onepD", f"yh{j4}"), (f"res{j4}",))
            tgt = ag1ia[k] if i < 8 else ag1ib[k]
            sfx = "a" if i < 8 else "b"
            pend_stores.append(lambda i=i, j4=j4, k=k, tgt=tgt, sfx=sfx: dma(
                "gpsimd", tgt[(i % 8) * 128:(i % 8 + 1) * 128, :], sb[f"res{j4}"][:],
                (f"res{j4}",), (f"ag1w{k}_{i}",), fam=f"agw{k % 2}{sfx}"))
            pend_rsq.append(lambda j4=j4: C(
                "scalar", lambda E, j4=j4: E.activation(sb[f"yh{j4}"][:], sb[f"res{j4}"][:], AF.Square),
                (f"res{j4}",), (f"yh{j4}",)))
            pend_res.append((i, j4))
            if DEBUG:
                pend_stores.append(lambda i=i, j4=j4, k=k: dma(
                    "gpsimd", dbg_res[i * 128:(i + 1) * 128, k * LC:(k + 1) * LC], sb[f"res{j4}"][:],
                    (f"res{j4}",), (f"dbgres{k}_{i}",), fam="dbg"))

    def ag1a(k):
        reads = tuple(f"ag1w{k}_{i}" for i in range(8))
        P.op("gpsimd", lambda E: E.collective_compute(
            "AllGather", ALU.bypass, replica_groups=RG, ins=[ag1ia[k]], outs=[ag1oa[k]]),
            reads=reads, writes=(f"gatha{k}",), kind="cc", name=f"ag1a_{k}", fam=f"cag{k % 2}")

    def ag1(k):
        flush_q(pend_rsq, keep=0)
        flush_stores()
        while pend_res:
            emit_one_res()
        C("scalar", lambda E: E.activation(sb["stat_sb"][0:1, :], ps["bst"][32:33, :], AF.Copy),
          ("st0", "st1", "st"), ("stat1a", "stat_sb"))
        C("scalar", lambda E: E.activation(sb["stat_sb"][32:33, :], ps["bst"][64:65, :], AF.Copy),
          ("st0", "st1", "st"), ("stat1b", "stat_sb"))
        dma("scalar", ag1si[k][0:1, :], sb["stat_sb"][0:1, :], ("stat1a",), (f"ag1w{k}_s0",), fam="ag1sw")
        dma("scalar", ag1si[k][1:2, :], sb["stat_sb"][32:33, :], ("stat1b",), (f"ag1w{k}_s1",), fam="ag1sw")
        P.op("gpsimd", lambda E: E.collective_compute(
            "AllReduce", ALU.add, replica_groups=RG, ins=[ag1si[k]], outs=[ag1so[k]]),
            reads=(f"ag1w{k}_s0", f"ag1w{k}_s1"), writes=(f"gaths{k}",), kind="cc", name=f"ag1s_{k}", fam=f"cag{k % 2}")
        reads = tuple(f"ag1w{k}_{i}" for i in range(8, NDT))
        P.op("gpsimd", lambda E: E.collective_compute(
            "AllGather", ALU.bypass, replica_groups=RG, ins=[ag1ib[k]], outs=[ag1ob[k]]),
            reads=reads, writes=(f"gathb{k}",), kind="cc", name=f"ag1b_{k}", fam=f"cag{k % 2}")

    def ln_stats(m_name, r_name, width=LC):
        stv = sb["stv"]
        s0 = stv[:, 0:width]; s1 = stv[:, LC:LC + width]; s2 = stv[:, 2 * LC:2 * LC + width]
        C("vector", lambda E: E.tensor_single_scalar(s0, s0, 1.0 / D, ALU.mult),
          ("stv0",), ("stv0",))
        C("tensor", lambda E, w=width: E.matmul(ps["misc"][:, 0:w], sb["ones_r"][0:1, :], s0[0:1, :], start=True, stop=True),
          ("stv0", "ones_r"), ("misc",))
        C("scalar", lambda E, w=width: E.activation(sb["mbc"][:, 0:w], ps["misc"][:, 0:w], AF.Copy),
          ("misc",), (m_name, "mbc"))
        C("vector", lambda E: E.tensor_single_scalar(s1, s1, 1.0 / D, ALU.mult),
          ("stv1",), ("stv1",))
        C("vector", lambda E: E.tensor_tensor(s2, s0, s0, ALU.mult),
          ("stv0",), ("stv2",))
        C("vector", lambda E: E.tensor_tensor(s1, s1, s2, ALU.subtract),
          ("stv1", "stv2"), ("stv1",))
        C("vector", lambda E: E.tensor_single_scalar(s1, s1, EPS, ALU.add),
          ("stv1",), ("stv1",))
        C("scalar", lambda E: E.activation(s2, s1, AF.Ln),
          ("stv1",), ("stv2",))
        C("scalar", lambda E: E.activation(s2, s2, AF.Exp, scale=-0.5),
          ("stv2",), ("stv2",))
        C("vector", lambda E: E.tensor_tensor(s0, s2, s2, ALU.mult),
          ("stv2",), ("stv0",))
        C("vector", lambda E: E.tensor_tensor(s0, s0, s1, ALU.mult),
          ("stv0", "stv1"), ("stv0",))
        C("vector", lambda E: E.tensor_scalar(s0, s0, -0.5, 1.5, ALU.mult, ALU.add),
          ("stv0",), ("stv0",))
        C("vector", lambda E: E.tensor_tensor(s2, s2, s0, ALU.mult),
          ("stv2", "stv0"), ("stv2",))
        C("tensor", lambda E, w=width: E.matmul(ps["misc"][:, 0:w], sb["ones_r"][0:1, :], s2[0:1, :], start=True, stop=True),
          ("stv2", "ones_r"), ("misc",))
        C("scalar", lambda E, w=width: E.activation(sb["rbc"][:, 0:w], ps["misc"][:, 0:w], AF.Copy),
          ("misc",), (r_name, "rbc"))

    def sparsemax_ops(sbi):
        pool = sb["sortp"]
        W2 = 2
        def sl(s):
            return pool[:, s * W2:(s + 1) * W2]
        slots = {}
        free = list(range(24))
        for ee in range(8):
            s0 = free.pop(0)
            slots[ee] = s0
            C("vector", lambda E, ee=ee, s0=s0: E.tensor_copy(sl(s0), sb["lgt"][:, ee * W2:(ee + 1) * W2]),
              ("lgt",), (f"sp{s0}",))
        for (i, jx) in SORT8:
            si, sj = slots[i], slots[jx]
            a = free.pop(0); b2 = free.pop(0)
            C("vector", lambda E, si=si, sj=sj, a=a: E.tensor_tensor(sl(a), sl(si), sl(sj), ALU.max),
              (f"sp{si}", f"sp{sj}"), (f"sp{a}",))
            C("vector", lambda E, si=si, sj=sj, b2=b2: E.tensor_tensor(sl(b2), sl(si), sl(sj), ALU.min),
              (f"sp{si}", f"sp{sj}"), (f"sp{b2}",))
            free.append(si); free.append(sj)
            slots[i], slots[jx] = a, b2
        cum = {}
        c0s = free.pop(0)
        cum[0] = c0s
        C("vector", lambda E, s=slots[0], c=c0s: E.tensor_copy(sl(c), sl(s)),
          (f"sp{slots[0]}",), (f"sp{c0s}",))
        for ee in range(1, 8):
            cs = free.pop(0)
            C("vector", lambda E, p=cum[ee - 1], z=slots[ee], cs=cs: E.tensor_tensor(sl(cs), sl(p), sl(z), ALU.add),
              (f"sp{cum[ee - 1]}", f"sp{slots[ee]}"), (f"sp{cs}",))
            cum[ee] = cs
        tsl = free.pop(0); msl = free.pop(0); ksl = free.pop(0)
        for ee in range(8):
            C("vector", lambda E, z=slots[ee], c=cum[ee], t=tsl, ee=ee: E.scalar_tensor_tensor(
                sl(t), sl(z), float(ee + 1), sl(c), ALU.mult, ALU.subtract),
                (f"sp{slots[ee]}", f"sp{cum[ee]}"), (f"sp{tsl}",))
            C("vector", lambda E, t=tsl, m=msl: E.tensor_single_scalar(sl(m), sl(t), -1.0, ALU.is_gt),
              (f"sp{tsl}",), (f"sp{msl}",))
            if ee == 0:
                C("vector", lambda E, m=msl, ks=ksl: E.tensor_copy(sl(ks), sl(m)),
                  (f"sp{msl}",), (f"sp{ksl}",))
            else:
                C("vector", lambda E, m=msl, ks=ksl: E.tensor_tensor(sl(ks), sl(ks), sl(m), ALU.add),
                  (f"sp{msl}", f"sp{ksl}"), (f"sp{ksl}",))
        acc, eq = tsl, msl
        t2 = free[0]
        for ee in range(8):
            C("vector", lambda E, ks=ksl, eqs=eq, ee=ee: E.tensor_single_scalar(sl(eqs), sl(ks), float(ee + 1), ALU.is_equal),
              (f"sp{ksl}",), (f"sp{eq}",))
            if ee == 0:
                C("vector", lambda E, c=cum[ee], eqs=eq, a=acc: E.tensor_tensor(sl(a), sl(c), sl(eqs), ALU.mult),
                  (f"sp{cum[ee]}", f"sp{eq}"), (f"sp{acc}",))
            else:
                C("vector", lambda E, c=cum[ee], eqs=eq, t2=t2: E.tensor_tensor(sl(t2), sl(c), sl(eqs), ALU.mult),
                  (f"sp{cum[ee]}", f"sp{eq}"), (f"sp{t2}",))
                C("vector", lambda E, a=acc, t2=t2: E.tensor_tensor(sl(a), sl(a), sl(t2), ALU.add),
                  (f"sp{acc}", f"sp{t2}"), (f"sp{acc}",))
        C("vector", lambda E, ks=ksl, eqs=eq: E.reciprocal(sl(eqs), sl(ks)),
          (f"sp{ksl}",), (f"sp{eq}",))
        C("vector", lambda E, a=acc: E.tensor_single_scalar(sl(a), sl(a), -1.0, ALU.add),
          (f"sp{acc}",), (f"sp{acc}",))
        C("vector", lambda E, a=acc, eqs=eq: E.tensor_tensor(sl(a), sl(a), sl(eqs), ALU.mult),
          (f"sp{acc}", f"sp{eq}"), (f"sp{acc}",))
        for ee in range(8):
            C("vector", lambda E, ee=ee, a=acc: E.tensor_tensor(
                sb["wfin"][:, ee * W2:(ee + 1) * W2], sb["lgt"][:, ee * W2:(ee + 1) * W2], sl(a), ALU.subtract),
                (f"sp{acc}", "lgt"), (f"wf{ee}",))
            C("vector", lambda E, ee=ee: E.tensor_single_scalar(
                sb["wfin"][:, ee * W2:(ee + 1) * W2], sb["wfin"][:, ee * W2:(ee + 1) * W2], 0.0, ALU.max),
                (f"wf{ee}",), (f"wf{ee}", "wfin"))

    def ffn_prep(k, sbi):
        cs = sbi * SBT
        stv = sb["stv"]
        if sbi == 0:
            dma("scalar", stv[0:1, 0:LC], ag1so[k][0:1, :], (f"gaths{k}",), ("stv0",), fam="stv")
            dma("scalar", stv[0:1, LC:2 * LC], ag1so[k][1:2, :], (f"gaths{k}",), ("stv1",), fam="stv")
            if DEBUG:
                dma("scalar", dbg_st[k, 0:1, :], stv[0:1, 0:LC], ("stv0",), (f"dbgst{k}a",), fam="dbg")
                dma("scalar", dbg_st[k, 1:2, :], stv[0:1, LC:2 * LC], ("stv1",), (f"dbgst{k}b",), fam="dbg")
            ln_stats("mbc_v", "rbc_v", width=LC)
        order = list(range(0, 8)) + list(range(16, 24)) + list(range(8, 16)) + list(range(24, 32))
        for pos, t in enumerate(order):
            j = pos % 2
            if t < 8:
                srcb, row, gk = ag1oa[k], t * 128, f"gatha{k}"
            elif t < 16:
                srcb, row, gk = ag1ob[k], (t - 8) * 128, f"gathb{k}"
            elif t < 24:
                srcb, row, gk = ag1oa[k], 1024 + (t - 16) * 128, f"gatha{k}"
            else:
                srcb, row, gk = ag1ob[k], 1024 + (t - 24) * 128, f"gathb{k}"
            dma("sync", sb[f"gt{j}"][:], srcb[row:row + 128, cs:cs + SBT], (gk,), (f"gt{j}",), fam=f"gt{j}")
            C("vector", lambda E, j=j, cs=cs: E.tensor_tensor(sb[f"gt{j}"][:], sb[f"gt{j}"][:], sb["mbc"][:, cs:cs + SBT], ALU.subtract),
              (f"gt{j}", "mbc_v", "mbc"), (f"gt{j}",))
            C("vector", lambda E, j=j, cs=cs: E.tensor_tensor(sb[f"gt{j}"][:], sb[f"gt{j}"][:], sb["rbc"][:, cs:cs + SBT], ALU.mult),
              (f"gt{j}", "rbc_v", "rbc"), (f"gt{j}",))
            C("scalar", lambda E, j=j, t=t: E.activation(sb["xn_hi"][:, t * SBT:(t + 1) * SBT], sb[f"gt{j}"][:], AF.Copy),
              (f"gt{j}",), (f"xnh{t}",))
            C("vector", lambda E, j=j, t=t: E.tensor_tensor(
                sb["xn_lo"][:, t * SBT:(t + 1) * SBT], sb[f"gt{j}"][:], sb["xn_hi"][:, t * SBT:(t + 1) * SBT], ALU.subtract),
                (f"gt{j}", f"xnh{t}"), (f"xnl{t}",))
            if DEBUG:
                dma("sync", dbg_xn[t * 128:(t + 1) * 128, k * LC + cs:k * LC + cs + SBT], sb[f"gt{j}"][:],
                    (f"gt{j}",), (f"dbgxn{k}_{sbi}_{t}",), fam="dbg")
        nm = [0]
        for t in range(NKT):
            for (wk, xk) in (("hi", "hi"), ("hi", "lo"), ("lo", "hi")):
                C("tensor", lambda E, t=t, wk=wk, xk=xk, s=(nm[0] == 0), st=(nm[0] == 95): E.matmul(
                    ps["rtl"][0:8, :], sb[f"wr_{wk}"][:, t * 8:(t + 1) * 8],
                    sb["xn_hi" if xk == "hi" else "xn_lo"][:, t * SBT:(t + 1) * SBT],
                    start=s, stop=st),
                    (f"wr_{wk}", f"xn{'h' if xk == 'hi' else 'l'}{t}"), ("rtl",))
                nm[0] += 1
        C("scalar", lambda E: E.activation(sb["rt8"][:], ps["rtl"][0:8, :], AF.Identity, bias=sb["br"][:, 0:1]),
          ("rtl", "br"), ("rt8",))
        dma("sync", wla[:, 0:4096], lah_hi, ("lahH", "wla0", "wla1"), ("wla0", "wla1"), fam="wla")
        dma("sync", wla[:, 4096:8192], lah_lo, ("lahL", "wla2", "wla3"), ("wla2", "wla3"), fam="wla")
        nm = [0]
        for t in range(NKT):
            for (wk, xk) in (("hi", "hi"), ("hi", "lo"), ("lo", "hi")):
                off = t * 128 if wk == "hi" else 4096 + t * 128
                rk = ("wla0", "wla1") if wk == "hi" else ("wla2", "wla3")
                C("tensor", lambda E, off=off, t=t, xk=xk, s=(nm[0] == 0), st=(nm[0] == 95): E.matmul(
                    ps["rtl"][:], wla[:, off:off + 128],
                    sb["xn_hi" if xk == "hi" else "xn_lo"][:, t * SBT:(t + 1) * SBT],
                    start=s, stop=st),
                    rk + (f"xn{'h' if xk == 'hi' else 'l'}{t}", "rt8"), ("rtl",))
                nm[0] += 1
        C("scalar", lambda E: E.activation(sb["lh"][:], ps["rtl"][:], AF.Copy),
          ("rtl",), ("lh",))
        lgt_v = sb["lgt"][:].rearrange("p (e g) -> p e g", g=2)
        for g in range(2):
            C("tensor", lambda E, g=g: E.transpose(ps["misc"][:, g * 8:(g + 1) * 8],
                                                   sb["rt8"][:, g * 128:(g + 1) * 128], sb["idn"][0:8, 0:8]),
              ("rt8", "idn"), ("misc",))
            C("scalar", lambda E, g=g, lv=lgt_v: E.activation(lv[:, :, g], ps["misc"][:, g * 8:(g + 1) * 8], AF.Copy),
              ("misc",), ("lgt",))
        sparsemax_ops(sbi)
        wf_keys = tuple(f"wf{ee}" for ee in range(8))
        wfin_v = sb["wfin"][:].rearrange("p (e g) -> p e g", g=2)
        for g in range(2):
            C("tensor", lambda E, g=g, wv=wfin_v: E.transpose(ps["misc"][0:8, 128 + g * 128:128 + (g + 1) * 128],
                                                              wv[:, :, g], sb["idn"][:]),
              wf_keys + ("idn",), ("misc",))
            C("scalar", lambda E, g=g: E.activation(sb["wt8"][:, g * 128:(g + 1) * 128],
                                                    ps["misc"][0:8, 128 + g * 128:128 + (g + 1) * 128], AF.Copy),
              ("misc",), ("wt8",))
        C("tensor", lambda E: E.matmul(ps["misc"][:, 0:SBT], sb["sel8"][:], sb["wt8"][:], start=True, stop=True),
          ("sel8", "wt8"), ("misc",))
        C("scalar", lambda E: E.activation(sb["wbc"][:], ps["misc"][:, 0:SBT], AF.Copy),
          ("misc",), ("wbc",))
        C("vector", lambda E: E.tensor_tensor(sb["lh"][:], sb["lh"][:], sb["wbc"][:], ALU.mult),
          ("lh", "wbc"), ("hw", "lh"))
        C("scalar", lambda E: E.activation(sb["hw_hi"][:], sb["lh"][:], AF.Copy),
          ("hw",), ("hw_hi",))
        C("vector", lambda E: E.tensor_tensor(sb["hw_lo"][:], sb["lh"][:], sb["hw_hi"][:], ALU.subtract),
          ("hw", "hw_hi"), ("hw_lo",))

    pend_stats = []
    pend_res = []

    def emit_res_stats():
        while pend_res:
            i, j = pend_res.pop(0)
            C("tensor", lambda E, j=j, s=(i == 0), t=(i == NDT - 1): E.matmul(
                ps["bst"][32:33, :], sb["ones_c"][:, 0:1], sb[f"res{j}"][:], start=s, stop=t),
                ("ones_c", f"res{j}"), ("st", "st0"))
            C("tensor", lambda E, j=j, s=(i == 0), t=(i == NDT - 1): E.matmul(
                ps["bst"][64:65, :], sb["ones_c"][:, 0:1], sb[f"yh{j}"][:], start=s, stop=t),
                ("ones_c", f"yh{j}"), ("st", "st1"))

    def emit_one_res():
        i, j = pend_res.pop(0)
        C("tensor", lambda E, j=j, s=(i == 0), t=(i == NDT - 1): E.matmul(
            ps["bst"][32:33, :], sb["ones_c"][:, 0:1], sb[f"res{j}"][:], start=s, stop=t),
            ("ones_c", f"res{j}"), ("st", "st0"))
        C("tensor", lambda E, j=j, s=(i == 0), t=(i == NDT - 1): E.matmul(
            ps["bst"][64:65, :], sb["ones_c"][:, 0:1], sb[f"yh{j}"][:], start=s, stop=t),
            ("ones_c", f"yh{j}"), ("st", "st1"))

    def emit_ff_stats(keep=2):
        while len(pend_stats) > keep:
            o, sbi, jo, sqn, cs = pend_stats.pop(0)
            C("tensor", lambda E, jo=jo, cs=cs, s=(o == 0), t=(o == NOT - 1): E.matmul(
                ps["xp2"][0:1, cs:cs + SBT], sb["ones_c"][:, 0:1], sb[f"fft{jo}"][:, 0:SBT], start=s, stop=t),
                ("ones_c", f"fft{jo}"), ("xp2", "xp2s"))
            C("tensor", lambda E, sqn=sqn, cs=cs, s=(o == 0), t=(o == NOT - 1): E.matmul(
                ps["xp2"][32:33, cs:cs + SBT], sb["ones_c"][:, 0:1], sb[sqn][:], start=s, stop=t),
                ("ones_c", sqn), ("xp2", "xp2q"))

    def ffn_otile(k, sbi, o):
        flush_q(pend_ffsq, keep=0)
        emit_ff_stats(keep=1)
        flush_stores(keep=2)
        while len(pend_res) > 2:
            emit_one_res()
        cs = sbi * SBT
        jo = o % 2
        fb = f"ff{jo}"
        rsrc = ag1ia[k] if o < 8 else ag1ib[k]
        dma("scalar", sb[f"rr{jo}"][:], rsrc[(o % 8) * 128:(o % 8 + 1) * 128, cs:cs + SBT],
            (f"ag1w{k}_{o}",), (f"rr{jo}",), fam=f"rr{jo}")
        dma("sync", wla[:, 0:2048], wh_hi[o, :, 0:2048], (f"wbm{o}_0H", "wla0"), ("wla0",), fam="wla")
        dma("sync", wla[:, 4096:6144], wh_lo[o, :, 0:2048], (f"wbm{o}_0L", "wla2"), ("wla2",), fam="wla")
        dma("sync", wla[:, 2048:4096], wh_hi[o, :, 2048:4096], (f"wbm{o}_1H", "wla1"), ("wla1",), fam="wla")
        dma("sync", wla[:, 6144:8192], wh_lo[o, :, 2048:4096], (f"wbm{o}_1L", "wla3"), ("wla3",), fam="wla")
        dma("sync", sb[f"lbt_hi{jo}"][:], lbh_hi[:, o * 128:(o + 1) * 128], ("lbhH",), (f"lbt_hi{jo}",), fam=f"lbt{jo}")
        dma("sync", sb[f"lbt_lo{jo}"][:], lbh_lo[:, o * 128:(o + 1) * 128], ("lbhL",), (f"lbt_lo{jo}",), fam=f"lbt{jo}")
        nm = [0]
        for t in range(NKT):
            for (wk, xk) in (("hi", "hi"), ("hi", "lo"), ("lo", "hi")):
                off = (0 if wk == "hi" else 4096) + t * 128
                rk = "wla0" if off < 2048 else ("wla1" if off < 4096 else ("wla2" if off < 6144 else "wla3"))
                C("tensor", lambda E, off=off, t=t, xk=xk, s=(nm[0] == 0), fb=fb: E.matmul(
                    ps[fb][:], wla[:, off:off + 128],
                    sb["xn_hi" if xk == "hi" else "xn_lo"][:, t * SBT:(t + 1) * SBT],
                    start=s, stop=False),
                    (rk, f"xn{'h' if xk == 'hi' else 'l'}{t}"), (fb,))
                nm[0] += 1
        for pi, (wk, xk) in enumerate((("hi", "hi"), ("hi", "lo"), ("lo", "hi"))):
            C("tensor", lambda E, wk=wk, xk=xk, fb=fb, st=(pi == 2), jo=jo: E.matmul(
                ps[fb][:], sb[f"lbt_{wk}{jo}"][:], sb[f"hw_{xk}"][:],
                start=False, stop=st),
                (f"lbt_{wk}{jo}", f"hw_{xk}"), (fb,))
        jo4 = o % 4
        C("scalar", lambda E, o=o, fb=fb, jo4=jo4: E.activation(
            sb[f"fft{jo4}"][:], ps[fb][:], AF.Identity, bias=sb["bb"][:, o:o + 1]),
            (fb, "bb"), (f"fft{jo4}",))
        C("vector", lambda E, jo=jo, cs=cs: E.tensor_tensor(sb[f"rr{jo}"][:], sb[f"rr{jo}"][:], sb["mbc"][:, cs:cs + SBT], ALU.subtract),
          (f"rr{jo}", "mbc_v", "mbc"), (f"rr{jo}",))
        C("vector", lambda E, jo=jo, cs=cs: E.tensor_tensor(sb[f"rr{jo}"][:], sb[f"rr{jo}"][:], sb["rbc"][:, cs:cs + SBT], ALU.mult),
          (f"rr{jo}", "rbc_v", "rbc"), (f"rr{jo}",))
        C("vector", lambda E, jo4=jo4: E.tensor_tensor(sb[f"fft{jo4}"][:], sb[f"fft{jo4}"][:], sb[f"rr{jo}"][:], ALU.add),
          (f"fft{jo4}", f"rr{jo}"), (f"fft{jo4}",))
        pend_stores.append(lambda o=o, cs=cs, jo4=jo4, k=k, sbi=sbi: dma(
            "gpsimd", r2scr[o * 128:(o + 1) * 128, cs:cs + SBT], sb[f"fft{jo4}"][:],
            (f"fft{jo4}",), (f"r2_{o}_{sbi}",), fam=f"r2s{k % 2}"))
        sqn = "sq2" if jo == 0 else "sq2b"
        pend_ffsq.append(lambda jo4=jo4, sqn=sqn: C(
            "scalar", lambda E, jo4=jo4, sqn=sqn: E.activation(sb[sqn][:], sb[f"fft{jo4}"][:], AF.Square),
            (f"fft{jo4}",), (sqn,)))
        pend_stats.append((o, sbi, jo4, sqn, cs))

    def ffn_fin(k):
        c0 = k * LC
        flush_q(pend_ffsq, keep=0)
        flush_stores()
        emit_ff_stats(keep=0)
        C("scalar", lambda E: E.activation(sb["stat_sb"][0:1, :], ps["xp2"][0:1, :], AF.Copy),
          ("xp2s", "xp2q", "xp2"), ("stat_sb",))
        C("scalar", lambda E: E.activation(sb["stat_sb"][32:33, :], ps["xp2"][32:33, :], AF.Copy),
          ("xp2s", "xp2q", "xp2"), ("stat_sb",))
        dma("scalar", ag2i[k][0:1, :], sb["stat_sb"][0:1, :], ("stat_sb",), (f"ag2w{k}",), fam="ag2w")
        dma("scalar", ag2i[k][1:2, :], sb["stat_sb"][32:33, :], ("stat_sb",), (f"ag2w{k}b",), fam="ag2w")
        P.op("gpsimd", lambda E: E.collective_compute(
            "AllReduce", ALU.add, replica_groups=RG, ins=[ag2i[k]], outs=[ag2o[k]]),
            reads=(f"ag2w{k}", f"ag2w{k}b"), writes=(f"gath2_{k}",), kind="cc", name=f"ar2_{k}", fam=f"car2{k % 2}")
        stv = sb["stv"]
        dma("scalar", stv[:, 0:LC], ag2o[k][0:1, :], (f"gath2_{k}",), ("stv0",), fam="stv")
        dma("scalar", stv[:, LC:2 * LC], ag2o[k][1:2, :], (f"gath2_{k}",), ("stv1",), fam="stv")
        ln_stats("mbc_v", "rbc_v")
        for o in range(NOT):
            jo = o % 2
            dma("scalar", sb[f"r2l{jo}"][:], r2scr[o * 128:(o + 1) * 128, :],
                (f"r2_{o}_0", f"r2_{o}_1"), (f"xlr{jo}",), fam=f"finl{jo}")
            C("vector", lambda E, jo=jo: E.tensor_tensor(sb[f"r2l{jo}"][:], sb[f"r2l{jo}"][:], sb["mbc"][:], ALU.subtract),
              (f"xlr{jo}", "mbc_v", "mbc"), (f"xlr{jo}",))
            C("vector", lambda E, jo=jo: E.tensor_tensor(sb[f"r2l{jo}"][:], sb[f"r2l{jo}"][:], sb["rbc"][:], ALU.mult),
              (f"xlr{jo}", "rbc_v", "rbc"), (f"xlr{jo}",))
            dma("scalar", out_c[o * 128:(o + 1) * 128, c0:c0 + LC], sb[f"r2l{jo}"][:],
                (f"xlr{jo}",), (f"outw{k}_{o}",), fam="out")

    SERIAL = SERIAL_EMIT
    for k in range(NCH):
        if DEBUG and k == 1:
            dma("sync", dbg_cr, sb["carry"][:],
                tuple(f"carry{cc2}" for cc2 in range(NDT * NST)), ("dbgcr",), fam="dbg")
        scan_side(k)
        for h in range(2):
            bcast_half(k, h)
            if k > 0 and not SERIAL:
                ffn_prep(k - 1, h)
            dtile(k, h, 0)
            for i in range(NDT):
                if i + 1 < NDT:
                    dtile(k, h, i + 1)
                for q in range(2):
                    quarter(k, h, i, q)
                dtile_end(k, h, i)
                if h == 1 and i == 10:
                    ag1a(k)
                if k > 0 and not SERIAL:
                    ffn_otile(k - 1, h, i)
                elif k == 0:
                    while len(pend_res) > 2:
                        emit_one_res()
                    emit_wcast(2)
            if h == 1 and k > 0 and not SERIAL:
                ffn_fin(k - 1)
        ag1(k)
    emit_wcast(64)
    if SERIAL:
        for k in range(NCH):
            for sbi in range(2):
                ffn_prep(k, sbi)
                for o in range(NOT):
                    ffn_otile(k, sbi, o)
            ffn_fin(k)
    else:
        for sbi in range(2):
            ffn_prep(NCH - 1, sbi)
            for o in range(NOT):
                ffn_otile(NCH - 1, sbi, o)
        ffn_fin(NCH - 1)

    P.op("vector", lambda E: E.engine_nop(),
         reads=tuple(f"outw{k}_{o}" for k in range(NCH) for o in range(NOT)),
         writes=(), kind="c", name="fin")

    P.finalize()
    global LAST_PROG
    LAST_PROG = P
    P.check_deadlock()
    with nc.Block() as block:
        @block.sync
        def _(E):
            P.emit_engine("sync", E, chans)

        @block.vector
        def _(E):
            P.emit_engine("vector", E, chans)

        @block.scalar
        def _(E):
            P.emit_engine("scalar", E, chans)

        @block.gpsimd
        def _(E):
            P.emit_engine("gpsimd", E, chans)

        @block.tensor
        def _(E):
            P.emit_engine("tensor", E, chans)
    stack.close()
    return nc


def make_inputs(ins):
    x = np.ascontiguousarray(ins["L0_attn_x"])
    A_log = np.asarray(ins["L0_attn_A_log"])
    Dv = np.asarray(ins["L0_attn_D"])
    xpw = np.asarray(ins["L0_attn_x_proj_w"])
    dtw = np.asarray(ins["L0_attn_dt_proj_w"])
    dtb = np.asarray(ins["L0_attn_dt_proj_b"])
    Wb = np.asarray(ins["L0_ffn_W_base"])
    bb = np.asarray(ins["L0_ffn_b_base"])
    Wr = np.asarray(ins["L0_ffn_W_router"])
    br = np.asarray(ins["L0_ffn_b_router"])
    lA = np.asarray(ins["L0_ffn_lora_A"])
    lB = np.asarray(ins["L0_ffn_lora_B"])

    idn = np.eye(128, dtype=np.float32)
    sel8 = np.zeros((8, 128), np.float32)
    for ee in range(8):
        sel8[ee, ee * 16:(ee + 1) * 16] = 1.0
    xpwT = np.ascontiguousarray(xpw.T)
    lap = np.ascontiguousarray(lA.transpose(1, 0, 2).reshape(D, 128))
    lbp = np.ascontiguousarray(lB.reshape(128, D))
    WrT = np.ascontiguousarray(Wr.T)

    maps = []
    for c in range(8):
        b, h = c // 2, c % 2
        own = slice(h * DH, (h + 1) * DH)
        xb = x[b].T
        xt_c = np.concatenate([xb[own], xb[(1 - h) * DH:(2 - h) * DH]], axis=0)
        perm = np.concatenate([np.arange(h * DH, (h + 1) * DH), np.arange((1 - h) * DH, (2 - h) * DH)])
        xpw_c = xpwT[perm]
        xpw_t = np.ascontiguousarray(xpw_c.reshape(32, 128, XPR).transpose(1, 0, 2).reshape(128, 32 * XPR))
        dtw_c = np.ascontiguousarray(dtw[own].T)
        alog_c = np.ascontiguousarray(A_log[own].reshape(NDT, 128, NST).transpose(1, 0, 2).reshape(128, NDT * NST))
        d_c = np.ascontiguousarray(Dv[own].reshape(NDT, 128).T)
        dtb_c = np.ascontiguousarray(dtb[own].reshape(NDT, 128).T)
        Wslice = Wb[own, :]
        wbs_c = np.ascontiguousarray(
            Wslice.reshape(NOT, 128, NKT, 128).transpose(0, 3, 2, 1).reshape(NOT, 128, NKT * 128))
        bb_c = np.ascontiguousarray(bb[own].reshape(NOT, 128).T)
        wrt_c = np.ascontiguousarray(WrT.reshape(NKT, 128, 8).transpose(1, 0, 2).reshape(128, NKT * 8))
        las_c = np.ascontiguousarray(lap.reshape(NKT, 128, 128).transpose(1, 0, 2).reshape(128, NKT * 128))
        lbs_c = np.ascontiguousarray(lbp[:, own])
        maps.append({
            "xt": xt_c, "xpw": xpw_t, "dtwi": dtw_c, "alog": alog_c,
            "dvec": d_c, "dtbi": dtb_c, "wbs": wbs_c, "bbt": bb_c,
            "wrti": wrt_c, "brt": br.reshape(8, 1).astype(np.float32),
            "las": las_c, "lbsi": lbs_c, "idni": idn, "sel8i": sel8,
        })
    return maps


_NC_CACHE = [None]


def kernel(**inputs):
    maps = make_inputs(inputs)
    if _NC_CACHE[0] is None:
        _NC_CACHE[0] = build_nc()
    nc = _NC_CACHE[0]
    res = run_bass_kernel_spmd(nc, maps, core_ids=list(range(8)))
    out = np.empty((B, L, D), np.float32)
    for c in range(8):
        b, h = c // 2, c % 2
        oc = res.results[c]["out_c"]
        out[b, :, h * DH:(h + 1) * DH] = oc.T
    return out




# version 11 (final)
# speedup vs baseline: 1.3614x; 1.0619x over previous
import numpy as np
import concourse.bass as bass
import concourse.mybir as mybir
from concourse.bass_utils import run_bass_kernel_spmd

F32 = mybir.dt.float32
BF16 = mybir.dt.bfloat16
AF = mybir.ActivationFunctionType
ALU = mybir.AluOpType

B, L, D = 4, 2048, 4096
DH, NST, DTR, XPR = 2048, 16, 128, 160
NCH, LC = 4, 512
SBT = 256
NDT = 16
OH, NOT, NKT = 2048, 16, 32
EPS = 1e-5
RG = [[0, 1], [2, 3], [4, 5], [6, 7]]
SORT8 = [(0, 1), (2, 3), (4, 5), (6, 7), (0, 2), (1, 3), (4, 6), (5, 7),
         (1, 2), (5, 6), (0, 4), (3, 7), (1, 5), (2, 6), (1, 4), (3, 6),
         (2, 4), (3, 5), (3, 4)]
DEBUG = False
SERIAL_EMIT = False
LAST_PROG = None


class Prog:
    ENGINES = ("sync", "vector", "scalar", "gpsimd", "tensor")

    def __init__(self):
        self.ops = []
        self.last_w = {}
        self.readers = {}
        self.fams = {}
        self.fam_R = {}

    def op(self, eng, fn, reads=(), writes=(), kind="c", name="", fam=None, R=0):
        import sys as _sys
        fr = _sys._getframe(1)
        if fr.f_code.co_name in ("dma", "C"):
            fr = fr.f_back
        lbl = f"{fr.f_code.co_name}:{fr.f_lineno}"
        deps = set()
        for kk in reads:
            if kk in self.last_w:
                deps.add(self.last_w[kk])
        for kk in writes:
            for r in self.readers.get(kk, ()):
                deps.add(r)
            if kk in self.last_w:
                deps.add(self.last_w[kk])
        i = len(self.ops)
        if kind in ("d", "cc"):
            assert fam is not None, name
            self.fams.setdefault(fam, []).append(i)
            self.fam_R[fam] = max(self.fam_R.get(fam, 0), R)
        self.ops.append(dict(eng=eng, fn=fn, deps=deps, kind=kind, name=name, fam=fam, lbl=lbl))
        for kk in reads:
            self.readers.setdefault(kk, []).append(i)
        for kk in writes:
            self.last_w[kk] = i
            self.readers[kk] = []
        return i

    def finalize(self):
        ops = self.ops
        self.need_inc = [False] * len(ops)
        for o in ops:
            for d in o["deps"]:
                p = ops[d]
                if p["kind"] == "c" and p["eng"] == o["eng"]:
                    continue
                if p["kind"] == "c":
                    self.need_inc[d] = True
        cnt = {}
        self.val = [0] * len(ops)
        for i, o in enumerate(ops):
            if o["kind"] == "c" and self.need_inc[i]:
                ch = (o["eng"], "c")
                cnt[ch] = cnt.get(ch, 0) + 1
                self.val[i] = cnt[ch]
        self.fam_idx = {}
        for fam, lst in self.fams.items():
            for q, gi in enumerate(lst):
                self.fam_idx[gi] = q

    def check_deadlock(self):
        import bisect
        ops = self.ops
        streams = {en: [i for i in range(len(ops)) if ops[i]["eng"] == en] for en in self.ENGINES}
        pos = {en: 0 for en in self.ENGINES}
        cnt = {}

        def waits_of(i):
            o = ops[i]
            ws = []
            for d in o["deps"]:
                p = ops[d]
                if p["kind"] == "c":
                    if p["eng"] == o["eng"]:
                        continue
                    ws.append((("c", p["eng"]), self.val[d]))
                else:
                    fam = p["fam"]
                    lst = self.fams[fam]
                    cx = max(bisect.bisect_left(lst, i), self.fam_idx[d] + 1)
                    tgt = min(cx + self.fam_R[fam], len(lst))
                    if ops[i]["kind"] == p["kind"] and ops[i].get("fam") == fam:
                        tgt = min(tgt, self.fam_idx[i])
                    tgt = max(tgt, self.fam_idx[d] + 1)
                    step = 16 if p["kind"] == "d" else 1
                    ws.append((("f", fam), tgt * step))
            return ws

        progressed = True
        while progressed:
            progressed = False
            for en in self.ENGINES:
                while pos[en] < len(streams[en]):
                    i = streams[en][pos[en]]
                    if any(cnt.get(ch, 0) < v for ch, v in waits_of(i)):
                        break
                    o = ops[i]
                    if o["kind"] == "c":
                        if self.need_inc[i]:
                            ch = ("c", en)
                            cnt[ch] = cnt.get(ch, 0) + 1
                    else:
                        step = 16 if o["kind"] == "d" else 1
                        ch = ("f", o["fam"])
                        cnt[ch] = cnt.get(ch, 0) + step
                    pos[en] += 1
                    progressed = True
        stuck = {en: pos[en] for en in self.ENGINES if pos[en] < len(streams[en])}
        if stuck:
            for en, p in stuck.items():
                i = streams[en][p]
                o = ops[i]
                print(f"DEADLOCK: {en} pos {p}/{len(streams[en])} op {i} kind={o['kind']} fam={o['fam']} name={o['name']}")
                for d in sorted(o["deps"]):
                    pp = ops[d]
                    print(f"   dep {d}: eng={pp['eng']} kind={pp['kind']} fam={pp['fam']} name={pp['name']} val={self.val[d]}")
                print("   waits:", waits_of(i), "cur:", {ch: cnt.get(ch, 0) for ch, v in waits_of(i)})
            raise RuntimeError("deadlock detected")
        print("deadlock check: OK")

    def emit_engine(self, ename, eng, chans):
        ops = self.ops
        import bisect
        waited = {}
        for i, o in enumerate(ops):
            if o["eng"] != ename:
                continue
            for d in sorted(o["deps"]):
                p = ops[d]
                if p["kind"] == "c":
                    if p["eng"] == ename:
                        continue
                    ch = ("c", p["eng"])
                    v = self.val[d]
                    assert v > 0, (i, o["name"], d, p["name"])
                    sem = chans[(p["eng"], "c")]
                else:
                    fam = p["fam"]
                    lst = self.fams[fam]
                    cx = bisect.bisect_left(lst, i)
                    cx = max(cx, self.fam_idx[d] + 1)
                    tgt = min(cx + self.fam_R[fam], len(lst))
                    if o["kind"] == p["kind"] and o.get("fam") == fam:
                        tgt = min(tgt, self.fam_idx[i])
                    tgt = max(tgt, self.fam_idx[d] + 1)
                    step = 16 if p["kind"] == "d" else 1
                    ch = ("f", fam)
                    v = tgt * step
                    sem = chans[("fam", fam)]
                if waited.get(ch, 0) >= v:
                    continue
                eng.wait_ge(sem, v)
                waited[ch] = v
            inst = o["fn"](eng)
            if o["kind"] == "c":
                if self.need_inc[i]:
                    inst.then_inc(chans[(ename, "c")], 1)
            else:
                step = 16 if o["kind"] == "d" else 1
                inst.then_inc(chans[("fam", o["fam"])], step)


def build_nc():
    nc = bass.Bass("TRN2", target_bir_lowering=False)
    P = Prog()

    xt = nc.dram_tensor("xt", [D, L], F32, kind="ExternalInput").ap()
    xpw = nc.dram_tensor("xpw", [128, 32 * XPR], F32, kind="ExternalInput").ap()
    dtwi = nc.dram_tensor("dtwi", [128, DH], F32, kind="ExternalInput").ap()
    alog = nc.dram_tensor("alog", [128, NDT * NST], F32, kind="ExternalInput").ap()
    dvec = nc.dram_tensor("dvec", [128, NDT], F32, kind="ExternalInput").ap()
    dtbi = nc.dram_tensor("dtbi", [128, NDT], F32, kind="ExternalInput").ap()
    wbs = nc.dram_tensor("wbs", [NOT, 128, NKT * 128], F32, kind="ExternalInput").ap()
    bbt = nc.dram_tensor("bbt", [128, NOT], F32, kind="ExternalInput").ap()
    wrti = nc.dram_tensor("wrti", [128, NKT * 8], F32, kind="ExternalInput").ap()
    brt = nc.dram_tensor("brt", [8, 1], F32, kind="ExternalInput").ap()
    las = nc.dram_tensor("las", [128, NKT * 128], F32, kind="ExternalInput").ap()
    lbsi = nc.dram_tensor("lbsi", [128, OH], F32, kind="ExternalInput").ap()
    idni = nc.dram_tensor("idni", [128, 128], F32, kind="ExternalInput").ap()
    sel8i = nc.dram_tensor("sel8i", [8, 128], F32, kind="ExternalInput").ap()
    out_c = nc.dram_tensor("out_c", [OH, L], F32, kind="ExternalOutput").ap()
    if DEBUG:
        dbg_res = nc.dram_tensor("dbg_res", [DH, L], F32, kind="ExternalOutput").ap()
        dbg_xn = nc.dram_tensor("dbg_xn", [D, L], F32, kind="ExternalOutput").ap()
        dbg_cr = nc.dram_tensor("dbg_cr", [128, NDT * NST], F32, kind="ExternalOutput").ap()
        dbg_st = nc.dram_tensor("dbg_st", [NCH, 2, LC], F32, kind="ExternalOutput").ap()
        dbg_bp = nc.dram_tensor("dbg_bp", [128, 8 * LC], F32, kind="ExternalOutput").ap()
        dbg_cp = nc.dram_tensor("dbg_cp", [128, 8 * LC], F32, kind="ExternalOutput").ap()
        dbg_dt = nc.dram_tensor("dbg_dt", [128, LC], F32, kind="ExternalOutput").ap()
        dbg_v = nc.dram_tensor("dbg_v", [128, LC], F32, kind="ExternalOutput").ap()

    xph_hi = nc.dram_tensor("xph_hi", [128, 32 * XPR], BF16).ap()
    xph_lo = nc.dram_tensor("xph_lo", [128, 32 * XPR], BF16).ap()
    lbh_hi = nc.dram_tensor("lbh_hi", [128, OH], BF16).ap()
    lbh_lo = nc.dram_tensor("lbh_lo", [128, OH], BF16).ap()
    lah_hi = nc.dram_tensor("lah_hi", [128, NKT * 128], BF16).ap()
    lah_lo = nc.dram_tensor("lah_lo", [128, NKT * 128], BF16).ap()
    wh_hi = nc.dram_tensor("wh_hi", [NOT, 128, NKT * 128], BF16).ap()
    wh_lo = nc.dram_tensor("wh_lo", [NOT, 128, NKT * 128], BF16).ap()
    yscr = nc.dram_tensor("yscr", [DH, LC], F32).ap()
    bchbm = nc.dram_tensor("bchbm", [32, LC], F32).ap()
    r2scr = nc.dram_tensor("r2scr", [OH, LC], F32).ap()
    ag1ia = [nc.dram_tensor(f"ag1ia{k}", [DH // 2, LC], F32).ap() for k in range(NCH)]
    ag1ib = [nc.dram_tensor(f"ag1ib{k}", [DH // 2, LC], F32).ap() for k in range(NCH)]
    ag1oa = [nc.dram_tensor(f"ag1oa{k}", [DH, LC], F32).ap() for k in range(NCH)]
    ag1ob = [nc.dram_tensor(f"ag1ob{k}", [DH, LC], F32).ap() for k in range(NCH)]
    ag1si = [nc.dram_tensor(f"ag1si{k}", [2, LC], F32).ap() for k in range(NCH)]
    ag1so = [nc.dram_tensor(f"ag1so{k}", [2, LC], F32).ap() for k in range(NCH)]
    ag2i = [nc.dram_tensor(f"ag2i{k}", [2, LC], F32).ap() for k in range(NCH)]
    ag2o = [nc.dram_tensor(f"ag2o{k}", [2, LC], F32).ap() for k in range(NCH)]

    from contextlib import ExitStack
    stack = ExitStack()
    e = stack.enter_context

    sb = {}
    def S(name, shape, dt=F32):
        sb[name] = e(nc.sbuf_tensor(name, shape, dt))
        return sb[name]

    S("ones_r", [1, 128]); S("ones_c", [128, 1])
    S("idn", [128, 128]); S("sel8", [8, 128])
    S("A", [128, NDT * NST]); S("onepD", [128, NDT]); S("dtb", [128, NDT])
    S("bb", [128, NOT]); S("br", [8, 1])
    S("dtw_hi", [128, DH], BF16); S("dtw_lo", [128, DH], BF16)
    S("wr_hi", [128, NKT * 8], BF16); S("wr_lo", [128, NKT * 8], BF16)
    S("carry", [128, NDT * NST])
    for j in range(2):
        S(f"xpw_hi{j}", [128, 2 * XPR], BF16); S(f"xpw_lo{j}", [128, 2 * XPR], BF16)
    S("Bp", [128, 8 * LC]); S("Cp", [128, 8 * LC])
    S("dAz0", [128, 4 * LC]); S("dAz1", [128, 4 * LC])
    S("dBx0", [128, 4 * LC]); S("dBx1", [128, 4 * LC])
    S("hp", [128, 4 * LC])
    for j in range(2):
        S(f"xhi{j}", [128, LC], BF16); S(f"xlo{j}", [128, LC], BF16)
        S(f"dtt{j}", [128, LC]); S(f"xot{j}", [128, LC]); S(f"vt{j}", [128, LC])
    for j in range(4):
        S(f"yh{j}", [128, LC]); S(f"res{j}", [128, LC])
    S("ysc", [128, LC]); S("zero4", [128, 4]); S("tc4", [128, 4])
    S("lbt_hi0", [128, 128], BF16); S("lbt_lo0", [128, 128], BF16)
    S("lbt_hi1", [128, 128], BF16); S("lbt_lo1", [128, 128], BF16)
    S("dtin_hi", [128, LC], BF16); S("dtin_lo", [128, LC], BF16)
    S("bc_c", [32, LC])
    S("stat_sb", [64, LC])
    S("xn_hi", [128, NKT * SBT], BF16); S("xn_lo", [128, NKT * SBT], BF16)
    S("wla", [128, 8192], BF16)
    for j in range(2):
        S(f"gt{j}", [128, SBT])
        S(f"rr{j}", [128, SBT])
        S(f"r2l{j}", [128, LC])
    for j in range(4):
        S(f"fft{j}", [128, SBT])
    S("sq2", [128, SBT])
    S("sq2b", [128, SBT])
    S("stv", [1, 3 * LC])
    S("mbc", [128, LC]); S("rbc", [128, LC])
    S("rt8", [8, SBT]); S("lgt", [128, 16])
    S("sortp", [128, 48])
    S("wfin", [128, 16]); S("wt8", [8, SBT]); S("wbc", [128, SBT])
    S("lh", [128, SBT])
    S("hw_hi", [128, SBT], BF16); S("hw_lo", [128, SBT], BF16)

    wla = sb["wla"]
    wla_f32 = wla[:].bitcast(F32)
    wf32 = wla_f32[:, 0:2048]
    cst_hi = wla[:, 4096:6144]
    cst_lo = wla[:, 6144:8192]
    WQ = ("wla0", "wla1", "wla2", "wla3")

    ps = {}
    def PSUM(name, shape):
        ps[name] = e(nc.psum_tensor(name, shape, F32))
    PSUM("xp1", [128, LC])
    PSUM("xp1b", [128, LC])
    PSUM("bst", [128, LC])
    PSUM("xp2", [64, LC])
    PSUM("rtl", [128, SBT])
    PSUM("misc", [128, LC])
    PSUM("ff0", [128, SBT]); PSUM("ff1", [128, SBT])

    chans = {}
    for en in Prog.ENGINES:
        chans[(en, "c")] = e(nc.semaphore(f"c_{en}"))
    FAMS = ["consts", "xphw", "lahw", "wbmw", "wla0f", "wla1f", "wla2f", "wla3f", "xlr0", "xlr1", "xot0", "xot1", "xpw",
            "bp", "cp", "gt0", "gt1", "rr0", "rr1", "yscl", "bchbm", "stv", "out",
            "yscr0", "yscr1", "r2s0", "r2s1", "agw0a", "agw0b", "agw1a", "agw1b", "ag1sw", "arw", "ag2w",
            "cag0", "cag1", "car0", "car1", "car20", "car21", "dbg",
            "xpwh0", "xpwh1", "xpwl0", "xpwl1", "lbt0", "lbt1", "finl0", "finl1", "lbhw"]
    for f in FAMS:
        chans[("fam", f)] = e(nc.semaphore(f"f_{f}"))

    def dma(eng, out_ap, in_ap, reads, writes, name="", fam="consts", R=0):
        P.op(eng, lambda E, o=out_ap, i=in_ap: E.dma_start(out=o, in_=i),
             reads=reads, writes=writes, kind="d", name=name, fam=fam, R=R)

    C = P.op

    C("vector", lambda E: E.memset(sb["ones_r"][:], 1.0), (), ("ones_r",))
    C("vector", lambda E: E.memset(sb["ones_c"][:], 1.0), (), ("ones_c",))
    C("vector", lambda E: E.memset(sb["zero4"][:], 0.0), (), ("zero4",))
    dma("sync", sb["idn"][:], idni, (), ("idn",))
    dma("sync", sb["sel8"][:], sel8i, (), ("sel8",))
    dma("sync", sb["A"][:], alog, (), ("A",))
    C("scalar", lambda E: E.activation(sb["A"][:], sb["A"][:], AF.Exp), ("A",), ("A",))
    C("vector", lambda E: E.tensor_single_scalar(sb["A"][:], sb["A"][:], -1.0, ALU.mult), ("A",), ("A",))
    dma("sync", sb["onepD"][:], dvec, (), ("onepD",))
    C("vector", lambda E: E.tensor_single_scalar(sb["onepD"][:], sb["onepD"][:], 1.0, ALU.add), ("onepD",), ("onepD",))
    dma("sync", sb["dtb"][:], dtbi, (), ("dtb",))
    dma("sync", sb["bb"][:], bbt, (), ("bb",))
    dma("sync", sb["br"][:], brt, (), ("br",))

    def stage_cast(src_ap, cols, hi_dst, hi_key, lo_dst, lo_key):
        dma("sync", wf32[:, 0:cols], src_ap, ("wla0", "wla1"), ("wla0", "wla1"), fam="wla0f")
        C("scalar", lambda E, c=cols, h=hi_dst: E.activation(h, wf32[:, 0:c], AF.Copy),
          ("wla0", "wla1"), (hi_key,))
        C("vector", lambda E, c=cols, h=hi_dst, l=lo_dst: E.tensor_tensor(l, wf32[:, 0:c], h, ALU.subtract),
          ("wla0", "wla1", hi_key), (lo_key,))

    stage_cast(dtwi, DH, sb["dtw_hi"][:], "dtw_hi", sb["dtw_lo"][:], "dtw_lo")
    stage_cast(wrti, NKT * 8, sb["wr_hi"][:], "wr_hi", sb["wr_lo"][:], "wr_lo")

    def cast_to_hbm(nm, src_ap, dst_hi, dst_lo, total):
        off = 0
        while off < total:
            w = min(2048, total - off)
            dma("sync", wf32[:, 0:w], src_ap[:, off:off + w], ("wla0", "wla1"), ("wla0", "wla1"), fam="wla0f")
            C("scalar", lambda E, w=w: E.activation(cst_hi[:, 0:w], wf32[:, 0:w], AF.Copy),
              ("wla0", "wla1"), ("wla2",))
            C("vector", lambda E, w=w: E.tensor_tensor(cst_lo[:, 0:w], wf32[:, 0:w], cst_hi[:, 0:w], ALU.subtract),
              ("wla0", "wla1", "wla2"), ("wla3",))
            dma("sync", dst_hi[:, off:off + w], cst_hi[:, 0:w], ("wla2",), (f"{nm}H",), fam=nm + "w")
            dma("sync", dst_lo[:, off:off + w], cst_lo[:, 0:w], ("wla3",), (f"{nm}L",), fam=nm + "w")
            off += w

    cast_to_hbm("xph", xpw, xph_hi, xph_lo, 32 * XPR)
    cast_to_hbm("lah", las, lah_hi, lah_lo, NKT * 128)
    cast_to_hbm("lbh", lbsi, lbh_hi, lbh_lo, OH)

    wblocks = [(o, hf) for o in range(NOT) for hf in range(2)]
    widx = [0]

    def emit_wcast(nblk):
        for _ in range(nblk):
            if widx[0] >= len(wblocks):
                return
            o, hf = wblocks[widx[0]]
            widx[0] += 1
            sl = slice(hf * 2048, (hf + 1) * 2048)
            dma("sync", wf32[:], wbs[o, :, sl], ("wla0", "wla1"), ("wla0", "wla1"), fam="wla0f")
            C("scalar", lambda E: E.activation(cst_hi[:], wf32[:], AF.Copy),
              ("wla0", "wla1"), ("wla2",))
            C("vector", lambda E: E.tensor_tensor(cst_lo[:], wf32[:], cst_hi[:], ALU.subtract),
              ("wla0", "wla1", "wla2"), ("wla3",))
            dma("sync", wh_hi[o, :, sl], cst_hi[:], ("wla2",), (f"wbm{o}_{hf}H",), fam="wbmw")
            dma("sync", wh_lo[o, :, sl], cst_lo[:], ("wla3",), (f"wbm{o}_{hf}L",), fam="wbmw")

    def scan_side(k):
        c0 = k * LC
        n_mm = [0]
        for pc in range(16):
            pb = pc % 2
            poff = pc * 2 * XPR
            dma("sync", sb[f"xpw_hi{pb}"][:], xph_hi[:, poff:poff + 2 * XPR], (f"xpw_hi{pb}",), (f"xpw_hi{pb}",), fam=f"xpwh{pb}")
            dma("sync", sb[f"xpw_lo{pb}"][:], xph_lo[:, poff:poff + 2 * XPR], (f"xpw_lo{pb}",), (f"xpw_lo{pb}",), fam=f"xpwl{pb}")
            for tl in range(2):
                t = pc * 2 + tl
                j = t % 2
                dma("sync" if j == 0 else "scalar", sb[f"r2l{j}"][:], xt[t * 128:(t + 1) * 128, c0:c0 + LC], (), (f"xlr{j}",), fam=f"xlr{j}")
                C("scalar", lambda E, j=j: E.activation(sb[f"xhi{j}"][:], sb[f"r2l{j}"][:], AF.Copy),
                  (f"xlr{j}",), (f"xhi{j}",))
                C("vector", lambda E, j=j: E.tensor_tensor(sb[f"xlo{j}"][:], sb[f"r2l{j}"][:], sb[f"xhi{j}"][:], ALU.subtract),
                  (f"xlr{j}", f"xhi{j}"), (f"xlo{j}",))
                for (wk, xk) in (("hi", "hi"), ("hi", "lo"), ("lo", "hi")):
                    s0 = (n_mm[0] == 0)
                    s1 = (n_mm[0] == 95)
                    C("tensor", lambda E, wk=wk, xk=xk, j=j, tl=tl, pb=pb, s0=s0, s1=s1: E.matmul(
                        ps["xp1"][:], sb[f"xpw_{wk}{pb}"][:, tl * XPR:tl * XPR + 128], sb[f"x{xk}{j}"][:],
                        start=s0, stop=s1),
                        (f"xpw_{wk}{pb}", f"x{xk}{j}"), ("xp1",))
                    C("tensor", lambda E, wk=wk, xk=xk, j=j, tl=tl, pb=pb, s0=s0, s1=s1: E.matmul(
                        ps["bst"][0:32, :], sb[f"xpw_{wk}{pb}"][:, tl * XPR + 128:(tl + 1) * XPR], sb[f"x{xk}{j}"][:],
                        start=s0, stop=s1),
                        (f"xpw_{wk}{pb}", f"x{xk}{j}"), ("bcb",))
                    n_mm[0] += 1
        C("scalar", lambda E: E.activation(sb["dtin_hi"][:], ps["xp1"][:], AF.Copy),
          ("xp1",), ("dtin_hi",))
        C("vector", lambda E: E.tensor_tensor(sb["dtin_lo"][:], ps["xp1"][:], sb["dtin_hi"][:], ALU.subtract),
          ("xp1", "dtin_hi"), ("dtin_lo",))
        C("scalar", lambda E: E.activation(sb["bc_c"][:], ps["bst"][0:32, :], AF.Copy),
          ("bcb",), ("bc_c",))
        dma("scalar", bchbm, sb["bc_c"][:], ("bc_c",), ("bchbm",), fam="bchbm")

    def bcast_half(k, h):
        srcB = bchbm[h * 8:(h + 1) * 8, :].flatten().unsqueeze(0).broadcast_to([128, 8 * LC])
        srcC = bchbm[16 + h * 8:16 + (h + 1) * 8, :].flatten().unsqueeze(0).broadcast_to([128, 8 * LC])
        dma("gpsimd", sb["Bp"][:], srcB, ("bchbm",), tuple(f"Bp{z}" for z in range(8)), fam="bp")
        dma("gpsimd", sb["Cp"][:], srcC, ("bchbm",), tuple(f"Cp{z}" for z in range(8)), fam="cp")
        if DEBUG and k == 1 and h == 0:
            dma("sync", dbg_bp, sb["Bp"][:], tuple(f"Bp{z}" for z in range(8)), ("dbgbp",), fam="dbg")
            dma("sync", dbg_cp, sb["Cp"][:], tuple(f"Cp{z}" for z in range(8)), ("dbgcp",), fam="dbg")

    def dtile(k, h, i):
        j = i % 2
        pn = "xp1" if i % 2 == 0 else "xp1b"
        for pi, (wk, xk) in enumerate((("hi", "hi"), ("hi", "lo"), ("lo", "hi"))):
            C("tensor", lambda E, wk=wk, xk=xk, i=i, pn=pn, s=(pi == 0), t=(pi == 2): E.matmul(
                ps[pn][:], sb[f"dtw_{wk}"][:, i * 128:(i + 1) * 128], sb[f"dtin_{xk}"][:],
                start=s, stop=t),
                (f"dtw_{wk}", f"dtin_{xk}"), (pn,))
        C("scalar", lambda E, i=i, j=j, pn=pn: E.activation(sb[f"dtt{j}"][:], ps[pn][:], AF.Exp, bias=sb["dtb"][:, i:i + 1]),
          (pn, "dtb"), (f"dtt{j}",))
        C("scalar", lambda E, j=j: E.activation(sb[f"dtt{j}"][:], sb[f"dtt{j}"][:], AF.Ln, bias=1.0),
          (f"dtt{j}",), (f"dtt{j}",))
        dma("sync", sb[f"xot{j}"][:], xt[i * 128:(i + 1) * 128, k * LC:(k + 1) * LC], (), (f"xot{j}",), fam=f"xot{j}")
        C("gpsimd", lambda E, j=j: E.tensor_tensor(sb[f"vt{j}"][:], sb[f"dtt{j}"][:], sb[f"xot{j}"][:], ALU.mult),
          (f"dtt{j}", f"xot{j}"), (f"vt{j}",))
        if DEBUG and k == 1 and h == 0 and i == 0:
            dma("sync", dbg_dt, sb[f"dtt{j}"][:], (f"dtt{j}",), ("dbgdt",), fam="dbg")
            dma("sync", dbg_v, sb[f"vt{j}"][:], (f"vt{j}",), ("dbgv",), fam="dbg")

    ucnt = [0]

    def quarter(k, h, i, q):
        j = i % 2
        ub = ucnt[0] % 2
        ucnt[0] += 1
        dA = f"dAz{ub}"
        dB = f"dBx{ub}"
        for jj in range(4):
            n = h * 8 + q * 4 + jj
            col = i * NST + n
            C("scalar", lambda E, jj=jj, col=col, j=j, dA=dA: E.activation(
                sb[dA][:, jj * LC:(jj + 1) * LC], sb[f"dtt{j}"][:], AF.Exp,
                scale=sb["A"][:, col:col + 1]),
                (f"dtt{j}", "A"), (dA,))
        vb = sb[f"vt{j}"][:].unsqueeze(1).broadcast_to([128, 4, LC])
        C("gpsimd", lambda E, q=q, dB=dB, vb=vb: E.tensor_tensor(
            sb[dB][:].rearrange("p (n l) -> p n l", n=4),
            sb["Bp"][:, q * 4 * LC:(q + 1) * 4 * LC].rearrange("p (n l) -> p n l", n=4),
            vb, ALU.mult),
            tuple([f"vt{j}"] + [f"Bp{q * 4 + z}" for z in range(4)]), (dB,))
        colb0 = i * NST + h * 8 + q * 4
        if k > 0:
            dA_f = sb[dA][:].rearrange("p (n l) -> p n l", n=4)[:, :, 0]
            dB_f = sb[dB][:].rearrange("p (n l) -> p n l", n=4)[:, :, 0]
            C("gpsimd", lambda E, dA_f=dA_f, colb0=colb0: E.tensor_tensor(
                sb["tc4"][:], dA_f, sb["carry"][:, colb0:colb0 + 4], ALU.mult),
                tuple([dA] + [f"carry{colb0 + z}" for z in range(4)]), ("tc4",))
            C("gpsimd", lambda E, dB_f=dB_f: E.tensor_tensor(
                dB_f, dB_f, sb["tc4"][:], ALU.add),
                (dB, "tc4"), (dB,))
        for jj in range(4):
            rd = [dA, dB]
            C("vector", lambda E, jj=jj, dA=dA, dB=dB: E.tensor_tensor_scan(
                sb["hp"][:, jj * LC:(jj + 1) * LC],
                sb[dA][:, jj * LC:(jj + 1) * LC],
                sb[dB][:, jj * LC:(jj + 1) * LC],
                0.0, ALU.mult, ALU.add),
                tuple(rd), ("hp",))
        colb = i * NST + h * 8 + q * 4
        C("vector", lambda E, colb=colb: E.tensor_tensor(
            sb["carry"][:, colb:colb + 3],
            sb["hp"][:, 0:3 * LC].rearrange("p (n l) -> p n l", n=3)[:, :, LC - 1],
            sb["zero4"][:, 0:3], ALU.add),
            ("hp", "zero4"), tuple(f"carry{colb + z}" for z in range(3)))
        C("vector", lambda E, colb=colb: E.tensor_tensor(
            sb["carry"][:, colb + 3:colb + 4],
            sb["hp"][:, 4 * LC - 1:4 * LC],
            sb["zero4"][:, 0:1], ALU.add),
            ("hp", "zero4"), (f"carry{colb + 3}",))
        C("vector", lambda E, q=q: E.tensor_tensor(
            sb["hp"][:], sb["hp"][:], sb["Cp"][:, q * 4 * LC:(q + 1) * 4 * LC], ALU.mult),
            tuple(["hp"] + [f"Cp{q * 4 + z}" for z in range(4)]), ("hp",))
        C("vector", lambda E: E.tensor_tensor(
            sb["hp"][:, 0:2 * LC], sb["hp"][:, 0:2 * LC], sb["hp"][:, 2 * LC:4 * LC], ALU.add),
            ("hp",), ("hp",))
        j4 = i % 4
        if q == 0:
            C("vector", lambda E, j4=j4: E.tensor_tensor(
                sb[f"yh{j4}"][:], sb["hp"][:, 0:LC], sb["hp"][:, LC:2 * LC], ALU.add),
                ("hp",), (f"yh{j4}",))
        else:
            C("vector", lambda E: E.tensor_tensor(
                sb["hp"][:, 2 * LC:3 * LC], sb["hp"][:, 0:LC], sb["hp"][:, LC:2 * LC], ALU.add),
                ("hp",), ("hp",))
            C("vector", lambda E, j4=j4: E.tensor_tensor(
                sb[f"yh{j4}"][:], sb[f"yh{j4}"][:], sb["hp"][:, 2 * LC:3 * LC], ALU.add),
                ("hp", f"yh{j4}"), (f"yh{j4}",))

    pend_stores = []
    pend_ffsq = []
    pend_rsq = []

    def flush_stores(keep=0):
        while len(pend_stores) > keep:
            pend_stores.pop(0)()

    def flush_q(q, keep=0):
        while len(q) > keep:
            q.pop(0)()

    def dtile_end(k, h, i):
        flush_q(pend_rsq, keep=0)
        flush_stores(keep=2)
        j = i % 2
        j4 = i % 4
        if h == 0:
            pend_stores.append(lambda i=i, j4=j4, k=k: dma(
                "gpsimd", yscr[i * 128:(i + 1) * 128, :], sb[f"yh{j4}"][:],
                (f"yh{j4}",), (f"yscr{i}",), fam=f"yscr{k % 2}"))
        else:
            dma("sync", sb["ysc"][:], yscr[i * 128:(i + 1) * 128, :],
                (f"yscr{i}",), ("ysc",), fam="yscl")
            C("vector", lambda E, j4=j4: E.tensor_tensor(sb[f"yh{j4}"][:], sb[f"yh{j4}"][:], sb["ysc"][:], ALU.add),
              (f"yh{j4}", "ysc"), (f"yh{j4}",))
            C("vector", lambda E, j=j, j4=j4, i=i: E.scalar_tensor_tensor(
                sb[f"res{j4}"][:], sb[f"xot{j}"][:], sb["onepD"][:, i:i + 1], sb[f"yh{j4}"][:],
                ALU.mult, ALU.add),
                (f"xot{j}", "onepD", f"yh{j4}"), (f"res{j4}",))
            tgt = ag1ia[k] if i < 8 else ag1ib[k]
            sfx = "a" if i < 8 else "b"
            pend_stores.append(lambda i=i, j4=j4, k=k, tgt=tgt, sfx=sfx: dma(
                "gpsimd", tgt[(i % 8) * 128:(i % 8 + 1) * 128, :], sb[f"res{j4}"][:],
                (f"res{j4}",), (f"ag1w{k}_{i}",), fam=f"agw{k % 2}{sfx}"))
            pend_rsq.append(lambda j4=j4: C(
                "scalar", lambda E, j4=j4: E.activation(sb[f"yh{j4}"][:], sb[f"res{j4}"][:], AF.Square),
                (f"res{j4}",), (f"yh{j4}",)))
            pend_res.append((i, j4))
            if DEBUG:
                pend_stores.append(lambda i=i, j4=j4, k=k: dma(
                    "gpsimd", dbg_res[i * 128:(i + 1) * 128, k * LC:(k + 1) * LC], sb[f"res{j4}"][:],
                    (f"res{j4}",), (f"dbgres{k}_{i}",), fam="dbg"))

    def ag1a(k):
        reads = tuple(f"ag1w{k}_{i}" for i in range(8))
        P.op("gpsimd", lambda E: E.collective_compute(
            "AllGather", ALU.bypass, replica_groups=RG, ins=[ag1ia[k]], outs=[ag1oa[k]]),
            reads=reads, writes=(f"gatha{k}",), kind="cc", name=f"ag1a_{k}", fam=f"cag{k % 2}")

    def ag1(k):
        flush_q(pend_rsq, keep=0)
        flush_stores()
        while pend_res:
            emit_one_res()
        C("scalar", lambda E: E.activation(sb["stat_sb"][0:1, :], ps["bst"][32:33, :], AF.Copy),
          ("st0", "st1", "st"), ("stat1a", "stat_sb"))
        C("scalar", lambda E: E.activation(sb["stat_sb"][32:33, :], ps["bst"][64:65, :], AF.Copy),
          ("st0", "st1", "st"), ("stat1b", "stat_sb"))
        dma("scalar", ag1si[k][0:1, :], sb["stat_sb"][0:1, :], ("stat1a",), (f"ag1w{k}_s0",), fam="ag1sw")
        dma("scalar", ag1si[k][1:2, :], sb["stat_sb"][32:33, :], ("stat1b",), (f"ag1w{k}_s1",), fam="ag1sw")
        P.op("gpsimd", lambda E: E.collective_compute(
            "AllReduce", ALU.add, replica_groups=RG, ins=[ag1si[k]], outs=[ag1so[k]]),
            reads=(f"ag1w{k}_s0", f"ag1w{k}_s1"), writes=(f"gaths{k}",), kind="cc", name=f"ag1s_{k}", fam=f"cag{k % 2}")
        reads = tuple(f"ag1w{k}_{i}" for i in range(8, NDT))
        P.op("gpsimd", lambda E: E.collective_compute(
            "AllGather", ALU.bypass, replica_groups=RG, ins=[ag1ib[k]], outs=[ag1ob[k]]),
            reads=reads, writes=(f"gathb{k}",), kind="cc", name=f"ag1b_{k}", fam=f"cag{k % 2}")

    def ln_stats(m_name, r_name, width=LC):
        stv = sb["stv"]
        s0 = stv[:, 0:width]; s1 = stv[:, LC:LC + width]; s2 = stv[:, 2 * LC:2 * LC + width]
        C("vector", lambda E: E.tensor_single_scalar(s0, s0, 1.0 / D, ALU.mult),
          ("stv0",), ("stv0",))
        C("tensor", lambda E, w=width: E.matmul(ps["misc"][:, 0:w], sb["ones_r"][0:1, :], s0[0:1, :], start=True, stop=True),
          ("stv0", "ones_r"), ("misc",))
        C("scalar", lambda E, w=width: E.activation(sb["mbc"][:, 0:w], ps["misc"][:, 0:w], AF.Copy),
          ("misc",), (m_name, "mbc"))
        C("vector", lambda E: E.tensor_single_scalar(s1, s1, 1.0 / D, ALU.mult),
          ("stv1",), ("stv1",))
        C("vector", lambda E: E.tensor_tensor(s2, s0, s0, ALU.mult),
          ("stv0",), ("stv2",))
        C("vector", lambda E: E.tensor_tensor(s1, s1, s2, ALU.subtract),
          ("stv1", "stv2"), ("stv1",))
        C("vector", lambda E: E.tensor_single_scalar(s1, s1, EPS, ALU.add),
          ("stv1",), ("stv1",))
        C("scalar", lambda E: E.activation(s2, s1, AF.Ln),
          ("stv1",), ("stv2",))
        C("scalar", lambda E: E.activation(s2, s2, AF.Exp, scale=-0.5),
          ("stv2",), ("stv2",))
        C("vector", lambda E: E.tensor_tensor(s0, s2, s2, ALU.mult),
          ("stv2",), ("stv0",))
        C("vector", lambda E: E.tensor_tensor(s0, s0, s1, ALU.mult),
          ("stv0", "stv1"), ("stv0",))
        C("vector", lambda E: E.tensor_scalar(s0, s0, -0.5, 1.5, ALU.mult, ALU.add),
          ("stv0",), ("stv0",))
        C("vector", lambda E: E.tensor_tensor(s2, s2, s0, ALU.mult),
          ("stv2", "stv0"), ("stv2",))
        C("tensor", lambda E, w=width: E.matmul(ps["misc"][:, 0:w], sb["ones_r"][0:1, :], s2[0:1, :], start=True, stop=True),
          ("stv2", "ones_r"), ("misc",))
        C("scalar", lambda E, w=width: E.activation(sb["rbc"][:, 0:w], ps["misc"][:, 0:w], AF.Copy),
          ("misc",), (r_name, "rbc"))

    def sparsemax_ops(sbi):
        pool = sb["sortp"]
        W2 = 2
        def sl(s):
            return pool[:, s * W2:(s + 1) * W2]
        slots = {}
        free = list(range(24))
        for ee in range(8):
            s0 = free.pop(0)
            slots[ee] = s0
            C("vector", lambda E, ee=ee, s0=s0: E.tensor_copy(sl(s0), sb["lgt"][:, ee * W2:(ee + 1) * W2]),
              ("lgt",), (f"sp{s0}",))
        for (i, jx) in SORT8:
            si, sj = slots[i], slots[jx]
            a = free.pop(0); b2 = free.pop(0)
            C("vector", lambda E, si=si, sj=sj, a=a: E.tensor_tensor(sl(a), sl(si), sl(sj), ALU.max),
              (f"sp{si}", f"sp{sj}"), (f"sp{a}",))
            C("vector", lambda E, si=si, sj=sj, b2=b2: E.tensor_tensor(sl(b2), sl(si), sl(sj), ALU.min),
              (f"sp{si}", f"sp{sj}"), (f"sp{b2}",))
            free.append(si); free.append(sj)
            slots[i], slots[jx] = a, b2
        cum = {}
        c0s = free.pop(0)
        cum[0] = c0s
        C("vector", lambda E, s=slots[0], c=c0s: E.tensor_copy(sl(c), sl(s)),
          (f"sp{slots[0]}",), (f"sp{c0s}",))
        for ee in range(1, 8):
            cs = free.pop(0)
            C("vector", lambda E, p=cum[ee - 1], z=slots[ee], cs=cs: E.tensor_tensor(sl(cs), sl(p), sl(z), ALU.add),
              (f"sp{cum[ee - 1]}", f"sp{slots[ee]}"), (f"sp{cs}",))
            cum[ee] = cs
        tsl = free.pop(0); msl = free.pop(0); ksl = free.pop(0)
        for ee in range(8):
            C("vector", lambda E, z=slots[ee], c=cum[ee], t=tsl, ee=ee: E.scalar_tensor_tensor(
                sl(t), sl(z), float(ee + 1), sl(c), ALU.mult, ALU.subtract),
                (f"sp{slots[ee]}", f"sp{cum[ee]}"), (f"sp{tsl}",))
            C("vector", lambda E, t=tsl, m=msl: E.tensor_single_scalar(sl(m), sl(t), -1.0, ALU.is_gt),
              (f"sp{tsl}",), (f"sp{msl}",))
            if ee == 0:
                C("vector", lambda E, m=msl, ks=ksl: E.tensor_copy(sl(ks), sl(m)),
                  (f"sp{msl}",), (f"sp{ksl}",))
            else:
                C("vector", lambda E, m=msl, ks=ksl: E.tensor_tensor(sl(ks), sl(ks), sl(m), ALU.add),
                  (f"sp{msl}", f"sp{ksl}"), (f"sp{ksl}",))
        acc, eq = tsl, msl
        t2 = free[0]
        for ee in range(8):
            C("vector", lambda E, ks=ksl, eqs=eq, ee=ee: E.tensor_single_scalar(sl(eqs), sl(ks), float(ee + 1), ALU.is_equal),
              (f"sp{ksl}",), (f"sp{eq}",))
            if ee == 0:
                C("vector", lambda E, c=cum[ee], eqs=eq, a=acc: E.tensor_tensor(sl(a), sl(c), sl(eqs), ALU.mult),
                  (f"sp{cum[ee]}", f"sp{eq}"), (f"sp{acc}",))
            else:
                C("vector", lambda E, c=cum[ee], eqs=eq, t2=t2: E.tensor_tensor(sl(t2), sl(c), sl(eqs), ALU.mult),
                  (f"sp{cum[ee]}", f"sp{eq}"), (f"sp{t2}",))
                C("vector", lambda E, a=acc, t2=t2: E.tensor_tensor(sl(a), sl(a), sl(t2), ALU.add),
                  (f"sp{acc}", f"sp{t2}"), (f"sp{acc}",))
        C("vector", lambda E, ks=ksl, eqs=eq: E.reciprocal(sl(eqs), sl(ks)),
          (f"sp{ksl}",), (f"sp{eq}",))
        C("vector", lambda E, a=acc: E.tensor_single_scalar(sl(a), sl(a), -1.0, ALU.add),
          (f"sp{acc}",), (f"sp{acc}",))
        C("vector", lambda E, a=acc, eqs=eq: E.tensor_tensor(sl(a), sl(a), sl(eqs), ALU.mult),
          (f"sp{acc}", f"sp{eq}"), (f"sp{acc}",))
        for ee in range(8):
            C("vector", lambda E, ee=ee, a=acc: E.tensor_tensor(
                sb["wfin"][:, ee * W2:(ee + 1) * W2], sb["lgt"][:, ee * W2:(ee + 1) * W2], sl(a), ALU.subtract),
                (f"sp{acc}", "lgt"), (f"wf{ee}",))
            C("vector", lambda E, ee=ee: E.tensor_single_scalar(
                sb["wfin"][:, ee * W2:(ee + 1) * W2], sb["wfin"][:, ee * W2:(ee + 1) * W2], 0.0, ALU.max),
                (f"wf{ee}",), (f"wf{ee}", "wfin"))

    def ffn_prep(k, sbi):
        cs = sbi * SBT
        stv = sb["stv"]
        if sbi == 0:
            dma("scalar", stv[0:1, 0:LC], ag1so[k][0:1, :], (f"gaths{k}",), ("stv0",), fam="stv")
            dma("scalar", stv[0:1, LC:2 * LC], ag1so[k][1:2, :], (f"gaths{k}",), ("stv1",), fam="stv")
            if DEBUG:
                dma("scalar", dbg_st[k, 0:1, :], stv[0:1, 0:LC], ("stv0",), (f"dbgst{k}a",), fam="dbg")
                dma("scalar", dbg_st[k, 1:2, :], stv[0:1, LC:2 * LC], ("stv1",), (f"dbgst{k}b",), fam="dbg")
            ln_stats("mbc_v", "rbc_v", width=LC)
        order = list(range(0, 8)) + list(range(16, 24)) + list(range(8, 16)) + list(range(24, 32))
        for pos, t in enumerate(order):
            j = pos % 2
            if t < 8:
                srcb, row, gk = ag1oa[k], t * 128, f"gatha{k}"
            elif t < 16:
                srcb, row, gk = ag1ob[k], (t - 8) * 128, f"gathb{k}"
            elif t < 24:
                srcb, row, gk = ag1oa[k], 1024 + (t - 16) * 128, f"gatha{k}"
            else:
                srcb, row, gk = ag1ob[k], 1024 + (t - 24) * 128, f"gathb{k}"
            dma("sync", sb[f"gt{j}"][:], srcb[row:row + 128, cs:cs + SBT], (gk,), (f"gt{j}",), fam=f"gt{j}")
            C("vector", lambda E, j=j, cs=cs: E.tensor_tensor(sb[f"gt{j}"][:], sb[f"gt{j}"][:], sb["mbc"][:, cs:cs + SBT], ALU.subtract),
              (f"gt{j}", "mbc_v", "mbc"), (f"gt{j}",))
            C("vector", lambda E, j=j, cs=cs: E.tensor_tensor(sb[f"gt{j}"][:], sb[f"gt{j}"][:], sb["rbc"][:, cs:cs + SBT], ALU.mult),
              (f"gt{j}", "rbc_v", "rbc"), (f"gt{j}",))
            C("scalar", lambda E, j=j, t=t: E.activation(sb["xn_hi"][:, t * SBT:(t + 1) * SBT], sb[f"gt{j}"][:], AF.Copy),
              (f"gt{j}",), (f"xnh{t}",))
            C("vector", lambda E, j=j, t=t: E.tensor_tensor(
                sb["xn_lo"][:, t * SBT:(t + 1) * SBT], sb[f"gt{j}"][:], sb["xn_hi"][:, t * SBT:(t + 1) * SBT], ALU.subtract),
                (f"gt{j}", f"xnh{t}"), (f"xnl{t}",))
            if DEBUG:
                dma("sync", dbg_xn[t * 128:(t + 1) * 128, k * LC + cs:k * LC + cs + SBT], sb[f"gt{j}"][:],
                    (f"gt{j}",), (f"dbgxn{k}_{sbi}_{t}",), fam="dbg")
        nm = [0]
        for t in range(NKT):
            for (wk, xk) in (("hi", "hi"), ("hi", "lo"), ("lo", "hi")):
                C("tensor", lambda E, t=t, wk=wk, xk=xk, s=(nm[0] == 0), st=(nm[0] == 95): E.matmul(
                    ps["rtl"][0:8, :], sb[f"wr_{wk}"][:, t * 8:(t + 1) * 8],
                    sb["xn_hi" if xk == "hi" else "xn_lo"][:, t * SBT:(t + 1) * SBT],
                    start=s, stop=st),
                    (f"wr_{wk}", f"xn{'h' if xk == 'hi' else 'l'}{t}"), ("rtl",))
                nm[0] += 1
        C("scalar", lambda E: E.activation(sb["rt8"][:], ps["rtl"][0:8, :], AF.Identity, bias=sb["br"][:, 0:1]),
          ("rtl", "br"), ("rt8",))
        dma("sync", wla[:, 0:4096], lah_hi, ("lahH", "wla0", "wla1"), ("wla0", "wla1"), fam="wla0f")
        dma("sync", wla[:, 4096:8192], lah_lo, ("lahL", "wla2", "wla3"), ("wla2", "wla3"), fam="wla2f")
        nm = [0]
        for t in range(NKT):
            for (wk, xk) in (("hi", "hi"), ("hi", "lo"), ("lo", "hi")):
                off = t * 128 if wk == "hi" else 4096 + t * 128
                rk = ("wla0", "wla1") if wk == "hi" else ("wla2", "wla3")
                C("tensor", lambda E, off=off, t=t, xk=xk, s=(nm[0] == 0), st=(nm[0] == 95): E.matmul(
                    ps["rtl"][:], wla[:, off:off + 128],
                    sb["xn_hi" if xk == "hi" else "xn_lo"][:, t * SBT:(t + 1) * SBT],
                    start=s, stop=st),
                    rk + (f"xn{'h' if xk == 'hi' else 'l'}{t}", "rt8"), ("rtl",))
                nm[0] += 1
        C("scalar", lambda E: E.activation(sb["lh"][:], ps["rtl"][:], AF.Copy),
          ("rtl",), ("lh",))
        lgt_v = sb["lgt"][:].rearrange("p (e g) -> p e g", g=2)
        for g in range(2):
            C("tensor", lambda E, g=g: E.transpose(ps["misc"][:, g * 8:(g + 1) * 8],
                                                   sb["rt8"][:, g * 128:(g + 1) * 128], sb["idn"][0:8, 0:8]),
              ("rt8", "idn"), ("misc",))
            C("scalar", lambda E, g=g, lv=lgt_v: E.activation(lv[:, :, g], ps["misc"][:, g * 8:(g + 1) * 8], AF.Copy),
              ("misc",), ("lgt",))
        sparsemax_ops(sbi)
        wf_keys = tuple(f"wf{ee}" for ee in range(8))
        wfin_v = sb["wfin"][:].rearrange("p (e g) -> p e g", g=2)
        for g in range(2):
            C("tensor", lambda E, g=g, wv=wfin_v: E.transpose(ps["misc"][0:8, 128 + g * 128:128 + (g + 1) * 128],
                                                              wv[:, :, g], sb["idn"][:]),
              wf_keys + ("idn",), ("misc",))
            C("scalar", lambda E, g=g: E.activation(sb["wt8"][:, g * 128:(g + 1) * 128],
                                                    ps["misc"][0:8, 128 + g * 128:128 + (g + 1) * 128], AF.Copy),
              ("misc",), ("wt8",))
        C("tensor", lambda E: E.matmul(ps["misc"][:, 0:SBT], sb["sel8"][:], sb["wt8"][:], start=True, stop=True),
          ("sel8", "wt8"), ("misc",))
        C("scalar", lambda E: E.activation(sb["wbc"][:], ps["misc"][:, 0:SBT], AF.Copy),
          ("misc",), ("wbc",))
        C("vector", lambda E: E.tensor_tensor(sb["lh"][:], sb["lh"][:], sb["wbc"][:], ALU.mult),
          ("lh", "wbc"), ("hw", "lh"))
        C("scalar", lambda E: E.activation(sb["hw_hi"][:], sb["lh"][:], AF.Copy),
          ("hw",), ("hw_hi",))
        C("vector", lambda E: E.tensor_tensor(sb["hw_lo"][:], sb["lh"][:], sb["hw_hi"][:], ALU.subtract),
          ("hw", "hw_hi"), ("hw_lo",))

    pend_stats = []
    pend_res = []

    def emit_res_stats():
        while pend_res:
            i, j = pend_res.pop(0)
            C("tensor", lambda E, j=j, s=(i == 0), t=(i == NDT - 1): E.matmul(
                ps["bst"][32:33, :], sb["ones_c"][:, 0:1], sb[f"res{j}"][:], start=s, stop=t),
                ("ones_c", f"res{j}"), ("st", "st0"))
            C("tensor", lambda E, j=j, s=(i == 0), t=(i == NDT - 1): E.matmul(
                ps["bst"][64:65, :], sb["ones_c"][:, 0:1], sb[f"yh{j}"][:], start=s, stop=t),
                ("ones_c", f"yh{j}"), ("st", "st1"))

    def emit_one_res():
        i, j = pend_res.pop(0)
        C("tensor", lambda E, j=j, s=(i == 0), t=(i == NDT - 1): E.matmul(
            ps["bst"][32:33, :], sb["ones_c"][:, 0:1], sb[f"res{j}"][:], start=s, stop=t),
            ("ones_c", f"res{j}"), ("st", "st0"))
        C("tensor", lambda E, j=j, s=(i == 0), t=(i == NDT - 1): E.matmul(
            ps["bst"][64:65, :], sb["ones_c"][:, 0:1], sb[f"yh{j}"][:], start=s, stop=t),
            ("ones_c", f"yh{j}"), ("st", "st1"))

    def emit_ff_stats(keep=2):
        while len(pend_stats) > keep:
            o, sbi, jo, sqn, cs = pend_stats.pop(0)
            C("tensor", lambda E, jo=jo, cs=cs, s=(o == 0), t=(o == NOT - 1): E.matmul(
                ps["xp2"][0:1, cs:cs + SBT], sb["ones_c"][:, 0:1], sb[f"fft{jo}"][:, 0:SBT], start=s, stop=t),
                ("ones_c", f"fft{jo}"), ("xp2", "xp2s"))
            C("tensor", lambda E, sqn=sqn, cs=cs, s=(o == 0), t=(o == NOT - 1): E.matmul(
                ps["xp2"][32:33, cs:cs + SBT], sb["ones_c"][:, 0:1], sb[sqn][:], start=s, stop=t),
                ("ones_c", sqn), ("xp2", "xp2q"))

    def ffn_otile(k, sbi, o):
        flush_q(pend_ffsq, keep=0)
        emit_ff_stats(keep=1)
        flush_stores(keep=2)
        while len(pend_res) > 3:
            emit_one_res()
        cs = sbi * SBT
        jo = o % 2
        fb = f"ff{jo}"
        rsrc = ag1ia[k] if o < 8 else ag1ib[k]
        dma("sync", sb[f"rr{jo}"][:], rsrc[(o % 8) * 128:(o % 8 + 1) * 128, cs:cs + SBT],
            (f"ag1w{k}_{o}",), (f"rr{jo}",), fam=f"rr{jo}")
        dma("sync", wla[:, 0:2048], wh_hi[o, :, 0:2048], (f"wbm{o}_0H", "wla0"), ("wla0",), fam="wla0f")
        dma("sync", wla[:, 4096:6144], wh_lo[o, :, 0:2048], (f"wbm{o}_0L", "wla2"), ("wla2",), fam="wla2f")
        dma("sync", wla[:, 2048:4096], wh_hi[o, :, 2048:4096], (f"wbm{o}_1H", "wla1"), ("wla1",), fam="wla1f")
        dma("sync", wla[:, 6144:8192], wh_lo[o, :, 2048:4096], (f"wbm{o}_1L", "wla3"), ("wla3",), fam="wla3f")
        dma("sync", sb[f"lbt_hi{jo}"][:], lbh_hi[:, o * 128:(o + 1) * 128], ("lbhH",), (f"lbt_hi{jo}",), fam=f"lbt{jo}")
        dma("sync", sb[f"lbt_lo{jo}"][:], lbh_lo[:, o * 128:(o + 1) * 128], ("lbhL",), (f"lbt_lo{jo}",), fam=f"lbt{jo}")
        nm = [0]
        for t in range(NKT):
            for (wk, xk) in (("hi", "hi"), ("hi", "lo"), ("lo", "hi")):
                off = (0 if wk == "hi" else 4096) + t * 128
                rk = "wla0" if off < 2048 else ("wla1" if off < 4096 else ("wla2" if off < 6144 else "wla3"))
                C("tensor", lambda E, off=off, t=t, xk=xk, s=(nm[0] == 0), fb=fb: E.matmul(
                    ps[fb][:], wla[:, off:off + 128],
                    sb["xn_hi" if xk == "hi" else "xn_lo"][:, t * SBT:(t + 1) * SBT],
                    start=s, stop=False),
                    (rk, f"xn{'h' if xk == 'hi' else 'l'}{t}"), (fb,))
                nm[0] += 1
        for pi, (wk, xk) in enumerate((("hi", "hi"), ("hi", "lo"), ("lo", "hi"))):
            C("tensor", lambda E, wk=wk, xk=xk, fb=fb, st=(pi == 2), jo=jo: E.matmul(
                ps[fb][:], sb[f"lbt_{wk}{jo}"][:], sb[f"hw_{xk}"][:],
                start=False, stop=st),
                (f"lbt_{wk}{jo}", f"hw_{xk}"), (fb,))
        jo4 = o % 4
        C("scalar", lambda E, o=o, fb=fb, jo4=jo4: E.activation(
            sb[f"fft{jo4}"][:], ps[fb][:], AF.Identity, bias=sb["bb"][:, o:o + 1]),
            (fb, "bb"), (f"fft{jo4}",))
        C("vector", lambda E, jo=jo, cs=cs: E.tensor_tensor(sb[f"rr{jo}"][:], sb[f"rr{jo}"][:], sb["mbc"][:, cs:cs + SBT], ALU.subtract),
          (f"rr{jo}", "mbc_v", "mbc"), (f"rr{jo}",))
        C("vector", lambda E, jo=jo, cs=cs: E.tensor_tensor(sb[f"rr{jo}"][:], sb[f"rr{jo}"][:], sb["rbc"][:, cs:cs + SBT], ALU.mult),
          (f"rr{jo}", "rbc_v", "rbc"), (f"rr{jo}",))
        C("vector", lambda E, jo4=jo4: E.tensor_tensor(sb[f"fft{jo4}"][:], sb[f"fft{jo4}"][:], sb[f"rr{jo}"][:], ALU.add),
          (f"fft{jo4}", f"rr{jo}"), (f"fft{jo4}",))
        pend_stores.append(lambda o=o, cs=cs, jo4=jo4, k=k, sbi=sbi: dma(
            "gpsimd", r2scr[o * 128:(o + 1) * 128, cs:cs + SBT], sb[f"fft{jo4}"][:],
            (f"fft{jo4}",), (f"r2_{o}_{sbi}",), fam=f"r2s{k % 2}"))
        sqn = "sq2" if jo == 0 else "sq2b"
        pend_ffsq.append(lambda jo4=jo4, sqn=sqn: C(
            "scalar", lambda E, jo4=jo4, sqn=sqn: E.activation(sb[sqn][:], sb[f"fft{jo4}"][:], AF.Square),
            (f"fft{jo4}",), (sqn,)))
        pend_stats.append((o, sbi, jo4, sqn, cs))

    def ffn_fin(k):
        c0 = k * LC
        flush_q(pend_ffsq, keep=0)
        flush_stores()
        emit_ff_stats(keep=0)
        C("scalar", lambda E: E.activation(sb["stat_sb"][0:1, :], ps["xp2"][0:1, :], AF.Copy),
          ("xp2s", "xp2q", "xp2"), ("stat_sb",))
        C("scalar", lambda E: E.activation(sb["stat_sb"][32:33, :], ps["xp2"][32:33, :], AF.Copy),
          ("xp2s", "xp2q", "xp2"), ("stat_sb",))
        dma("scalar", ag2i[k][0:1, :], sb["stat_sb"][0:1, :], ("stat_sb",), (f"ag2w{k}",), fam="ag2w")
        dma("scalar", ag2i[k][1:2, :], sb["stat_sb"][32:33, :], ("stat_sb",), (f"ag2w{k}b",), fam="ag2w")
        P.op("gpsimd", lambda E: E.collective_compute(
            "AllReduce", ALU.add, replica_groups=RG, ins=[ag2i[k]], outs=[ag2o[k]]),
            reads=(f"ag2w{k}", f"ag2w{k}b"), writes=(f"gath2_{k}",), kind="cc", name=f"ar2_{k}", fam=f"car2{k % 2}")
        stv = sb["stv"]
        dma("scalar", stv[:, 0:LC], ag2o[k][0:1, :], (f"gath2_{k}",), ("stv0",), fam="stv")
        dma("scalar", stv[:, LC:2 * LC], ag2o[k][1:2, :], (f"gath2_{k}",), ("stv1",), fam="stv")
        ln_stats("mbc_v", "rbc_v")
        for o in range(NOT):
            jo = o % 2
            dma("scalar", sb[f"r2l{jo}"][:], r2scr[o * 128:(o + 1) * 128, :],
                (f"r2_{o}_0", f"r2_{o}_1"), (f"xlr{jo}",), fam=f"finl{jo}")
            C("vector", lambda E, jo=jo: E.tensor_tensor(sb[f"r2l{jo}"][:], sb[f"r2l{jo}"][:], sb["mbc"][:], ALU.subtract),
              (f"xlr{jo}", "mbc_v", "mbc"), (f"xlr{jo}",))
            C("vector", lambda E, jo=jo: E.tensor_tensor(sb[f"r2l{jo}"][:], sb[f"r2l{jo}"][:], sb["rbc"][:], ALU.mult),
              (f"xlr{jo}", "rbc_v", "rbc"), (f"xlr{jo}",))
            dma("scalar", out_c[o * 128:(o + 1) * 128, c0:c0 + LC], sb[f"r2l{jo}"][:],
                (f"xlr{jo}",), (f"outw{k}_{o}",), fam="out")

    SERIAL = SERIAL_EMIT
    for k in range(NCH):
        if DEBUG and k == 1:
            dma("sync", dbg_cr, sb["carry"][:],
                tuple(f"carry{cc2}" for cc2 in range(NDT * NST)), ("dbgcr",), fam="dbg")
        scan_side(k)
        for h in range(2):
            bcast_half(k, h)
            if k > 0 and not SERIAL:
                ffn_prep(k - 1, h)
            dtile(k, h, 0)
            for i in range(NDT):
                if i + 1 < NDT:
                    dtile(k, h, i + 1)
                for q in range(2):
                    quarter(k, h, i, q)
                dtile_end(k, h, i)
                if h == 1 and i == 10:
                    ag1a(k)
                if k > 0 and not SERIAL:
                    ffn_otile(k - 1, h, i)
                elif k == 0:
                    while len(pend_res) > 2:
                        emit_one_res()
                    emit_wcast(2)
            if h == 1 and k > 0 and not SERIAL:
                ffn_fin(k - 1)
        ag1(k)
    emit_wcast(64)
    if SERIAL:
        for k in range(NCH):
            for sbi in range(2):
                ffn_prep(k, sbi)
                for o in range(NOT):
                    ffn_otile(k, sbi, o)
            ffn_fin(k)
    else:
        for sbi in range(2):
            ffn_prep(NCH - 1, sbi)
            for o in range(NOT):
                ffn_otile(NCH - 1, sbi, o)
        ffn_fin(NCH - 1)

    P.op("vector", lambda E: E.engine_nop(),
         reads=tuple(f"outw{k}_{o}" for k in range(NCH) for o in range(NOT)),
         writes=(), kind="c", name="fin")

    P.finalize()
    global LAST_PROG
    LAST_PROG = P
    P.check_deadlock()
    with nc.Block() as block:
        @block.sync
        def _(E):
            P.emit_engine("sync", E, chans)

        @block.vector
        def _(E):
            P.emit_engine("vector", E, chans)

        @block.scalar
        def _(E):
            P.emit_engine("scalar", E, chans)

        @block.gpsimd
        def _(E):
            P.emit_engine("gpsimd", E, chans)

        @block.tensor
        def _(E):
            P.emit_engine("tensor", E, chans)
    stack.close()
    return nc


def make_inputs(ins):
    x = np.ascontiguousarray(ins["L0_attn_x"])
    A_log = np.asarray(ins["L0_attn_A_log"])
    Dv = np.asarray(ins["L0_attn_D"])
    xpw = np.asarray(ins["L0_attn_x_proj_w"])
    dtw = np.asarray(ins["L0_attn_dt_proj_w"])
    dtb = np.asarray(ins["L0_attn_dt_proj_b"])
    Wb = np.asarray(ins["L0_ffn_W_base"])
    bb = np.asarray(ins["L0_ffn_b_base"])
    Wr = np.asarray(ins["L0_ffn_W_router"])
    br = np.asarray(ins["L0_ffn_b_router"])
    lA = np.asarray(ins["L0_ffn_lora_A"])
    lB = np.asarray(ins["L0_ffn_lora_B"])

    idn = np.eye(128, dtype=np.float32)
    sel8 = np.zeros((8, 128), np.float32)
    for ee in range(8):
        sel8[ee, ee * 16:(ee + 1) * 16] = 1.0
    xpwT = np.ascontiguousarray(xpw.T)
    lap = np.ascontiguousarray(lA.transpose(1, 0, 2).reshape(D, 128))
    lbp = np.ascontiguousarray(lB.reshape(128, D))
    WrT = np.ascontiguousarray(Wr.T)

    maps = []
    for c in range(8):
        b, h = c // 2, c % 2
        own = slice(h * DH, (h + 1) * DH)
        xb = x[b].T
        xt_c = np.concatenate([xb[own], xb[(1 - h) * DH:(2 - h) * DH]], axis=0)
        perm = np.concatenate([np.arange(h * DH, (h + 1) * DH), np.arange((1 - h) * DH, (2 - h) * DH)])
        xpw_c = xpwT[perm]
        xpw_t = np.ascontiguousarray(xpw_c.reshape(32, 128, XPR).transpose(1, 0, 2).reshape(128, 32 * XPR))
        dtw_c = np.ascontiguousarray(dtw[own].T)
        alog_c = np.ascontiguousarray(A_log[own].reshape(NDT, 128, NST).transpose(1, 0, 2).reshape(128, NDT * NST))
        d_c = np.ascontiguousarray(Dv[own].reshape(NDT, 128).T)
        dtb_c = np.ascontiguousarray(dtb[own].reshape(NDT, 128).T)
        Wslice = Wb[own, :]
        wbs_c = np.ascontiguousarray(
            Wslice.reshape(NOT, 128, NKT, 128).transpose(0, 3, 2, 1).reshape(NOT, 128, NKT * 128))
        bb_c = np.ascontiguousarray(bb[own].reshape(NOT, 128).T)
        wrt_c = np.ascontiguousarray(WrT.reshape(NKT, 128, 8).transpose(1, 0, 2).reshape(128, NKT * 8))
        las_c = np.ascontiguousarray(lap.reshape(NKT, 128, 128).transpose(1, 0, 2).reshape(128, NKT * 128))
        lbs_c = np.ascontiguousarray(lbp[:, own])
        maps.append({
            "xt": xt_c, "xpw": xpw_t, "dtwi": dtw_c, "alog": alog_c,
            "dvec": d_c, "dtbi": dtb_c, "wbs": wbs_c, "bbt": bb_c,
            "wrti": wrt_c, "brt": br.reshape(8, 1).astype(np.float32),
            "las": las_c, "lbsi": lbs_c, "idni": idn, "sel8i": sel8,
        })
    return maps


_NC_CACHE = [None]


def kernel(**inputs):
    maps = make_inputs(inputs)
    if _NC_CACHE[0] is None:
        _NC_CACHE[0] = build_nc()
    nc = _NC_CACHE[0]
    res = run_bass_kernel_spmd(nc, maps, core_ids=list(range(8)))
    out = np.empty((B, L, D), np.float32)
    for c in range(8):
        b, h = c // 2, c % 2
        oc = res.results[c]["out_c"]
        out[b, :, h * DH:(h + 1) * DH] = oc.T
    return out


